# version 14
# baseline (speedup 1.0000x reference)
import math
import numpy as np
import ml_dtypes
from contextlib import ExitStack

import concourse.bass as bass
import concourse.mybir as mybir
from concourse.bass_utils import run_bass_kernel_spmd

F32 = mybir.dt.float32
BF16 = mybir.dt.bfloat16
I32 = mybir.dt.int32
AF = mybir.ActivationFunctionType
ALU = mybir.AluOpType
AX = mybir.AxisListType

S, D, H = 2048, 2048, 16
DN, DR, DV = 64, 32, 64
RQ, RKV = 768, 256
DFF = 8192
NSEQ = 2
SC = 1.0 / math.sqrt(DN + DR)
NEG = -1.0e30
LN_EPS, RMS_EPS, Q_EPS = 1e-5, 1e-6, 1e-5
TC = 512
NTC = S // TC
NKT = S // 128
ENGS = ("tensor", "vector", "scalar", "gpsimd", "sync")
SHUF = list(range(16, 32)) + list(range(16))


class Sem:
    def __init__(self, nc, es, name):
        self.h = es.enter_context(nc.semaphore(name))
        self.v = 0
        self.name = name
        self.engs = set()

    def take(self, n=1):
        self.v += n
        return (self.h, n)


class Builder:
    def __init__(self, nc, es, dbg_xp=False):
        self.nc = nc
        self.es = es
        self.dbg_xp = dbg_xp
        self.streams = {e: [] for e in ENGS}
        self.sems = {}
        self.gates = {}
        self.pp_steps = []

    def sem(self, name):
        if name not in self.sems:
            self.sems[name] = Sem(self.nc, self.es, name)
        return self.sems[name]

    def _emit(self, eng, fn):
        self.streams[eng].append(fn)

    def dma(self, eng, dst, src, sem):
        sem.engs.add(eng)
        tgt = sem.take(16)
        self.dma_sems = getattr(self, "dma_sems", {})
        self.dma_sems[(eng, id(sem))] = sem
        self._emit(eng, lambda e, dst=dst, src=src, tgt=tgt:
                   e.dma_start(dst, src).then_inc(*tgt))

    def wait(self, eng, sem, val=None):
        v = sem.v if val is None else val
        if v > 0:
            self._emit(eng, lambda e, h=sem.h, v=v: e.wait_ge(h, v))

    def op(self, eng, fn, sem=None, n=1):
        if sem is None:
            self._emit(eng, fn)
        else:
            sem.engs.add(eng)
            tgt = sem.take(n)
            self._emit(eng, lambda e, fn=fn, tgt=tgt: fn(e).then_inc(*tgt))

    def chain(self, eng, steps, final_sem, n=1):
        s = self.sem(f"s_ch_{eng}")
        for i, fn in enumerate(steps):
            last = (i == len(steps) - 1)
            self.op(eng, fn, final_sem if last else s, n=n if last else 1)
            if not last:
                self.wait(eng, s)

    def g_wait(self, eng, key):
        if key in self.gates:
            s, v = self.gates[key]
            self.wait(eng, s, v)

    def g_set(self, key, sem):
        self.gates[key] = (sem, sem.v)

    def pump(self, k=2):
        for _ in range(min(k, len(self.pp_steps))):
            lim = getattr(self, "pp_limit", None)
            if lim is not None and getattr(self, "pp_popped", 0) >= lim:
                return
            eng, fn = self.pp_steps.pop(0)
            self.pp_popped = getattr(self, "pp_popped", 0) + 1
            self._emit(eng, fn)

    def flush_pp(self):
        while self.pp_steps:
            eng, fn = self.pp_steps.pop(0)
            self._emit(eng, fn)

    def dop(self, eng, fn, sem=None, n=1):
        if sem is None:
            self.pp_steps.append((eng, fn))
        else:
            sem.engs.add(eng)
            tgt = sem.take(n)
            self.pp_steps.append((eng, lambda e, fn=fn, tgt=tgt:
                                  fn(e).then_inc(*tgt)))

    def dstep(self, eng, waits, fn, sem=None, n=1):
        cw = [(s.h, (s.v if v is None else v)) for (s, v) in waits]
        cw = [(h, v) for (h, v) in cw if v > 0]
        if sem is None:
            self.pp_steps.append((eng, lambda e, cw=cw, fn=fn:
                                  ([e.wait_ge(h, v) for h, v in cw],
                                   fn(e))[-1]))
        else:
            sem.engs.add(eng)
            tgt = sem.take(n)
            self.pp_steps.append((eng, lambda e, cw=cw, fn=fn, tgt=tgt:
                                  ([e.wait_ge(h, v) for h, v in cw],
                                   fn(e).then_inc(*tgt))[-1]))

    def dwait(self, eng, sem, val=None):
        v = sem.v if val is None else val
        if v > 0:
            self.pp_steps.append((eng, lambda e, h=sem.h, v=v: e.wait_ge(h, v)))

    def declare_io(self):
        nc = self.nc
        EI, EO, IN = "ExternalInput", "ExternalOutput", "Internal"
        self.d_x = nc.dram_tensor("x_in", [NSEQ, S, D], F32, kind=EI).ap()
        self.d_wcq = nc.dram_tensor("w_cq", [D, RQ], F32, kind=EI).ap()
        self.d_wqn = nc.dram_tensor("w_qn", [RQ, H * DN], F32, kind=EI).ap()
        self.d_wqr = nc.dram_tensor("w_qr", [RQ, H * DR], F32, kind=EI).ap()
        self.d_qg = nc.dram_tensor("q_g", [128, 6], F32, kind=EI).ap()
        self.d_wckv = nc.dram_tensor("w_ckv", [D, RKV], F32, kind=EI).ap()
        self.d_wkn = nc.dram_tensor("w_kn", [RKV, H * DN], F32, kind=EI).ap()
        self.d_wv = nc.dram_tensor("w_v", [RKV, H * DV], F32, kind=EI).ap()
        self.d_kvg = nc.dram_tensor("kv_g", [128, 2], F32, kind=EI).ap()
        self.d_wkr = nc.dram_tensor("w_kr", [D, DR], F32, kind=EI).ap()
        self.d_wpr = nc.dram_tensor("w_pr", [128, 8 * D], F32, kind=EI).ap()
        self.d_w1T = nc.dram_tensor("w1T", [D, DFF], F32, kind=EI).ap()
        self.d_w2T = nc.dram_tensor("w2T", [DFF, D], F32, kind=EI).ap()
        self.d_ident = nc.dram_tensor("ident", [128, 128], BF16, kind=EI).ap()
        self.d_cones = nc.dram_tensor("cones", [96, 33], BF16, kind=EI).ap()
        self.d_ones64 = nc.dram_tensor("ones64", [1, 64], BF16, kind=EI).ap()
        self.d_onesbc = nc.dram_tensor("onesbc", [1, 128], F32, kind=EI).ap()
        self.d_onescol = nc.dram_tensor("onescol", [128, 1], F32, kind=EI).ap()
        self.d_onescolb = nc.dram_tensor("onescolb", [128, 1], BF16, kind=EI).ap()
        self.d_crq = nc.dram_tensor("crq", [32, S], BF16, kind=EI).ap()
        self.d_crk = nc.dram_tensor("crk", [32, S], BF16, kind=EI).ap()
        self.d_sig = nc.dram_tensor("sig", [32, 1], F32, kind=EI).ap()
        self.d_mask = nc.dram_tensor("mask", [128, 4 * 512], BF16, kind=EI).ap()
        self.d_out = nc.dram_tensor("out", [NSEQ, S, D], F32, kind=EO).ap()
        self.d_xp = nc.dram_tensor("xp_scr", [NSEQ, S, D], BF16, kind=IN).ap()
        self.d_w1q = nc.dram_tensor("w1q_c", [16, 16, 128, 512], BF16, kind=IN).ap()
        self.d_w2q = nc.dram_tensor("w2q_c", [4, 64, 128, 512], BF16, kind=IN).ap()
        if self.dbg_xp:
            self.d_xpdbg = nc.dram_tensor("xp_dbg", [NSEQ, S, D], F32, kind=EO).ap()
            self.d_dbg_qk = nc.dram_tensor("dbg_qk", [97, H * S], F32, kind=EO).ap()
            self.d_dbg_qa = nc.dram_tensor("dbg_qa", [97, H * TC], F32, kind=EO).ap()
            self.d_dbg_af = nc.dram_tensor("dbg_af", [128, 8 * TC], F32, kind=EO).ap()
            self.d_dbg_vt = nc.dram_tensor("dbg_vt", [128, NKT * H * 65], F32, kind=EO).ap()
            self.d_dbg_sr = nc.dram_tensor("dbg_sr", [64, 512], F32, kind=EO).ap()
            self.d_dbg_nq = nc.dram_tensor("dbg_nq", [128, 6 * S], F32, kind=EO).ap()
            self.d_dbg_y1 = nc.dram_tensor("dbg_y1", [128, 16 * TC], F32, kind=EO).ap()
            self.d_dbg_st = nc.dram_tensor("dbg_st", [128, 16], F32, kind=EO).ap()
            self.d_dbg_cq = nc.dram_tensor("dbg_cq", [128, 6 * TC], F32, kind=EO).ap()

    def alloc_persistent(self):
        nc, es = self.nc, self.es
        SB = lambda n, sh, dt=F32: es.enter_context(nc.sbuf_tensor(n, sh, dt))
        PS = lambda n, sh, dt=F32: es.enter_context(nc.psum_tensor(n, sh, dt))
        self.pA, self.pB = PS("pA", [128, 512]), PS("pB", [128, 512])
        self.pC, self.pD = PS("pC", [128, 512]), PS("pD", [128, 512])
        self.pE, self.pF = PS("pE", [128, 512]), PS("pF", [128, 512])
        self.pT = [PS("pT0", [128, 1024], BF16), PS("pT1", [128, 1024], BF16)]
        self.rot = [self.pA, self.pB, self.pC, self.pD]
        self.bname = {id(self.pA): "A", id(self.pB): "B", id(self.pC): "C",
                      id(self.pD): "D", id(self.pE): "E", id(self.pF): "F",
                      id(self.pT[0]): "T0", id(self.pT[1]): "T1"}

        self.c_ident = SB("c_ident", [128, 128], BF16)
        self.c_cones = SB("c_cones", [96, 33], BF16)
        self.c_ones64 = SB("c_ones64", [1, 64], BF16)
        self.c_onesbc = SB("c_onesbc", [1, 128], F32)
        self.c_onescol = SB("c_onescol", [128, 1], F32)
        self.c_onescolb = SB("c_onescolb", [128, 1], BF16)
        self.c_crq = SB("c_crq", [96, S], BF16)
        self.c_crk = SB("c_crk", [96, S], BF16)
        self.c_sig = SB("c_sig", [96, 1], F32)
        self.c_mask = SB("c_mask", [128, 4 * 512], BF16)
        self.gam = SB("gam", [128, 8], F32)
        self.rowb = SB("rowb", [128, 16], F32)
        self.stats = SB("stats", [128, 16], F32)

        s = self.sem("s_const")
        for dst, src in [(self.c_ident[:, :], self.d_ident),
                         (self.c_cones[:, :], self.d_cones),
                         (self.c_ones64[:, :], self.d_ones64),
                         (self.c_onesbc[:, :], self.d_onesbc),
                         (self.c_onescol[:, :], self.d_onescol),
                         (self.c_onescolb[:, :], self.d_onescolb),
                         (self.c_crq[64:96, :], self.d_crq),
                         (self.c_crk[64:96, :], self.d_crk),
                         (self.c_sig[64:96, :], self.d_sig),
                         (self.c_mask[:, :], self.d_mask)]:
            self.dma("gpsimd", dst, src, s)
        for eng in ("tensor", "vector", "scalar"):
            self.wait(eng, s)

    def barrier(self):
        sb = self.sem("s_bar")
        self._barn = getattr(self, "_barn", 0) + 1
        self.op("tensor", lambda t: t.nop(), sb)
        self.op("vector", lambda v: v.engine_nop(), sb)
        self.op("scalar", lambda x: x.nop(), sb)
        self.op("gpsimd", lambda g: g.engine_nop(), sb)
        tgt = 4 * self._barn
        assert sb.v == tgt
        for eng in ("tensor", "vector", "scalar", "gpsimd"):
            self.wait(eng, sb, tgt)

    def pbank(self, eng, pp):
        self.g_wait(eng, ("bank", self.bname[id(pp)]))

    def pbank_d(self, eng, pp):
        key = ("bank", self.bname[id(pp)])
        if key in self.gates:
            s, v = self.gates[key]
            self.dwait(eng, s, v)

    def freeze_dump(self, pairs):
        if not self.dbg_xp:
            return
        self.barrier()
        sd = self.sem("s_frz")
        for dst, src in pairs:
            self.dma("gpsimd", dst, src, sd)
        for eng in ("tensor", "vector", "scalar", "gpsimd"):
            self.wait(eng, sd)

    def emit_prepass_streams(self, es):
        nc = self.nc
        SB = lambda n, sh, dt=F32: self.es.enter_context(nc.sbuf_tensor(n, sh, dt))
        self.pwl = [SB(f"pwl{i}", [128, 512], F32) for i in range(4)]
        self.ptau = [SB(f"ptau{i}", [128, 512], BF16) for i in range(2)]
        self.pacc = SB("pacc", [128, 8], F32)
        self.prow = SB("prow", [128, 2], F32)
        self.gamrep = [SB(f"sgnm{i}", [128, 512], BF16) for i in range(2)]
        self.gamneg = [SB(f"sgnp{i}", [128, 512], BF16) for i in range(2)]
        pwl, ptau, pacc, prow = (self.pwl, self.ptau, self.pacc, self.prow)
        pdump = ptau[0]

        s_gld = [self.sem(f"s_gld{i}") for i in range(4)]
        s_gab = [self.sem(f"s_gab{i}") for i in range(4)]
        s_gfr = [self.sem("s_gfr0"), self.sem("s_gfr1")]
        gj = ([(self.d_w1T[r * 128:(r + 1) * 128, c * 512:(c + 1) * 512], 0)
               for r in range(16) for c in range(16)]
              + [(self.d_w2T[r * 128:(r + 1) * 128, c * 512:(c + 1) * 512], 1)
                 for r in range(64) for c in range(4)])
        for idx, (src, col) in enumerate(gj):
            bb = idx % 4
            par = idx % 2
            slot = col * 4 + idx % 4
            if idx >= 4:
                self.wait("sync", s_gab[bb], idx // 4)
            self.dma("sync", pwl[bb][:, :], src, s_gld[bb])
            sw = [(s_gld[bb], None)]
            if idx >= 2:
                sw.append((s_gfr[par], None))
            self.dstep("scalar", sw, lambda e, bb=bb, par=par:
                       e.activation(pdump[:, :], pwl[bb][:, :], AF.Abs,
                                    bias=0.0, scale=1.0,
                                    accum_out=prow[:, par:par + 1]), s_gab[bb])
            def vadd(g, par=par, slot=slot, first=(idx < 4 or 256 <= idx < 260)):
                if first:
                    return g.tensor_copy(pacc[:, slot:slot + 1],
                                         prow[:, par:par + 1])
                return g.tensor_add(pacc[:, slot:slot + 1], pacc[:, slot:slot + 1],
                                    prow[:, par:par + 1])
            self.dstep("vector", [(s_gab[bb], None)], vadd, s_gfr[par])
        self.dop("vector", lambda g: g.engine_nop(), self.sem("s_gred"))
        self.pp_gamma_end = len(self.pp_steps)
        self.pp_limit = self.pp_gamma_end

        s_qd = [self.sem("s_qd0"), self.sem("s_qd1")]
        s_qf = [self.sem("s_qf0"), self.sem("s_qf1")]
        qj = []
        for r in range(16):
            for c in range(16):
                qj.append((self.d_w1T[r * 128:(r + 1) * 128, c * 512:(c + 1) * 512],
                           self.d_w1q[c, r], 0))
        for c in range(4):
            for r in range(64):
                qj.append((self.d_w2T[r * 128:(r + 1) * 128, c * 512:(c + 1) * 512],
                           self.d_w2q[c, r], 1))
        for b in range(4):
            self.wait("sync", s_gab[b])
        s_qsg = [self.sem(f"s_qsg{i}") for i in range(4)]
        self.pp_steps.append(("scalar", lambda e: e.wait_ge(self.sem("s_gamma").h, 1)))
        for idx, (src, dst, w) in enumerate(qj):
            bb = idx % 4
            par = idx % 2
            if idx >= 4:
                self.wait("sync", s_qsg[bb], idx // 4)
            self.dma("sync", pwl[bb][:, :], src, s_qd[par])
            bpos, bneg = (self.gam[:, 0:1], self.gam[:, 1:2]) if w == 0 else \
                         (self.gam[:, 2:3], self.gam[:, 3:4])
            sw = [(s_qd[par], None)]
            if idx >= 2:
                sw.append((s_qf[par], None))
            def sgn2(e, bb=bb, par=par, bpos=bpos, bneg=bneg):
                e.activation(self.gamrep[par][:, :], pwl[bb][:, :], AF.Sign,
                             bias=bneg, scale=1.0)
                return e.activation(self.gamneg[par][:, :], pwl[bb][:, :],
                                    AF.Sign, bias=bpos, scale=1.0)
            self.dstep("scalar", sw, sgn2, s_qsg[bb])
            self.dstep("vector", [(s_qsg[bb], None)], lambda v, par=par:
                       v.tensor_add(ptau[par][:, :], self.gamrep[par][:, :],
                                    self.gamneg[par][:, :]), s_qf[par])
            self.wait("sync", s_qf[par])
            self.dma("sync", dst, ptau[par][:, :], s_qd[par])
            if idx == 255:
                self.pp_w1q_end = len(self.pp_steps)
            if idx >= 256 and (idx - 255) % 64 == 0:
                self.pp_w2g_end = getattr(self, "pp_w2g_end", {})
                self.pp_w2g_end[(idx - 319) // 64] = len(self.pp_steps)
            if idx == 255:
                self.wait("sync", s_qd[0])
                self.wait("sync", s_qd[1])
                self.op("sync", lambda sy: sy.nop(), self.sem("s_cachew1"))
            elif idx >= 256 and (idx - 255) % 64 == 0:
                self.wait("sync", s_qd[0])
                self.wait("sync", s_qd[1])
                self.op("sync", lambda sy: sy.nop(), self.sem("s_w2c"))
        self.op("sync", lambda sy: sy.nop(), self.sem("s_cache"))

    def flush_gamma(self):
        while getattr(self, "pp_popped", 0) < getattr(self, "pp_gamma_end", 0) \
                and self.pp_steps:
            self.pump(1)

    def emit_gamma_finalize(self):
        self.flush_gamma()
        self.pp_limit = None
        """Small tensor/vector chain turning pacc into the gam columns.
        Emitted at a controlled point between attention phases."""
        pE, rowb, gam = self.pE, self.rowb, self.gam
        self.wait("tensor", self.sem("s_gred"), 1)
        self.pbank("tensor", pE)
        self.op("tensor", lambda t: t.matmul(pE[0:1, 0:8], self.c_onescol[:, :],
                                             self.pacc[:, 0:8], start=True, stop=True),
                self.sem("s_gsum"))
        self.wait("vector", self.sem("s_gsum"), 1)
        def gfin_a(v):
            return v.tensor_scalar_mul(rowb[0:1, 4:12], pE[0:1, 0:8],
                                       float(2.0 ** -24))
        def gfin_b(v):
            v.tensor_add(rowb[0:1, 12:13], rowb[0:1, 4:5], rowb[0:1, 5:6])
            v.tensor_add(rowb[0:1, 13:14], rowb[0:1, 6:7], rowb[0:1, 7:8])
            v.tensor_add(rowb[0:1, 14:15], rowb[0:1, 8:9], rowb[0:1, 9:10])
            return v.tensor_add(rowb[0:1, 15:16], rowb[0:1, 10:11],
                                rowb[0:1, 11:12])
        def gfin_c(v):
            v.scalar_tensor_tensor(rowb[0:1, 0:1], rowb[0:1, 12:13], 1.0,
                                   rowb[0:1, 13:14], ALU.mult, ALU.add)
            return v.scalar_tensor_tensor(rowb[0:1, 1:2], rowb[0:1, 14:15], 1.0,
                                          rowb[0:1, 15:16], ALU.mult, ALU.add)
        def gfin_d(v):
            return v.tensor_scalar(rowb[0:1, 0:2], rowb[0:1, 0:2], 1.0, Q_EPS,
                                   ALU.mult, ALU.add)
        def gfin_e(v):
            return v.tensor_scalar_mul(rowb[0:1, 2:4], rowb[0:1, 0:2], 0.5)
        self.chain("vector", [gfin_a, gfin_b, gfin_c, gfin_d, gfin_e],
                   self.sem("s_gfin"))
        self.wait("tensor", self.sem("s_gfin"), 1)
        self.op("tensor", lambda t: t.matmul(pE[:, 0:4], self.c_onesbc[:, :],
                                             rowb[0:1, 0:4], start=True, stop=True),
                self.sem("s_gbc"))
        self.wait("vector", self.sem("s_gbc"), 1)
        def gbc_a(v):
            v.tensor_copy(gam[:, 0:1], pE[:, 2:3])
            v.tensor_scalar_mul(gam[:, 1:2], pE[:, 2:3], -1.0)
            v.tensor_copy(gam[:, 2:3], pE[:, 3:4])
            v.tensor_scalar_mul(gam[:, 3:4], pE[:, 3:4], -1.0)
            v.tensor_copy(gam[:, 6:7], pE[:, 0:1])
            return v.tensor_scalar_mul(gam[:, 5:6], pE[:, 1:2], 1.0 / 254.0)
        def gbc_b(v):
            return v.scalar_tensor_tensor(gam[:, 4:5], gam[:, 6:7], 4096.0,
                                          gam[:, 6:7], ALU.mult, ALU.mult)
        self.chain("vector", [gbc_a, gbc_b], self.sem("s_gamma"))
        self.g_set(("bank", "E"), self.sem("s_gamma"))

    def alloc_attn_outer(self, esA):
        nc = self.nc
        SB = lambda n, sh, dt=F32: esA.enter_context(nc.sbuf_tensor(n, sh, dt))
        a = self
        a.qk_k = SB("qk_k", [97, H * S], BF16)
        a.v_t = SB("v_t", [128, NKT * H * 65], BF16)
        a.nq = SB("nq", [128, 6 * S], BF16)
        a.kmx = SB("kmx", [97, H], F32)
        a.r32 = SB("r32", [97, TC], BF16)
        a.crow = SB("crow", [97, TC], BF16)
        a.qsq = [SB(f"qsq{i}", [96, TC], BF16) for i in range(2)]

    def alloc_nkv(self, es, si):
        self.nkv = es.enter_context(
            self.nc.sbuf_tensor(f"nkv_{si}", [128, 2 * S], BF16))

    def alloc_p1(self, es, si):
        nc = self.nc
        SB = lambda n, sh, dt=F32: es.enter_context(nc.sbuf_tensor(n + f"_{si}", sh, dt))
        a = self
        a.xb = [SB(f"xb{i}", [128, D], BF16) for i in range(2)]
        a.y1t = SB("y1t", [128, 16 * TC], BF16)
        a.cq_fm = SB("cq_fm", [128, 6 * TC], BF16)
        a.sq_t = [SB(f"sq_t{i}", [128, TC], BF16) for i in range(2)]
        a.rowc = SB("rowc", [1, 4 * TC], F32)
        a.lndump = SB("lndump", [128, 1024], BF16)
        a.wl1 = [SB(f"wl1{i}", [128, RQ], BF16) for i in range(2)]
        a.wl2 = [SB(f"wl2{i}", [128, RKV + DR], BF16) for i in range(2)]
        a.ag_q = SB("ag_q", [128, 6], F32)
        a.ag_kv = SB("ag_kv", [128, 2], F32)
        sg = self.sem("s_gains")
        self.dma("gpsimd", a.ag_q[:, :], self.d_qg, sg)
        self.dma("gpsimd", a.ag_kv[:, :], self.d_kvg, sg)

    def alloc_p2(self, es, si):
        nc = self.nc
        SB = lambda n, sh, dt=F32: es.enter_context(nc.sbuf_tensor(n + f"_{si}", sh, dt))
        a = self
        a.aw_kn = SB("aw_kn", [128, 2 * H * DN], BF16)
        a.aw_v = SB("aw_v", [128, 2 * H * DV], BF16)
        sw = self.sem("s_p2w")
        rr = lambda d: d.rearrange("(c p) o -> p c o", p=128)
        self.dma("gpsimd", a.aw_kn[:, :], rr(self.d_wkn), sw)
        self.dma("gpsimd", a.aw_v[:, :], rr(self.d_wv), sw)
        self.wait("tensor", sw)

    def alloc_p3(self, es, si):
        nc = self.nc
        SB = lambda n, sh, dt=F32: es.enter_context(nc.sbuf_tensor(n + f"_{si}", sh, dt))
        a = self
        a.aw_qn = SB("aw_qn", [128, 6 * H * DN], BF16)
        a.aw_qr = SB("aw_qr", [128, 6 * H * DR], BF16)
        a.q_aug = SB("q_aug", [97, H * TC], BF16)
        a.att_fm = SB("att_fm", [128, 8 * TC], BF16)
        a.att_tmp = SB("att_tmp", [64, TC], BF16)
        a.sexp = [SB(f"sexp{i}", [128, 512], BF16) for i in range(2)]
        a.den_b = SB("den_b", [65, 512], BF16)
        a.srec = SB("srec", [64, 512], F32)
        a.wpb = [SB(f"wpb{i}", [128, 512], BF16) for i in range(2)]
        a.xb2 = [SB(f"xb2_{i}", [128, 512], BF16) for i in range(2)]
        a.xpst = [SB(f"xpst_{i}", [128, 512], BF16) for i in range(2)]
        sw = self.sem("s_p3w")
        rr = lambda d: d.rearrange("(c p) o -> p c o", p=128)
        self.dma("gpsimd", a.aw_qn[:, :], rr(self.d_wqn), sw)
        self.dma("gpsimd", a.aw_qr[:, :], rr(self.d_wqr), sw)
        self.wait("tensor", sw)

    def emit_p1(self, si):
        a = self
        xb = a.xb; y1t = a.y1t; cq_fm = a.cq_fm; sq_t = a.sq_t; rowc = a.rowc; lndump = a.lndump; wl1 = a.wl1; wl2 = a.wl2; ag_q = a.ag_q; ag_kv = a.ag_kv; nkv = a.nkv; stats = a.stats
        sfx = f"_{si}"
        xs = self.d_x[si]
        s_xld = [self.sem(f"s_xld{i}") for i in range(2)]
        s_sqs = self.sem(f"s_sqs")
        s_ln1 = self.sem(f"s_ln1")
        s_y1 = self.sem(f"s_y1")
        s_tp = self.sem(f"s_tp")
        s_tpe = self.sem(f"s_tpe")
        s_lat = self.sem(f"s_lat")
        s_late = self.sem(f"s_late")
        s_row = self.sem(f"s_row")
        s_rbc = self.sem(f"s_rbc")
        s_nrm = self.sem(f"s_nrm")
        s_kve = self.sem(f"s_kve")

        for tc in range(NTC):
            self.pump(12)
            for tt in range(4):
                ti = tc * 4 + tt
                par = ti % 2
                self.g_wait("gpsimd", ("xb", par))
                self.dma("gpsimd", xb[par][:, :], xs[ti * 128:(ti + 1) * 128, :],
                         s_xld[par])
                pc = par * 8
                self.wait("scalar", s_xld[par])
                self.g_wait("scalar", ("stats", par))
                self.op("scalar", lambda s, par=par, pc=pc:
                        s.activation(lndump[:, :], xb[par][:, 0:1024], AF.Square,
                                     bias=0.0, scale=1.0,
                                     accum_out=stats[:, pc:pc + 1]), s_sqs)
                self.wait("scalar", s_sqs)
                self.op("scalar", lambda s, par=par, pc=pc:
                        s.activation(lndump[:, :], xb[par][:, 1024:2048],
                                     AF.Square, bias=0.0, scale=1.0,
                                     accum_out=stats[:, pc + 2:pc + 3]), s_sqs)
                self.wait("vector", s_xld[par])
                self.wait("vector", s_sqs)
                self.op("vector", lambda v, par=par, pc=pc:
                        v.reduce_sum(stats[:, pc + 1:pc + 2], xb[par][:, :],
                                     axis=AX.X), self.sem("s_rr"))
                self.wait("vector", self.sem("s_rr"))
                def lnv_a(v, pc=pc):
                    st = stats
                    v.tensor_add(st[:, pc:pc + 1], st[:, pc:pc + 1], st[:, pc + 2:pc + 3])
                    return v.tensor_scalar_mul(st[:, pc + 1:pc + 2],
                                               st[:, pc + 1:pc + 2], 1.0 / D)
                def lnv_b(v, pc=pc):
                    st = stats
                    v.tensor_scalar(st[:, pc:pc + 1], st[:, pc:pc + 1], 1.0 / D,
                                    LN_EPS, ALU.mult, ALU.add)
                    return v.tensor_tensor(st[:, pc + 2:pc + 3], st[:, pc + 1:pc + 2],
                                           st[:, pc + 1:pc + 2], ALU.mult)
                def lnv_c(v, pc=pc):
                    st = stats
                    return v.tensor_sub(st[:, pc:pc + 1], st[:, pc:pc + 1],
                                        st[:, pc + 2:pc + 3])
                self.chain("vector", [lnv_a, lnv_b, lnv_c], s_ln1)
                self.wait("scalar", s_ln1)
                self.op("scalar", lambda s, pc=pc:
                        s.activation(stats[:, pc + 3:pc + 4], stats[:, pc:pc + 1],
                                     AF.Sqrt, bias=0.0, scale=1.0), s_sqs)
                self.wait("vector", s_sqs)
                def l2_a(v, pc=pc):
                    return v.reciprocal(stats[:, pc + 4:pc + 5],
                                        stats[:, pc + 3:pc + 4])
                def l2_b(v, pc=pc):
                    return v.scalar_tensor_tensor(stats[:, pc + 6:pc + 7],
                                                  stats[:, pc + 1:pc + 2], -1.0,
                                                  stats[:, pc + 4:pc + 5],
                                                  ALU.mult, ALU.mult)
                def l2_c(v, par=par, pc=pc):
                    return v.tensor_scalar(xb[par][:, :], xb[par][:, :],
                                           stats[:, pc + 4:pc + 5],
                                           stats[:, pc + 6:pc + 7],
                                           ALU.mult, ALU.add)
                self.chain("vector", [l2_a, l2_b, l2_c], s_y1)
                self.g_set(("stats", par), s_y1)
                if self.dbg_xp and si == 0 and ti == 0:
                    self.freeze_dump([(self.d_dbg_st, stats[:, :]),
                                      (self.d_dbg_y1[0:128, 0:2048],
                                       xb[par][:, :])])
                self.wait("tensor", s_y1)
                if tt == 0:
                    self.g_wait("vector", ("y1t",))
                for fb in range(16):
                    ptp = self.pT[fb % 2]
                    self.pbank("tensor", ptp)
                    self.op("tensor", lambda t, par=par, fb=fb, ptp=ptp:
                            t.transpose(ptp[:, 0:128],
                                        xb[par][:, fb * 128:(fb + 1) * 128],
                                        self.c_ident[:, :]), s_tp)
                    self.wait("vector", s_tp)
                    self.op("vector", lambda v, fb=fb, tt=tt, ptp=ptp:
                            v.tensor_copy(y1t[:, fb * TC + tt * 128:
                                                fb * TC + (tt + 1) * 128],
                                          ptp[:, 0:128]), s_tpe)
                    self.g_set(("bank", self.bname[id(ptp)]), s_tpe)
                self.op("vector", lambda v: v.engine_nop(), self.sem("s_dummy"))
                self.g_set(("xb", par), s_tpe)

            s_wl1 = [self.sem(f"s_wl1{i}") for i in range(2)]
            s_wl2 = [self.sem(f"s_wl2{i}") for i in range(2)]
            self.wait("tensor", s_tpe)

            def evac_chunk(j, pp, last_cs):
                self.op("vector", lambda v, j=j, pp=pp:
                        v.tensor_copy(cq_fm[:, j * TC:(j + 1) * TC], pp[:, :]),
                        s_late)
                self.wait("scalar", s_late)
                self.g_wait("scalar", ("sq_t", j % 2))
                self.op("scalar", lambda s, j=j, pp=pp:
                        s.activation(sq_t[j % 2][:, :], pp[:, :], AF.Square,
                                     bias=0.0, scale=1.0), s_sqs)
                self.g_set(("bank", self.bname[id(pp)]), s_sqs)
                self.wait("tensor", s_sqs)
                if j == 0:
                    self.pbank("tensor", self.pF)
                self.op("tensor", lambda t, j=j, last_cs=last_cs:
                        t.matmul(self.pF[0:1, 0:TC], self.c_onescolb[:, :],
                                 sq_t[j % 2][:, :], start=(j == 0),
                                 stop=last_cs), s_row)
                self.g_set(("sq_t", j % 2), s_row)

            def rms_rows(rank, gain, ntens, n_ch, tc):
                self.wait("vector", s_row)
                self.op("vector", lambda v, rank=rank:
                        v.tensor_scalar(rowc[0:1, 0:TC], self.pF[0:1, 0:TC],
                                        1.0 / rank, RMS_EPS, ALU.mult, ALU.add),
                        s_row)
                self.g_set(("bank", "F"), s_row)
                self.wait("scalar", s_row)
                self.op("scalar", lambda s:
                        s.activation(rowc[0:1, TC:2 * TC], rowc[0:1, 0:TC],
                                     AF.Sqrt, bias=0.0, scale=1.0), s_sqs)
                self.wait("vector", s_sqs)
                self.op("vector", lambda v:
                        v.reciprocal(rowc[0:1, 2 * TC:3 * TC],
                                     rowc[0:1, TC:2 * TC]), s_row)
                self.wait("tensor", s_row)
                self.pbank("tensor", self.pF)
                self.op("tensor", lambda t:
                        t.matmul(self.pF[:, :], self.c_onesbc[:, :],
                                 rowc[0:1, 2 * TC:3 * TC],
                                 start=True, stop=True), s_rbc)
                self.wait("vector", s_rbc)
                if tc == 0:
                    self.g_wait("vector", ("nq",))
                    self.g_wait("vector", ("nkv",))
                for j in range(n_ch):
                    self.op("vector", lambda v, j=j, gain=gain, ntens=ntens, tc=tc:
                            v.scalar_tensor_tensor(
                                ntens[:, j * S + tc * TC:j * S + (tc + 1) * TC],
                                cq_fm[:, j * TC:(j + 1) * TC],
                                gain[:, j:j + 1], self.pF[:, :],
                                ALU.mult, ALU.mult), s_nrm)
                self.g_set(("bank", "F"), s_nrm)

            banks5 = [self.pA, self.pB, self.pC, self.pD, self.pE]
            for bk in banks5:
                self.pbank("tensor", bk)
            for i in range(16):
                par = i % 2
                self.g_wait("gpsimd", ("wl1", par))
                self.dma("gpsimd", wl1[par][:, :],
                         self.d_wcq[i * 128:(i + 1) * 128, :], s_wl1[par])
                self.wait("tensor", s_wl1[par])
                def lat1(t, i=i, par=par):
                    for oc in range(5):
                        li = t.matmul(banks5[oc][:, :],
                                      wl1[par][:, oc * 128:(oc + 1) * 128],
                                      y1t[:, i * TC:(i + 1) * TC],
                                      start=(i == 0), stop=(i == 15))
                    return li
                self.op("tensor", lat1, s_lat)
                self.g_set(("wl1", par), s_lat)
            base = s_lat.v
            for j in range(5):
                self.wait("vector", s_lat, base)
                if si == 0 and tc == 0 and j == 0:
                    self.wait("vector", self.sem("s_gains"))
                evac_chunk(j, banks5[j], last_cs=False)
            for i in range(16):
                par = i % 2
                self.g_wait("gpsimd", ("wl1", par))
                self.dma("gpsimd", wl1[par][:, :128],
                         self.d_wcq[i * 128:(i + 1) * 128, 5 * 128:6 * 128],
                         s_wl1[par])
                self.wait("tensor", s_wl1[par])
                if i == 0:
                    self.pbank("tensor", self.pA)
                def lat1b(t, i=i, par=par):
                    return t.matmul(self.pA[:, :], wl1[par][:, 0:128],
                                    y1t[:, i * TC:(i + 1) * TC],
                                    start=(i == 0), stop=(i == 15))
                self.op("tensor", lat1b, s_lat)
                self.g_set(("wl1", par), s_lat)
            self.wait("vector", s_lat)
            evac_chunk(5, self.pA, last_cs=True)
            rms_rows(RQ, ag_q, a.nq, 6, tc)

            self.pbank("tensor", self.pA)
            self.pbank("tensor", self.pB)
            self.pbank("tensor", self.pE)
            for i in range(16):
                par = i % 2
                self.g_wait("gpsimd", ("wl2", par))
                self.dma("gpsimd", wl2[par][:, 0:RKV],
                         self.d_wckv[i * 128:(i + 1) * 128, :], s_wl2[par])
                self.dma("gpsimd", wl2[par][:, RKV:RKV + DR],
                         self.d_wkr[i * 128:(i + 1) * 128, :], s_wl2[par])
                self.wait("tensor", s_wl2[par])
                def lat2(t, i=i, par=par):
                    t.matmul(self.pA[:, :], wl2[par][:, 0:128],
                             y1t[:, i * TC:(i + 1) * TC],
                             start=(i == 0), stop=(i == 15))
                    t.matmul(self.pB[:, :], wl2[par][:, 128:256],
                             y1t[:, i * TC:(i + 1) * TC],
                             start=(i == 0), stop=(i == 15))
                    return t.matmul(self.pE[64:96, :], wl2[par][:, 256:288],
                                    y1t[:, i * TC:(i + 1) * TC],
                                    start=(i == 0), stop=(i == 15))
                self.op("tensor", lat2, s_lat)
                self.g_set(("wl2", par), s_lat)
            self.g_set(("y1t",), s_lat)
            base = s_lat.v
            for j in range(2):
                self.wait("vector", s_lat, base)
                evac_chunk(j, [self.pA, self.pB][j], last_cs=(j == 1))
            rms_rows(RKV, ag_kv, nkv, 2, tc)

            if self.dbg_xp and si == 0 and tc == 0:
                sdbg = self.sem("s_dbg")
                self.wait("gpsimd", s_nrm)
                self.dma("gpsimd", self.d_dbg_cq, cq_fm[:, :], sdbg)
                self.wait("gpsimd", sdbg)
            self.wait("vector", s_lat)
            if si == 0 and tc == 0:
                self.wait("vector", self.sem("s_const"))
            self.g_wait("vector", ("crow",))
            self.g_wait("vector", ("qk_k",))
            def krope2(v, tc=tc):
                raw = a.r32[64:96, :]
                sh = a.crow[64:96, :]
                v.tensor_copy(raw, self.pE[64:96, :])
                v.stream_shuffle(sh, raw, SHUF)
                v.scalar_tensor_tensor(sh, sh, self.c_sig[64:96, :], raw,
                                       ALU.mult, ALU.add)
                return v.tensor_tensor(a.qk_k[64:96, tc * TC:(tc + 1) * TC], sh,
                                       self.c_crk[64:96, tc * TC:(tc + 1) * TC],
                                       ALU.mult)
            self.op("vector", krope2, s_kve)
            self.g_set(("bank", "E"), s_kve)
            self.g_set(("crow",), s_kve)

        def krep(v):
            for h in range(1, H):
                v.tensor_copy(a.qk_k[64:96, h * S:(h + 1) * S], a.qk_k[64:96, 0:S])
            return v.memset(a.qk_k[96:97, :], 1.0)
        self.op("vector", krep, s_kve)

    def emit_p2(self, si):
        a = self
        aw_kn = a.aw_kn; aw_v = a.aw_v; nkv = a.nkv
        sfx = f"_{si}"
        s_nrm = self.sem(f"s_nrm")
        s_kve = self.sem(f"s_kve")
        s_kv = self.sem(f"s_kv")
        s_kev = self.sem(f"s_kev")
        s_ksq = self.sem(f"s_ksq")
        s_kcs = self.sem(f"s_kcs")
        s_kmx = self.sem(f"s_kmx")

        self.g_wait("vector", ("v_t",))
        self.g_wait("vector", ("qk_k",))
        self.op("vector", lambda v: v.memset(a.v_t[:, :], 1.0), s_kev)

        self.wait("tensor", s_nrm)
        for h in range(H):
            for kc in range(NTC):
                self.pump(4)
                pp = self.rot[(h * NTC + kc) % 4]
                self.pbank("tensor", pp)
                def knmm(t, h=h, kc=kc, pp=pp):
                    for i in range(2):
                        li = t.matmul(pp[0:64, :],
                                      aw_kn[:, i * H * DN + h * DN:i * H * DN + (h + 1) * DN],
                                      nkv[:, i * S + kc * TC:i * S + (kc + 1) * TC],
                                      start=(i == 0), stop=(i == 1))
                    return li
                self.op("tensor", knmm, s_kv)
                self.wait("vector", s_kv)
                self.op("vector", lambda v, h=h, kc=kc, pp=pp:
                        v.tensor_copy(a.qk_k[0:64, h * S + kc * TC:h * S + (kc + 1) * TC],
                                      pp[0:64, :]), s_kev)
                self.g_set(("bank", self.bname[id(pp)]), s_kev)
        for kt in range(NKT):
            self.pump(4)
            for half in range(2):
                pp = self.rot[(kt % 2) * 2 + half]
                self.pbank("tensor", pp)
                def vmm(t, kt=kt, half=half, pp=pp):
                    for i in range(2):
                        li = t.matmul(pp[:, :],
                                      nkv[:, i * S + kt * 128:i * S + (kt + 1) * 128],
                                      aw_v[:, i * H * DV + half * 512:
                                             i * H * DV + (half + 1) * 512],
                                      start=(i == 0), stop=(i == 1))
                    return li
                self.op("tensor", vmm, s_kv)
                self.wait("vector", s_kv)
                self.op("vector", lambda v, kt=kt, half=half, pp=pp:
                        v.tensor_copy(
                            a.v_t[:, kt * H * 65 + half * 8 * 65:
                                  kt * H * 65 + (half * 8 + 8) * 65]
                            .rearrange("p (h e) -> p h e", e=65)[:, :, 0:64],
                            pp[:, :].rearrange("p (h e) -> p h e", e=64)), s_kev)
                self.g_set(("bank", self.bname[id(pp)]), s_kev)

        for h in range(H):
            self.pump(4)
            for kc in range(NTC):
                idx = h * NTC + kc
                qs = a.qsq[idx % 2]
                pX = [self.pE, self.pF][idx % 2]
                self.wait("scalar", s_kev)
                self.wait("scalar", s_kve)
                self.g_wait("scalar", ("qsq", idx % 2))
                self.op("scalar", lambda s, h=h, kc=kc, qs=qs:
                        s.activation(qs[0:96, :],
                                     a.qk_k[0:96, h * S + kc * TC:h * S + (kc + 1) * TC],
                                     AF.Square, bias=0.0, scale=1.0), s_ksq)
                self.wait("tensor", s_ksq)
                self.pbank("tensor", pX)
                self.op("tensor", lambda t, qs=qs, pX=pX:
                        t.matmul(pX[64:97, 0:TC], self.c_cones[:, :],
                                 qs[0:96, :], start=True, stop=True), s_kcs)
                self.g_set(("qsq", idx % 2), s_kcs)
                self.wait("vector", s_kcs)
                if h == 0 and kc == 0:
                    self.g_wait("vector", ("kmx",))
                if kc == 0:
                    self.op("vector", lambda v, h=h, pX=pX:
                            v.reduce_max(a.kmx[96:97, h:h + 1],
                                         pX[96:97, 0:TC], axis=AX.X), s_kmx)
                else:
                    self.op("vector", lambda v, pX=pX:
                            v.reduce_max(a.crow[96:97, 0:1],
                                         pX[96:97, 0:TC], axis=AX.X),
                            self.sem("s_rr"))
                    self.wait("vector", self.sem("s_rr"))
                    self.wait("vector", s_kmx)
                    self.op("vector", lambda v, h=h:
                            v.tensor_max(a.kmx[96:97, h:h + 1], a.kmx[96:97, h:h + 1],
                                         a.crow[96:97, 0:1]), s_kmx)
                self.g_set(("bank", self.bname[id(pX)]), s_kmx)
        self.wait("scalar", s_kmx)
        self.op("scalar", lambda s:
                s.activation(a.kmx[96:97, 0:H], a.kmx[96:97, 0:H], AF.Sqrt,
                             bias=0.0, scale=1.0), s_ksq)
        self.wait("vector", s_ksq)
        self.op("vector", lambda v:
                v.tensor_scalar_mul(a.kmx[96:97, 0:H], a.kmx[96:97, 0:H], -1.0),
                s_kmx)

    def emit_p3(self, si):
        a = self
        aw_qn = a.aw_qn; aw_qr = a.aw_qr; q_aug = a.q_aug; att_fm = a.att_fm; att_tmp = a.att_tmp; sexp = a.sexp; den_b = a.den_b; srec = a.srec; wpb = a.wpb; xb2 = a.xb2; xpst = a.xpst
        sfx = f"_{si}"
        s_nrm = self.sem(f"s_nrm")
        s_kmx = self.sem(f"s_kmx")
        s_kev = self.sem(f"s_kev")
        s_kve = self.sem(f"s_kve")
        s_qb = self.sem(f"s_qb")
        s_qbe = self.sem(f"s_qbe")
        s_qsq = self.sem(f"s_qsq")
        s_qcs = self.sem(f"s_qcs")
        s_crw = self.sem(f"s_crw")
        s_qk = self.sem(f"s_qk")
        s_msk = self.sem(f"s_msk")
        s_exp = self.sem(f"s_exp")
        s_pvk = self.sem(f"s_pvk")
        s_dnc = self.sem(f"s_dnc")
        s_dnd = self.sem(f"s_dnd")
        s_dbc = self.sem(f"s_dbc")
        s_rcp = self.sem(f"s_rcp")
        s_div = self.sem(f"s_div")
        s_odm = self.sem(f"s_odm")
        s_wp = [self.sem(f"s_wp{i}") for i in range(2)]
        s_op = self.sem(f"s_op")
        s_ope = self.sem(f"s_ope")
        s_x2 = [self.sem(f"s_x2{i}") for i in range(2)]
        s_xpw = self.sem(f"s_xpw")
        xs = self.d_x[si]
        xps = self.d_xp[si]

        self.wait("tensor", s_kev)
        self.wait("tensor", s_kve)
        for qc in range(NTC):
            nkt = 4 * (qc + 1)
            if qc == 0:
                self.wait("tensor", s_nrm)
            self.g_wait("tensor", ("q_aug",))
            qbv = {}

            def SQNR(h, qc=qc):
                self.pump(4)
                ppn = self.rot[(2 * h) % 4]
                ppr = self.rot[(2 * h + 1) % 4]
                self.pbank("tensor", ppn)
                def qn(t, h=h, ppn=ppn, qc=qc):
                    for i in range(6):
                        li = t.matmul(ppn[0:64, :],
                                      aw_qn[:, i * H * DN + h * DN:i * H * DN + (h + 1) * DN],
                                      a.nq[:, i * S + qc * TC:i * S + (qc + 1) * TC],
                                      start=(i == 0), stop=(i == 5))
                    return li
                self.op("tensor", qn, s_qb)
                self.pbank("tensor", ppr)
                def qr(t, h=h, ppr=ppr, qc=qc):
                    for i in range(6):
                        li = t.matmul(ppr[64:96, :],
                                      aw_qr[:, i * H * DR + h * DR:i * H * DR + (h + 1) * DR],
                                      a.nq[:, i * S + qc * TC:i * S + (qc + 1) * TC],
                                      start=(i == 0), stop=(i == 5))
                    return li
                self.op("tensor", qr, s_qb)
                self.wait("vector", s_qb)
                def qev(v, h=h, ppn=ppn, ppr=ppr, qc=qc):
                    dn = q_aug[0:64, h * TC:(h + 1) * TC]
                    dr = q_aug[64:96, h * TC:(h + 1) * TC]
                    raw = a.r32[64:96, :]
                    v.tensor_scalar_mul(dn, ppn[0:64, :], SC)
                    v.tensor_copy(raw, ppr[64:96, :])
                    v.stream_shuffle(dr, raw, SHUF)
                    v.scalar_tensor_tensor(dr, dr, self.c_sig[64:96, :], raw,
                                           ALU.mult, ALU.add)
                    return v.tensor_tensor(dr, dr,
                                           self.c_crq[64:96, qc * TC:(qc + 1) * TC],
                                           ALU.mult)
                self.op("vector", qev, s_qbe)
                qbv[h] = s_qbe.v
                self.g_set(("bank", self.bname[id(ppn)]), s_qbe)
                self.g_set(("bank", self.bname[id(ppr)]), s_qbe)

            def SCHAIN(h, qc=qc):
                qs = a.qsq[h % 2]
                pX = [self.pE, self.pF][h % 2]
                self.wait("scalar", s_qbe, qbv[h])
                self.g_wait("scalar", ("qsq", h % 2))
                self.op("scalar", lambda s, h=h, qs=qs:
                        s.activation(qs[0:96, :], q_aug[0:96, h * TC:(h + 1) * TC],
                                     AF.Square, bias=0.0, scale=1.0), s_qsq)
                self.wait("tensor", s_qsq)
                self.pbank("tensor", pX)
                self.op("tensor", lambda t, qs=qs, pX=pX:
                        t.matmul(pX[64:97, 0:TC], self.c_cones[:, :],
                                 qs[0:96, :], start=True, stop=True), s_qcs)
                self.g_set(("qsq", h % 2), s_qcs)
                self.wait("scalar", s_qcs)
                self.g_wait("scalar", ("crow",))
                self.op("scalar", lambda s, pX=pX:
                        s.activation(a.crow[96:97, :], pX[96:97, 0:TC], AF.Sqrt,
                                     bias=0.0, scale=1.0), s_qsq)
                self.g_set(("bank", self.bname[id(pX)]), s_qsq)
                self.wait("vector", s_qsq)
                if qc == 0 and h == 0:
                    self.wait("vector", s_kmx)
                self.op("vector", lambda v, h=h:
                        v.tensor_single_scalar(q_aug[96:97, h * TC:(h + 1) * TC],
                                               a.crow[96:97, :],
                                               a.kmx[96:97, h:h + 1], ALU.mult),
                        s_crw)
                self.g_set(("crow",), s_crw)

            SQNR(0)
            for h in range(1, H):
                SQNR(h)
                SCHAIN(h - 1)
            SCHAIN(H - 1)
            if si == 0 and qc == 3:
                self.emit_gamma_finalize()

            self.wait("tensor", s_crw)
            pend_den = [None]

            def mk_den(h, pv, pvkv):
                def den():
                    self.wait("vector", s_pvk, pvkv)
                    self.g_wait("vector", ("den",))
                    self.op("vector", lambda v, pv=pv:
                            v.tensor_copy(den_b[64:65, :], pv[64:65, :]), s_dnc)
                    self.wait("scalar", s_dnc)
                    self.dma("scalar", den_b[0:1, :], den_b[64:65, :], s_dnd)
                    self.wait("tensor", s_dnd)
                    self.pbank("tensor", self.pE)
                    self.op("tensor", lambda t:
                            t.matmul(self.pE[0:64, :], self.c_ones64[:, :],
                                     den_b[0:1, :], start=True, stop=True), s_dbc)
                    self.g_set(("den",), s_dbc)
                    self.wait("scalar", s_dbc)
                    self.pbank("scalar", self.pF)
                    self.g_wait("scalar", ("srec",))
                    def rcp(s):
                        s.activation(self.pF[0:64, :], self.pE[0:64, :], AF.Ln,
                                     bias=0.0, scale=1.0)
                        return s.activation(srec[:, :], self.pF[0:64, :], AF.Exp,
                                            bias=0.0, scale=-1.0)
                    self.op("scalar", rcp, s_rcp)
                    self.g_set(("bank", "E"), s_rcp)
                    self.g_set(("bank", "F"), s_rcp)
                    self.wait("vector", s_rcp)
                    even = (h % 2 == 0)
                    if even:
                        dst = att_fm[0:64, (h // 2) * TC:(h // 2 + 1) * TC]
                    else:
                        self.g_wait("vector", ("att_tmp",))
                        dst = att_tmp[0:64, :]
                    self.op("vector", lambda v, pv=pv, dst=dst:
                            v.tensor_mul(dst, pv[0:64, :], srec[:, :]), s_div)
                    self.g_set(("srec",), s_div)
                    self.g_set(("bank", self.bname[id(pv)]), s_div)
                    if not even:
                        self.wait("gpsimd", s_div)
                        self.dma("gpsimd",
                                 att_fm[64:128, (h // 2) * TC:(h // 2 + 1) * TC],
                                 att_tmp[0:64, :], s_odm)
                        self.g_set(("att_tmp",), s_odm)
                return den

            for h in range(H):
                self.pump(3)
                pv = [self.pC, self.pD][h % 2]
                self.pbank("tensor", pv)
                expv = {}

                def SQK(kt, h=h, qc=qc):
                    psc = [self.pA, self.pB][kt % 2]
                    self.pbank("tensor", psc)
                    self.op("tensor", lambda t, h=h, kt=kt, psc=psc:
                            t.matmul(psc[:, :],
                                     a.qk_k[0:97, h * S + kt * 128:h * S + (kt + 1) * 128],
                                     q_aug[0:97, h * TC:(h + 1) * TC],
                                     start=True, stop=True), s_qk)
                    return psc

                def SME(kt, psc, qc=qc):
                    diag = kt >= 4 * qc
                    if diag:
                        r = kt - 4 * qc
                        self.wait("vector", s_qk)
                        self.op("vector", lambda v, psc=psc, r=r:
                                v.tensor_add(psc[:, :], psc[:, :],
                                             self.c_mask[:, r * 512:(r + 1) * 512]),
                                s_msk)
                        self.wait("scalar", s_msk)
                    else:
                        self.wait("scalar", s_qk)
                    self.g_wait("scalar", ("sexp", kt % 2))
                    self.op("scalar", lambda s, psc=psc, kt=kt:
                            s.activation(sexp[kt % 2][:, :], psc[:, :], AF.Exp,
                                         bias=0.0, scale=1.0), s_exp)
                    expv[kt] = s_exp.v
                    self.g_set(("bank", self.bname[id(psc)]), s_exp)

                def SPV(kt, h=h, pv=pv, nkt=nkt):
                    self.wait("tensor", s_exp, expv[kt])
                    self.op("tensor", lambda t, h=h, kt=kt, pv=pv,
                            last=(kt == nkt - 1):
                            t.matmul(pv[0:65, :],
                                     a.v_t[:, kt * H * 65 + h * 65:
                                           kt * H * 65 + (h + 1) * 65],
                                     sexp[kt % 2][:, :],
                                     start=(kt == 0), stop=last), s_pvk)
                    self.g_set(("sexp", kt % 2), s_pvk)

                p0 = SQK(0)
                SME(0, p0)
                for kt in range(1, nkt):
                    p = SQK(kt)
                    if kt == 3 and pend_den[0] is not None:
                        pend_den[0]()
                        pend_den[0] = None
                    SME(kt, p)
                    SPV(kt - 1)
                SPV(nkt - 1)
                pend_den[0] = mk_den(h, pv, s_pvk.v)
            pend_den[0]()
            pend_den[0] = None

            self.wait("tensor", s_div)
            self.wait("tensor", s_odm)
            for dc in range(4):
                self.pump(3)
                for ic in range(8):
                    par = ic % 2
                    self.g_wait("gpsimd", ("wpb", par))
                    self.dma("gpsimd", wpb[par][:, :],
                             self.d_wpr[:, ic * D + dc * 512:
                                        ic * D + (dc + 1) * 512], s_wp[par])
                    if ic < 2:
                        tt = ic
                        self.g_wait("gpsimd", ("xb2", tt % 2))
                        self.dma("gpsimd", xb2[tt % 2][:, :],
                                 xs[(qc * 4 + tt) * 128:(qc * 4 + tt + 1) * 128,
                                    dc * 512:(dc + 1) * 512], s_x2[tt % 2])
                    self.wait("tensor", s_wp[par])
                    if ic == 0:
                        for tt in range(4):
                            self.pbank("tensor", self.rot[tt])
                    def opmm(t, ic=ic, par=par, qc=qc):
                        li = None
                        for tt in range(4):
                            li = t.matmul(self.rot[tt][:, :],
                                          att_fm[:, ic * TC + tt * 128:
                                                   ic * TC + (tt + 1) * 128],
                                          wpb[par][:, :],
                                          start=(ic == 0), stop=(ic == 7))
                        return li
                    self.op("tensor", opmm, s_op)
                    self.g_set(("wpb", par), s_op)
                opv = s_op.v
                for tt in range(4):
                    ti = qc * 4 + tt
                    self.wait("vector", s_op, opv)
                    self.wait("vector", s_x2[tt % 2])
                    self.g_wait("vector", ("xpst", tt % 2))
                    self.op("vector", lambda v, tt=tt:
                            v.scalar_tensor_tensor(
                                xpst[tt % 2][:, :], self.rot[tt][:, :], 1.0,
                                xb2[tt % 2][:, :], ALU.mult, ALU.add), s_ope)
                    self.g_set(("bank", self.bname[id(self.rot[tt])]), s_ope)
                    self.g_set(("xb2", tt % 2), s_ope)
                    if tt < 2:
                        nt = tt + 2
                        self.g_wait("gpsimd", ("xb2", nt % 2))
                        self.dma("gpsimd", xb2[nt % 2][:, :],
                                 xs[(qc * 4 + nt) * 128:(qc * 4 + nt + 1) * 128,
                                    dc * 512:(dc + 1) * 512], s_x2[nt % 2])
                    self.wait("scalar", s_ope)
                    self.dma("scalar",
                             xps[ti * 128:(ti + 1) * 128, dc * 512:(dc + 1) * 512],
                             xpst[tt % 2][:, :], s_xpw)
                    if self.dbg_xp:
                        self.wait("gpsimd", s_ope)
                        self.dma("gpsimd",
                                 self.d_xpdbg[si][ti * 128:(ti + 1) * 128,
                                                  dc * 512:(dc + 1) * 512],
                                 xpst[tt % 2][:, :], s_xpw)
                    self.g_set(("xpst", tt % 2), s_xpw)
            self.g_set(("att_fm",), s_op)
            self.g_set(("q_aug",), s_qk)
            if self.dbg_xp and si == 0 and qc == 0:
                sdbg = self.sem("s_dbg")
                self.wait("gpsimd", s_div)
                self.wait("gpsimd", s_odm)
                self.dma("gpsimd", self.d_dbg_qk, a.qk_k[:, :], sdbg)
                self.dma("gpsimd", self.d_dbg_qa, q_aug[:, :], sdbg)
                self.dma("gpsimd", self.d_dbg_af, att_fm[:, :], sdbg)
                self.dma("gpsimd", self.d_dbg_vt, a.v_t[:, :], sdbg)
                self.dma("gpsimd", self.d_dbg_sr, srec[:, :], sdbg)
                self.dma("gpsimd", self.d_dbg_nq, a.nq[:, :], sdbg)
                self.wait("gpsimd", sdbg)
        self.g_set(("nq",), s_qb)
        self.g_set(("nkv",), self.sem("s_kv"))
        self.g_set(("qk_k",), s_qk)
        self.g_set(("v_t",), s_pvk)
        self.g_set(("kmx",), s_crw)

    def emit_ffn(self, esF):
        a = self
        nc = self.nc
        SB = lambda n, sh, dt=F32: esF.enter_context(nc.sbuf_tensor(n, sh, dt))
        h_sb = SB("h_sb", [128, 64 * 512], BF16)
        xqT = [SB(f"xqT{i}", [128, 16 * 512], BF16) for i in range(2)]
        w1b = [SB(f"w1b{i}", [128, 4 * 512], BF16) for i in range(2)]
        w2b = [SB(f"w2b{i}", [128, 4 * 512], BF16) for i in range(2)]
        outst = [SB(f"outst{i}", [128, 512], F32) for i in range(2)]
        xw = [SB(f"xw{i}", [128, 4 * D], BF16) for i in range(2)]
        y2f = SB("y2f", [128, D], F32)
        fdump = y2f.bitcast(BF16)
        xq_t = [SB(f"xq_t{i}", [128, D], BF16) for i in range(2)]
        t32 = SB("t32", [128, 512], F32)
        ci32 = SB("ci32", [128, 512], I32)
        d32 = SB("d32", [128, 512], F32)
        gbc = SB("gbc", [128, 512], F32)
        fst = SB("fst", [128, 16], F32)
        fcol = SB("fcol", [128, 8], F32)
        hmxb = SB("hmxb", [128, 512], BF16)
        umrep = SB("umrep", [128, 128], BF16)
        umc = SB("umc", [128, 4], F32)
        rcol = SB("rcol", [128, 4], F32)

        s_xw = [self.sem(f"s_fxw{i}") for i in range(2)]
        s_sq = self.sem("s_fsq")
        s_ln = self.sem("s_fln")
        s_y2 = self.sem("s_fy2")
        s_xq = self.sem("s_fxq")
        s_tp = self.sem("s_ftp")
        s_tpe = self.sem("s_ftpe")
        s_w1 = [self.sem(f"s_fw1{i}") for i in range(2)]
        s_w2 = [self.sem(f"s_fw2{i}") for i in range(2)]
        s_m1 = self.sem("s_fm1")
        s_ev = self.sem("s_fev")
        s_mx = self.sem("s_fmx")
        s_row = self.sem("s_frow")
        s_rtp = self.sem("s_frtp")
        s_rte = self.sem("s_frte")
        s_gb = self.sem("s_fgb")
        s_hq = self.sem("s_fhq")
        s_m2 = self.sem("s_fm2")
        s_oev = self.sem("s_foev")
        s_out = self.sem("s_fout")

        def flush_to(mark):
            while getattr(self, "pp_popped", 0) < mark and self.pp_steps:
                self.pump(1)

        flush_to(getattr(self, "pp_w1q_end", 0))
        self.wait("scalar", self.sem("s_cachew1"))
        self.wait("gpsimd", self.sem("s_xpw"))
        self.wait("gpsimd", self.sem("s_cachew1"))

        NW = 2 * NTC
        xqv = {}

        def xw_loads(w):
            wp = w % 2
            xps = self.d_xp[w // NTC]
            t0 = (w % NTC) * TC
            self.g_wait("gpsimd", ("xw", wp))
            for tt in range(4):
                self.dma("gpsimd", xw[wp][:, tt * D:(tt + 1) * D],
                         xps[(t0 + tt * 128):(t0 + (tt + 1) * 128), :], s_xw[wp])

        def ln_tile(w, tt):
            wp = w % 2
            pc = (tt % 2) * 8
            xwt = xw[wp]
            self.wait("scalar", s_xw[wp])
            self.g_wait("scalar", ("fst", tt % 2))
            self.g_wait("scalar", ("y2f",))
            def sq2(s, tt=tt, pc=pc, xwt=xwt):
                s.activation(fdump[:, 0:1024], xwt[:, tt * D:tt * D + 1024],
                             AF.Square, bias=0.0, scale=1.0,
                             accum_out=fst[:, pc:pc + 1])
                return s.activation(fdump[:, 0:1024],
                                    xwt[:, tt * D + 1024:(tt + 1) * D],
                                    AF.Square, bias=0.0, scale=1.0,
                                    accum_out=fst[:, pc + 2:pc + 3])
            self.op("scalar", sq2, s_sq)
            self.wait("vector", s_xw[wp])
            self.wait("vector", s_sq)
            self.op("vector", lambda v, tt=tt, pc=pc, xwt=xwt:
                    v.reduce_sum(fst[:, pc + 1:pc + 2], xwt[:, tt * D:(tt + 1) * D],
                                 axis=AX.X), self.sem("s_rr"))
            self.wait("vector", self.sem("s_rr"))
            def flnv_a(v, pc=pc):
                st = fst
                v.tensor_add(st[:, pc:pc + 1], st[:, pc:pc + 1],
                             st[:, pc + 2:pc + 3])
                return v.tensor_scalar_mul(st[:, pc + 1:pc + 2],
                                           st[:, pc + 1:pc + 2], 1.0 / D)
            def flnv_b(v, pc=pc):
                st = fst
                v.tensor_scalar(st[:, pc:pc + 1], st[:, pc:pc + 1],
                                1.0 / D, LN_EPS, ALU.mult, ALU.add)
                return v.tensor_tensor(st[:, pc + 2:pc + 3], st[:, pc + 1:pc + 2],
                                       st[:, pc + 1:pc + 2], ALU.mult)
            def flnv_c(v, pc=pc):
                st = fst
                return v.tensor_sub(st[:, pc:pc + 1], st[:, pc:pc + 1],
                                    st[:, pc + 2:pc + 3])
            self.chain("vector", [flnv_a, flnv_b, flnv_c], s_ln)
            self.wait("scalar", s_ln)
            self.op("scalar", lambda s, pc=pc:
                    s.activation(fst[:, pc + 3:pc + 4], fst[:, pc:pc + 1],
                                 AF.Sqrt, bias=0.0, scale=1.0), s_sq)
            self.wait("vector", s_sq)
            def fl2_a(v, pc=pc):
                return v.reciprocal(fst[:, pc + 4:pc + 5], fst[:, pc + 3:pc + 4])
            def fl2_b(v, pc=pc):
                return v.scalar_tensor_tensor(fst[:, pc + 6:pc + 7],
                                              fst[:, pc + 1:pc + 2], -1.0,
                                              fst[:, pc + 4:pc + 5],
                                              ALU.mult, ALU.mult)
            def fl2_c(v, tt=tt, pc=pc, xwt=xwt):
                return v.tensor_scalar(y2f[:, :], xwt[:, tt * D:(tt + 1) * D],
                                       fst[:, pc + 4:pc + 5],
                                       fst[:, pc + 6:pc + 7], ALU.mult, ALU.add)
            self.chain("vector", [fl2_a, fl2_b, fl2_c], s_y2)
            self.g_set(("fst", tt % 2), s_y2)
            self.op("vector", lambda v, pc=pc:
                    v.reduce_max(fst[:, pc + 7:pc + 8], y2f[:, :], axis=AX.X,
                                 apply_absolute_value=True), self.sem("s_rr"))
            self.wait("vector", self.sem("s_rr"))
            def qa_a(v, pc=pc):
                return v.tensor_scalar_max(fst[:, pc + 7:pc + 8],
                                           fst[:, pc + 7:pc + 8], Q_EPS)
            def qa_b(v, tt=tt, pc=pc):
                v.tensor_scalar_mul(fcol[:, tt + 4:tt + 5],
                                    fst[:, pc + 7:pc + 8], 1.0 / 127.0)
                return v.reciprocal(fst[:, pc + 7:pc + 8], fst[:, pc + 7:pc + 8])
            def qa_c(v, tt=tt, pc=pc):
                v.scalar_tensor_tensor(fcol[:, tt:tt + 1], fcol[:, tt + 4:tt + 5],
                                       1.0, fcol[:, tt + 4:tt + 5],
                                       ALU.mult, ALU.mult)
                return v.tensor_scalar_mul(fst[:, pc + 7:pc + 8],
                                           fst[:, pc + 7:pc + 8], 127.0)
            def qa_d(v, tt=tt, pc=pc):
                v.tensor_tensor(fcol[:, tt:tt + 1], fcol[:, tt:tt + 1],
                                self.gam[:, 4:5], ALU.mult)
                return v.tensor_single_scalar(y2f[:, :], y2f[:, :],
                                              fst[:, pc + 7:pc + 8], ALU.mult)
            self.chain("vector", [qa_a, qa_b, qa_c, qa_d], s_y2)
            self.g_wait("vector", ("xq_t", tt % 2))
            xqt = xq_t[tt % 2]
            for c in range(4):
                def rnd(v, c=c, xqt=xqt):
                    sl = slice(c * 512, (c + 1) * 512)
                    v.tensor_copy(ci32[:, :], y2f[:, sl])
                    v.scalar_tensor_tensor(d32[:, :], ci32[:, :], -1.0,
                                           y2f[:, sl], ALU.mult, ALU.add)
                    v.scalar_tensor_tensor(t32[:, :], d32[:, :], 0.5,
                                           ci32[:, :], ALU.is_ge, ALU.add)
                    v.tensor_scalar(d32[:, :], d32[:, :], -0.5, -1.0,
                                    ALU.is_le, ALU.mult)
                    return v.tensor_add(xqt[:, sl], t32[:, :], d32[:, :])
                self.op("vector", rnd, s_xq)
            self.g_set(("y2f",), s_xq)
            xqv[(w, tt)] = s_xq.v

        def tp_tile(w, tt):
            wp = w % 2
            self.wait("tensor", s_xq, xqv[(w, tt)])
            if tt == 0:
                self.g_wait("tensor", ("xqT", wp))
            xqt = xq_t[tt % 2]
            for fb in range(16):
                ptp = self.pT[fb % 2]
                self.pbank("tensor", ptp)
                self.op("tensor", lambda t, fb=fb, ptp=ptp, xqt=xqt:
                        t.transpose(ptp[:, 0:128],
                                    xqt[:, fb * 128:(fb + 1) * 128],
                                    self.c_ident[:, :]), s_tp)
                self.wait("vector", s_tp)
                self.op("vector", lambda v, fb=fb, tt=tt, ptp=ptp, wp=wp:
                        v.tensor_copy(xqT[wp][:, fb * 512 + tt * 128:
                                              fb * 512 + (tt + 1) * 128],
                                      ptp[:, 0:128]), s_tpe)
                self.g_set(("bank", self.bname[id(ptp)]), s_tpe)
            self.g_set(("xq_t", tt % 2), s_tp)

        xw_loads(0)
        for tt in range(4):
            ln_tile(0, tt)
            tp_tile(0, tt)

        for w in range(NW):
            si = w // NTC
            t0 = (w % NTC) * TC
            wp = w % 2
            if w + 1 < NW:
                xw_loads(w + 1)

            self.wait("tensor", s_tpe)
            self.g_wait("tensor", ("h_sb",))
            for og in range(16):
                self.pump(3)
                for ih in range(4):
                    par = (og * 4 + ih) % 2
                    self.g_wait("scalar", ("w1b", par))
                    self.dma("scalar", w1b[par][:, :],
                             self.d_w1q[og, ih * 4:(ih + 1) * 4]
                             .rearrange("i p f -> p i f"), s_w1[par])
                    self.wait("tensor", s_w1[par])
                    for ocin in range(4):
                        oc = og * 4 + ocin
                        pp = self.rot[oc % 4]
                        if ih == 0:
                            self.pbank("tensor", pp)
                        def m1(t, par=par, ih=ih, ocin=ocin, pp=pp, wp=wp):
                            for ii in range(4):
                                i = ih * 4 + ii
                                li = t.matmul(pp[:, :],
                                              w1b[par][:, ii * 512 + ocin * 128:
                                                       ii * 512 + (ocin + 1) * 128],
                                              xqT[wp][:, i * 512:(i + 1) * 512],
                                              start=(i == 0), stop=(i == 15))
                            return li
                        self.op("tensor", m1, s_m1)
                        if ih == 3:
                            self.wait("scalar", s_m1)
                            def ev(s, oc=oc, pp=pp):
                                s.activation(h_sb[:, oc * 512:(oc + 1) * 512], pp[:, :],
                                             AF.Relu, bias=0.0, scale=1.0 / 128.0)
                                return s.activation(h_sb[:, oc * 512:(oc + 1) * 512],
                                                    h_sb[:, oc * 512:(oc + 1) * 512],
                                                    AF.Square, bias=0.0, scale=1.0)
                            self.op("scalar", ev, s_ev)
                            self.g_set(("bank", self.bname[id(pp)]), s_ev)
                            self.wait("vector", s_ev)
                            if oc == 0:
                                self.op("vector", lambda v:
                                        v.tensor_copy(hmxb[:, :], h_sb[:, 0:512]),
                                        s_mx)
                            else:
                                self.op("vector", lambda v, oc=oc:
                                        v.tensor_max(hmxb[:, :], hmxb[:, :],
                                                     h_sb[:, oc * 512:(oc + 1) * 512]),
                                        s_mx)
                    self.g_set(("w1b", par), s_m1)

            for c in range(4):
                ptp = self.pT[c % 2]
                self.wait("tensor", s_mx)
                self.pbank("tensor", ptp)
                self.op("tensor", lambda t, c=c, ptp=ptp:
                        t.transpose(ptp[:, 0:128], hmxb[:, c * 128:(c + 1) * 128],
                                    self.c_ident[:, :]), s_rtp)
                self.wait("vector", s_rtp)
                self.op("vector", lambda v, c=c, ptp=ptp:
                        v.reduce_max(umc[:, c:c + 1], ptp[:, 0:128], axis=AX.X),
                        self.sem("s_rr"))
                self.wait("scalar", self.sem("s_rr"))
                self.op("scalar", lambda s, c=c:
                        s.activation(umrep[:, :], self.c_ident[:, :], AF.Relu,
                                     bias=umc[:, c:c + 1], scale=0.0),
                        self.sem("s_rr"))
                self.wait("tensor", self.sem("s_rr"))
                self.g_set(("bank", self.bname[id(ptp)]), self.sem("s_rr"))
                self.pbank("tensor", ptp)
                self.op("tensor", lambda t, ptp=ptp:
                        t.transpose(ptp[:, 0:128], umrep[:, :], self.c_ident[:, :]),
                        s_rtp)
                self.wait("vector", s_rtp)
                self.op("vector", lambda v, c=c, ptp=ptp:
                        v.tensor_copy(gbc[:, c * 128:(c + 1) * 128], ptp[:, 0:128]),
                        s_rte)
                self.g_set(("bank", self.bname[id(ptp)]), s_rte)
            def gfin(v):
                v.tensor_scalar_max(gbc[:, :], gbc[:, :], 1e-30)
                v.reciprocal(gbc[:, :], gbc[:, :])
                return v.tensor_scalar_mul(gbc[:, :], gbc[:, :], 127.0)
            self.op("vector", gfin, s_gb)
            def rc_a(v):
                r = None
                for c in range(4):
                    r = v.tensor_tensor(rcol[:, c:c + 1], fcol[:, c:c + 1],
                                        umc[:, c:c + 1], ALU.mult)
                return r
            def rc_b(v):
                return v.tensor_scalar_max(rcol[:, :], rcol[:, :], Q_EPS)
            def rc_c(v):
                r = None
                for c in range(4):
                    r = v.tensor_tensor(rcol[:, c:c + 1], rcol[:, c:c + 1],
                                        self.gam[:, 5:6], ALU.mult)
                return r
            self.chain("vector", [rc_a, rc_b, rc_c], s_rte)

            hqb = s_hq.v
            for oc in range(64):
                sl = slice(oc * 512, (oc + 1) * 512)
                def hqc(v, sl=sl):
                    v.tensor_tensor(t32[:, :], h_sb[:, sl], gbc[:, :], ALU.mult)
                    v.tensor_copy(ci32[:, :], t32[:, :])
                    v.scalar_tensor_tensor(d32[:, :], ci32[:, :], -1.0,
                                           t32[:, :], ALU.mult, ALU.add)
                    return v.scalar_tensor_tensor(h_sb[:, sl], d32[:, :], 0.5,
                                                  ci32[:, :], ALU.is_ge, ALU.add)
                self.op("vector", hqc, s_hq)

            if w + 1 < NW:
                ln_tile(w + 1, 0)
                ln_tile(w + 1, 1)

            for dq in range(4):
                self.pump(2)
                for ob in range(16):
                    par = ob % 2
                    if w == 0 and ob == 0:
                        flush_to(self.pp_w2g_end[dq])
                        self.wait("scalar", self.sem("s_w2c"), dq + 1)
                    self.g_wait("scalar", ("w2b", par))
                    self.dma("scalar", w2b[par][:, :],
                             self.d_w2q[dq, ob * 4:(ob + 1) * 4]
                             .rearrange("o p f -> p o f"), s_w2[par])
                    self.wait("tensor", s_w2[par])
                    for oi in range(4):
                        o = ob * 4 + oi
                        if dq == 0:
                            self.wait("tensor", s_hq, hqb + o + 1)
                        if o == 0:
                            for tt in range(4):
                                self.pbank("tensor", self.rot[tt])
                        def m2(t, dq=dq, o=o, oi=oi, par=par):
                            li = None
                            for tt in range(4):
                                li = t.matmul(self.rot[tt][:, :],
                                              h_sb[:, o * 512 + tt * 128:
                                                   o * 512 + (tt + 1) * 128],
                                              w2b[par][:, oi * 512:(oi + 1) * 512],
                                              start=(o == 0), stop=(o == 63))
                            return li
                        self.op("tensor", m2, s_m2)
                    self.g_set(("w2b", par), s_m2)
                self.wait("vector", s_m2)
                self.wait("vector", s_rte)
                for tt in range(4):
                    pst = (dq * 4 + tt) % 2
                    self.g_wait("vector", ("outst", pst))
                    def oev(v, dq=dq, tt=tt, pst=pst, wp=wp):
                        return v.scalar_tensor_tensor(
                            outst[pst][:, :], self.rot[tt][:, :],
                            rcol[:, tt:tt + 1],
                            xw[wp][:, tt * D + dq * 512:tt * D + (dq + 1) * 512],
                            ALU.mult, ALU.add)
                    self.op("vector", oev, s_oev)
                    self.g_set(("bank", self.bname[id(self.rot[tt])]), s_oev)
                    self.wait("gpsimd", s_oev)
                    self.dma("gpsimd",
                             self.d_out[si][(t0 + tt * 128):(t0 + (tt + 1) * 128),
                                            dq * 512:(dq + 1) * 512],
                             outst[pst][:, :], s_out)
                    self.g_set(("outst", pst), s_out)
                if w + 1 < NW:
                    tp_tile(w + 1, dq)
                    if dq == 0:
                        ln_tile(w + 1, 2)
                    elif dq == 1:
                        ln_tile(w + 1, 3)
            self.g_set(("xw", wp), s_oev)
            self.g_set(("h_sb",), s_m2)
            self.g_set(("xqT", wp), s_m1)

        self.flush_pp()
        self.wait("gpsimd", s_out)
        for (eng, _), sem in getattr(self, "dma_sems", {}).items():
            self.wait(eng, sem)

    def finalize(self):
        blk = self.es.enter_context(self.nc.Block())
        for eng in ENGS:
            def mk(eng):
                def run(e):
                    for fn in self.streams[eng]:
                        fn(e)
                return run
            getattr(blk, eng)(mk(eng))


def build_program(dbg_xp=False):
    nc = bass.Bass("TRN2", target_bir_lowering=False,
                   detect_race_conditions=False)
    with ExitStack() as es:
        b = Builder(nc, es, dbg_xp=dbg_xp)
        b.declare_io()
        b.alloc_persistent()
        with ExitStack() as esA:
            b.emit_prepass_streams(esA)
            b.alloc_attn_outer(esA)
            for si in range(NSEQ):
                with ExitStack() as es12:
                    b.alloc_nkv(es12, si)
                    with ExitStack() as es1:
                        b.alloc_p1(es1, si)
                        b.emit_p1(si)
                    b.barrier()
                    b.alloc_p2(es12, si)
                    b.emit_p2(si)
                b.barrier()
                with ExitStack() as es3:
                    b.alloc_p3(es3, si)
                    b.emit_p3(si)
                b.barrier()
            b.barrier()
        with ExitStack() as esF:
            b.emit_ffn(esF)
        b.finalize()
    return nc


def _consts():
    c = {}
    c["ident"] = np.eye(128, dtype=ml_dtypes.bfloat16)
    c["cones"] = np.ones((96, 33), dtype=ml_dtypes.bfloat16)
    c["ones64"] = np.ones((1, 64), dtype=ml_dtypes.bfloat16)
    c["onesbc"] = np.ones((1, 128), dtype=np.float32)
    c["onescol"] = np.ones((128, 1), dtype=np.float32)
    c["onescolb"] = np.ones((128, 1), dtype=ml_dtypes.bfloat16)
    freqs = 1.0 / (10000.0 ** (np.arange(0, DR, 2, dtype=np.float64) / DR))
    ang = np.outer(np.arange(S, dtype=np.float64), freqs)
    cos = np.cos(ang).T
    crep = np.concatenate([cos, cos], axis=0)
    c["crq"] = (crep * SC).astype(ml_dtypes.bfloat16)
    c["crk"] = (crep / H).astype(ml_dtypes.bfloat16)
    c["sig"] = np.concatenate([-np.ones((16, 1)), np.ones((16, 1))]).astype(np.float32)
    m = np.zeros((128, 4 * 512), dtype=np.float32)
    kk = np.arange(128)[:, None]
    qq = np.arange(512)[None, :]
    for r in range(4):
        m[:, r * 512:(r + 1) * 512] = np.where(qq >= kk + r * 128, 0.0, NEG)
    c["mask"] = m.astype(ml_dtypes.bfloat16)
    return c


def _perm_rope_cols(w, nheads):
    rows = w.shape[0]
    w = w.reshape(rows, nheads, 16, 2)
    return np.ascontiguousarray(
        np.concatenate([w[:, :, :, 0], w[:, :, :, 1]], axis=2).reshape(rows, -1))


def prepare_base(L0_attn_x, L0_attn_w_cq, L0_attn_w_q_nope, L0_attn_w_q_rope,
                 L0_attn_q_g, L0_attn_w_ckv, L0_attn_w_k_nope, L0_attn_w_v,
                 L0_attn_kv_g, L0_attn_w_k_rope, L0_attn_w_proj, L0_ffn_w1,
                 L0_ffn_w2):
    x = np.ascontiguousarray(np.asarray(L0_attn_x, dtype=np.float32))
    f32 = lambda z: np.ascontiguousarray(np.asarray(z, dtype=np.float32))
    wqr = _perm_rope_cols(f32(L0_attn_w_q_rope), H)
    wkr = _perm_rope_cols(f32(L0_attn_w_k_rope), 1)
    wpr = np.ascontiguousarray(
        f32(L0_attn_w_proj).reshape(8, 128, D).transpose(1, 0, 2).reshape(128, 8 * D))
    w1T = np.ascontiguousarray(f32(L0_ffn_w1).T)
    w2T = np.ascontiguousarray(f32(L0_ffn_w2).T)
    base = dict(
        w_cq=f32(L0_attn_w_cq), w_qn=f32(L0_attn_w_q_nope), w_qr=wqr,
        q_g=np.ascontiguousarray(f32(L0_attn_q_g).reshape(6, 128).T),
        w_ckv=f32(L0_attn_w_ckv),
        w_kn=f32(L0_attn_w_k_nope), w_v=f32(L0_attn_w_v),
        kv_g=np.ascontiguousarray(f32(L0_attn_kv_g).reshape(2, 128).T),
        w_kr=wkr, w_pr=wpr,
        w1T=w1T, w2T=w2T, **_consts())
    return base, x


def kernel(**inputs):
    base, x = prepare_base(**inputs)
    nc = build_program()
    in_maps = []
    for c in range(8):
        m = dict(base)
        m["x_in"] = x[2 * c:2 * c + 2]
        in_maps.append(m)
    res = run_bass_kernel_spmd(nc, in_maps, core_ids=list(range(8)))
    out = np.empty((16, S, D), dtype=np.float32)
    for c in range(8):
        out[2 * c:2 * c + 2] = res.results[c]["out"]
    return out


# version 15
# speedup vs baseline: 1.0773x; 1.0773x over previous
import math
import numpy as np
import ml_dtypes
from contextlib import ExitStack

import concourse.bass as bass
import concourse.mybir as mybir
from concourse.bass_utils import run_bass_kernel_spmd

F32 = mybir.dt.float32
BF16 = mybir.dt.bfloat16
I32 = mybir.dt.int32
AF = mybir.ActivationFunctionType
ALU = mybir.AluOpType
AX = mybir.AxisListType

S, D, H = 2048, 2048, 16
DN, DR, DV = 64, 32, 64
RQ, RKV = 768, 256
DFF = 8192
NSEQ = 2
SC = 1.0 / math.sqrt(DN + DR)
NEG = -1.0e30
LN_EPS, RMS_EPS, Q_EPS = 1e-5, 1e-6, 1e-5
TC = 512
NTC = S // TC
NKT = S // 128
ENGS = ("tensor", "vector", "scalar", "gpsimd", "sync")
SHUF = list(range(16, 32)) + list(range(16))


class Sem:
    def __init__(self, nc, es, name):
        self.h = es.enter_context(nc.semaphore(name))
        self.v = 0
        self.name = name
        self.engs = set()

    def take(self, n=1):
        self.v += n
        return (self.h, n)


class Builder:
    def __init__(self, nc, es, dbg_xp=False):
        self.nc = nc
        self.es = es
        self.dbg_xp = dbg_xp
        self.streams = {e: [] for e in ENGS}
        self.sems = {}
        self.gates = {}
        self.pp_steps = []

    def sem(self, name):
        if name not in self.sems:
            self.sems[name] = Sem(self.nc, self.es, name)
        return self.sems[name]

    def _emit(self, eng, fn):
        self.streams[eng].append(fn)

    def dma(self, eng, dst, src, sem):
        sem.engs.add(eng)
        tgt = sem.take(16)
        self.dma_sems = getattr(self, "dma_sems", {})
        self.dma_sems[(eng, id(sem))] = sem
        self._emit(eng, lambda e, dst=dst, src=src, tgt=tgt:
                   e.dma_start(dst, src).then_inc(*tgt))

    def wait(self, eng, sem, val=None):
        v = sem.v if val is None else val
        if v > 0:
            self._emit(eng, lambda e, h=sem.h, v=v: e.wait_ge(h, v))

    def op(self, eng, fn, sem=None, n=1):
        if sem is None:
            self._emit(eng, fn)
        else:
            sem.engs.add(eng)
            tgt = sem.take(n)
            self._emit(eng, lambda e, fn=fn, tgt=tgt: fn(e).then_inc(*tgt))

    def chain(self, eng, steps, final_sem, n=1):
        s = self.sem(f"s_ch_{eng}")
        for i, fn in enumerate(steps):
            last = (i == len(steps) - 1)
            self.op(eng, fn, final_sem if last else s, n=n if last else 1)
            if not last:
                self.wait(eng, s)

    def g_wait(self, eng, key):
        if key in self.gates:
            s, v = self.gates[key]
            self.wait(eng, s, v)

    def g_set(self, key, sem):
        self.gates[key] = (sem, sem.v)

    def pump(self, k=2):
        for _ in range(min(k, len(self.pp_steps))):
            lim = getattr(self, "pp_limit", None)
            if lim is not None and getattr(self, "pp_popped", 0) >= lim:
                return
            eng, fn = self.pp_steps.pop(0)
            self.pp_popped = getattr(self, "pp_popped", 0) + 1
            self._emit(eng, fn)

    def flush_pp(self):
        while self.pp_steps:
            eng, fn = self.pp_steps.pop(0)
            self._emit(eng, fn)

    def dop(self, eng, fn, sem=None, n=1):
        if sem is None:
            self.pp_steps.append((eng, fn))
        else:
            sem.engs.add(eng)
            tgt = sem.take(n)
            self.pp_steps.append((eng, lambda e, fn=fn, tgt=tgt:
                                  fn(e).then_inc(*tgt)))

    def ddma(self, eng, waits, dst, src, sem):
        cw = [(s.h, (s.v if v is None else v)) for (s, v) in waits]
        cw = [(h, v) for (h, v) in cw if v > 0]
        sem.engs.add(eng)
        tgt = sem.take(16)
        self.dma_sems = getattr(self, "dma_sems", {})
        self.dma_sems[(eng, id(sem))] = sem
        self.pp_steps.append((eng, lambda e, cw=cw, dst=dst, src=src, tgt=tgt:
                              ([e.wait_ge(h, v) for h, v in cw],
                               e.dma_start(dst, src).then_inc(*tgt))[-1]))

    def dstep(self, eng, waits, fn, sem=None, n=1):
        cw = [(s.h, (s.v if v is None else v)) for (s, v) in waits]
        cw = [(h, v) for (h, v) in cw if v > 0]
        if sem is None:
            self.pp_steps.append((eng, lambda e, cw=cw, fn=fn:
                                  ([e.wait_ge(h, v) for h, v in cw],
                                   fn(e))[-1]))
        else:
            sem.engs.add(eng)
            tgt = sem.take(n)
            self.pp_steps.append((eng, lambda e, cw=cw, fn=fn, tgt=tgt:
                                  ([e.wait_ge(h, v) for h, v in cw],
                                   fn(e).then_inc(*tgt))[-1]))

    def dwait(self, eng, sem, val=None):
        v = sem.v if val is None else val
        if v > 0:
            self.pp_steps.append((eng, lambda e, h=sem.h, v=v: e.wait_ge(h, v)))

    def declare_io(self):
        nc = self.nc
        EI, EO, IN = "ExternalInput", "ExternalOutput", "Internal"
        self.d_x = nc.dram_tensor("x_in", [NSEQ, S, D], F32, kind=EI).ap()
        self.d_wcq = nc.dram_tensor("w_cq", [D, RQ], F32, kind=EI).ap()
        self.d_wqn = nc.dram_tensor("w_qn", [RQ, H * DN], F32, kind=EI).ap()
        self.d_wqr = nc.dram_tensor("w_qr", [RQ, H * DR], F32, kind=EI).ap()
        self.d_qg = nc.dram_tensor("q_g", [128, 6], F32, kind=EI).ap()
        self.d_wckv = nc.dram_tensor("w_ckv", [D, RKV], F32, kind=EI).ap()
        self.d_wkn = nc.dram_tensor("w_kn", [RKV, H * DN], F32, kind=EI).ap()
        self.d_wv = nc.dram_tensor("w_v", [RKV, H * DV], F32, kind=EI).ap()
        self.d_kvg = nc.dram_tensor("kv_g", [128, 2], F32, kind=EI).ap()
        self.d_wkr = nc.dram_tensor("w_kr", [D, DR], F32, kind=EI).ap()
        self.d_wpr = nc.dram_tensor("w_pr", [128, 8 * D], F32, kind=EI).ap()
        self.d_w1T = nc.dram_tensor("w1T", [D, DFF], F32, kind=EI).ap()
        self.d_w2T = nc.dram_tensor("w2T", [DFF, D], F32, kind=EI).ap()
        self.d_ident = nc.dram_tensor("ident", [128, 128], BF16, kind=EI).ap()
        self.d_cones = nc.dram_tensor("cones", [96, 33], BF16, kind=EI).ap()
        self.d_ones64 = nc.dram_tensor("ones64", [1, 64], BF16, kind=EI).ap()
        self.d_onesbc = nc.dram_tensor("onesbc", [1, 128], F32, kind=EI).ap()
        self.d_onescol = nc.dram_tensor("onescol", [128, 1], F32, kind=EI).ap()
        self.d_onescolb = nc.dram_tensor("onescolb", [128, 1], BF16, kind=EI).ap()
        self.d_crq = nc.dram_tensor("crq", [32, S], BF16, kind=EI).ap()
        self.d_crk = nc.dram_tensor("crk", [32, S], BF16, kind=EI).ap()
        self.d_sig = nc.dram_tensor("sig", [32, 1], F32, kind=EI).ap()
        self.d_mask = nc.dram_tensor("mask", [128, 4 * 512], BF16, kind=EI).ap()
        self.d_out = nc.dram_tensor("out", [NSEQ, S, D], F32, kind=EO).ap()
        self.d_xp = nc.dram_tensor("xp_scr", [NSEQ, S, D], BF16, kind=IN).ap()
        self.d_w1q = nc.dram_tensor("w1q_c", [16, 16, 128, 512], BF16, kind=IN).ap()
        self.d_w2q = nc.dram_tensor("w2q_c", [4, 64, 128, 512], BF16, kind=IN).ap()
        if self.dbg_xp:
            self.d_xpdbg = nc.dram_tensor("xp_dbg", [NSEQ, S, D], F32, kind=EO).ap()
            self.d_dbg_qk = nc.dram_tensor("dbg_qk", [97, H * S], F32, kind=EO).ap()
            self.d_dbg_qa = nc.dram_tensor("dbg_qa", [97, H * TC], F32, kind=EO).ap()
            self.d_dbg_af = nc.dram_tensor("dbg_af", [128, 8 * TC], F32, kind=EO).ap()
            self.d_dbg_vt = nc.dram_tensor("dbg_vt", [128, NKT * H * 65], F32, kind=EO).ap()
            self.d_dbg_sr = nc.dram_tensor("dbg_sr", [64, 512], F32, kind=EO).ap()
            self.d_dbg_nq = nc.dram_tensor("dbg_nq", [128, 6 * S], F32, kind=EO).ap()
            self.d_dbg_y1 = nc.dram_tensor("dbg_y1", [128, 16 * TC], F32, kind=EO).ap()
            self.d_dbg_st = nc.dram_tensor("dbg_st", [128, 16], F32, kind=EO).ap()
            self.d_dbg_cq = nc.dram_tensor("dbg_cq", [128, 6 * TC], F32, kind=EO).ap()

    def alloc_persistent(self):
        nc, es = self.nc, self.es
        SB = lambda n, sh, dt=F32: es.enter_context(nc.sbuf_tensor(n, sh, dt))
        PS = lambda n, sh, dt=F32: es.enter_context(nc.psum_tensor(n, sh, dt))
        self.pA, self.pB = PS("pA", [128, 512]), PS("pB", [128, 512])
        self.pC, self.pD = PS("pC", [128, 512]), PS("pD", [128, 512])
        self.pE, self.pF = PS("pE", [128, 512]), PS("pF", [128, 512])
        self.pT = [PS("pT0", [128, 1024], BF16), PS("pT1", [128, 1024], BF16)]
        self.rot = [self.pA, self.pB, self.pC, self.pD]
        self.bname = {id(self.pA): "A", id(self.pB): "B", id(self.pC): "C",
                      id(self.pD): "D", id(self.pE): "E", id(self.pF): "F",
                      id(self.pT[0]): "T0", id(self.pT[1]): "T1"}

        self.c_ident = SB("c_ident", [128, 128], BF16)
        self.c_cones = SB("c_cones", [96, 33], BF16)
        self.c_ones64 = SB("c_ones64", [1, 64], BF16)
        self.c_onesbc = SB("c_onesbc", [1, 128], F32)
        self.c_onescol = SB("c_onescol", [128, 1], F32)
        self.c_onescolb = SB("c_onescolb", [128, 1], BF16)
        self.c_crq = SB("c_crq", [96, S], BF16)
        self.c_crk = SB("c_crk", [96, S], BF16)
        self.c_sig = SB("c_sig", [96, 1], F32)
        self.c_mask = SB("c_mask", [128, 4 * 512], BF16)
        self.gam = SB("gam", [128, 8], F32)
        self.rowb = SB("rowb", [128, 16], F32)
        self.stats = SB("stats", [128, 16], F32)

        s = self.sem("s_const")
        for dst, src in [(self.c_ident[:, :], self.d_ident),
                         (self.c_cones[:, :], self.d_cones),
                         (self.c_ones64[:, :], self.d_ones64),
                         (self.c_onesbc[:, :], self.d_onesbc),
                         (self.c_onescol[:, :], self.d_onescol),
                         (self.c_onescolb[:, :], self.d_onescolb),
                         (self.c_crq[64:96, :], self.d_crq),
                         (self.c_crk[64:96, :], self.d_crk),
                         (self.c_sig[64:96, :], self.d_sig),
                         (self.c_mask[:, :], self.d_mask)]:
            self.dma("gpsimd", dst, src, s)
        for eng in ("tensor", "vector", "scalar"):
            self.wait(eng, s)

    def barrier(self):
        sb = self.sem("s_bar")
        self._barn = getattr(self, "_barn", 0) + 1
        self.op("tensor", lambda t: t.nop(), sb)
        self.op("vector", lambda v: v.engine_nop(), sb)
        self.op("scalar", lambda x: x.nop(), sb)
        self.op("gpsimd", lambda g: g.engine_nop(), sb)
        tgt = 4 * self._barn
        assert sb.v == tgt
        for eng in ("tensor", "vector", "scalar", "gpsimd"):
            self.wait(eng, sb, tgt)

    def pbank(self, eng, pp):
        self.g_wait(eng, ("bank", self.bname[id(pp)]))

    def pbank_d(self, eng, pp):
        key = ("bank", self.bname[id(pp)])
        if key in self.gates:
            s, v = self.gates[key]
            self.dwait(eng, s, v)

    def freeze_dump(self, pairs):
        if not self.dbg_xp:
            return
        self.barrier()
        sd = self.sem("s_frz")
        for dst, src in pairs:
            self.dma("gpsimd", dst, src, sd)
        for eng in ("tensor", "vector", "scalar", "gpsimd"):
            self.wait(eng, sd)

    def emit_prepass_streams(self, es):
        nc = self.nc
        SB = lambda n, sh, dt=F32: self.es.enter_context(nc.sbuf_tensor(n, sh, dt))
        self.pwl = [SB(f"pwl{i}", [128, 512], F32) for i in range(4)]
        self.ptau = [SB(f"ptau{i}", [128, 512], BF16) for i in range(2)]
        self.pacc = SB("pacc", [128, 8], F32)
        self.prow = SB("prow", [128, 2], F32)
        self.gamrep = [SB(f"sgnm{i}", [128, 512], BF16) for i in range(2)]
        self.gamneg = [SB(f"sgnp{i}", [128, 512], BF16) for i in range(2)]
        pwl, ptau, pacc, prow = (self.pwl, self.ptau, self.pacc, self.prow)
        pdump = ptau[0]

        s_gld = [self.sem(f"s_gld{i}") for i in range(4)]
        s_gab = [self.sem(f"s_gab{i}") for i in range(4)]
        s_gfr = [self.sem("s_gfr0"), self.sem("s_gfr1")]
        gj = ([(self.d_w1T[r * 128:(r + 1) * 128, c * 512:(c + 1) * 512], 0)
               for r in range(16) for c in range(16)]
              + [(self.d_w2T[r * 128:(r + 1) * 128, c * 512:(c + 1) * 512], 1)
                 for r in range(64) for c in range(4)])
        for idx, (src, col) in enumerate(gj):
            bb = idx % 4
            par = idx % 2
            slot = col * 4 + idx % 4
            if idx >= 4:
                self.wait("sync", s_gab[bb], idx // 4)
            self.dma("sync", pwl[bb][:, :], src, s_gld[bb])
            sw = [(s_gld[bb], None)]
            if idx >= 2:
                sw.append((s_gfr[par], None))
            self.dstep("scalar", sw, lambda e, bb=bb, par=par:
                       e.activation(pdump[:, :], pwl[bb][:, :], AF.Abs,
                                    bias=0.0, scale=1.0,
                                    accum_out=prow[:, par:par + 1]), s_gab[bb])
            def vadd(g, par=par, slot=slot, first=(idx < 4 or 256 <= idx < 260)):
                if first:
                    return g.tensor_copy(pacc[:, slot:slot + 1],
                                         prow[:, par:par + 1])
                return g.tensor_add(pacc[:, slot:slot + 1], pacc[:, slot:slot + 1],
                                    prow[:, par:par + 1])
            self.dstep("vector", [(s_gab[bb], None)], vadd, s_gfr[par])
        self.dop("vector", lambda g: g.engine_nop(), self.sem("s_gred"))
        self.pp_gamma_end = len(self.pp_steps)
        self.pp_limit = self.pp_gamma_end

        s_qd = [self.sem("s_qd0"), self.sem("s_qd1")]
        s_qf = [self.sem("s_qf0"), self.sem("s_qf1")]
        qj = []
        for r in range(16):
            for c in range(16):
                qj.append((self.d_w1T[r * 128:(r + 1) * 128, c * 512:(c + 1) * 512],
                           self.d_w1q[c, r], 0))
        for c in range(4):
            for r in range(64):
                qj.append((self.d_w2T[r * 128:(r + 1) * 128, c * 512:(c + 1) * 512],
                           self.d_w2q[c, r], 1))
        for b in range(4):
            self.wait("sync", s_gab[b])
        s_qsg = [self.sem(f"s_qsg{i}") for i in range(4)]
        self.pp_steps.append(("scalar", lambda e: e.wait_ge(self.sem("s_gamma").h, 1)))
        for idx, (src, dst, w) in enumerate(qj):
            bb = idx % 4
            par = idx % 2
            defer = idx >= 256
            if not defer:
                if idx >= 4:
                    self.wait("sync", s_qsg[bb], idx // 4)
                self.dma("sync", pwl[bb][:, :], src, s_qd[par])
            else:
                lw = [(s_qsg[bb], idx // 4)] if idx >= 4 else []
                self.ddma("sync", lw, pwl[bb][:, :], src, s_qd[par])
            bpos, bneg = (self.gam[:, 0:1], self.gam[:, 1:2]) if w == 0 else \
                         (self.gam[:, 2:3], self.gam[:, 3:4])
            sw = [(s_qd[par], None)]
            if idx >= 2:
                sw.append((s_qf[par], None))
            def sgn2(e, bb=bb, par=par, bpos=bpos, bneg=bneg):
                e.activation(self.gamrep[par][:, :], pwl[bb][:, :], AF.Sign,
                             bias=bneg, scale=1.0)
                return e.activation(self.gamneg[par][:, :], pwl[bb][:, :],
                                    AF.Sign, bias=bpos, scale=1.0)
            self.dstep("scalar", sw, sgn2, s_qsg[bb])
            self.dstep("vector", [(s_qsg[bb], None)], lambda v, par=par:
                       v.tensor_add(ptau[par][:, :], self.gamrep[par][:, :],
                                    self.gamneg[par][:, :]), s_qf[par])
            if not defer:
                self.wait("sync", s_qf[par])
                self.dma("sync", dst, ptau[par][:, :], s_qd[par])
            else:
                self.ddma("sync", [(s_qf[par], None)], dst, ptau[par][:, :],
                          s_qd[par])
            if idx == 255:
                self.pp_w1q_end = len(self.pp_steps)
                self.wait("sync", s_qd[0])
                self.wait("sync", s_qd[1])
                self.op("sync", lambda sy: sy.nop(), self.sem("s_cachew1"))
            elif idx >= 256 and (idx - 255) % 64 == 0:
                self.dstep("sync", [(s_qd[0], None), (s_qd[1], None)],
                           lambda sy: sy.nop(), self.sem("s_w2c"))
                self.pp_w2g_end = getattr(self, "pp_w2g_end", {})
                self.pp_w2g_end[(idx - 319) // 64] = len(self.pp_steps)

    def flush_gamma(self):
        while getattr(self, "pp_popped", 0) < getattr(self, "pp_gamma_end", 0) \
                and self.pp_steps:
            self.pump(1)

    def emit_gamma_finalize(self):
        self.flush_gamma()
        self.pp_limit = None
        """Small tensor/vector chain turning pacc into the gam columns.
        Emitted at a controlled point between attention phases."""
        pE, rowb, gam = self.pE, self.rowb, self.gam
        self.wait("tensor", self.sem("s_gred"), 1)
        self.pbank("tensor", pE)
        self.op("tensor", lambda t: t.matmul(pE[0:1, 0:8], self.c_onescol[:, :],
                                             self.pacc[:, 0:8], start=True, stop=True),
                self.sem("s_gsum"))
        self.wait("vector", self.sem("s_gsum"), 1)
        def gfin_a(v):
            return v.tensor_scalar_mul(rowb[0:1, 4:12], pE[0:1, 0:8],
                                       float(2.0 ** -24))
        def gfin_b(v):
            v.tensor_add(rowb[0:1, 12:13], rowb[0:1, 4:5], rowb[0:1, 5:6])
            v.tensor_add(rowb[0:1, 13:14], rowb[0:1, 6:7], rowb[0:1, 7:8])
            v.tensor_add(rowb[0:1, 14:15], rowb[0:1, 8:9], rowb[0:1, 9:10])
            return v.tensor_add(rowb[0:1, 15:16], rowb[0:1, 10:11],
                                rowb[0:1, 11:12])
        def gfin_c(v):
            v.scalar_tensor_tensor(rowb[0:1, 0:1], rowb[0:1, 12:13], 1.0,
                                   rowb[0:1, 13:14], ALU.mult, ALU.add)
            return v.scalar_tensor_tensor(rowb[0:1, 1:2], rowb[0:1, 14:15], 1.0,
                                          rowb[0:1, 15:16], ALU.mult, ALU.add)
        def gfin_d(v):
            return v.tensor_scalar(rowb[0:1, 0:2], rowb[0:1, 0:2], 1.0, Q_EPS,
                                   ALU.mult, ALU.add)
        def gfin_e(v):
            return v.tensor_scalar_mul(rowb[0:1, 2:4], rowb[0:1, 0:2], 0.5)
        self.chain("vector", [gfin_a, gfin_b, gfin_c, gfin_d, gfin_e],
                   self.sem("s_gfin"))
        self.wait("tensor", self.sem("s_gfin"), 1)
        self.op("tensor", lambda t: t.matmul(pE[:, 0:4], self.c_onesbc[:, :],
                                             rowb[0:1, 0:4], start=True, stop=True),
                self.sem("s_gbc"))
        self.wait("vector", self.sem("s_gbc"), 1)
        def gbc_a(v):
            v.tensor_copy(gam[:, 0:1], pE[:, 2:3])
            v.tensor_scalar_mul(gam[:, 1:2], pE[:, 2:3], -1.0)
            v.tensor_copy(gam[:, 2:3], pE[:, 3:4])
            v.tensor_scalar_mul(gam[:, 3:4], pE[:, 3:4], -1.0)
            v.tensor_copy(gam[:, 6:7], pE[:, 0:1])
            return v.tensor_scalar_mul(gam[:, 5:6], pE[:, 1:2], 1.0 / 254.0)
        def gbc_b(v):
            return v.scalar_tensor_tensor(gam[:, 4:5], gam[:, 6:7], 4096.0,
                                          gam[:, 6:7], ALU.mult, ALU.mult)
        self.chain("vector", [gbc_a, gbc_b], self.sem("s_gamma"))
        self.g_set(("bank", "E"), self.sem("s_gamma"))

    def alloc_attn_outer(self, esA):
        nc = self.nc
        SB = lambda n, sh, dt=F32: esA.enter_context(nc.sbuf_tensor(n, sh, dt))
        a = self
        a.qk_k = SB("qk_k", [97, H * S], BF16)
        a.v_t = SB("v_t", [128, NKT * H * 65], BF16)
        a.nq = SB("nq", [128, 6 * S], BF16)
        a.kmx = SB("kmx", [97, H], F32)
        a.r32 = SB("r32", [97, TC], BF16)
        a.crow = SB("crow", [97, TC], BF16)
        a.qsq = [SB(f"qsq{i}", [96, TC], BF16) for i in range(2)]

    def alloc_nkv(self, es, si):
        self.nkv = es.enter_context(
            self.nc.sbuf_tensor(f"nkv_{si}", [128, 2 * S], BF16))

    def alloc_p1(self, es, si):
        nc = self.nc
        SB = lambda n, sh, dt=F32: es.enter_context(nc.sbuf_tensor(n + f"_{si}", sh, dt))
        a = self
        a.xb = [SB(f"xb{i}", [128, D], BF16) for i in range(2)]
        a.y1t = SB("y1t", [128, 16 * TC], BF16)
        a.cq_fm = SB("cq_fm", [128, 6 * TC], BF16)
        a.sq_t = [SB(f"sq_t{i}", [128, TC], BF16) for i in range(2)]
        a.rowc = SB("rowc", [1, 4 * TC], F32)
        a.lndump = SB("lndump", [128, 1024], BF16)
        a.wl1 = [SB(f"wl1{i}", [128, RQ], BF16) for i in range(2)]
        a.wl2 = [SB(f"wl2{i}", [128, RKV + DR], BF16) for i in range(2)]
        a.ag_q = SB("ag_q", [128, 6], F32)
        a.ag_kv = SB("ag_kv", [128, 2], F32)
        sg = self.sem("s_gains")
        self.dma("gpsimd", a.ag_q[:, :], self.d_qg, sg)
        self.dma("gpsimd", a.ag_kv[:, :], self.d_kvg, sg)

    def alloc_p2(self, es, si):
        nc = self.nc
        SB = lambda n, sh, dt=F32: es.enter_context(nc.sbuf_tensor(n + f"_{si}", sh, dt))
        a = self
        a.aw_kn = SB("aw_kn", [128, 2 * H * DN], BF16)
        a.aw_v = SB("aw_v", [128, 2 * H * DV], BF16)
        sw = self.sem("s_p2w")
        rr = lambda d: d.rearrange("(c p) o -> p c o", p=128)
        self.dma("gpsimd", a.aw_kn[:, :], rr(self.d_wkn), sw)
        self.dma("gpsimd", a.aw_v[:, :], rr(self.d_wv), sw)
        self.wait("tensor", sw)

    def alloc_p3(self, es, si):
        nc = self.nc
        SB = lambda n, sh, dt=F32: es.enter_context(nc.sbuf_tensor(n + f"_{si}", sh, dt))
        a = self
        a.aw_qn = SB("aw_qn", [128, 6 * H * DN], BF16)
        a.aw_qr = SB("aw_qr", [128, 6 * H * DR], BF16)
        a.q_aug = SB("q_aug", [97, H * TC], BF16)
        a.att_fm = SB("att_fm", [128, 8 * TC], BF16)
        a.att_tmp = SB("att_tmp", [64, TC], BF16)
        a.sexp = [SB(f"sexp{i}", [128, 512], BF16) for i in range(2)]
        a.den_b = SB("den_b", [65, 512], BF16)
        a.srec = SB("srec", [64, 512], F32)
        a.wpb = [SB(f"wpb{i}", [128, 512], BF16) for i in range(2)]
        a.xb2 = [SB(f"xb2_{i}", [128, 512], BF16) for i in range(2)]
        a.xpst = [SB(f"xpst_{i}", [128, 512], BF16) for i in range(2)]
        sw = self.sem("s_p3w")
        rr = lambda d: d.rearrange("(c p) o -> p c o", p=128)
        self.dma("gpsimd", a.aw_qn[:, :], rr(self.d_wqn), sw)
        self.dma("gpsimd", a.aw_qr[:, :], rr(self.d_wqr), sw)
        self.wait("tensor", sw)

    def emit_p1(self, si):
        a = self
        xb = a.xb; y1t = a.y1t; cq_fm = a.cq_fm; sq_t = a.sq_t; rowc = a.rowc; lndump = a.lndump; wl1 = a.wl1; wl2 = a.wl2; ag_q = a.ag_q; ag_kv = a.ag_kv; nkv = a.nkv; stats = a.stats
        sfx = f"_{si}"
        xs = self.d_x[si]
        s_xld = [self.sem(f"s_xld{i}") for i in range(2)]
        s_sqs = self.sem(f"s_sqs")
        s_ln1 = self.sem(f"s_ln1")
        s_y1 = self.sem(f"s_y1")
        s_tp = self.sem(f"s_tp")
        s_tpe = self.sem(f"s_tpe")
        s_lat = self.sem(f"s_lat")
        s_late = self.sem(f"s_late")
        s_row = self.sem(f"s_row")
        s_rbc = self.sem(f"s_rbc")
        s_nrm = self.sem(f"s_nrm")
        s_kve = self.sem(f"s_kve")

        for tc in range(NTC):
            self.pump(12)
            for tt in range(4):
                ti = tc * 4 + tt
                par = ti % 2
                self.g_wait("gpsimd", ("xb", par))
                self.dma("gpsimd", xb[par][:, :], xs[ti * 128:(ti + 1) * 128, :],
                         s_xld[par])
                pc = par * 8
                self.wait("scalar", s_xld[par])
                self.g_wait("scalar", ("stats", par))
                self.op("scalar", lambda s, par=par, pc=pc:
                        s.activation(lndump[:, :], xb[par][:, 0:1024], AF.Square,
                                     bias=0.0, scale=1.0,
                                     accum_out=stats[:, pc:pc + 1]), s_sqs)
                self.wait("scalar", s_sqs)
                self.op("scalar", lambda s, par=par, pc=pc:
                        s.activation(lndump[:, :], xb[par][:, 1024:2048],
                                     AF.Square, bias=0.0, scale=1.0,
                                     accum_out=stats[:, pc + 2:pc + 3]), s_sqs)
                self.wait("vector", s_xld[par])
                self.wait("vector", s_sqs)
                self.op("vector", lambda v, par=par, pc=pc:
                        v.reduce_sum(stats[:, pc + 1:pc + 2], xb[par][:, :],
                                     axis=AX.X), self.sem("s_rr"))
                self.wait("vector", self.sem("s_rr"))
                def lnv_a(v, pc=pc):
                    st = stats
                    v.tensor_add(st[:, pc:pc + 1], st[:, pc:pc + 1], st[:, pc + 2:pc + 3])
                    return v.tensor_scalar_mul(st[:, pc + 1:pc + 2],
                                               st[:, pc + 1:pc + 2], 1.0 / D)
                def lnv_b(v, pc=pc):
                    st = stats
                    v.tensor_scalar(st[:, pc:pc + 1], st[:, pc:pc + 1], 1.0 / D,
                                    LN_EPS, ALU.mult, ALU.add)
                    return v.tensor_tensor(st[:, pc + 2:pc + 3], st[:, pc + 1:pc + 2],
                                           st[:, pc + 1:pc + 2], ALU.mult)
                def lnv_c(v, pc=pc):
                    st = stats
                    return v.tensor_sub(st[:, pc:pc + 1], st[:, pc:pc + 1],
                                        st[:, pc + 2:pc + 3])
                self.chain("vector", [lnv_a, lnv_b, lnv_c], s_ln1)
                self.wait("scalar", s_ln1)
                self.op("scalar", lambda s, pc=pc:
                        s.activation(stats[:, pc + 3:pc + 4], stats[:, pc:pc + 1],
                                     AF.Sqrt, bias=0.0, scale=1.0), s_sqs)
                self.wait("vector", s_sqs)
                def l2_a(v, pc=pc):
                    return v.reciprocal(stats[:, pc + 4:pc + 5],
                                        stats[:, pc + 3:pc + 4])
                def l2_b(v, pc=pc):
                    return v.scalar_tensor_tensor(stats[:, pc + 6:pc + 7],
                                                  stats[:, pc + 1:pc + 2], -1.0,
                                                  stats[:, pc + 4:pc + 5],
                                                  ALU.mult, ALU.mult)
                def l2_c(v, par=par, pc=pc):
                    return v.tensor_scalar(xb[par][:, :], xb[par][:, :],
                                           stats[:, pc + 4:pc + 5],
                                           stats[:, pc + 6:pc + 7],
                                           ALU.mult, ALU.add)
                self.chain("vector", [l2_a, l2_b, l2_c], s_y1)
                self.g_set(("stats", par), s_y1)
                if self.dbg_xp and si == 0 and ti == 0:
                    self.freeze_dump([(self.d_dbg_st, stats[:, :]),
                                      (self.d_dbg_y1[0:128, 0:2048],
                                       xb[par][:, :])])
                self.wait("tensor", s_y1)
                if tt == 0:
                    self.g_wait("vector", ("y1t",))
                for fb in range(16):
                    ptp = self.pT[fb % 2]
                    self.pbank("tensor", ptp)
                    self.op("tensor", lambda t, par=par, fb=fb, ptp=ptp:
                            t.transpose(ptp[:, 0:128],
                                        xb[par][:, fb * 128:(fb + 1) * 128],
                                        self.c_ident[:, :]), s_tp)
                    self.wait("vector", s_tp)
                    self.op("vector", lambda v, fb=fb, tt=tt, ptp=ptp:
                            v.tensor_copy(y1t[:, fb * TC + tt * 128:
                                                fb * TC + (tt + 1) * 128],
                                          ptp[:, 0:128]), s_tpe)
                    self.g_set(("bank", self.bname[id(ptp)]), s_tpe)
                self.op("vector", lambda v: v.engine_nop(), self.sem("s_dummy"))
                self.g_set(("xb", par), s_tpe)

            s_wl1 = [self.sem(f"s_wl1{i}") for i in range(2)]
            s_wl2 = [self.sem(f"s_wl2{i}") for i in range(2)]
            self.wait("tensor", s_tpe)

            def evac_chunk(j, pp, last_cs):
                self.op("vector", lambda v, j=j, pp=pp:
                        v.tensor_copy(cq_fm[:, j * TC:(j + 1) * TC], pp[:, :]),
                        s_late)
                self.wait("scalar", s_late)
                self.g_wait("scalar", ("sq_t", j % 2))
                self.op("scalar", lambda s, j=j, pp=pp:
                        s.activation(sq_t[j % 2][:, :], pp[:, :], AF.Square,
                                     bias=0.0, scale=1.0), s_sqs)
                self.g_set(("bank", self.bname[id(pp)]), s_sqs)
                self.wait("tensor", s_sqs)
                if j == 0:
                    self.pbank("tensor", self.pF)
                self.op("tensor", lambda t, j=j, last_cs=last_cs:
                        t.matmul(self.pF[0:1, 0:TC], self.c_onescolb[:, :],
                                 sq_t[j % 2][:, :], start=(j == 0),
                                 stop=last_cs), s_row)
                self.g_set(("sq_t", j % 2), s_row)

            def rms_rows(rank, gain, ntens, n_ch, tc):
                self.wait("vector", s_row)
                self.op("vector", lambda v, rank=rank:
                        v.tensor_scalar(rowc[0:1, 0:TC], self.pF[0:1, 0:TC],
                                        1.0 / rank, RMS_EPS, ALU.mult, ALU.add),
                        s_row)
                self.g_set(("bank", "F"), s_row)
                self.wait("scalar", s_row)
                self.op("scalar", lambda s:
                        s.activation(rowc[0:1, TC:2 * TC], rowc[0:1, 0:TC],
                                     AF.Sqrt, bias=0.0, scale=1.0), s_sqs)
                self.wait("vector", s_sqs)
                self.op("vector", lambda v:
                        v.reciprocal(rowc[0:1, 2 * TC:3 * TC],
                                     rowc[0:1, TC:2 * TC]), s_row)
                self.wait("tensor", s_row)
                self.pbank("tensor", self.pF)
                self.op("tensor", lambda t:
                        t.matmul(self.pF[:, :], self.c_onesbc[:, :],
                                 rowc[0:1, 2 * TC:3 * TC],
                                 start=True, stop=True), s_rbc)
                self.wait("vector", s_rbc)
                if tc == 0:
                    self.g_wait("vector", ("nq",))
                    self.g_wait("vector", ("nkv",))
                for j in range(n_ch):
                    self.op("vector", lambda v, j=j, gain=gain, ntens=ntens, tc=tc:
                            v.scalar_tensor_tensor(
                                ntens[:, j * S + tc * TC:j * S + (tc + 1) * TC],
                                cq_fm[:, j * TC:(j + 1) * TC],
                                gain[:, j:j + 1], self.pF[:, :],
                                ALU.mult, ALU.mult), s_nrm)
                self.g_set(("bank", "F"), s_nrm)

            banks5 = [self.pA, self.pB, self.pC, self.pD, self.pE]
            for bk in banks5:
                self.pbank("tensor", bk)
            for i in range(16):
                par = i % 2
                self.g_wait("gpsimd", ("wl1", par))
                self.dma("gpsimd", wl1[par][:, :],
                         self.d_wcq[i * 128:(i + 1) * 128, :], s_wl1[par])
                self.wait("tensor", s_wl1[par])
                def lat1(t, i=i, par=par):
                    for oc in range(5):
                        li = t.matmul(banks5[oc][:, :],
                                      wl1[par][:, oc * 128:(oc + 1) * 128],
                                      y1t[:, i * TC:(i + 1) * TC],
                                      start=(i == 0), stop=(i == 15))
                    return li
                self.op("tensor", lat1, s_lat)
                self.g_set(("wl1", par), s_lat)
            base = s_lat.v
            for j in range(5):
                self.wait("vector", s_lat, base)
                if si == 0 and tc == 0 and j == 0:
                    self.wait("vector", self.sem("s_gains"))
                evac_chunk(j, banks5[j], last_cs=False)
            for i in range(16):
                par = i % 2
                self.g_wait("gpsimd", ("wl1", par))
                self.dma("gpsimd", wl1[par][:, :128],
                         self.d_wcq[i * 128:(i + 1) * 128, 5 * 128:6 * 128],
                         s_wl1[par])
                self.wait("tensor", s_wl1[par])
                if i == 0:
                    self.pbank("tensor", self.pA)
                def lat1b(t, i=i, par=par):
                    return t.matmul(self.pA[:, :], wl1[par][:, 0:128],
                                    y1t[:, i * TC:(i + 1) * TC],
                                    start=(i == 0), stop=(i == 15))
                self.op("tensor", lat1b, s_lat)
                self.g_set(("wl1", par), s_lat)
            self.wait("vector", s_lat)
            evac_chunk(5, self.pA, last_cs=True)
            rms_rows(RQ, ag_q, a.nq, 6, tc)

            self.pbank("tensor", self.pA)
            self.pbank("tensor", self.pB)
            self.pbank("tensor", self.pE)
            for i in range(16):
                par = i % 2
                self.g_wait("gpsimd", ("wl2", par))
                self.dma("gpsimd", wl2[par][:, 0:RKV],
                         self.d_wckv[i * 128:(i + 1) * 128, :], s_wl2[par])
                self.dma("gpsimd", wl2[par][:, RKV:RKV + DR],
                         self.d_wkr[i * 128:(i + 1) * 128, :], s_wl2[par])
                self.wait("tensor", s_wl2[par])
                def lat2(t, i=i, par=par):
                    t.matmul(self.pA[:, :], wl2[par][:, 0:128],
                             y1t[:, i * TC:(i + 1) * TC],
                             start=(i == 0), stop=(i == 15))
                    t.matmul(self.pB[:, :], wl2[par][:, 128:256],
                             y1t[:, i * TC:(i + 1) * TC],
                             start=(i == 0), stop=(i == 15))
                    return t.matmul(self.pE[64:96, :], wl2[par][:, 256:288],
                                    y1t[:, i * TC:(i + 1) * TC],
                                    start=(i == 0), stop=(i == 15))
                self.op("tensor", lat2, s_lat)
                self.g_set(("wl2", par), s_lat)
            self.g_set(("y1t",), s_lat)
            base = s_lat.v
            for j in range(2):
                self.wait("vector", s_lat, base)
                evac_chunk(j, [self.pA, self.pB][j], last_cs=(j == 1))
            rms_rows(RKV, ag_kv, nkv, 2, tc)

            if self.dbg_xp and si == 0 and tc == 0:
                sdbg = self.sem("s_dbg")
                self.wait("gpsimd", s_nrm)
                self.dma("gpsimd", self.d_dbg_cq, cq_fm[:, :], sdbg)
                self.wait("gpsimd", sdbg)
            self.wait("vector", s_lat)
            if si == 0 and tc == 0:
                self.wait("vector", self.sem("s_const"))
            self.g_wait("vector", ("crow",))
            self.g_wait("vector", ("qk_k",))
            def krope2(v, tc=tc):
                raw = a.r32[64:96, :]
                sh = a.crow[64:96, :]
                v.tensor_copy(raw, self.pE[64:96, :])
                v.stream_shuffle(sh, raw, SHUF)
                v.scalar_tensor_tensor(sh, sh, self.c_sig[64:96, :], raw,
                                       ALU.mult, ALU.add)
                return v.tensor_tensor(a.qk_k[64:96, tc * TC:(tc + 1) * TC], sh,
                                       self.c_crk[64:96, tc * TC:(tc + 1) * TC],
                                       ALU.mult)
            self.op("vector", krope2, s_kve)
            self.g_set(("bank", "E"), s_kve)
            self.g_set(("crow",), s_kve)

        def krep(v):
            for h in range(1, H):
                v.tensor_copy(a.qk_k[64:96, h * S:(h + 1) * S], a.qk_k[64:96, 0:S])
            return v.memset(a.qk_k[96:97, :], 1.0)
        self.op("vector", krep, s_kve)

    def emit_p2(self, si):
        a = self
        aw_kn = a.aw_kn; aw_v = a.aw_v; nkv = a.nkv
        sfx = f"_{si}"
        s_nrm = self.sem(f"s_nrm")
        s_kve = self.sem(f"s_kve")
        s_kv = self.sem(f"s_kv")
        s_kev = self.sem(f"s_kev")
        s_ksq = self.sem(f"s_ksq")
        s_kcs = self.sem(f"s_kcs")
        s_kmx = self.sem(f"s_kmx")

        self.g_wait("vector", ("v_t",))
        self.g_wait("vector", ("qk_k",))
        self.op("vector", lambda v: v.memset(a.v_t[:, :], 1.0), s_kev)

        self.wait("tensor", s_nrm)
        for h in range(H):
            for kc in range(NTC):
                self.pump(4)
                pp = self.rot[(h * NTC + kc) % 4]
                self.pbank("tensor", pp)
                def knmm(t, h=h, kc=kc, pp=pp):
                    for i in range(2):
                        li = t.matmul(pp[0:64, :],
                                      aw_kn[:, i * H * DN + h * DN:i * H * DN + (h + 1) * DN],
                                      nkv[:, i * S + kc * TC:i * S + (kc + 1) * TC],
                                      start=(i == 0), stop=(i == 1))
                    return li
                self.op("tensor", knmm, s_kv)
                self.wait("vector", s_kv)
                self.op("vector", lambda v, h=h, kc=kc, pp=pp:
                        v.tensor_copy(a.qk_k[0:64, h * S + kc * TC:h * S + (kc + 1) * TC],
                                      pp[0:64, :]), s_kev)
                self.g_set(("bank", self.bname[id(pp)]), s_kev)
        for kt in range(NKT):
            self.pump(4)
            for half in range(2):
                pp = self.rot[(kt % 2) * 2 + half]
                self.pbank("tensor", pp)
                def vmm(t, kt=kt, half=half, pp=pp):
                    for i in range(2):
                        li = t.matmul(pp[:, :],
                                      nkv[:, i * S + kt * 128:i * S + (kt + 1) * 128],
                                      aw_v[:, i * H * DV + half * 512:
                                             i * H * DV + (half + 1) * 512],
                                      start=(i == 0), stop=(i == 1))
                    return li
                self.op("tensor", vmm, s_kv)
                self.wait("vector", s_kv)
                self.op("vector", lambda v, kt=kt, half=half, pp=pp:
                        v.tensor_copy(
                            a.v_t[:, kt * H * 65 + half * 8 * 65:
                                  kt * H * 65 + (half * 8 + 8) * 65]
                            .rearrange("p (h e) -> p h e", e=65)[:, :, 0:64],
                            pp[:, :].rearrange("p (h e) -> p h e", e=64)), s_kev)
                self.g_set(("bank", self.bname[id(pp)]), s_kev)

        for h in range(H):
            self.pump(4)
            for kc in range(NTC):
                idx = h * NTC + kc
                qs = a.qsq[idx % 2]
                pX = [self.pE, self.pF][idx % 2]
                self.wait("scalar", s_kev)
                self.wait("scalar", s_kve)
                self.g_wait("scalar", ("qsq", idx % 2))
                self.op("scalar", lambda s, h=h, kc=kc, qs=qs:
                        s.activation(qs[0:96, :],
                                     a.qk_k[0:96, h * S + kc * TC:h * S + (kc + 1) * TC],
                                     AF.Square, bias=0.0, scale=1.0), s_ksq)
                self.wait("tensor", s_ksq)
                self.pbank("tensor", pX)
                self.op("tensor", lambda t, qs=qs, pX=pX:
                        t.matmul(pX[64:97, 0:TC], self.c_cones[:, :],
                                 qs[0:96, :], start=True, stop=True), s_kcs)
                self.g_set(("qsq", idx % 2), s_kcs)
                self.wait("vector", s_kcs)
                if h == 0 and kc == 0:
                    self.g_wait("vector", ("kmx",))
                if kc == 0:
                    self.op("vector", lambda v, h=h, pX=pX:
                            v.reduce_max(a.kmx[96:97, h:h + 1],
                                         pX[96:97, 0:TC], axis=AX.X), s_kmx)
                else:
                    self.op("vector", lambda v, pX=pX:
                            v.reduce_max(a.crow[96:97, 0:1],
                                         pX[96:97, 0:TC], axis=AX.X),
                            self.sem("s_rr"))
                    self.wait("vector", self.sem("s_rr"))
                    self.wait("vector", s_kmx)
                    self.op("vector", lambda v, h=h:
                            v.tensor_max(a.kmx[96:97, h:h + 1], a.kmx[96:97, h:h + 1],
                                         a.crow[96:97, 0:1]), s_kmx)
                self.g_set(("bank", self.bname[id(pX)]), s_kmx)
        self.wait("scalar", s_kmx)
        self.op("scalar", lambda s:
                s.activation(a.kmx[96:97, 0:H], a.kmx[96:97, 0:H], AF.Sqrt,
                             bias=0.0, scale=1.0), s_ksq)
        self.wait("vector", s_ksq)
        self.op("vector", lambda v:
                v.tensor_scalar_mul(a.kmx[96:97, 0:H], a.kmx[96:97, 0:H], -1.0),
                s_kmx)

    def emit_p3(self, si):
        a = self
        aw_qn = a.aw_qn; aw_qr = a.aw_qr; q_aug = a.q_aug; att_fm = a.att_fm; att_tmp = a.att_tmp; sexp = a.sexp; den_b = a.den_b; srec = a.srec; wpb = a.wpb; xb2 = a.xb2; xpst = a.xpst
        sfx = f"_{si}"
        s_nrm = self.sem(f"s_nrm")
        s_kmx = self.sem(f"s_kmx")
        s_kev = self.sem(f"s_kev")
        s_kve = self.sem(f"s_kve")
        s_qb = self.sem(f"s_qb")
        s_qbe = self.sem(f"s_qbe")
        s_qsq = self.sem(f"s_qsq")
        s_qcs = self.sem(f"s_qcs")
        s_crw = self.sem(f"s_crw")
        s_qk = self.sem(f"s_qk")
        s_msk = self.sem(f"s_msk")
        s_exp = self.sem(f"s_exp")
        s_pvk = self.sem(f"s_pvk")
        s_dnc = self.sem(f"s_dnc")
        s_dnd = self.sem(f"s_dnd")
        s_dbc = self.sem(f"s_dbc")
        s_rcp = self.sem(f"s_rcp")
        s_div = self.sem(f"s_div")
        s_odm = self.sem(f"s_odm")
        s_wp = [self.sem(f"s_wp{i}") for i in range(2)]
        s_op = self.sem(f"s_op")
        s_ope = self.sem(f"s_ope")
        s_x2 = [self.sem(f"s_x2{i}") for i in range(2)]
        s_xpw = self.sem(f"s_xpw")
        xs = self.d_x[si]
        xps = self.d_xp[si]

        self.wait("tensor", s_kev)
        self.wait("tensor", s_kve)
        for qc in range(NTC):
            nkt = 4 * (qc + 1)
            if qc == 0:
                self.wait("tensor", s_nrm)
            self.g_wait("tensor", ("q_aug",))
            qbv = {}

            def SQNR(h, qc=qc):
                self.pump(4)
                ppn = self.rot[(2 * h) % 4]
                ppr = self.rot[(2 * h + 1) % 4]
                self.pbank("tensor", ppn)
                def qn(t, h=h, ppn=ppn, qc=qc):
                    for i in range(6):
                        li = t.matmul(ppn[0:64, :],
                                      aw_qn[:, i * H * DN + h * DN:i * H * DN + (h + 1) * DN],
                                      a.nq[:, i * S + qc * TC:i * S + (qc + 1) * TC],
                                      start=(i == 0), stop=(i == 5))
                    return li
                self.op("tensor", qn, s_qb)
                self.pbank("tensor", ppr)
                def qr(t, h=h, ppr=ppr, qc=qc):
                    for i in range(6):
                        li = t.matmul(ppr[64:96, :],
                                      aw_qr[:, i * H * DR + h * DR:i * H * DR + (h + 1) * DR],
                                      a.nq[:, i * S + qc * TC:i * S + (qc + 1) * TC],
                                      start=(i == 0), stop=(i == 5))
                    return li
                self.op("tensor", qr, s_qb)
                self.wait("vector", s_qb)
                def qev(v, h=h, ppn=ppn, ppr=ppr, qc=qc):
                    dn = q_aug[0:64, h * TC:(h + 1) * TC]
                    dr = q_aug[64:96, h * TC:(h + 1) * TC]
                    raw = a.r32[64:96, :]
                    v.tensor_scalar_mul(dn, ppn[0:64, :], SC)
                    v.tensor_copy(raw, ppr[64:96, :])
                    v.stream_shuffle(dr, raw, SHUF)
                    v.scalar_tensor_tensor(dr, dr, self.c_sig[64:96, :], raw,
                                           ALU.mult, ALU.add)
                    return v.tensor_tensor(dr, dr,
                                           self.c_crq[64:96, qc * TC:(qc + 1) * TC],
                                           ALU.mult)
                self.op("vector", qev, s_qbe)
                qbv[h] = s_qbe.v
                self.g_set(("bank", self.bname[id(ppn)]), s_qbe)
                self.g_set(("bank", self.bname[id(ppr)]), s_qbe)

            def SCHAIN(h, qc=qc):
                qs = a.qsq[h % 2]
                pX = [self.pE, self.pF][h % 2]
                self.wait("scalar", s_qbe, qbv[h])
                self.g_wait("scalar", ("qsq", h % 2))
                self.op("scalar", lambda s, h=h, qs=qs:
                        s.activation(qs[0:96, :], q_aug[0:96, h * TC:(h + 1) * TC],
                                     AF.Square, bias=0.0, scale=1.0), s_qsq)
                self.wait("tensor", s_qsq)
                self.pbank("tensor", pX)
                self.op("tensor", lambda t, qs=qs, pX=pX:
                        t.matmul(pX[64:97, 0:TC], self.c_cones[:, :],
                                 qs[0:96, :], start=True, stop=True), s_qcs)
                self.g_set(("qsq", h % 2), s_qcs)
                self.wait("scalar", s_qcs)
                self.g_wait("scalar", ("crow",))
                self.op("scalar", lambda s, pX=pX:
                        s.activation(a.crow[96:97, :], pX[96:97, 0:TC], AF.Sqrt,
                                     bias=0.0, scale=1.0), s_qsq)
                self.g_set(("bank", self.bname[id(pX)]), s_qsq)
                self.wait("vector", s_qsq)
                if qc == 0 and h == 0:
                    self.wait("vector", s_kmx)
                self.op("vector", lambda v, h=h:
                        v.tensor_single_scalar(q_aug[96:97, h * TC:(h + 1) * TC],
                                               a.crow[96:97, :],
                                               a.kmx[96:97, h:h + 1], ALU.mult),
                        s_crw)
                self.g_set(("crow",), s_crw)

            SQNR(0)
            for h in range(1, H):
                SQNR(h)
                SCHAIN(h - 1)
            SCHAIN(H - 1)
            if si == 0 and qc == 3:
                self.emit_gamma_finalize()

            self.wait("tensor", s_crw)
            pend_den = [None]

            def mk_den(h, pv, pvkv):
                def den():
                    self.wait("vector", s_pvk, pvkv)
                    self.g_wait("vector", ("den",))
                    self.op("vector", lambda v, pv=pv:
                            v.tensor_copy(den_b[64:65, :], pv[64:65, :]), s_dnc)
                    self.wait("scalar", s_dnc)
                    self.dma("scalar", den_b[0:1, :], den_b[64:65, :], s_dnd)
                    self.wait("tensor", s_dnd)
                    self.pbank("tensor", self.pE)
                    self.op("tensor", lambda t:
                            t.matmul(self.pE[0:64, :], self.c_ones64[:, :],
                                     den_b[0:1, :], start=True, stop=True), s_dbc)
                    self.g_set(("den",), s_dbc)
                    self.wait("scalar", s_dbc)
                    self.pbank("scalar", self.pF)
                    self.g_wait("scalar", ("srec",))
                    def rcp(s):
                        s.activation(self.pF[0:64, :], self.pE[0:64, :], AF.Ln,
                                     bias=0.0, scale=1.0)
                        return s.activation(srec[:, :], self.pF[0:64, :], AF.Exp,
                                            bias=0.0, scale=-1.0)
                    self.op("scalar", rcp, s_rcp)
                    self.g_set(("bank", "E"), s_rcp)
                    self.g_set(("bank", "F"), s_rcp)
                    self.wait("vector", s_rcp)
                    even = (h % 2 == 0)
                    if even:
                        dst = att_fm[0:64, (h // 2) * TC:(h // 2 + 1) * TC]
                    else:
                        self.g_wait("vector", ("att_tmp",))
                        dst = att_tmp[0:64, :]
                    self.op("vector", lambda v, pv=pv, dst=dst:
                            v.tensor_mul(dst, pv[0:64, :], srec[:, :]), s_div)
                    self.g_set(("srec",), s_div)
                    self.g_set(("bank", self.bname[id(pv)]), s_div)
                    if not even:
                        self.wait("gpsimd", s_div)
                        self.dma("gpsimd",
                                 att_fm[64:128, (h // 2) * TC:(h // 2 + 1) * TC],
                                 att_tmp[0:64, :], s_odm)
                        self.g_set(("att_tmp",), s_odm)
                return den

            for h in range(H):
                self.pump(3)
                pv = [self.pC, self.pD][h % 2]
                self.pbank("tensor", pv)
                expv = {}

                def SQK(kt, h=h, qc=qc):
                    psc = [self.pA, self.pB][kt % 2]
                    self.pbank("tensor", psc)
                    self.op("tensor", lambda t, h=h, kt=kt, psc=psc:
                            t.matmul(psc[:, :],
                                     a.qk_k[0:97, h * S + kt * 128:h * S + (kt + 1) * 128],
                                     q_aug[0:97, h * TC:(h + 1) * TC],
                                     start=True, stop=True), s_qk)
                    return psc

                def SME(kt, psc, qc=qc):
                    diag = kt >= 4 * qc
                    if diag:
                        r = kt - 4 * qc
                        self.wait("vector", s_qk)
                        self.op("vector", lambda v, psc=psc, r=r:
                                v.tensor_add(psc[:, :], psc[:, :],
                                             self.c_mask[:, r * 512:(r + 1) * 512]),
                                s_msk)
                        self.wait("scalar", s_msk)
                    else:
                        self.wait("scalar", s_qk)
                    self.g_wait("scalar", ("sexp", kt % 2))
                    self.op("scalar", lambda s, psc=psc, kt=kt:
                            s.activation(sexp[kt % 2][:, :], psc[:, :], AF.Exp,
                                         bias=0.0, scale=1.0), s_exp)
                    expv[kt] = s_exp.v
                    self.g_set(("bank", self.bname[id(psc)]), s_exp)

                def SPV(kt, h=h, pv=pv, nkt=nkt):
                    self.wait("tensor", s_exp, expv[kt])
                    self.op("tensor", lambda t, h=h, kt=kt, pv=pv,
                            last=(kt == nkt - 1):
                            t.matmul(pv[0:65, :],
                                     a.v_t[:, kt * H * 65 + h * 65:
                                           kt * H * 65 + (h + 1) * 65],
                                     sexp[kt % 2][:, :],
                                     start=(kt == 0), stop=last), s_pvk)
                    self.g_set(("sexp", kt % 2), s_pvk)

                p0 = SQK(0)
                SME(0, p0)
                for kt in range(1, nkt):
                    p = SQK(kt)
                    if kt == 3 and pend_den[0] is not None:
                        pend_den[0]()
                        pend_den[0] = None
                    SME(kt, p)
                    SPV(kt - 1)
                SPV(nkt - 1)
                pend_den[0] = mk_den(h, pv, s_pvk.v)
            pend_den[0]()
            pend_den[0] = None

            self.wait("tensor", s_div)
            self.wait("tensor", s_odm)
            for dc in range(4):
                self.pump(3)
                for ic in range(8):
                    par = ic % 2
                    self.g_wait("gpsimd", ("wpb", par))
                    self.dma("gpsimd", wpb[par][:, :],
                             self.d_wpr[:, ic * D + dc * 512:
                                        ic * D + (dc + 1) * 512], s_wp[par])
                    if ic < 2:
                        tt = ic
                        self.g_wait("gpsimd", ("xb2", tt % 2))
                        self.dma("gpsimd", xb2[tt % 2][:, :],
                                 xs[(qc * 4 + tt) * 128:(qc * 4 + tt + 1) * 128,
                                    dc * 512:(dc + 1) * 512], s_x2[tt % 2])
                    self.wait("tensor", s_wp[par])
                    if ic == 0:
                        for tt in range(4):
                            self.pbank("tensor", self.rot[tt])
                    def opmm(t, ic=ic, par=par, qc=qc):
                        li = None
                        for tt in range(4):
                            li = t.matmul(self.rot[tt][:, :],
                                          att_fm[:, ic * TC + tt * 128:
                                                   ic * TC + (tt + 1) * 128],
                                          wpb[par][:, :],
                                          start=(ic == 0), stop=(ic == 7))
                        return li
                    self.op("tensor", opmm, s_op)
                    self.g_set(("wpb", par), s_op)
                opv = s_op.v
                for tt in range(4):
                    ti = qc * 4 + tt
                    self.wait("vector", s_op, opv)
                    self.wait("vector", s_x2[tt % 2])
                    self.g_wait("vector", ("xpst", tt % 2))
                    self.op("vector", lambda v, tt=tt:
                            v.scalar_tensor_tensor(
                                xpst[tt % 2][:, :], self.rot[tt][:, :], 1.0,
                                xb2[tt % 2][:, :], ALU.mult, ALU.add), s_ope)
                    self.g_set(("bank", self.bname[id(self.rot[tt])]), s_ope)
                    self.g_set(("xb2", tt % 2), s_ope)
                    if tt < 2:
                        nt = tt + 2
                        self.g_wait("gpsimd", ("xb2", nt % 2))
                        self.dma("gpsimd", xb2[nt % 2][:, :],
                                 xs[(qc * 4 + nt) * 128:(qc * 4 + nt + 1) * 128,
                                    dc * 512:(dc + 1) * 512], s_x2[nt % 2])
                    self.wait("scalar", s_ope)
                    self.dma("scalar",
                             xps[ti * 128:(ti + 1) * 128, dc * 512:(dc + 1) * 512],
                             xpst[tt % 2][:, :], s_xpw)
                    if self.dbg_xp:
                        self.wait("gpsimd", s_ope)
                        self.dma("gpsimd",
                                 self.d_xpdbg[si][ti * 128:(ti + 1) * 128,
                                                  dc * 512:(dc + 1) * 512],
                                 xpst[tt % 2][:, :], s_xpw)
                    self.g_set(("xpst", tt % 2), s_xpw)
            self.g_set(("att_fm",), s_op)
            self.g_set(("q_aug",), s_qk)
            if self.dbg_xp and si == 0 and qc == 0:
                sdbg = self.sem("s_dbg")
                self.wait("gpsimd", s_div)
                self.wait("gpsimd", s_odm)
                self.dma("gpsimd", self.d_dbg_qk, a.qk_k[:, :], sdbg)
                self.dma("gpsimd", self.d_dbg_qa, q_aug[:, :], sdbg)
                self.dma("gpsimd", self.d_dbg_af, att_fm[:, :], sdbg)
                self.dma("gpsimd", self.d_dbg_vt, a.v_t[:, :], sdbg)
                self.dma("gpsimd", self.d_dbg_sr, srec[:, :], sdbg)
                self.dma("gpsimd", self.d_dbg_nq, a.nq[:, :], sdbg)
                self.wait("gpsimd", sdbg)
        self.g_set(("nq",), s_qb)
        self.g_set(("nkv",), self.sem("s_kv"))
        self.g_set(("qk_k",), s_qk)
        self.g_set(("v_t",), s_pvk)
        self.g_set(("kmx",), s_crw)

    def emit_ffn(self, esF):
        a = self
        nc = self.nc
        SB = lambda n, sh, dt=F32: esF.enter_context(nc.sbuf_tensor(n, sh, dt))
        h_sb = SB("h_sb", [128, 64 * 512], BF16)
        xqT = [SB(f"xqT{i}", [128, 16 * 512], BF16) for i in range(2)]
        w1b = [SB(f"w1b{i}", [128, 4 * 512], BF16) for i in range(2)]
        w2b = [SB(f"w2b{i}", [128, 4 * 512], BF16) for i in range(2)]
        outst = [SB(f"outst{i}", [128, 512], F32) for i in range(2)]
        xw = [SB(f"xw{i}", [128, 4 * D], BF16) for i in range(2)]
        y2f = SB("y2f", [128, D], F32)
        fdump = y2f.bitcast(BF16)
        xq_t = [SB(f"xq_t{i}", [128, D], BF16) for i in range(2)]
        t32 = SB("t32", [128, 512], F32)
        ci32 = SB("ci32", [128, 512], I32)
        d32 = SB("d32", [128, 512], F32)
        gbc = SB("gbc", [128, 512], F32)
        fst = SB("fst", [128, 16], F32)
        fcol = SB("fcol", [128, 8], F32)
        hmxb = SB("hmxb", [128, 512], BF16)
        umrep = SB("umrep", [128, 128], BF16)
        umc = SB("umc", [128, 4], F32)
        rcol = SB("rcol", [128, 4], F32)

        s_xw = [self.sem(f"s_fxw{i}") for i in range(2)]
        s_sq = self.sem("s_fsq")
        s_ln = self.sem("s_fln")
        s_y2 = self.sem("s_fy2")
        s_xq = self.sem("s_fxq")
        s_tp = self.sem("s_ftp")
        s_tpe = self.sem("s_ftpe")
        s_w1 = [self.sem(f"s_fw1{i}") for i in range(2)]
        s_w2 = [self.sem(f"s_fw2{i}") for i in range(2)]
        s_m1 = self.sem("s_fm1")
        s_ev = self.sem("s_fev")
        s_mx = self.sem("s_fmx")
        s_row = self.sem("s_frow")
        s_rtp = self.sem("s_frtp")
        s_rte = self.sem("s_frte")
        s_gb = self.sem("s_fgb")
        s_hq = self.sem("s_fhq")
        s_m2 = self.sem("s_fm2")
        s_oev = self.sem("s_foev")
        s_out = self.sem("s_fout")

        def flush_to(mark):
            while getattr(self, "pp_popped", 0) < mark and self.pp_steps:
                self.pump(1)

        flush_to(getattr(self, "pp_w1q_end", 0))
        self.wait("sync", self.sem("s_cachew1"))
        self.wait("gpsimd", self.sem("s_xpw"))
        self.wait("gpsimd", self.sem("s_cachew1"))

        NW = 2 * NTC
        xqv = {}

        def xw_loads(w):
            wp = w % 2
            xps = self.d_xp[w // NTC]
            t0 = (w % NTC) * TC
            self.g_wait("gpsimd", ("xw", wp))
            for tt in range(4):
                self.dma("gpsimd", xw[wp][:, tt * D:(tt + 1) * D],
                         xps[(t0 + tt * 128):(t0 + (tt + 1) * 128), :], s_xw[wp])

        def ln_tile(w, tt):
            wp = w % 2
            pc = (tt % 2) * 8
            xwt = xw[wp]
            self.wait("scalar", s_xw[wp])
            self.g_wait("scalar", ("fst", tt % 2))
            self.g_wait("scalar", ("y2f",))
            def sq2(s, tt=tt, pc=pc, xwt=xwt):
                s.activation(fdump[:, 0:1024], xwt[:, tt * D:tt * D + 1024],
                             AF.Square, bias=0.0, scale=1.0,
                             accum_out=fst[:, pc:pc + 1])
                return s.activation(fdump[:, 0:1024],
                                    xwt[:, tt * D + 1024:(tt + 1) * D],
                                    AF.Square, bias=0.0, scale=1.0,
                                    accum_out=fst[:, pc + 2:pc + 3])
            self.op("scalar", sq2, s_sq)
            self.wait("vector", s_xw[wp])
            self.wait("vector", s_sq)
            self.op("vector", lambda v, tt=tt, pc=pc, xwt=xwt:
                    v.reduce_sum(fst[:, pc + 1:pc + 2], xwt[:, tt * D:(tt + 1) * D],
                                 axis=AX.X), self.sem("s_rr"))
            self.wait("vector", self.sem("s_rr"))
            def flnv_a(v, pc=pc):
                st = fst
                v.tensor_add(st[:, pc:pc + 1], st[:, pc:pc + 1],
                             st[:, pc + 2:pc + 3])
                return v.tensor_scalar_mul(st[:, pc + 1:pc + 2],
                                           st[:, pc + 1:pc + 2], 1.0 / D)
            def flnv_b(v, pc=pc):
                st = fst
                v.tensor_scalar(st[:, pc:pc + 1], st[:, pc:pc + 1],
                                1.0 / D, LN_EPS, ALU.mult, ALU.add)
                return v.tensor_tensor(st[:, pc + 2:pc + 3], st[:, pc + 1:pc + 2],
                                       st[:, pc + 1:pc + 2], ALU.mult)
            def flnv_c(v, pc=pc):
                st = fst
                return v.tensor_sub(st[:, pc:pc + 1], st[:, pc:pc + 1],
                                    st[:, pc + 2:pc + 3])
            self.chain("vector", [flnv_a, flnv_b, flnv_c], s_ln)
            self.wait("scalar", s_ln)
            self.op("scalar", lambda s, pc=pc:
                    s.activation(fst[:, pc + 3:pc + 4], fst[:, pc:pc + 1],
                                 AF.Sqrt, bias=0.0, scale=1.0), s_sq)
            self.wait("vector", s_sq)
            def fl2_a(v, pc=pc):
                return v.reciprocal(fst[:, pc + 4:pc + 5], fst[:, pc + 3:pc + 4])
            def fl2_b(v, pc=pc):
                return v.scalar_tensor_tensor(fst[:, pc + 6:pc + 7],
                                              fst[:, pc + 1:pc + 2], -1.0,
                                              fst[:, pc + 4:pc + 5],
                                              ALU.mult, ALU.mult)
            def fl2_c(v, tt=tt, pc=pc, xwt=xwt):
                return v.tensor_scalar(y2f[:, :], xwt[:, tt * D:(tt + 1) * D],
                                       fst[:, pc + 4:pc + 5],
                                       fst[:, pc + 6:pc + 7], ALU.mult, ALU.add)
            self.chain("vector", [fl2_a, fl2_b, fl2_c], s_y2)
            self.g_set(("fst", tt % 2), s_y2)
            self.op("vector", lambda v, pc=pc:
                    v.reduce_max(fst[:, pc + 7:pc + 8], y2f[:, :], axis=AX.X,
                                 apply_absolute_value=True), self.sem("s_rr"))
            self.wait("vector", self.sem("s_rr"))
            def qa_a(v, pc=pc):
                return v.tensor_scalar_max(fst[:, pc + 7:pc + 8],
                                           fst[:, pc + 7:pc + 8], Q_EPS)
            def qa_b(v, tt=tt, pc=pc):
                v.tensor_scalar_mul(fcol[:, tt + 4:tt + 5],
                                    fst[:, pc + 7:pc + 8], 1.0 / 127.0)
                return v.reciprocal(fst[:, pc + 7:pc + 8], fst[:, pc + 7:pc + 8])
            def qa_c(v, tt=tt, pc=pc):
                v.scalar_tensor_tensor(fcol[:, tt:tt + 1], fcol[:, tt + 4:tt + 5],
                                       1.0, fcol[:, tt + 4:tt + 5],
                                       ALU.mult, ALU.mult)
                return v.tensor_scalar_mul(fst[:, pc + 7:pc + 8],
                                           fst[:, pc + 7:pc + 8], 127.0)
            def qa_d(v, tt=tt, pc=pc):
                v.tensor_tensor(fcol[:, tt:tt + 1], fcol[:, tt:tt + 1],
                                self.gam[:, 4:5], ALU.mult)
                return v.tensor_single_scalar(y2f[:, :], y2f[:, :],
                                              fst[:, pc + 7:pc + 8], ALU.mult)
            self.chain("vector", [qa_a, qa_b, qa_c, qa_d], s_y2)
            self.g_wait("vector", ("xq_t", tt % 2))
            xqt = xq_t[tt % 2]
            for c in range(4):
                def rnd(v, c=c, xqt=xqt):
                    sl = slice(c * 512, (c + 1) * 512)
                    v.tensor_copy(ci32[:, :], y2f[:, sl])
                    v.scalar_tensor_tensor(d32[:, :], ci32[:, :], -1.0,
                                           y2f[:, sl], ALU.mult, ALU.add)
                    v.scalar_tensor_tensor(t32[:, :], d32[:, :], 0.5,
                                           ci32[:, :], ALU.is_ge, ALU.add)
                    v.tensor_scalar(d32[:, :], d32[:, :], -0.5, -1.0,
                                    ALU.is_le, ALU.mult)
                    return v.tensor_add(xqt[:, sl], t32[:, :], d32[:, :])
                self.op("vector", rnd, s_xq)
            self.g_set(("y2f",), s_xq)
            xqv[(w, tt)] = s_xq.v

        def tp_tile(w, tt):
            wp = w % 2
            self.wait("tensor", s_xq, xqv[(w, tt)])
            if tt == 0:
                self.g_wait("tensor", ("xqT", wp))
            xqt = xq_t[tt % 2]
            for fb in range(16):
                ptp = self.pT[fb % 2]
                self.pbank("tensor", ptp)
                self.op("tensor", lambda t, fb=fb, ptp=ptp, xqt=xqt:
                        t.transpose(ptp[:, 0:128],
                                    xqt[:, fb * 128:(fb + 1) * 128],
                                    self.c_ident[:, :]), s_tp)
                self.wait("vector", s_tp)
                self.op("vector", lambda v, fb=fb, tt=tt, ptp=ptp, wp=wp:
                        v.tensor_copy(xqT[wp][:, fb * 512 + tt * 128:
                                              fb * 512 + (tt + 1) * 128],
                                      ptp[:, 0:128]), s_tpe)
                self.g_set(("bank", self.bname[id(ptp)]), s_tpe)
            self.g_set(("xq_t", tt % 2), s_tp)

        xw_loads(0)
        for tt in range(4):
            ln_tile(0, tt)
            tp_tile(0, tt)

        for w in range(NW):
            si = w // NTC
            t0 = (w % NTC) * TC
            wp = w % 2
            if w + 1 < NW:
                xw_loads(w + 1)

            self.wait("tensor", s_tpe)
            self.g_wait("tensor", ("h_sb",))
            for og in range(16):
                self.pump(4)
                for ih in range(4):
                    par = (og * 4 + ih) % 2
                    self.g_wait("sync", ("w1b", par))
                    self.dma("sync", w1b[par][:, :],
                             self.d_w1q[og, ih * 4:(ih + 1) * 4]
                             .rearrange("i p f -> p i f"), s_w1[par])
                    self.wait("tensor", s_w1[par])
                    for ocin in range(4):
                        oc = og * 4 + ocin
                        pp = self.rot[oc % 4]
                        if ih == 0:
                            self.pbank("tensor", pp)
                        def m1(t, par=par, ih=ih, ocin=ocin, pp=pp, wp=wp):
                            for ii in range(4):
                                i = ih * 4 + ii
                                li = t.matmul(pp[:, :],
                                              w1b[par][:, ii * 512 + ocin * 128:
                                                       ii * 512 + (ocin + 1) * 128],
                                              xqT[wp][:, i * 512:(i + 1) * 512],
                                              start=(i == 0), stop=(i == 15))
                            return li
                        self.op("tensor", m1, s_m1)
                        if ih == 3:
                            self.wait("scalar", s_m1)
                            def ev(s, oc=oc, pp=pp):
                                s.activation(h_sb[:, oc * 512:(oc + 1) * 512], pp[:, :],
                                             AF.Relu, bias=0.0, scale=1.0 / 128.0)
                                return s.activation(h_sb[:, oc * 512:(oc + 1) * 512],
                                                    h_sb[:, oc * 512:(oc + 1) * 512],
                                                    AF.Square, bias=0.0, scale=1.0)
                            self.op("scalar", ev, s_ev)
                            self.g_set(("bank", self.bname[id(pp)]), s_ev)
                            self.wait("vector", s_ev)
                            if oc == 0:
                                self.op("vector", lambda v:
                                        v.tensor_copy(hmxb[:, :], h_sb[:, 0:512]),
                                        s_mx)
                            else:
                                self.op("vector", lambda v, oc=oc:
                                        v.tensor_max(hmxb[:, :], hmxb[:, :],
                                                     h_sb[:, oc * 512:(oc + 1) * 512]),
                                        s_mx)
                    self.g_set(("w1b", par), s_m1)

            for c in range(4):
                ptp = self.pT[c % 2]
                self.wait("tensor", s_mx)
                self.pbank("tensor", ptp)
                self.op("tensor", lambda t, c=c, ptp=ptp:
                        t.transpose(ptp[:, 0:128], hmxb[:, c * 128:(c + 1) * 128],
                                    self.c_ident[:, :]), s_rtp)
                self.wait("vector", s_rtp)
                self.op("vector", lambda v, c=c, ptp=ptp:
                        v.reduce_max(umc[:, c:c + 1], ptp[:, 0:128], axis=AX.X),
                        self.sem("s_rr"))
                self.wait("scalar", self.sem("s_rr"))
                self.op("scalar", lambda s, c=c:
                        s.activation(umrep[:, :], self.c_ident[:, :], AF.Relu,
                                     bias=umc[:, c:c + 1], scale=0.0),
                        self.sem("s_rr"))
                self.wait("tensor", self.sem("s_rr"))
                self.g_set(("bank", self.bname[id(ptp)]), self.sem("s_rr"))
                self.pbank("tensor", ptp)
                self.op("tensor", lambda t, ptp=ptp:
                        t.transpose(ptp[:, 0:128], umrep[:, :], self.c_ident[:, :]),
                        s_rtp)
                self.wait("vector", s_rtp)
                self.op("vector", lambda v, c=c, ptp=ptp:
                        v.tensor_copy(gbc[:, c * 128:(c + 1) * 128], ptp[:, 0:128]),
                        s_rte)
                self.g_set(("bank", self.bname[id(ptp)]), s_rte)
            def gfin(v):
                v.tensor_scalar_max(gbc[:, :], gbc[:, :], 1e-30)
                v.reciprocal(gbc[:, :], gbc[:, :])
                return v.tensor_scalar_mul(gbc[:, :], gbc[:, :], 127.0)
            self.op("vector", gfin, s_gb)
            def rc_a(v):
                r = None
                for c in range(4):
                    r = v.tensor_tensor(rcol[:, c:c + 1], fcol[:, c:c + 1],
                                        umc[:, c:c + 1], ALU.mult)
                return r
            def rc_b(v):
                return v.tensor_scalar_max(rcol[:, :], rcol[:, :], Q_EPS)
            def rc_c(v):
                r = None
                for c in range(4):
                    r = v.tensor_tensor(rcol[:, c:c + 1], rcol[:, c:c + 1],
                                        self.gam[:, 5:6], ALU.mult)
                return r
            self.chain("vector", [rc_a, rc_b, rc_c], s_rte)

            hqb = s_hq.v
            for oc in range(64):
                sl = slice(oc * 512, (oc + 1) * 512)
                def hqc(v, sl=sl):
                    v.tensor_tensor(t32[:, :], h_sb[:, sl], gbc[:, :], ALU.mult)
                    v.tensor_copy(ci32[:, :], t32[:, :])
                    v.scalar_tensor_tensor(d32[:, :], ci32[:, :], -1.0,
                                           t32[:, :], ALU.mult, ALU.add)
                    return v.scalar_tensor_tensor(h_sb[:, sl], d32[:, :], 0.5,
                                                  ci32[:, :], ALU.is_ge, ALU.add)
                self.op("vector", hqc, s_hq)

            if w + 1 < NW:
                ln_tile(w + 1, 0)
                ln_tile(w + 1, 1)

            for dq in range(4):
                self.pump(4)
                for ob in range(16):
                    par = ob % 2
                    if w == 0 and ob == 0:
                        flush_to(self.pp_w2g_end[dq])
                        self.wait("sync", self.sem("s_w2c"), dq + 1)
                    self.g_wait("sync", ("w2b", par))
                    self.dma("sync", w2b[par][:, :],
                             self.d_w2q[dq, ob * 4:(ob + 1) * 4]
                             .rearrange("o p f -> p o f"), s_w2[par])
                    self.wait("tensor", s_w2[par])
                    for oi in range(4):
                        o = ob * 4 + oi
                        if dq == 0:
                            self.wait("tensor", s_hq, hqb + o + 1)
                        if o == 0:
                            for tt in range(4):
                                self.pbank("tensor", self.rot[tt])
                        def m2(t, dq=dq, o=o, oi=oi, par=par):
                            li = None
                            for tt in range(4):
                                li = t.matmul(self.rot[tt][:, :],
                                              h_sb[:, o * 512 + tt * 128:
                                                   o * 512 + (tt + 1) * 128],
                                              w2b[par][:, oi * 512:(oi + 1) * 512],
                                              start=(o == 0), stop=(o == 63))
                            return li
                        self.op("tensor", m2, s_m2)
                    self.g_set(("w2b", par), s_m2)
                self.wait("vector", s_m2)
                self.wait("vector", s_rte)
                for tt in range(4):
                    pst = (dq * 4 + tt) % 2
                    self.g_wait("vector", ("outst", pst))
                    def oev(v, dq=dq, tt=tt, pst=pst, wp=wp):
                        return v.scalar_tensor_tensor(
                            outst[pst][:, :], self.rot[tt][:, :],
                            rcol[:, tt:tt + 1],
                            xw[wp][:, tt * D + dq * 512:tt * D + (dq + 1) * 512],
                            ALU.mult, ALU.add)
                    self.op("vector", oev, s_oev)
                    self.g_set(("bank", self.bname[id(self.rot[tt])]), s_oev)
                    self.wait("gpsimd", s_oev)
                    self.dma("gpsimd",
                             self.d_out[si][(t0 + tt * 128):(t0 + (tt + 1) * 128),
                                            dq * 512:(dq + 1) * 512],
                             outst[pst][:, :], s_out)
                    self.g_set(("outst", pst), s_out)
                if w + 1 < NW:
                    tp_tile(w + 1, dq)
                    if dq == 0:
                        ln_tile(w + 1, 2)
                    elif dq == 1:
                        ln_tile(w + 1, 3)
            self.g_set(("xw", wp), s_oev)
            self.g_set(("h_sb",), s_m2)
            self.g_set(("xqT", wp), s_m1)

        self.flush_pp()
        self.wait("gpsimd", s_out)
        for (eng, _), sem in getattr(self, "dma_sems", {}).items():
            self.wait(eng, sem)

    def finalize(self):
        blk = self.es.enter_context(self.nc.Block())
        for eng in ENGS:
            def mk(eng):
                def run(e):
                    for fn in self.streams[eng]:
                        fn(e)
                return run
            getattr(blk, eng)(mk(eng))


def build_program(dbg_xp=False):
    nc = bass.Bass("TRN2", target_bir_lowering=False,
                   detect_race_conditions=False)
    with ExitStack() as es:
        b = Builder(nc, es, dbg_xp=dbg_xp)
        b.declare_io()
        b.alloc_persistent()
        with ExitStack() as esA:
            b.emit_prepass_streams(esA)
            b.alloc_attn_outer(esA)
            for si in range(NSEQ):
                with ExitStack() as es12:
                    b.alloc_nkv(es12, si)
                    with ExitStack() as es1:
                        b.alloc_p1(es1, si)
                        b.emit_p1(si)
                    b.barrier()
                    b.alloc_p2(es12, si)
                    b.emit_p2(si)
                b.barrier()
                with ExitStack() as es3:
                    b.alloc_p3(es3, si)
                    b.emit_p3(si)
                b.barrier()
            b.barrier()
        with ExitStack() as esF:
            b.emit_ffn(esF)
        b.finalize()
    return nc


def _consts():
    c = {}
    c["ident"] = np.eye(128, dtype=ml_dtypes.bfloat16)
    c["cones"] = np.ones((96, 33), dtype=ml_dtypes.bfloat16)
    c["ones64"] = np.ones((1, 64), dtype=ml_dtypes.bfloat16)
    c["onesbc"] = np.ones((1, 128), dtype=np.float32)
    c["onescol"] = np.ones((128, 1), dtype=np.float32)
    c["onescolb"] = np.ones((128, 1), dtype=ml_dtypes.bfloat16)
    freqs = 1.0 / (10000.0 ** (np.arange(0, DR, 2, dtype=np.float64) / DR))
    ang = np.outer(np.arange(S, dtype=np.float64), freqs)
    cos = np.cos(ang).T
    crep = np.concatenate([cos, cos], axis=0)
    c["crq"] = (crep * SC).astype(ml_dtypes.bfloat16)
    c["crk"] = (crep / H).astype(ml_dtypes.bfloat16)
    c["sig"] = np.concatenate([-np.ones((16, 1)), np.ones((16, 1))]).astype(np.float32)
    m = np.zeros((128, 4 * 512), dtype=np.float32)
    kk = np.arange(128)[:, None]
    qq = np.arange(512)[None, :]
    for r in range(4):
        m[:, r * 512:(r + 1) * 512] = np.where(qq >= kk + r * 128, 0.0, NEG)
    c["mask"] = m.astype(ml_dtypes.bfloat16)
    return c


def _perm_rope_cols(w, nheads):
    rows = w.shape[0]
    w = w.reshape(rows, nheads, 16, 2)
    return np.ascontiguousarray(
        np.concatenate([w[:, :, :, 0], w[:, :, :, 1]], axis=2).reshape(rows, -1))


def prepare_base(L0_attn_x, L0_attn_w_cq, L0_attn_w_q_nope, L0_attn_w_q_rope,
                 L0_attn_q_g, L0_attn_w_ckv, L0_attn_w_k_nope, L0_attn_w_v,
                 L0_attn_kv_g, L0_attn_w_k_rope, L0_attn_w_proj, L0_ffn_w1,
                 L0_ffn_w2):
    x = np.ascontiguousarray(np.asarray(L0_attn_x, dtype=np.float32))
    f32 = lambda z: np.ascontiguousarray(np.asarray(z, dtype=np.float32))
    wqr = _perm_rope_cols(f32(L0_attn_w_q_rope), H)
    wkr = _perm_rope_cols(f32(L0_attn_w_k_rope), 1)
    wpr = np.ascontiguousarray(
        f32(L0_attn_w_proj).reshape(8, 128, D).transpose(1, 0, 2).reshape(128, 8 * D))
    w1T = np.ascontiguousarray(f32(L0_ffn_w1).T)
    w2T = np.ascontiguousarray(f32(L0_ffn_w2).T)
    base = dict(
        w_cq=f32(L0_attn_w_cq), w_qn=f32(L0_attn_w_q_nope), w_qr=wqr,
        q_g=np.ascontiguousarray(f32(L0_attn_q_g).reshape(6, 128).T),
        w_ckv=f32(L0_attn_w_ckv),
        w_kn=f32(L0_attn_w_k_nope), w_v=f32(L0_attn_w_v),
        kv_g=np.ascontiguousarray(f32(L0_attn_kv_g).reshape(2, 128).T),
        w_kr=wkr, w_pr=wpr,
        w1T=w1T, w2T=w2T, **_consts())
    return base, x


def kernel(**inputs):
    base, x = prepare_base(**inputs)
    nc = build_program()
    in_maps = []
    for c in range(8):
        m = dict(base)
        m["x_in"] = x[2 * c:2 * c + 2]
        in_maps.append(m)
    res = run_bass_kernel_spmd(nc, in_maps, core_ids=list(range(8)))
    out = np.empty((16, S, D), dtype=np.float32)
    for c in range(8):
        out[2 * c:2 * c + 2] = res.results[c]["out"]
    return out


# version 16
# speedup vs baseline: 1.1009x; 1.0219x over previous
import math
import numpy as np
import ml_dtypes
from contextlib import ExitStack

import concourse.bass as bass
import concourse.mybir as mybir
from concourse.bass_utils import run_bass_kernel_spmd

F32 = mybir.dt.float32
BF16 = mybir.dt.bfloat16
I32 = mybir.dt.int32
AF = mybir.ActivationFunctionType
ALU = mybir.AluOpType
AX = mybir.AxisListType

S, D, H = 2048, 2048, 16
DN, DR, DV = 64, 32, 64
RQ, RKV = 768, 256
DFF = 8192
NSEQ = 2
SC = 1.0 / math.sqrt(DN + DR)
NEG = -1.0e30
LN_EPS, RMS_EPS, Q_EPS = 1e-5, 1e-6, 1e-5
TC = 512
NTC = S // TC
NKT = S // 128
ENGS = ("tensor", "vector", "scalar", "gpsimd", "sync")
SHUF = list(range(16, 32)) + list(range(16))


class Sem:
    def __init__(self, nc, es, name):
        self.h = es.enter_context(nc.semaphore(name))
        self.v = 0
        self.name = name
        self.engs = set()

    def take(self, n=1):
        self.v += n
        return (self.h, n)


class Builder:
    def __init__(self, nc, es, dbg_xp=False):
        self.nc = nc
        self.es = es
        self.dbg_xp = dbg_xp
        self.streams = {e: [] for e in ENGS}
        self.sems = {}
        self.gates = {}
        self.pp_steps = []

    def sem(self, name):
        if name not in self.sems:
            self.sems[name] = Sem(self.nc, self.es, name)
        return self.sems[name]

    def _emit(self, eng, fn):
        self.streams[eng].append(fn)

    def dma(self, eng, dst, src, sem):
        sem.engs.add(eng)
        tgt = sem.take(16)
        self.dma_sems = getattr(self, "dma_sems", {})
        self.dma_sems[(eng, id(sem))] = sem
        self._emit(eng, lambda e, dst=dst, src=src, tgt=tgt:
                   e.dma_start(dst, src).then_inc(*tgt))

    def wait(self, eng, sem, val=None):
        v = sem.v if val is None else val
        if v > 0:
            self._emit(eng, lambda e, h=sem.h, v=v: e.wait_ge(h, v))

    def op(self, eng, fn, sem=None, n=1):
        if sem is None:
            self._emit(eng, fn)
        else:
            sem.engs.add(eng)
            tgt = sem.take(n)
            self._emit(eng, lambda e, fn=fn, tgt=tgt: fn(e).then_inc(*tgt))

    def chain(self, eng, steps, final_sem, n=1):
        s = self.sem(f"s_ch_{eng}")
        for i, fn in enumerate(steps):
            last = (i == len(steps) - 1)
            self.op(eng, fn, final_sem if last else s, n=n if last else 1)
            if not last:
                self.wait(eng, s)

    def g_wait(self, eng, key):
        if key in self.gates:
            s, v = self.gates[key]
            self.wait(eng, s, v)

    def g_set(self, key, sem):
        self.gates[key] = (sem, sem.v)

    def pump(self, k=2):
        for _ in range(min(k, len(self.pp_steps))):
            lim = getattr(self, "pp_limit", None)
            if lim is not None and getattr(self, "pp_popped", 0) >= lim:
                return
            eng, fn = self.pp_steps.pop(0)
            self.pp_popped = getattr(self, "pp_popped", 0) + 1
            self._emit(eng, fn)

    def flush_pp(self):
        while self.pp_steps:
            eng, fn = self.pp_steps.pop(0)
            self._emit(eng, fn)

    def dop(self, eng, fn, sem=None, n=1):
        if sem is None:
            self.pp_steps.append((eng, fn))
        else:
            sem.engs.add(eng)
            tgt = sem.take(n)
            self.pp_steps.append((eng, lambda e, fn=fn, tgt=tgt:
                                  fn(e).then_inc(*tgt)))

    def ddma(self, eng, waits, dst, src, sem):
        cw = [(s.h, (s.v if v is None else v)) for (s, v) in waits]
        cw = [(h, v) for (h, v) in cw if v > 0]
        sem.engs.add(eng)
        tgt = sem.take(16)
        self.dma_sems = getattr(self, "dma_sems", {})
        self.dma_sems[(eng, id(sem))] = sem
        self.pp_steps.append((eng, lambda e, cw=cw, dst=dst, src=src, tgt=tgt:
                              ([e.wait_ge(h, v) for h, v in cw],
                               e.dma_start(dst, src).then_inc(*tgt))[-1]))

    def dstep(self, eng, waits, fn, sem=None, n=1):
        cw = [(s.h, (s.v if v is None else v)) for (s, v) in waits]
        cw = [(h, v) for (h, v) in cw if v > 0]
        if sem is None:
            self.pp_steps.append((eng, lambda e, cw=cw, fn=fn:
                                  ([e.wait_ge(h, v) for h, v in cw],
                                   fn(e))[-1]))
        else:
            sem.engs.add(eng)
            tgt = sem.take(n)
            self.pp_steps.append((eng, lambda e, cw=cw, fn=fn, tgt=tgt:
                                  ([e.wait_ge(h, v) for h, v in cw],
                                   fn(e).then_inc(*tgt))[-1]))

    def dwait(self, eng, sem, val=None):
        v = sem.v if val is None else val
        if v > 0:
            self.pp_steps.append((eng, lambda e, h=sem.h, v=v: e.wait_ge(h, v)))

    def declare_io(self):
        nc = self.nc
        EI, EO, IN = "ExternalInput", "ExternalOutput", "Internal"
        self.d_x = nc.dram_tensor("x_in", [NSEQ, S, D], F32, kind=EI).ap()
        self.d_wcq = nc.dram_tensor("w_cq", [D, RQ], F32, kind=EI).ap()
        self.d_wqn = nc.dram_tensor("w_qn", [RQ, H * DN], F32, kind=EI).ap()
        self.d_wqr = nc.dram_tensor("w_qr", [RQ, H * DR], F32, kind=EI).ap()
        self.d_qg = nc.dram_tensor("q_g", [128, 6], F32, kind=EI).ap()
        self.d_wckv = nc.dram_tensor("w_ckv", [D, RKV], F32, kind=EI).ap()
        self.d_wkn = nc.dram_tensor("w_kn", [RKV, H * DN], F32, kind=EI).ap()
        self.d_wv = nc.dram_tensor("w_v", [RKV, H * DV], F32, kind=EI).ap()
        self.d_kvg = nc.dram_tensor("kv_g", [128, 2], F32, kind=EI).ap()
        self.d_wkr = nc.dram_tensor("w_kr", [D, DR], F32, kind=EI).ap()
        self.d_wpr = nc.dram_tensor("w_pr", [128, 8 * D], F32, kind=EI).ap()
        self.d_w1T = nc.dram_tensor("w1T", [D, DFF], F32, kind=EI).ap()
        self.d_w2T = nc.dram_tensor("w2T", [DFF, D], F32, kind=EI).ap()
        self.d_ident = nc.dram_tensor("ident", [128, 128], BF16, kind=EI).ap()
        self.d_cones = nc.dram_tensor("cones", [96, 33], BF16, kind=EI).ap()
        self.d_ones64 = nc.dram_tensor("ones64", [1, 64], BF16, kind=EI).ap()
        self.d_onesbc = nc.dram_tensor("onesbc", [1, 128], F32, kind=EI).ap()
        self.d_onescol = nc.dram_tensor("onescol", [128, 1], F32, kind=EI).ap()
        self.d_onescolb = nc.dram_tensor("onescolb", [128, 1], BF16, kind=EI).ap()
        self.d_crq = nc.dram_tensor("crq", [32, S], BF16, kind=EI).ap()
        self.d_crk = nc.dram_tensor("crk", [32, S], BF16, kind=EI).ap()
        self.d_sig = nc.dram_tensor("sig", [32, 1], F32, kind=EI).ap()
        self.d_mask = nc.dram_tensor("mask", [128, 4 * 512], BF16, kind=EI).ap()
        self.d_out = nc.dram_tensor("out", [NSEQ, S, D], F32, kind=EO).ap()
        self.d_xp = nc.dram_tensor("xp_scr", [NSEQ, S, D], BF16, kind=IN).ap()
        self.d_w1q = nc.dram_tensor("w1q_c", [16, 16, 128, 512], BF16, kind=IN).ap()
        self.d_w2q = nc.dram_tensor("w2q_c", [4, 64, 128, 512], BF16, kind=IN).ap()
        if self.dbg_xp:
            self.d_xpdbg = nc.dram_tensor("xp_dbg", [NSEQ, S, D], F32, kind=EO).ap()
            self.d_dbg_qk = nc.dram_tensor("dbg_qk", [97, H * S], F32, kind=EO).ap()
            self.d_dbg_qa = nc.dram_tensor("dbg_qa", [97, H * TC], F32, kind=EO).ap()
            self.d_dbg_af = nc.dram_tensor("dbg_af", [128, 8 * TC], F32, kind=EO).ap()
            self.d_dbg_vt = nc.dram_tensor("dbg_vt", [128, NKT * H * 65], F32, kind=EO).ap()
            self.d_dbg_sr = nc.dram_tensor("dbg_sr", [64, 512], F32, kind=EO).ap()
            self.d_dbg_nq = nc.dram_tensor("dbg_nq", [128, 6 * S], F32, kind=EO).ap()
            self.d_dbg_y1 = nc.dram_tensor("dbg_y1", [128, 16 * TC], F32, kind=EO).ap()
            self.d_dbg_st = nc.dram_tensor("dbg_st", [128, 16], F32, kind=EO).ap()
            self.d_dbg_cq = nc.dram_tensor("dbg_cq", [128, 6 * TC], F32, kind=EO).ap()

    def alloc_persistent(self):
        nc, es = self.nc, self.es
        SB = lambda n, sh, dt=F32: es.enter_context(nc.sbuf_tensor(n, sh, dt))
        PS = lambda n, sh, dt=F32: es.enter_context(nc.psum_tensor(n, sh, dt))
        self.pA, self.pB = PS("pA", [128, 512]), PS("pB", [128, 512])
        self.pC, self.pD = PS("pC", [128, 512]), PS("pD", [128, 512])
        self.pE, self.pF = PS("pE", [128, 512]), PS("pF", [128, 512])
        self.pT = [PS("pT0", [128, 1024], BF16), PS("pT1", [128, 1024], BF16)]
        self.rot = [self.pA, self.pB, self.pC, self.pD]
        self.bname = {id(self.pA): "A", id(self.pB): "B", id(self.pC): "C",
                      id(self.pD): "D", id(self.pE): "E", id(self.pF): "F",
                      id(self.pT[0]): "T0", id(self.pT[1]): "T1"}

        self.c_ident = SB("c_ident", [128, 128], BF16)
        self.c_cones = SB("c_cones", [96, 33], BF16)
        self.c_ones64 = SB("c_ones64", [1, 64], BF16)
        self.c_onesbc = SB("c_onesbc", [1, 128], F32)
        self.c_onescol = SB("c_onescol", [128, 1], F32)
        self.c_onescolb = SB("c_onescolb", [128, 1], BF16)
        self.c_crq = SB("c_crq", [96, S], BF16)
        self.c_crk = SB("c_crk", [96, S], BF16)
        self.c_sig = SB("c_sig", [96, 1], F32)
        self.c_mask = SB("c_mask", [128, 4 * 512], BF16)
        self.gam = SB("gam", [128, 8], F32)
        self.rowb = SB("rowb", [128, 16], F32)
        self.stats = SB("stats", [128, 16], F32)

        s = self.sem("s_const")
        for dst, src in [(self.c_ident[:, :], self.d_ident),
                         (self.c_cones[:, :], self.d_cones),
                         (self.c_ones64[:, :], self.d_ones64),
                         (self.c_onesbc[:, :], self.d_onesbc),
                         (self.c_onescol[:, :], self.d_onescol),
                         (self.c_onescolb[:, :], self.d_onescolb),
                         (self.c_crq[64:96, :], self.d_crq),
                         (self.c_crk[64:96, :], self.d_crk),
                         (self.c_sig[64:96, :], self.d_sig),
                         (self.c_mask[:, :], self.d_mask)]:
            self.dma("gpsimd", dst, src, s)
        for eng in ("tensor", "vector", "scalar"):
            self.wait(eng, s)

    def barrier(self):
        sb = self.sem("s_bar")
        self._barn = getattr(self, "_barn", 0) + 1
        self.op("tensor", lambda t: t.nop(), sb)
        self.op("vector", lambda v: v.engine_nop(), sb)
        self.op("scalar", lambda x: x.nop(), sb)
        self.op("gpsimd", lambda g: g.engine_nop(), sb)
        tgt = 4 * self._barn
        assert sb.v == tgt
        for eng in ("tensor", "vector", "scalar", "gpsimd"):
            self.wait(eng, sb, tgt)

    def pbank(self, eng, pp):
        self.g_wait(eng, ("bank", self.bname[id(pp)]))

    def pbank_d(self, eng, pp):
        key = ("bank", self.bname[id(pp)])
        if key in self.gates:
            s, v = self.gates[key]
            self.dwait(eng, s, v)

    def freeze_dump(self, pairs):
        if not self.dbg_xp:
            return
        self.barrier()
        sd = self.sem("s_frz")
        for dst, src in pairs:
            self.dma("gpsimd", dst, src, sd)
        for eng in ("tensor", "vector", "scalar", "gpsimd"):
            self.wait(eng, sd)

    def emit_prepass_streams(self, es):
        nc = self.nc
        SB = lambda n, sh, dt=F32: self.es.enter_context(nc.sbuf_tensor(n, sh, dt))
        self.pwl = [SB(f"pwl{i}", [128, 512], F32) for i in range(4)]
        self.ptau = [SB(f"ptau{i}", [128, 512], BF16) for i in range(2)]
        self.pacc = SB("pacc", [128, 8], F32)
        self.prow = SB("prow", [128, 2], F32)
        self.gamrep = [SB(f"sgnm{i}", [128, 512], BF16) for i in range(2)]
        self.gamneg = [SB(f"sgnp{i}", [128, 512], BF16) for i in range(2)]
        pwl, ptau, pacc, prow = (self.pwl, self.ptau, self.pacc, self.prow)
        pdump = ptau[0]

        s_gld = [self.sem(f"s_gld{i}") for i in range(4)]
        s_gab = [self.sem(f"s_gab{i}") for i in range(4)]
        s_gfr = [self.sem("s_gfr0"), self.sem("s_gfr1")]
        gj = ([(self.d_w1T[r * 128:(r + 1) * 128, c * 512:(c + 1) * 512], 0)
               for r in range(16) for c in range(16)]
              + [(self.d_w2T[r * 128:(r + 1) * 128, c * 512:(c + 1) * 512], 1)
                 for r in range(64) for c in range(4)])
        for idx, (src, col) in enumerate(gj):
            bb = idx % 4
            par = idx % 2
            slot = col * 4 + idx % 4
            if idx >= 4:
                self.wait("sync", s_gab[bb], idx // 4)
            self.dma("sync", pwl[bb][:, :], src, s_gld[bb])
            sw = [(s_gld[bb], None)]
            if idx >= 2:
                sw.append((s_gfr[par], None))
            self.dstep("scalar", sw, lambda e, bb=bb, par=par:
                       e.activation(pdump[:, :], pwl[bb][:, :], AF.Abs,
                                    bias=0.0, scale=1.0,
                                    accum_out=prow[:, par:par + 1]), s_gab[bb])
            def vadd(g, par=par, slot=slot, first=(idx < 4 or 256 <= idx < 260)):
                if first:
                    return g.tensor_copy(pacc[:, slot:slot + 1],
                                         prow[:, par:par + 1])
                return g.tensor_add(pacc[:, slot:slot + 1], pacc[:, slot:slot + 1],
                                    prow[:, par:par + 1])
            self.dstep("vector", [(s_gab[bb], None)], vadd, s_gfr[par])
        self.dop("vector", lambda g: g.engine_nop(), self.sem("s_gred"))
        self.pp_gamma_end = len(self.pp_steps)
        self.pp_limit = self.pp_gamma_end

        s_qd = [self.sem("s_qd0"), self.sem("s_qd1")]
        s_qf = [self.sem("s_qf0"), self.sem("s_qf1")]
        qj = []
        for r in range(16):
            for c in range(16):
                qj.append((self.d_w1T[r * 128:(r + 1) * 128, c * 512:(c + 1) * 512],
                           self.d_w1q[c, r], 0))
        for c in range(4):
            for r in range(64):
                qj.append((self.d_w2T[r * 128:(r + 1) * 128, c * 512:(c + 1) * 512],
                           self.d_w2q[c, r], 1))
        for b in range(4):
            self.wait("sync", s_gab[b])
        s_qsg = [self.sem(f"s_qsg{i}") for i in range(4)]
        self.pp_steps.append(("scalar", lambda e: e.wait_ge(self.sem("s_gamma").h, 1)))
        for idx, (src, dst, w) in enumerate(qj):
            bb = idx % 4
            par = idx % 2
            defer = idx >= 256
            if not defer:
                if idx >= 4:
                    self.wait("sync", s_qsg[bb], idx // 4)
                self.dma("sync", pwl[bb][:, :], src, s_qd[par])
            else:
                lw = [(s_qsg[bb], idx // 4)] if idx >= 4 else []
                self.ddma("sync", lw, pwl[bb][:, :], src, s_qd[par])
            bpos, bneg = (self.gam[:, 0:1], self.gam[:, 1:2]) if w == 0 else \
                         (self.gam[:, 2:3], self.gam[:, 3:4])
            sw = [(s_qd[par], None)]
            if idx >= 2:
                sw.append((s_qf[par], None))
            def sgn2(e, bb=bb, par=par, bpos=bpos, bneg=bneg):
                e.activation(self.gamrep[par][:, :], pwl[bb][:, :], AF.Sign,
                             bias=bneg, scale=1.0)
                return e.activation(self.gamneg[par][:, :], pwl[bb][:, :],
                                    AF.Sign, bias=bpos, scale=1.0)
            self.dstep("scalar", sw, sgn2, s_qsg[bb])
            self.dstep("vector", [(s_qsg[bb], None)], lambda v, par=par:
                       v.tensor_add(ptau[par][:, :], self.gamrep[par][:, :],
                                    self.gamneg[par][:, :]), s_qf[par])
            if not defer:
                self.wait("sync", s_qf[par])
                self.dma("sync", dst, ptau[par][:, :], s_qd[par])
            else:
                self.ddma("sync", [(s_qf[par], None)], dst, ptau[par][:, :],
                          s_qd[par])
            if idx == 255:
                self.pp_w1q_end = len(self.pp_steps)
                self.wait("sync", s_qd[0])
                self.wait("sync", s_qd[1])
                self.op("sync", lambda sy: sy.nop(), self.sem("s_cachew1"))
            elif idx >= 256 and (idx - 255) % 64 == 0:
                self.dstep("sync", [(s_qd[0], None), (s_qd[1], None)],
                           lambda sy: sy.nop(), self.sem("s_w2c"))
                self.pp_w2g_end = getattr(self, "pp_w2g_end", {})
                self.pp_w2g_end[(idx - 319) // 64] = len(self.pp_steps)

    def flush_gamma(self):
        while getattr(self, "pp_popped", 0) < getattr(self, "pp_gamma_end", 0) \
                and self.pp_steps:
            self.pump(1)

    def emit_gamma_finalize(self):
        self.flush_gamma()
        self.pp_limit = None
        """Small tensor/vector chain turning pacc into the gam columns.
        Emitted at a controlled point between attention phases."""
        pE, rowb, gam = self.pE, self.rowb, self.gam
        self.wait("tensor", self.sem("s_gred"), 1)
        self.pbank("tensor", pE)
        self.op("tensor", lambda t: t.matmul(pE[0:1, 0:8], self.c_onescol[:, :],
                                             self.pacc[:, 0:8], start=True, stop=True),
                self.sem("s_gsum"))
        self.wait("vector", self.sem("s_gsum"), 1)
        def gfin_a(v):
            return v.tensor_scalar_mul(rowb[0:1, 4:12], pE[0:1, 0:8],
                                       float(2.0 ** -24))
        def gfin_b(v):
            v.tensor_add(rowb[0:1, 12:13], rowb[0:1, 4:5], rowb[0:1, 5:6])
            v.tensor_add(rowb[0:1, 13:14], rowb[0:1, 6:7], rowb[0:1, 7:8])
            v.tensor_add(rowb[0:1, 14:15], rowb[0:1, 8:9], rowb[0:1, 9:10])
            return v.tensor_add(rowb[0:1, 15:16], rowb[0:1, 10:11],
                                rowb[0:1, 11:12])
        def gfin_c(v):
            v.scalar_tensor_tensor(rowb[0:1, 0:1], rowb[0:1, 12:13], 1.0,
                                   rowb[0:1, 13:14], ALU.mult, ALU.add)
            return v.scalar_tensor_tensor(rowb[0:1, 1:2], rowb[0:1, 14:15], 1.0,
                                          rowb[0:1, 15:16], ALU.mult, ALU.add)
        def gfin_d(v):
            return v.tensor_scalar(rowb[0:1, 0:2], rowb[0:1, 0:2], 1.0, Q_EPS,
                                   ALU.mult, ALU.add)
        def gfin_e(v):
            return v.tensor_scalar_mul(rowb[0:1, 2:4], rowb[0:1, 0:2], 0.5)
        self.chain("vector", [gfin_a, gfin_b, gfin_c, gfin_d, gfin_e],
                   self.sem("s_gfin"))
        self.wait("tensor", self.sem("s_gfin"), 1)
        self.op("tensor", lambda t: t.matmul(pE[:, 0:4], self.c_onesbc[:, :],
                                             rowb[0:1, 0:4], start=True, stop=True),
                self.sem("s_gbc"))
        self.wait("vector", self.sem("s_gbc"), 1)
        def gbc_a(v):
            v.tensor_copy(gam[:, 0:1], pE[:, 2:3])
            v.tensor_scalar_mul(gam[:, 1:2], pE[:, 2:3], -1.0)
            v.tensor_copy(gam[:, 2:3], pE[:, 3:4])
            v.tensor_scalar_mul(gam[:, 3:4], pE[:, 3:4], -1.0)
            v.tensor_copy(gam[:, 6:7], pE[:, 0:1])
            return v.tensor_scalar_mul(gam[:, 5:6], pE[:, 1:2], 1.0 / 254.0)
        def gbc_b(v):
            return v.scalar_tensor_tensor(gam[:, 4:5], gam[:, 6:7], 4096.0,
                                          gam[:, 6:7], ALU.mult, ALU.mult)
        self.chain("vector", [gbc_a, gbc_b], self.sem("s_gamma"))
        self.g_set(("bank", "E"), self.sem("s_gamma"))

    def alloc_attn_outer(self, esA):
        nc = self.nc
        SB = lambda n, sh, dt=F32: esA.enter_context(nc.sbuf_tensor(n, sh, dt))
        a = self
        a.qk_k = SB("qk_k", [97, H * S], BF16)
        a.v_t = SB("v_t", [128, NKT * H * 65], BF16)
        a.nq = SB("nq", [128, 6 * S], BF16)
        a.kmx = SB("kmx", [97, H], F32)
        a.r32 = SB("r32", [97, TC], BF16)
        a.crow = SB("crow", [97, TC], BF16)
        a.qsq = [SB(f"qsq{i}", [96, TC], BF16) for i in range(2)]

    def alloc_nkv(self, es, si):
        self.nkv = es.enter_context(
            self.nc.sbuf_tensor(f"nkv_{si}", [128, 2 * S], BF16))

    def alloc_p1(self, es, si):
        nc = self.nc
        SB = lambda n, sh, dt=F32: es.enter_context(nc.sbuf_tensor(n + f"_{si}", sh, dt))
        a = self
        a.xb = [SB(f"xb{i}", [128, D], BF16) for i in range(2)]
        a.y1t = SB("y1t", [128, 16 * TC], BF16)
        a.cq_fm = SB("cq_fm", [128, 6 * TC], BF16)
        a.sq_t = [SB(f"sq_t{i}", [128, TC], BF16) for i in range(2)]
        a.rowc = SB("rowc", [1, 4 * TC], F32)
        a.lndump = SB("lndump", [128, 1024], BF16)
        a.wl1 = [SB(f"wl1{i}", [128, RQ], BF16) for i in range(2)]
        a.wl2 = [SB(f"wl2{i}", [128, RKV + DR], BF16) for i in range(2)]
        a.ag_q = SB("ag_q", [128, 6], F32)
        a.ag_kv = SB("ag_kv", [128, 2], F32)
        sg = self.sem("s_gains")
        self.dma("gpsimd", a.ag_q[:, :], self.d_qg, sg)
        self.dma("gpsimd", a.ag_kv[:, :], self.d_kvg, sg)

    def alloc_p2(self, es, si):
        nc = self.nc
        SB = lambda n, sh, dt=F32: es.enter_context(nc.sbuf_tensor(n + f"_{si}", sh, dt))
        a = self
        a.aw_kn = SB("aw_kn", [128, 2 * H * DN], BF16)
        a.aw_v = SB("aw_v", [128, 2 * H * DV], BF16)
        sw = self.sem("s_p2w")
        rr = lambda d: d.rearrange("(c p) o -> p c o", p=128)
        self.dma("gpsimd", a.aw_kn[:, :], rr(self.d_wkn), sw)
        self.dma("gpsimd", a.aw_v[:, :], rr(self.d_wv), sw)
        self.wait("tensor", sw)

    def alloc_p3(self, es, si):
        nc = self.nc
        SB = lambda n, sh, dt=F32: es.enter_context(nc.sbuf_tensor(n + f"_{si}", sh, dt))
        a = self
        a.aw_qn = SB("aw_qn", [128, 6 * H * DN], BF16)
        a.aw_qr = SB("aw_qr", [128, 6 * H * DR], BF16)
        a.q_aug = SB("q_aug", [97, H * TC], BF16)
        a.att_fm = SB("att_fm", [128, 8 * TC], BF16)
        a.att_tmp = SB("att_tmp", [64, TC], BF16)
        a.sexp = [SB(f"sexp{i}", [128, 512], BF16) for i in range(2)]
        a.den_b = SB("den_b", [65, 512], BF16)
        a.srec = SB("srec", [64, 512], F32)
        a.wpb = [SB(f"wpb{i}", [128, 512], BF16) for i in range(2)]
        a.xb2 = [SB(f"xb2_{i}", [128, 512], BF16) for i in range(2)]
        a.xpst = [SB(f"xpst_{i}", [128, 512], BF16) for i in range(2)]
        sw = self.sem("s_p3w")
        rr = lambda d: d.rearrange("(c p) o -> p c o", p=128)
        self.dma("gpsimd", a.aw_qn[:, :], rr(self.d_wqn), sw)
        self.dma("gpsimd", a.aw_qr[:, :], rr(self.d_wqr), sw)
        self.wait("tensor", sw)

    def emit_p1(self, si):
        a = self
        xb = a.xb; y1t = a.y1t; cq_fm = a.cq_fm; sq_t = a.sq_t; rowc = a.rowc; lndump = a.lndump; wl1 = a.wl1; wl2 = a.wl2; ag_q = a.ag_q; ag_kv = a.ag_kv; nkv = a.nkv; stats = a.stats
        sfx = f"_{si}"
        xs = self.d_x[si]
        s_xld = [self.sem(f"s_xld{i}") for i in range(2)]
        s_sqs = self.sem(f"s_sqs")
        s_ln1 = self.sem(f"s_ln1")
        s_y1 = self.sem(f"s_y1")
        s_tp = self.sem(f"s_tp")
        s_tpe = self.sem(f"s_tpe")
        s_lat = self.sem(f"s_lat")
        s_late = self.sem(f"s_late")
        s_row = self.sem(f"s_row")
        s_rbc = self.sem(f"s_rbc")
        s_nrm = self.sem(f"s_nrm")
        s_kve = self.sem(f"s_kve")

        for tc in range(NTC):
            self.pump(12)
            for tt in range(4):
                ti = tc * 4 + tt
                par = ti % 2
                self.g_wait("gpsimd", ("xb", par))
                self.dma("gpsimd", xb[par][:, :], xs[ti * 128:(ti + 1) * 128, :],
                         s_xld[par])
                pc = par * 8
                self.wait("scalar", s_xld[par])
                self.g_wait("scalar", ("stats", par))
                self.op("scalar", lambda s, par=par, pc=pc:
                        s.activation(lndump[:, :], xb[par][:, 0:1024], AF.Square,
                                     bias=0.0, scale=1.0,
                                     accum_out=stats[:, pc:pc + 1]), s_sqs)
                self.wait("scalar", s_sqs)
                self.op("scalar", lambda s, par=par, pc=pc:
                        s.activation(lndump[:, :], xb[par][:, 1024:2048],
                                     AF.Square, bias=0.0, scale=1.0,
                                     accum_out=stats[:, pc + 2:pc + 3]), s_sqs)
                self.wait("vector", s_xld[par])
                self.wait("vector", s_sqs)
                self.op("vector", lambda v, par=par, pc=pc:
                        v.reduce_sum(stats[:, pc + 1:pc + 2], xb[par][:, :],
                                     axis=AX.X), self.sem("s_rr"))
                self.wait("vector", self.sem("s_rr"))
                def lnv_a(v, pc=pc):
                    st = stats
                    v.tensor_add(st[:, pc:pc + 1], st[:, pc:pc + 1], st[:, pc + 2:pc + 3])
                    return v.tensor_scalar_mul(st[:, pc + 1:pc + 2],
                                               st[:, pc + 1:pc + 2], 1.0 / D)
                def lnv_b(v, pc=pc):
                    st = stats
                    v.tensor_scalar(st[:, pc:pc + 1], st[:, pc:pc + 1], 1.0 / D,
                                    LN_EPS, ALU.mult, ALU.add)
                    return v.tensor_tensor(st[:, pc + 2:pc + 3], st[:, pc + 1:pc + 2],
                                           st[:, pc + 1:pc + 2], ALU.mult)
                def lnv_c(v, pc=pc):
                    st = stats
                    return v.tensor_sub(st[:, pc:pc + 1], st[:, pc:pc + 1],
                                        st[:, pc + 2:pc + 3])
                self.chain("vector", [lnv_a, lnv_b, lnv_c], s_ln1)
                self.wait("scalar", s_ln1)
                self.op("scalar", lambda s, pc=pc:
                        s.activation(stats[:, pc + 3:pc + 4], stats[:, pc:pc + 1],
                                     AF.Sqrt, bias=0.0, scale=1.0), s_sqs)
                self.wait("vector", s_sqs)
                def l2_a(v, pc=pc):
                    return v.reciprocal(stats[:, pc + 4:pc + 5],
                                        stats[:, pc + 3:pc + 4])
                def l2_b(v, pc=pc):
                    return v.scalar_tensor_tensor(stats[:, pc + 6:pc + 7],
                                                  stats[:, pc + 1:pc + 2], -1.0,
                                                  stats[:, pc + 4:pc + 5],
                                                  ALU.mult, ALU.mult)
                def l2_c(v, par=par, pc=pc):
                    return v.tensor_scalar(xb[par][:, :], xb[par][:, :],
                                           stats[:, pc + 4:pc + 5],
                                           stats[:, pc + 6:pc + 7],
                                           ALU.mult, ALU.add)
                self.chain("vector", [l2_a, l2_b, l2_c], s_y1)
                self.g_set(("stats", par), s_y1)
                if self.dbg_xp and si == 0 and ti == 0:
                    self.freeze_dump([(self.d_dbg_st, stats[:, :]),
                                      (self.d_dbg_y1[0:128, 0:2048],
                                       xb[par][:, :])])
                self.wait("tensor", s_y1)
                if tt == 0:
                    self.g_wait("vector", ("y1t",))
                for fb in range(16):
                    ptp = self.pT[fb % 2]
                    self.pbank("tensor", ptp)
                    self.op("tensor", lambda t, par=par, fb=fb, ptp=ptp:
                            t.transpose(ptp[:, 0:128],
                                        xb[par][:, fb * 128:(fb + 1) * 128],
                                        self.c_ident[:, :]), s_tp)
                    self.wait("vector", s_tp)
                    self.op("vector", lambda v, fb=fb, tt=tt, ptp=ptp:
                            v.tensor_copy(y1t[:, fb * TC + tt * 128:
                                                fb * TC + (tt + 1) * 128],
                                          ptp[:, 0:128]), s_tpe)
                    self.g_set(("bank", self.bname[id(ptp)]), s_tpe)
                self.op("vector", lambda v: v.engine_nop(), self.sem("s_dummy"))
                self.g_set(("xb", par), s_tpe)

            s_wl1 = [self.sem(f"s_wl1{i}") for i in range(2)]
            s_wl2 = [self.sem(f"s_wl2{i}") for i in range(2)]
            self.wait("tensor", s_tpe)

            def evac_chunk(j, pp, last_cs):
                self.op("vector", lambda v, j=j, pp=pp:
                        v.tensor_copy(cq_fm[:, j * TC:(j + 1) * TC], pp[:, :]),
                        s_late)
                self.wait("scalar", s_late)
                self.g_wait("scalar", ("sq_t", j % 2))
                self.op("scalar", lambda s, j=j, pp=pp:
                        s.activation(sq_t[j % 2][:, :], pp[:, :], AF.Square,
                                     bias=0.0, scale=1.0), s_sqs)
                self.g_set(("bank", self.bname[id(pp)]), s_sqs)
                self.wait("tensor", s_sqs)
                if j == 0:
                    self.pbank("tensor", self.pF)
                self.op("tensor", lambda t, j=j, last_cs=last_cs:
                        t.matmul(self.pF[0:1, 0:TC], self.c_onescolb[:, :],
                                 sq_t[j % 2][:, :], start=(j == 0),
                                 stop=last_cs), s_row)
                self.g_set(("sq_t", j % 2), s_row)

            def rms_rows(rank, gain, ntens, n_ch, tc):
                self.wait("vector", s_row)
                self.op("vector", lambda v, rank=rank:
                        v.tensor_scalar(rowc[0:1, 0:TC], self.pF[0:1, 0:TC],
                                        1.0 / rank, RMS_EPS, ALU.mult, ALU.add),
                        s_row)
                self.g_set(("bank", "F"), s_row)
                self.wait("scalar", s_row)
                self.op("scalar", lambda s:
                        s.activation(rowc[0:1, TC:2 * TC], rowc[0:1, 0:TC],
                                     AF.Sqrt, bias=0.0, scale=1.0), s_sqs)
                self.wait("vector", s_sqs)
                self.op("vector", lambda v:
                        v.reciprocal(rowc[0:1, 2 * TC:3 * TC],
                                     rowc[0:1, TC:2 * TC]), s_row)
                self.wait("tensor", s_row)
                self.pbank("tensor", self.pF)
                self.op("tensor", lambda t:
                        t.matmul(self.pF[:, :], self.c_onesbc[:, :],
                                 rowc[0:1, 2 * TC:3 * TC],
                                 start=True, stop=True), s_rbc)
                self.wait("vector", s_rbc)
                if tc == 0:
                    self.g_wait("vector", ("nq",))
                    self.g_wait("vector", ("nkv",))
                for j in range(n_ch):
                    self.op("vector", lambda v, j=j, gain=gain, ntens=ntens, tc=tc:
                            v.scalar_tensor_tensor(
                                ntens[:, j * S + tc * TC:j * S + (tc + 1) * TC],
                                cq_fm[:, j * TC:(j + 1) * TC],
                                gain[:, j:j + 1], self.pF[:, :],
                                ALU.mult, ALU.mult), s_nrm)
                self.g_set(("bank", "F"), s_nrm)

            banks5 = [self.pA, self.pB, self.pC, self.pD, self.pE]
            for bk in banks5:
                self.pbank("tensor", bk)
            for i in range(16):
                par = i % 2
                self.g_wait("gpsimd", ("wl1", par))
                self.dma("gpsimd", wl1[par][:, :],
                         self.d_wcq[i * 128:(i + 1) * 128, :], s_wl1[par])
                self.wait("tensor", s_wl1[par])
                def lat1(t, i=i, par=par):
                    for oc in range(5):
                        li = t.matmul(banks5[oc][:, :],
                                      wl1[par][:, oc * 128:(oc + 1) * 128],
                                      y1t[:, i * TC:(i + 1) * TC],
                                      start=(i == 0), stop=(i == 15))
                    return li
                self.op("tensor", lat1, s_lat)
                self.g_set(("wl1", par), s_lat)
            base = s_lat.v
            for j in range(5):
                self.wait("vector", s_lat, base)
                if si == 0 and tc == 0 and j == 0:
                    self.wait("vector", self.sem("s_gains"))
                evac_chunk(j, banks5[j], last_cs=False)
            for i in range(16):
                par = i % 2
                self.g_wait("gpsimd", ("wl1", par))
                self.dma("gpsimd", wl1[par][:, :128],
                         self.d_wcq[i * 128:(i + 1) * 128, 5 * 128:6 * 128],
                         s_wl1[par])
                self.wait("tensor", s_wl1[par])
                if i == 0:
                    self.pbank("tensor", self.pA)
                def lat1b(t, i=i, par=par):
                    return t.matmul(self.pA[:, :], wl1[par][:, 0:128],
                                    y1t[:, i * TC:(i + 1) * TC],
                                    start=(i == 0), stop=(i == 15))
                self.op("tensor", lat1b, s_lat)
                self.g_set(("wl1", par), s_lat)
            self.wait("vector", s_lat)
            evac_chunk(5, self.pA, last_cs=True)
            rms_rows(RQ, ag_q, a.nq, 6, tc)

            self.pbank("tensor", self.pA)
            self.pbank("tensor", self.pB)
            self.pbank("tensor", self.pE)
            for i in range(16):
                par = i % 2
                self.g_wait("gpsimd", ("wl2", par))
                self.dma("gpsimd", wl2[par][:, 0:RKV],
                         self.d_wckv[i * 128:(i + 1) * 128, :], s_wl2[par])
                self.dma("gpsimd", wl2[par][:, RKV:RKV + DR],
                         self.d_wkr[i * 128:(i + 1) * 128, :], s_wl2[par])
                self.wait("tensor", s_wl2[par])
                def lat2(t, i=i, par=par):
                    t.matmul(self.pA[:, :], wl2[par][:, 0:128],
                             y1t[:, i * TC:(i + 1) * TC],
                             start=(i == 0), stop=(i == 15))
                    t.matmul(self.pB[:, :], wl2[par][:, 128:256],
                             y1t[:, i * TC:(i + 1) * TC],
                             start=(i == 0), stop=(i == 15))
                    return t.matmul(self.pE[64:96, :], wl2[par][:, 256:288],
                                    y1t[:, i * TC:(i + 1) * TC],
                                    start=(i == 0), stop=(i == 15))
                self.op("tensor", lat2, s_lat)
                self.g_set(("wl2", par), s_lat)
            self.g_set(("y1t",), s_lat)
            base = s_lat.v
            for j in range(2):
                self.wait("vector", s_lat, base)
                evac_chunk(j, [self.pA, self.pB][j], last_cs=(j == 1))
            rms_rows(RKV, ag_kv, nkv, 2, tc)

            if self.dbg_xp and si == 0 and tc == 0:
                sdbg = self.sem("s_dbg")
                self.wait("gpsimd", s_nrm)
                self.dma("gpsimd", self.d_dbg_cq, cq_fm[:, :], sdbg)
                self.wait("gpsimd", sdbg)
            self.wait("vector", s_lat)
            if si == 0 and tc == 0:
                self.wait("vector", self.sem("s_const"))
            self.g_wait("vector", ("crow",))
            self.g_wait("vector", ("qk_k",))
            def krope2(v, tc=tc):
                raw = a.r32[64:96, :]
                sh = a.crow[64:96, :]
                v.tensor_copy(raw, self.pE[64:96, :])
                v.stream_shuffle(sh, raw, SHUF)
                v.scalar_tensor_tensor(sh, sh, self.c_sig[64:96, :], raw,
                                       ALU.mult, ALU.add)
                return v.tensor_tensor(a.qk_k[64:96, tc * TC:(tc + 1) * TC], sh,
                                       self.c_crk[64:96, tc * TC:(tc + 1) * TC],
                                       ALU.mult)
            self.op("vector", krope2, s_kve)
            self.g_set(("bank", "E"), s_kve)
            self.g_set(("crow",), s_kve)

        def krep(v):
            for h in range(1, H):
                v.tensor_copy(a.qk_k[64:96, h * S:(h + 1) * S], a.qk_k[64:96, 0:S])
            return v.memset(a.qk_k[96:97, :], 1.0)
        self.op("vector", krep, s_kve)

    def emit_p2(self, si):
        a = self
        aw_kn = a.aw_kn; aw_v = a.aw_v; nkv = a.nkv
        sfx = f"_{si}"
        s_nrm = self.sem(f"s_nrm")
        s_kve = self.sem(f"s_kve")
        s_kv = self.sem(f"s_kv")
        s_kev = self.sem(f"s_kev")
        s_ksq = self.sem(f"s_ksq")
        s_kcs = self.sem(f"s_kcs")
        s_kmx = self.sem(f"s_kmx")

        self.g_wait("vector", ("v_t",))
        self.g_wait("vector", ("qk_k",))
        self.op("vector", lambda v: v.memset(a.v_t[:, :], 1.0), s_kev)

        self.wait("tensor", s_nrm)
        for h in range(H):
            for kc in range(NTC):
                self.pump(4)
                pp = self.rot[(h * NTC + kc) % 4]
                self.pbank("tensor", pp)
                def knmm(t, h=h, kc=kc, pp=pp):
                    for i in range(2):
                        li = t.matmul(pp[0:64, :],
                                      aw_kn[:, i * H * DN + h * DN:i * H * DN + (h + 1) * DN],
                                      nkv[:, i * S + kc * TC:i * S + (kc + 1) * TC],
                                      start=(i == 0), stop=(i == 1))
                    return li
                self.op("tensor", knmm, s_kv)
                self.wait("vector", s_kv)
                self.op("vector", lambda v, h=h, kc=kc, pp=pp:
                        v.tensor_copy(a.qk_k[0:64, h * S + kc * TC:h * S + (kc + 1) * TC],
                                      pp[0:64, :]), s_kev)
                self.g_set(("bank", self.bname[id(pp)]), s_kev)
        for kt in range(NKT):
            self.pump(4)
            for half in range(2):
                pp = self.rot[(kt % 2) * 2 + half]
                self.pbank("tensor", pp)
                def vmm(t, kt=kt, half=half, pp=pp):
                    for i in range(2):
                        li = t.matmul(pp[:, :],
                                      nkv[:, i * S + kt * 128:i * S + (kt + 1) * 128],
                                      aw_v[:, i * H * DV + half * 512:
                                             i * H * DV + (half + 1) * 512],
                                      start=(i == 0), stop=(i == 1))
                    return li
                self.op("tensor", vmm, s_kv)
                self.wait("vector", s_kv)
                self.op("vector", lambda v, kt=kt, half=half, pp=pp:
                        v.tensor_copy(
                            a.v_t[:, kt * H * 65 + half * 8 * 65:
                                  kt * H * 65 + (half * 8 + 8) * 65]
                            .rearrange("p (h e) -> p h e", e=65)[:, :, 0:64],
                            pp[:, :].rearrange("p (h e) -> p h e", e=64)), s_kev)
                self.g_set(("bank", self.bname[id(pp)]), s_kev)

        for h in range(H):
            self.pump(4)
            for kc in range(NTC):
                idx = h * NTC + kc
                qs = a.qsq[idx % 2]
                pX = [self.pE, self.pF][idx % 2]
                self.wait("scalar", s_kev)
                self.wait("scalar", s_kve)
                self.g_wait("scalar", ("qsq", idx % 2))
                self.op("scalar", lambda s, h=h, kc=kc, qs=qs:
                        s.activation(qs[0:96, :],
                                     a.qk_k[0:96, h * S + kc * TC:h * S + (kc + 1) * TC],
                                     AF.Square, bias=0.0, scale=1.0), s_ksq)
                self.wait("tensor", s_ksq)
                self.pbank("tensor", pX)
                self.op("tensor", lambda t, qs=qs, pX=pX:
                        t.matmul(pX[64:97, 0:TC], self.c_cones[:, :],
                                 qs[0:96, :], start=True, stop=True), s_kcs)
                self.g_set(("qsq", idx % 2), s_kcs)
                self.wait("vector", s_kcs)
                if h == 0 and kc == 0:
                    self.g_wait("vector", ("kmx",))
                if kc == 0:
                    self.op("vector", lambda v, h=h, pX=pX:
                            v.reduce_max(a.kmx[96:97, h:h + 1],
                                         pX[96:97, 0:TC], axis=AX.X), s_kmx)
                else:
                    self.op("vector", lambda v, pX=pX:
                            v.reduce_max(a.crow[96:97, 0:1],
                                         pX[96:97, 0:TC], axis=AX.X),
                            self.sem("s_rr"))
                    self.wait("vector", self.sem("s_rr"))
                    self.wait("vector", s_kmx)
                    self.op("vector", lambda v, h=h:
                            v.tensor_max(a.kmx[96:97, h:h + 1], a.kmx[96:97, h:h + 1],
                                         a.crow[96:97, 0:1]), s_kmx)
                self.g_set(("bank", self.bname[id(pX)]), s_kmx)
        self.wait("scalar", s_kmx)
        self.op("scalar", lambda s:
                s.activation(a.kmx[96:97, 0:H], a.kmx[96:97, 0:H], AF.Sqrt,
                             bias=0.0, scale=1.0), s_ksq)
        self.wait("vector", s_ksq)
        self.op("vector", lambda v:
                v.tensor_scalar_mul(a.kmx[96:97, 0:H], a.kmx[96:97, 0:H], -1.0),
                s_kmx)

    def emit_p3(self, si):
        a = self
        aw_qn = a.aw_qn; aw_qr = a.aw_qr; q_aug = a.q_aug; att_fm = a.att_fm; att_tmp = a.att_tmp; sexp = a.sexp; den_b = a.den_b; srec = a.srec; wpb = a.wpb; xb2 = a.xb2; xpst = a.xpst
        sfx = f"_{si}"
        s_nrm = self.sem(f"s_nrm")
        s_kmx = self.sem(f"s_kmx")
        s_kev = self.sem(f"s_kev")
        s_kve = self.sem(f"s_kve")
        s_qb = self.sem(f"s_qb")
        s_qbe = self.sem(f"s_qbe")
        s_qsq = self.sem(f"s_qsq")
        s_qcs = self.sem(f"s_qcs")
        s_crw = self.sem(f"s_crw")
        s_qk = self.sem(f"s_qk")
        s_msk = self.sem(f"s_msk")
        s_exp = self.sem(f"s_exp")
        s_pvk = self.sem(f"s_pvk")
        s_dnc = self.sem(f"s_dnc")
        s_dnd = self.sem(f"s_dnd")
        s_dbc = self.sem(f"s_dbc")
        s_rcp = self.sem(f"s_rcp")
        s_div = self.sem(f"s_div")
        s_odm = self.sem(f"s_odm")
        s_wp = [self.sem(f"s_wp{i}") for i in range(2)]
        s_op = self.sem(f"s_op")
        s_ope = self.sem(f"s_ope")
        s_x2 = [self.sem(f"s_x2{i}") for i in range(2)]
        s_xpw = self.sem(f"s_xpw")
        xs = self.d_x[si]
        xps = self.d_xp[si]

        self.wait("tensor", s_kev)
        self.wait("tensor", s_kve)
        for qc in range(NTC):
            nkt = 4 * (qc + 1)
            if qc == 0:
                self.wait("tensor", s_nrm)
            self.g_wait("tensor", ("q_aug",))
            qbv = {}

            def SQNR(h, qc=qc):
                self.pump(4)
                ppn = self.rot[(2 * h) % 4]
                ppr = self.rot[(2 * h + 1) % 4]
                self.pbank("tensor", ppn)
                def qn(t, h=h, ppn=ppn, qc=qc):
                    for i in range(6):
                        li = t.matmul(ppn[0:64, :],
                                      aw_qn[:, i * H * DN + h * DN:i * H * DN + (h + 1) * DN],
                                      a.nq[:, i * S + qc * TC:i * S + (qc + 1) * TC],
                                      start=(i == 0), stop=(i == 5))
                    return li
                self.op("tensor", qn, s_qb)
                self.pbank("tensor", ppr)
                def qr(t, h=h, ppr=ppr, qc=qc):
                    for i in range(6):
                        li = t.matmul(ppr[64:96, :],
                                      aw_qr[:, i * H * DR + h * DR:i * H * DR + (h + 1) * DR],
                                      a.nq[:, i * S + qc * TC:i * S + (qc + 1) * TC],
                                      start=(i == 0), stop=(i == 5))
                    return li
                self.op("tensor", qr, s_qb)
                self.wait("vector", s_qb)
                def qev(v, h=h, ppn=ppn, ppr=ppr, qc=qc):
                    dn = q_aug[0:64, h * TC:(h + 1) * TC]
                    dr = q_aug[64:96, h * TC:(h + 1) * TC]
                    raw = a.r32[64:96, :]
                    v.tensor_scalar_mul(dn, ppn[0:64, :], SC)
                    v.tensor_copy(raw, ppr[64:96, :])
                    v.stream_shuffle(dr, raw, SHUF)
                    v.scalar_tensor_tensor(dr, dr, self.c_sig[64:96, :], raw,
                                           ALU.mult, ALU.add)
                    return v.tensor_tensor(dr, dr,
                                           self.c_crq[64:96, qc * TC:(qc + 1) * TC],
                                           ALU.mult)
                self.op("vector", qev, s_qbe)
                qbv[h] = s_qbe.v
                self.g_set(("bank", self.bname[id(ppn)]), s_qbe)
                self.g_set(("bank", self.bname[id(ppr)]), s_qbe)

            def SCHAIN(h, qc=qc):
                qs = a.qsq[h % 2]
                pX = [self.pE, self.pF][h % 2]
                self.wait("scalar", s_qbe, qbv[h])
                self.g_wait("scalar", ("qsq", h % 2))
                self.op("scalar", lambda s, h=h, qs=qs:
                        s.activation(qs[0:96, :], q_aug[0:96, h * TC:(h + 1) * TC],
                                     AF.Square, bias=0.0, scale=1.0), s_qsq)
                self.wait("tensor", s_qsq)
                self.pbank("tensor", pX)
                self.op("tensor", lambda t, qs=qs, pX=pX:
                        t.matmul(pX[64:97, 0:TC], self.c_cones[:, :],
                                 qs[0:96, :], start=True, stop=True), s_qcs)
                self.g_set(("qsq", h % 2), s_qcs)
                self.wait("scalar", s_qcs)
                self.g_wait("scalar", ("crow",))
                self.op("scalar", lambda s, pX=pX:
                        s.activation(a.crow[96:97, :], pX[96:97, 0:TC], AF.Sqrt,
                                     bias=0.0, scale=1.0), s_qsq)
                self.g_set(("bank", self.bname[id(pX)]), s_qsq)
                self.wait("vector", s_qsq)
                if qc == 0 and h == 0:
                    self.wait("vector", s_kmx)
                self.op("vector", lambda v, h=h:
                        v.tensor_single_scalar(q_aug[96:97, h * TC:(h + 1) * TC],
                                               a.crow[96:97, :],
                                               a.kmx[96:97, h:h + 1], ALU.mult),
                        s_crw)
                self.g_set(("crow",), s_crw)

            SQNR(0)
            for h in range(1, H):
                SQNR(h)
                SCHAIN(h - 1)
            SCHAIN(H - 1)
            if si == 0 and qc == 3:
                self.emit_gamma_finalize()

            self.wait("tensor", s_crw)
            pend_den = [None]

            def mk_den(h, pv, pvkv):
                def den():
                    self.wait("vector", s_pvk, pvkv)
                    self.g_wait("vector", ("den",))
                    self.op("vector", lambda v, pv=pv:
                            v.tensor_copy(den_b[64:65, :], pv[64:65, :]), s_dnc)
                    self.wait("scalar", s_dnc)
                    self.dma("scalar", den_b[0:1, :], den_b[64:65, :], s_dnd)
                    self.wait("tensor", s_dnd)
                    self.pbank("tensor", self.pE)
                    self.op("tensor", lambda t:
                            t.matmul(self.pE[0:64, :], self.c_ones64[:, :],
                                     den_b[0:1, :], start=True, stop=True), s_dbc)
                    self.g_set(("den",), s_dbc)
                    self.wait("scalar", s_dbc)
                    self.pbank("scalar", self.pF)
                    self.g_wait("scalar", ("srec",))
                    def rcp(s):
                        s.activation(self.pF[0:64, :], self.pE[0:64, :], AF.Ln,
                                     bias=0.0, scale=1.0)
                        return s.activation(srec[:, :], self.pF[0:64, :], AF.Exp,
                                            bias=0.0, scale=-1.0)
                    self.op("scalar", rcp, s_rcp)
                    self.g_set(("bank", "E"), s_rcp)
                    self.g_set(("bank", "F"), s_rcp)
                    self.wait("vector", s_rcp)
                    even = (h % 2 == 0)
                    if even:
                        dst = att_fm[0:64, (h // 2) * TC:(h // 2 + 1) * TC]
                    else:
                        self.g_wait("vector", ("att_tmp",))
                        dst = att_tmp[0:64, :]
                    self.op("vector", lambda v, pv=pv, dst=dst:
                            v.tensor_mul(dst, pv[0:64, :], srec[:, :]), s_div)
                    self.g_set(("srec",), s_div)
                    self.g_set(("bank", self.bname[id(pv)]), s_div)
                    if not even:
                        self.wait("gpsimd", s_div)
                        self.dma("gpsimd",
                                 att_fm[64:128, (h // 2) * TC:(h // 2 + 1) * TC],
                                 att_tmp[0:64, :], s_odm)
                        self.g_set(("att_tmp",), s_odm)
                return den

            for h in range(H):
                self.pump(3)
                pv = [self.pC, self.pD][h % 2]
                self.pbank("tensor", pv)
                expv = {}

                def SQK(kt, h=h, qc=qc):
                    psc = [self.pA, self.pB][kt % 2]
                    self.pbank("tensor", psc)
                    self.op("tensor", lambda t, h=h, kt=kt, psc=psc:
                            t.matmul(psc[:, :],
                                     a.qk_k[0:97, h * S + kt * 128:h * S + (kt + 1) * 128],
                                     q_aug[0:97, h * TC:(h + 1) * TC],
                                     start=True, stop=True), s_qk)
                    return psc

                def SME(kt, psc, qc=qc):
                    diag = kt >= 4 * qc
                    if diag:
                        r = kt - 4 * qc
                        self.wait("vector", s_qk)
                        self.op("vector", lambda v, psc=psc, r=r:
                                v.tensor_add(psc[:, :], psc[:, :],
                                             self.c_mask[:, r * 512:(r + 1) * 512]),
                                s_msk)
                        self.wait("scalar", s_msk)
                    else:
                        self.wait("scalar", s_qk)
                    self.g_wait("scalar", ("sexp", kt % 2))
                    self.op("scalar", lambda s, psc=psc, kt=kt:
                            s.activation(sexp[kt % 2][:, :], psc[:, :], AF.Exp,
                                         bias=0.0, scale=1.0), s_exp)
                    expv[kt] = s_exp.v
                    self.g_set(("bank", self.bname[id(psc)]), s_exp)

                def SPV(kt, h=h, pv=pv, nkt=nkt):
                    self.wait("tensor", s_exp, expv[kt])
                    self.op("tensor", lambda t, h=h, kt=kt, pv=pv,
                            last=(kt == nkt - 1):
                            t.matmul(pv[0:65, :],
                                     a.v_t[:, kt * H * 65 + h * 65:
                                           kt * H * 65 + (h + 1) * 65],
                                     sexp[kt % 2][:, :],
                                     start=(kt == 0), stop=last), s_pvk)
                    self.g_set(("sexp", kt % 2), s_pvk)

                p0 = SQK(0)
                SME(0, p0)
                for kt in range(1, nkt):
                    p = SQK(kt)
                    if kt == 3 and pend_den[0] is not None:
                        pend_den[0]()
                        pend_den[0] = None
                    SME(kt, p)
                    SPV(kt - 1)
                SPV(nkt - 1)
                pend_den[0] = mk_den(h, pv, s_pvk.v)
            pend_den[0]()
            pend_den[0] = None

            self.wait("tensor", s_div)
            self.wait("tensor", s_odm)
            for dc in range(4):
                self.pump(3)
                for ic in range(8):
                    par = ic % 2
                    self.g_wait("gpsimd", ("wpb", par))
                    self.dma("gpsimd", wpb[par][:, :],
                             self.d_wpr[:, ic * D + dc * 512:
                                        ic * D + (dc + 1) * 512], s_wp[par])
                    if ic < 2:
                        tt = ic
                        self.g_wait("gpsimd", ("xb2", tt % 2))
                        self.dma("gpsimd", xb2[tt % 2][:, :],
                                 xs[(qc * 4 + tt) * 128:(qc * 4 + tt + 1) * 128,
                                    dc * 512:(dc + 1) * 512], s_x2[tt % 2])
                    self.wait("tensor", s_wp[par])
                    if ic == 0:
                        for tt in range(4):
                            self.pbank("tensor", self.rot[tt])
                    def opmm(t, ic=ic, par=par, qc=qc):
                        li = None
                        for tt in range(4):
                            li = t.matmul(self.rot[tt][:, :],
                                          att_fm[:, ic * TC + tt * 128:
                                                   ic * TC + (tt + 1) * 128],
                                          wpb[par][:, :],
                                          start=(ic == 0), stop=(ic == 7))
                        return li
                    self.op("tensor", opmm, s_op)
                    self.g_set(("wpb", par), s_op)
                opv = s_op.v
                for tt in range(4):
                    ti = qc * 4 + tt
                    self.wait("vector", s_op, opv)
                    self.wait("vector", s_x2[tt % 2])
                    self.g_wait("vector", ("xpst", tt % 2))
                    self.op("vector", lambda v, tt=tt:
                            v.scalar_tensor_tensor(
                                xpst[tt % 2][:, :], self.rot[tt][:, :], 1.0,
                                xb2[tt % 2][:, :], ALU.mult, ALU.add), s_ope)
                    self.g_set(("bank", self.bname[id(self.rot[tt])]), s_ope)
                    self.g_set(("xb2", tt % 2), s_ope)
                    if tt < 2:
                        nt = tt + 2
                        self.g_wait("gpsimd", ("xb2", nt % 2))
                        self.dma("gpsimd", xb2[nt % 2][:, :],
                                 xs[(qc * 4 + nt) * 128:(qc * 4 + nt + 1) * 128,
                                    dc * 512:(dc + 1) * 512], s_x2[nt % 2])
                    self.wait("scalar", s_ope)
                    self.dma("scalar",
                             xps[ti * 128:(ti + 1) * 128, dc * 512:(dc + 1) * 512],
                             xpst[tt % 2][:, :], s_xpw)
                    if self.dbg_xp:
                        self.wait("gpsimd", s_ope)
                        self.dma("gpsimd",
                                 self.d_xpdbg[si][ti * 128:(ti + 1) * 128,
                                                  dc * 512:(dc + 1) * 512],
                                 xpst[tt % 2][:, :], s_xpw)
                    self.g_set(("xpst", tt % 2), s_xpw)
            self.g_set(("att_fm",), s_op)
            self.g_set(("q_aug",), s_qk)
            if self.dbg_xp and si == 0 and qc == 0:
                sdbg = self.sem("s_dbg")
                self.wait("gpsimd", s_div)
                self.wait("gpsimd", s_odm)
                self.dma("gpsimd", self.d_dbg_qk, a.qk_k[:, :], sdbg)
                self.dma("gpsimd", self.d_dbg_qa, q_aug[:, :], sdbg)
                self.dma("gpsimd", self.d_dbg_af, att_fm[:, :], sdbg)
                self.dma("gpsimd", self.d_dbg_vt, a.v_t[:, :], sdbg)
                self.dma("gpsimd", self.d_dbg_sr, srec[:, :], sdbg)
                self.dma("gpsimd", self.d_dbg_nq, a.nq[:, :], sdbg)
                self.wait("gpsimd", sdbg)
        self.g_set(("nq",), s_qb)
        self.g_set(("nkv",), self.sem("s_kv"))
        self.g_set(("qk_k",), s_qk)
        self.g_set(("v_t",), s_pvk)
        self.g_set(("kmx",), s_crw)

    def emit_ffn(self, esF):
        a = self
        nc = self.nc
        SB = lambda n, sh, dt=F32: esF.enter_context(nc.sbuf_tensor(n, sh, dt))
        h_sb = SB("h_sb", [128, 64 * 512], BF16)
        xqT = [SB(f"xqT{i}", [128, 16 * 512], BF16) for i in range(2)]
        w1b = [SB(f"w1b{i}", [128, 4 * 512], BF16) for i in range(2)]
        w2b = [SB(f"w2b{i}", [128, 8 * 512], BF16) for i in range(2)]
        outst = [SB(f"outst{i}", [128, 512], F32) for i in range(2)]
        xw = [SB(f"xw{i}", [128, 4 * D], BF16) for i in range(2)]
        y2f = SB("y2f", [128, D], F32)
        fdump = y2f.bitcast(BF16)
        xq_t = [SB(f"xq_t{i}", [128, D], BF16) for i in range(2)]
        t32 = SB("t32", [128, 512], F32)
        ci32 = SB("ci32", [128, 512], I32)
        d32 = SB("d32", [128, 512], F32)
        gbc = SB("gbc", [128, 512], F32)
        hmxb = d32.bitcast(BF16)[:, 0:512]
        fst = SB("fst", [128, 16], F32)
        fcol = SB("fcol", [128, 8], F32)
        umrep = SB("umrep", [128, 128], BF16)
        umc = SB("umc", [128, 4], F32)
        rcol = SB("rcol", [128, 4], F32)

        s_xw = [self.sem(f"s_fxw{i}") for i in range(2)]
        s_sq = self.sem("s_fsq")
        s_ln = self.sem("s_fln")
        s_y2 = self.sem("s_fy2")
        s_xq = self.sem("s_fxq")
        s_tp = self.sem("s_ftp")
        s_tpe = self.sem("s_ftpe")
        s_w1 = [self.sem(f"s_fw1{i}") for i in range(2)]
        s_w2 = [self.sem(f"s_fw2{i}") for i in range(2)]
        s_m1 = self.sem("s_fm1")
        s_ev = self.sem("s_fev")
        s_mx = self.sem("s_fmx")
        s_row = self.sem("s_frow")
        s_rtp = self.sem("s_frtp")
        s_rte = self.sem("s_frte")
        s_gb = self.sem("s_fgb")
        s_hq = self.sem("s_fhq")
        s_m2 = self.sem("s_fm2")
        s_oev = self.sem("s_foev")
        s_out = self.sem("s_fout")

        def flush_to(mark):
            while getattr(self, "pp_popped", 0) < mark and self.pp_steps:
                self.pump(1)

        flush_to(getattr(self, "pp_w1q_end", 0))
        self.wait("sync", self.sem("s_cachew1"))
        self.wait("gpsimd", self.sem("s_xpw"))
        self.wait("gpsimd", self.sem("s_cachew1"))

        NW = 2 * NTC
        xqv = {}

        def xw_loads(w):
            wp = w % 2
            xps = self.d_xp[w // NTC]
            t0 = (w % NTC) * TC
            self.g_wait("gpsimd", ("xw", wp))
            for tt in range(4):
                self.dma("gpsimd", xw[wp][:, tt * D:(tt + 1) * D],
                         xps[(t0 + tt * 128):(t0 + (tt + 1) * 128), :], s_xw[wp])

        def ln_tile(w, tt):
            wp = w % 2
            pc = (tt % 2) * 8
            xwt = xw[wp]
            self.wait("scalar", s_xw[wp])
            self.g_wait("scalar", ("fst", tt % 2))
            self.g_wait("scalar", ("y2f",))
            def sq2(s, tt=tt, pc=pc, xwt=xwt):
                s.activation(fdump[:, 0:1024], xwt[:, tt * D:tt * D + 1024],
                             AF.Square, bias=0.0, scale=1.0,
                             accum_out=fst[:, pc:pc + 1])
                return s.activation(fdump[:, 0:1024],
                                    xwt[:, tt * D + 1024:(tt + 1) * D],
                                    AF.Square, bias=0.0, scale=1.0,
                                    accum_out=fst[:, pc + 2:pc + 3])
            self.op("scalar", sq2, s_sq)
            self.wait("vector", s_xw[wp])
            self.wait("vector", s_sq)
            self.op("vector", lambda v, tt=tt, pc=pc, xwt=xwt:
                    v.reduce_sum(fst[:, pc + 1:pc + 2], xwt[:, tt * D:(tt + 1) * D],
                                 axis=AX.X), self.sem("s_rr"))
            self.wait("vector", self.sem("s_rr"))
            def flnv_a(v, pc=pc):
                st = fst
                v.tensor_add(st[:, pc:pc + 1], st[:, pc:pc + 1],
                             st[:, pc + 2:pc + 3])
                return v.tensor_scalar_mul(st[:, pc + 1:pc + 2],
                                           st[:, pc + 1:pc + 2], 1.0 / D)
            def flnv_b(v, pc=pc):
                st = fst
                v.tensor_scalar(st[:, pc:pc + 1], st[:, pc:pc + 1],
                                1.0 / D, LN_EPS, ALU.mult, ALU.add)
                return v.tensor_tensor(st[:, pc + 2:pc + 3], st[:, pc + 1:pc + 2],
                                       st[:, pc + 1:pc + 2], ALU.mult)
            def flnv_c(v, pc=pc):
                st = fst
                return v.tensor_sub(st[:, pc:pc + 1], st[:, pc:pc + 1],
                                    st[:, pc + 2:pc + 3])
            self.chain("vector", [flnv_a, flnv_b, flnv_c], s_ln)
            self.wait("scalar", s_ln)
            self.op("scalar", lambda s, pc=pc:
                    s.activation(fst[:, pc + 3:pc + 4], fst[:, pc:pc + 1],
                                 AF.Sqrt, bias=0.0, scale=1.0), s_sq)
            self.wait("vector", s_sq)
            def fl2_a(v, pc=pc):
                return v.reciprocal(fst[:, pc + 4:pc + 5], fst[:, pc + 3:pc + 4])
            def fl2_b(v, pc=pc):
                return v.scalar_tensor_tensor(fst[:, pc + 6:pc + 7],
                                              fst[:, pc + 1:pc + 2], -1.0,
                                              fst[:, pc + 4:pc + 5],
                                              ALU.mult, ALU.mult)
            def fl2_c(v, tt=tt, pc=pc, xwt=xwt):
                return v.tensor_scalar(y2f[:, :], xwt[:, tt * D:(tt + 1) * D],
                                       fst[:, pc + 4:pc + 5],
                                       fst[:, pc + 6:pc + 7], ALU.mult, ALU.add)
            self.chain("vector", [fl2_a, fl2_b, fl2_c], s_y2)
            self.g_set(("fst", tt % 2), s_y2)
            self.op("vector", lambda v, pc=pc:
                    v.reduce_max(fst[:, pc + 7:pc + 8], y2f[:, :], axis=AX.X,
                                 apply_absolute_value=True), self.sem("s_rr"))
            self.wait("vector", self.sem("s_rr"))
            def qa_a(v, pc=pc):
                return v.tensor_scalar_max(fst[:, pc + 7:pc + 8],
                                           fst[:, pc + 7:pc + 8], Q_EPS)
            def qa_b(v, tt=tt, pc=pc):
                v.tensor_scalar_mul(fcol[:, tt + 4:tt + 5],
                                    fst[:, pc + 7:pc + 8], 1.0 / 127.0)
                return v.reciprocal(fst[:, pc + 7:pc + 8], fst[:, pc + 7:pc + 8])
            def qa_c(v, tt=tt, pc=pc):
                v.scalar_tensor_tensor(fcol[:, tt:tt + 1], fcol[:, tt + 4:tt + 5],
                                       1.0, fcol[:, tt + 4:tt + 5],
                                       ALU.mult, ALU.mult)
                return v.tensor_scalar_mul(fst[:, pc + 7:pc + 8],
                                           fst[:, pc + 7:pc + 8], 127.0)
            def qa_d(v, tt=tt, pc=pc):
                v.tensor_tensor(fcol[:, tt:tt + 1], fcol[:, tt:tt + 1],
                                self.gam[:, 4:5], ALU.mult)
                return v.tensor_single_scalar(y2f[:, :], y2f[:, :],
                                              fst[:, pc + 7:pc + 8], ALU.mult)
            self.chain("vector", [qa_a, qa_b, qa_c, qa_d], s_y2)
            self.g_wait("vector", ("xq_t", tt % 2))
            xqt = xq_t[tt % 2]
            for c in range(4):
                def rnd(v, c=c, xqt=xqt):
                    sl = slice(c * 512, (c + 1) * 512)
                    v.tensor_copy(ci32[:, :], y2f[:, sl])
                    v.scalar_tensor_tensor(d32[:, :], ci32[:, :], -1.0,
                                           y2f[:, sl], ALU.mult, ALU.add)
                    v.scalar_tensor_tensor(t32[:, :], d32[:, :], 0.5,
                                           ci32[:, :], ALU.is_ge, ALU.add)
                    v.tensor_scalar(d32[:, :], d32[:, :], -0.5, -1.0,
                                    ALU.is_le, ALU.mult)
                    return v.tensor_add(xqt[:, sl], t32[:, :], d32[:, :])
                self.op("vector", rnd, s_xq)
            self.g_set(("y2f",), s_xq)
            xqv[(w, tt)] = s_xq.v

        def tp_tile(w, tt):
            wp = w % 2
            self.wait("tensor", s_xq, xqv[(w, tt)])
            if tt == 0:
                self.g_wait("tensor", ("xqT", wp))
            xqt = xq_t[tt % 2]
            for fb in range(16):
                ptp = self.pT[fb % 2]
                self.pbank("tensor", ptp)
                self.op("tensor", lambda t, fb=fb, ptp=ptp, xqt=xqt:
                        t.transpose(ptp[:, 0:128],
                                    xqt[:, fb * 128:(fb + 1) * 128],
                                    self.c_ident[:, :]), s_tp)
                self.wait("vector", s_tp)
                self.op("vector", lambda v, fb=fb, tt=tt, ptp=ptp, wp=wp:
                        v.tensor_copy(xqT[wp][:, fb * 512 + tt * 128:
                                              fb * 512 + (tt + 1) * 128],
                                      ptp[:, 0:128]), s_tpe)
                self.g_set(("bank", self.bname[id(ptp)]), s_tpe)
            self.g_set(("xq_t", tt % 2), s_tp)

        xw_loads(0)
        for tt in range(4):
            ln_tile(0, tt)
            tp_tile(0, tt)

        for w in range(NW):
            si = w // NTC
            t0 = (w % NTC) * TC
            wp = w % 2
            if w + 1 < NW:
                xw_loads(w + 1)

            self.wait("tensor", s_tpe)
            self.g_wait("tensor", ("h_sb",))
            for og in range(16):
                self.pump(4)
                for ih in range(4):
                    par = (og * 4 + ih) % 2
                    self.g_wait("sync", ("w1b", par))
                    self.dma("sync", w1b[par][:, :],
                             self.d_w1q[og, ih * 4:(ih + 1) * 4]
                             .rearrange("i p f -> p i f"), s_w1[par])
                    self.wait("tensor", s_w1[par])
                    for ocin in range(4):
                        oc = og * 4 + ocin
                        pp = self.rot[oc % 4]
                        if ih == 0:
                            self.pbank("tensor", pp)
                        def m1(t, par=par, ih=ih, ocin=ocin, pp=pp, wp=wp):
                            for ii in range(4):
                                i = ih * 4 + ii
                                li = t.matmul(pp[:, :],
                                              w1b[par][:, ii * 512 + ocin * 128:
                                                       ii * 512 + (ocin + 1) * 128],
                                              xqT[wp][:, i * 512:(i + 1) * 512],
                                              start=(i == 0), stop=(i == 15))
                            return li
                        self.op("tensor", m1, s_m1)
                        if ih == 3:
                            self.wait("scalar", s_m1)
                            def ev(s, oc=oc, pp=pp):
                                s.activation(h_sb[:, oc * 512:(oc + 1) * 512], pp[:, :],
                                             AF.Relu, bias=0.0, scale=1.0 / 128.0)
                                return s.activation(h_sb[:, oc * 512:(oc + 1) * 512],
                                                    h_sb[:, oc * 512:(oc + 1) * 512],
                                                    AF.Square, bias=0.0, scale=1.0)
                            self.op("scalar", ev, s_ev)
                            self.g_set(("bank", self.bname[id(pp)]), s_ev)
                            self.wait("vector", s_ev)
                            if oc == 0:
                                self.op("vector", lambda v:
                                        v.tensor_copy(hmxb[:, :], h_sb[:, 0:512]),
                                        s_mx)
                            else:
                                self.op("vector", lambda v, oc=oc:
                                        v.tensor_max(hmxb[:, :], hmxb[:, :],
                                                     h_sb[:, oc * 512:(oc + 1) * 512]),
                                        s_mx)
                    self.g_set(("w1b", par), s_m1)

            for c in range(4):
                ptp = self.pT[c % 2]
                self.wait("tensor", s_mx)
                self.pbank("tensor", ptp)
                self.op("tensor", lambda t, c=c, ptp=ptp:
                        t.transpose(ptp[:, 0:128], hmxb[:, c * 128:(c + 1) * 128],
                                    self.c_ident[:, :]), s_rtp)
                self.wait("vector", s_rtp)
                self.op("vector", lambda v, c=c, ptp=ptp:
                        v.reduce_max(umc[:, c:c + 1], ptp[:, 0:128], axis=AX.X),
                        self.sem("s_rr"))
                self.wait("scalar", self.sem("s_rr"))
                self.op("scalar", lambda s, c=c:
                        s.activation(umrep[:, :], self.c_ident[:, :], AF.Relu,
                                     bias=umc[:, c:c + 1], scale=0.0),
                        self.sem("s_rr"))
                self.wait("tensor", self.sem("s_rr"))
                self.g_set(("bank", self.bname[id(ptp)]), self.sem("s_rr"))
                self.pbank("tensor", ptp)
                self.op("tensor", lambda t, ptp=ptp:
                        t.transpose(ptp[:, 0:128], umrep[:, :], self.c_ident[:, :]),
                        s_rtp)
                self.wait("vector", s_rtp)
                self.op("vector", lambda v, c=c, ptp=ptp:
                        v.tensor_copy(gbc[:, c * 128:(c + 1) * 128], ptp[:, 0:128]),
                        s_rte)
                self.g_set(("bank", self.bname[id(ptp)]), s_rte)
            def gfin(v):
                v.tensor_scalar_max(gbc[:, :], gbc[:, :], 1e-30)
                v.reciprocal(gbc[:, :], gbc[:, :])
                return v.tensor_scalar_mul(gbc[:, :], gbc[:, :], 127.0)
            self.op("vector", gfin, s_gb)
            def rc_a(v):
                r = None
                for c in range(4):
                    r = v.tensor_tensor(rcol[:, c:c + 1], fcol[:, c:c + 1],
                                        umc[:, c:c + 1], ALU.mult)
                return r
            def rc_b(v):
                return v.tensor_scalar_max(rcol[:, :], rcol[:, :], Q_EPS)
            def rc_c(v):
                r = None
                for c in range(4):
                    r = v.tensor_tensor(rcol[:, c:c + 1], rcol[:, c:c + 1],
                                        self.gam[:, 5:6], ALU.mult)
                return r
            self.chain("vector", [rc_a, rc_b, rc_c], s_rte)

            hqb = s_hq.v
            for oc in range(64):
                sl = slice(oc * 512, (oc + 1) * 512)
                def hqc(v, sl=sl):
                    v.tensor_tensor(t32[:, :], h_sb[:, sl], gbc[:, :], ALU.mult)
                    v.tensor_copy(ci32[:, :], t32[:, :])
                    v.scalar_tensor_tensor(d32[:, :], ci32[:, :], -1.0,
                                           t32[:, :], ALU.mult, ALU.add)
                    return v.scalar_tensor_tensor(h_sb[:, sl], d32[:, :], 0.5,
                                                  ci32[:, :], ALU.is_ge, ALU.add)
                self.op("vector", hqc, s_hq)

            if w + 1 < NW:
                ln_tile(w + 1, 0)
                ln_tile(w + 1, 1)

            for dq in range(4):
                self.pump(4)
                for ob in range(8):
                    par = ob % 2
                    if w == 0 and ob == 0:
                        flush_to(self.pp_w2g_end[dq])
                        self.wait("sync", self.sem("s_w2c"), dq + 1)
                    self.g_wait("sync", ("w2b", par))
                    self.dma("sync", w2b[par][:, :],
                             self.d_w2q[dq, ob * 8:(ob + 1) * 8]
                             .rearrange("o p f -> p o f"), s_w2[par])
                    self.wait("tensor", s_w2[par])
                    for oi in range(8):
                        o = ob * 8 + oi
                        if dq == 0:
                            self.wait("tensor", s_hq, hqb + o + 1)
                        if o == 0:
                            for tt in range(4):
                                self.pbank("tensor", self.rot[tt])
                        def m2(t, dq=dq, o=o, oi=oi, par=par):
                            li = None
                            for tt in range(4):
                                li = t.matmul(self.rot[tt][:, :],
                                              h_sb[:, o * 512 + tt * 128:
                                                   o * 512 + (tt + 1) * 128],
                                              w2b[par][:, oi * 512:(oi + 1) * 512],
                                              start=(o == 0), stop=(o == 63))
                            return li
                        self.op("tensor", m2, s_m2)
                    self.g_set(("w2b", par), s_m2)
                self.wait("vector", s_m2)
                self.wait("vector", s_rte)
                for tt in range(4):
                    pst = (dq * 4 + tt) % 2
                    self.g_wait("vector", ("outst", pst))
                    def oev(v, dq=dq, tt=tt, pst=pst, wp=wp):
                        return v.scalar_tensor_tensor(
                            outst[pst][:, :], self.rot[tt][:, :],
                            rcol[:, tt:tt + 1],
                            xw[wp][:, tt * D + dq * 512:tt * D + (dq + 1) * 512],
                            ALU.mult, ALU.add)
                    self.op("vector", oev, s_oev)
                    self.g_set(("bank", self.bname[id(self.rot[tt])]), s_oev)
                    self.wait("gpsimd", s_oev)
                    self.dma("gpsimd",
                             self.d_out[si][(t0 + tt * 128):(t0 + (tt + 1) * 128),
                                            dq * 512:(dq + 1) * 512],
                             outst[pst][:, :], s_out)
                    self.g_set(("outst", pst), s_out)
                if w + 1 < NW:
                    tp_tile(w + 1, dq)
                    if dq == 0:
                        ln_tile(w + 1, 2)
                    elif dq == 1:
                        ln_tile(w + 1, 3)
            self.g_set(("xw", wp), s_oev)
            self.g_set(("h_sb",), s_m2)
            self.g_set(("xqT", wp), s_m1)

        self.flush_pp()
        self.wait("gpsimd", s_out)
        for (eng, _), sem in getattr(self, "dma_sems", {}).items():
            self.wait(eng, sem)

    def finalize(self):
        blk = self.es.enter_context(self.nc.Block())
        for eng in ENGS:
            def mk(eng):
                def run(e):
                    for fn in self.streams[eng]:
                        fn(e)
                return run
            getattr(blk, eng)(mk(eng))


def build_program(dbg_xp=False):
    nc = bass.Bass("TRN2", target_bir_lowering=False,
                   detect_race_conditions=False)
    with ExitStack() as es:
        b = Builder(nc, es, dbg_xp=dbg_xp)
        b.declare_io()
        b.alloc_persistent()
        with ExitStack() as esA:
            b.emit_prepass_streams(esA)
            b.alloc_attn_outer(esA)
            for si in range(NSEQ):
                with ExitStack() as es12:
                    b.alloc_nkv(es12, si)
                    with ExitStack() as es1:
                        b.alloc_p1(es1, si)
                        b.emit_p1(si)
                    b.barrier()
                    b.alloc_p2(es12, si)
                    b.emit_p2(si)
                b.barrier()
                with ExitStack() as es3:
                    b.alloc_p3(es3, si)
                    b.emit_p3(si)
                b.barrier()
            b.barrier()
        with ExitStack() as esF:
            b.emit_ffn(esF)
        b.finalize()
    return nc


def _consts():
    c = {}
    c["ident"] = np.eye(128, dtype=ml_dtypes.bfloat16)
    c["cones"] = np.ones((96, 33), dtype=ml_dtypes.bfloat16)
    c["ones64"] = np.ones((1, 64), dtype=ml_dtypes.bfloat16)
    c["onesbc"] = np.ones((1, 128), dtype=np.float32)
    c["onescol"] = np.ones((128, 1), dtype=np.float32)
    c["onescolb"] = np.ones((128, 1), dtype=ml_dtypes.bfloat16)
    freqs = 1.0 / (10000.0 ** (np.arange(0, DR, 2, dtype=np.float64) / DR))
    ang = np.outer(np.arange(S, dtype=np.float64), freqs)
    cos = np.cos(ang).T
    crep = np.concatenate([cos, cos], axis=0)
    c["crq"] = (crep * SC).astype(ml_dtypes.bfloat16)
    c["crk"] = (crep / H).astype(ml_dtypes.bfloat16)
    c["sig"] = np.concatenate([-np.ones((16, 1)), np.ones((16, 1))]).astype(np.float32)
    m = np.zeros((128, 4 * 512), dtype=np.float32)
    kk = np.arange(128)[:, None]
    qq = np.arange(512)[None, :]
    for r in range(4):
        m[:, r * 512:(r + 1) * 512] = np.where(qq >= kk + r * 128, 0.0, NEG)
    c["mask"] = m.astype(ml_dtypes.bfloat16)
    return c


def _perm_rope_cols(w, nheads):
    rows = w.shape[0]
    w = w.reshape(rows, nheads, 16, 2)
    return np.ascontiguousarray(
        np.concatenate([w[:, :, :, 0], w[:, :, :, 1]], axis=2).reshape(rows, -1))


def prepare_base(L0_attn_x, L0_attn_w_cq, L0_attn_w_q_nope, L0_attn_w_q_rope,
                 L0_attn_q_g, L0_attn_w_ckv, L0_attn_w_k_nope, L0_attn_w_v,
                 L0_attn_kv_g, L0_attn_w_k_rope, L0_attn_w_proj, L0_ffn_w1,
                 L0_ffn_w2):
    x = np.ascontiguousarray(np.asarray(L0_attn_x, dtype=np.float32))
    f32 = lambda z: np.ascontiguousarray(np.asarray(z, dtype=np.float32))
    wqr = _perm_rope_cols(f32(L0_attn_w_q_rope), H)
    wkr = _perm_rope_cols(f32(L0_attn_w_k_rope), 1)
    wpr = np.ascontiguousarray(
        f32(L0_attn_w_proj).reshape(8, 128, D).transpose(1, 0, 2).reshape(128, 8 * D))
    w1T = np.ascontiguousarray(f32(L0_ffn_w1).T)
    w2T = np.ascontiguousarray(f32(L0_ffn_w2).T)
    base = dict(
        w_cq=f32(L0_attn_w_cq), w_qn=f32(L0_attn_w_q_nope), w_qr=wqr,
        q_g=np.ascontiguousarray(f32(L0_attn_q_g).reshape(6, 128).T),
        w_ckv=f32(L0_attn_w_ckv),
        w_kn=f32(L0_attn_w_k_nope), w_v=f32(L0_attn_w_v),
        kv_g=np.ascontiguousarray(f32(L0_attn_kv_g).reshape(2, 128).T),
        w_kr=wkr, w_pr=wpr,
        w1T=w1T, w2T=w2T, **_consts())
    return base, x


def kernel(**inputs):
    base, x = prepare_base(**inputs)
    nc = build_program()
    in_maps = []
    for c in range(8):
        m = dict(base)
        m["x_in"] = x[2 * c:2 * c + 2]
        in_maps.append(m)
    res = run_bass_kernel_spmd(nc, in_maps, core_ids=list(range(8)))
    out = np.empty((16, S, D), dtype=np.float32)
    for c in range(8):
        out[2 * c:2 * c + 2] = res.results[c]["out"]
    return out


# version 17
# speedup vs baseline: 1.1108x; 1.0089x over previous
import math
import numpy as np
import ml_dtypes
from contextlib import ExitStack

import concourse.bass as bass
import concourse.mybir as mybir
from concourse.bass_utils import run_bass_kernel_spmd

F32 = mybir.dt.float32
BF16 = mybir.dt.bfloat16
I32 = mybir.dt.int32
AF = mybir.ActivationFunctionType
ALU = mybir.AluOpType
AX = mybir.AxisListType

S, D, H = 2048, 2048, 16
DN, DR, DV = 64, 32, 64
RQ, RKV = 768, 256
DFF = 8192
NSEQ = 2
SC = 1.0 / math.sqrt(DN + DR)
NEG = -1.0e30
LN_EPS, RMS_EPS, Q_EPS = 1e-5, 1e-6, 1e-5
TC = 512
NTC = S // TC
NKT = S // 128
ENGS = ("tensor", "vector", "scalar", "gpsimd", "sync")
SHUF = list(range(16, 32)) + list(range(16))


class Sem:
    def __init__(self, nc, es, name):
        self.h = es.enter_context(nc.semaphore(name))
        self.v = 0
        self.name = name
        self.engs = set()

    def take(self, n=1):
        self.v += n
        return (self.h, n)


class Builder:
    def __init__(self, nc, es, dbg_xp=False):
        self.nc = nc
        self.es = es
        self.dbg_xp = dbg_xp
        self.streams = {e: [] for e in ENGS}
        self.sems = {}
        self.gates = {}
        self.pp_steps = []

    def sem(self, name):
        if name not in self.sems:
            self.sems[name] = Sem(self.nc, self.es, name)
        return self.sems[name]

    def _emit(self, eng, fn):
        self.streams[eng].append(fn)

    def dma(self, eng, dst, src, sem):
        sem.engs.add(eng)
        tgt = sem.take(16)
        self.dma_sems = getattr(self, "dma_sems", {})
        self.dma_sems[(eng, id(sem))] = sem
        self._emit(eng, lambda e, dst=dst, src=src, tgt=tgt:
                   e.dma_start(dst, src).then_inc(*tgt))

    def wait(self, eng, sem, val=None):
        v = sem.v if val is None else val
        if v > 0:
            self._emit(eng, lambda e, h=sem.h, v=v: e.wait_ge(h, v))

    def op(self, eng, fn, sem=None, n=1):
        if sem is None:
            self._emit(eng, fn)
        else:
            sem.engs.add(eng)
            tgt = sem.take(n)
            self._emit(eng, lambda e, fn=fn, tgt=tgt: fn(e).then_inc(*tgt))

    def chain(self, eng, steps, final_sem, n=1):
        s = self.sem(f"s_ch_{eng}")
        for i, fn in enumerate(steps):
            last = (i == len(steps) - 1)
            self.op(eng, fn, final_sem if last else s, n=n if last else 1)
            if not last:
                self.wait(eng, s)

    def g_wait(self, eng, key):
        if key in self.gates:
            s, v = self.gates[key]
            self.wait(eng, s, v)

    def g_set(self, key, sem):
        self.gates[key] = (sem, sem.v)

    def pump(self, k=2):
        for _ in range(min(k, len(self.pp_steps))):
            lim = getattr(self, "pp_limit", None)
            if lim is not None and getattr(self, "pp_popped", 0) >= lim:
                return
            eng, fn = self.pp_steps.pop(0)
            self.pp_popped = getattr(self, "pp_popped", 0) + 1
            self._emit(eng, fn)

    def flush_pp(self):
        while self.pp_steps:
            eng, fn = self.pp_steps.pop(0)
            self._emit(eng, fn)

    def dop(self, eng, fn, sem=None, n=1):
        if sem is None:
            self.pp_steps.append((eng, fn))
        else:
            sem.engs.add(eng)
            tgt = sem.take(n)
            self.pp_steps.append((eng, lambda e, fn=fn, tgt=tgt:
                                  fn(e).then_inc(*tgt)))

    def ddma(self, eng, waits, dst, src, sem):
        cw = [(s.h, (s.v if v is None else v)) for (s, v) in waits]
        cw = [(h, v) for (h, v) in cw if v > 0]
        sem.engs.add(eng)
        tgt = sem.take(16)
        self.dma_sems = getattr(self, "dma_sems", {})
        self.dma_sems[(eng, id(sem))] = sem
        self.pp_steps.append((eng, lambda e, cw=cw, dst=dst, src=src, tgt=tgt:
                              ([e.wait_ge(h, v) for h, v in cw],
                               e.dma_start(dst, src).then_inc(*tgt))[-1]))

    def dstep(self, eng, waits, fn, sem=None, n=1):
        cw = [(s.h, (s.v if v is None else v)) for (s, v) in waits]
        cw = [(h, v) for (h, v) in cw if v > 0]
        if sem is None:
            self.pp_steps.append((eng, lambda e, cw=cw, fn=fn:
                                  ([e.wait_ge(h, v) for h, v in cw],
                                   fn(e))[-1]))
        else:
            sem.engs.add(eng)
            tgt = sem.take(n)
            self.pp_steps.append((eng, lambda e, cw=cw, fn=fn, tgt=tgt:
                                  ([e.wait_ge(h, v) for h, v in cw],
                                   fn(e).then_inc(*tgt))[-1]))

    def dwait(self, eng, sem, val=None):
        v = sem.v if val is None else val
        if v > 0:
            self.pp_steps.append((eng, lambda e, h=sem.h, v=v: e.wait_ge(h, v)))

    def declare_io(self):
        nc = self.nc
        EI, EO, IN = "ExternalInput", "ExternalOutput", "Internal"
        self.d_x = nc.dram_tensor("x_in", [NSEQ, S, D], F32, kind=EI).ap()
        self.d_wcq = nc.dram_tensor("w_cq", [D, RQ], F32, kind=EI).ap()
        self.d_wqn = nc.dram_tensor("w_qn", [RQ, H * DN], F32, kind=EI).ap()
        self.d_wqr = nc.dram_tensor("w_qr", [RQ, H * DR], F32, kind=EI).ap()
        self.d_qg = nc.dram_tensor("q_g", [128, 6], F32, kind=EI).ap()
        self.d_wckv = nc.dram_tensor("w_ckv", [D, RKV], F32, kind=EI).ap()
        self.d_wkn = nc.dram_tensor("w_kn", [RKV, H * DN], F32, kind=EI).ap()
        self.d_wv = nc.dram_tensor("w_v", [RKV, H * DV], F32, kind=EI).ap()
        self.d_kvg = nc.dram_tensor("kv_g", [128, 2], F32, kind=EI).ap()
        self.d_wkr = nc.dram_tensor("w_kr", [D, DR], F32, kind=EI).ap()
        self.d_wpr = nc.dram_tensor("w_pr", [128, 8 * D], F32, kind=EI).ap()
        self.d_w1T = nc.dram_tensor("w1T", [D, DFF], F32, kind=EI).ap()
        self.d_w2T = nc.dram_tensor("w2T", [DFF, D], F32, kind=EI).ap()
        self.d_ident = nc.dram_tensor("ident", [128, 128], BF16, kind=EI).ap()
        self.d_cones = nc.dram_tensor("cones", [96, 33], BF16, kind=EI).ap()
        self.d_ones64 = nc.dram_tensor("ones64", [1, 64], BF16, kind=EI).ap()
        self.d_onesbc = nc.dram_tensor("onesbc", [1, 128], F32, kind=EI).ap()
        self.d_onescol = nc.dram_tensor("onescol", [128, 1], F32, kind=EI).ap()
        self.d_onescolb = nc.dram_tensor("onescolb", [128, 1], BF16, kind=EI).ap()
        self.d_crq = nc.dram_tensor("crq", [32, S], BF16, kind=EI).ap()
        self.d_crk = nc.dram_tensor("crk", [32, S], BF16, kind=EI).ap()
        self.d_sig = nc.dram_tensor("sig", [32, 1], F32, kind=EI).ap()
        self.d_mask = nc.dram_tensor("mask", [128, 4 * 512], BF16, kind=EI).ap()
        self.d_out = nc.dram_tensor("out", [NSEQ, S, D], F32, kind=EO).ap()
        self.d_xp = nc.dram_tensor("xp_scr", [NSEQ, S, D], BF16, kind=IN).ap()
        self.d_w1q = nc.dram_tensor("w1q_c", [16, 16, 128, 512], BF16, kind=IN).ap()
        self.d_w2q = nc.dram_tensor("w2q_c", [4, 64, 128, 512], BF16, kind=IN).ap()
        if self.dbg_xp:
            self.d_xpdbg = nc.dram_tensor("xp_dbg", [NSEQ, S, D], F32, kind=EO).ap()
            self.d_dbg_qk = nc.dram_tensor("dbg_qk", [97, H * S], F32, kind=EO).ap()
            self.d_dbg_qa = nc.dram_tensor("dbg_qa", [97, H * TC], F32, kind=EO).ap()
            self.d_dbg_af = nc.dram_tensor("dbg_af", [128, 8 * TC], F32, kind=EO).ap()
            self.d_dbg_vt = nc.dram_tensor("dbg_vt", [128, NKT * H * 65], F32, kind=EO).ap()
            self.d_dbg_sr = nc.dram_tensor("dbg_sr", [64, 512], F32, kind=EO).ap()
            self.d_dbg_nq = nc.dram_tensor("dbg_nq", [128, 6 * S], F32, kind=EO).ap()
            self.d_dbg_y1 = nc.dram_tensor("dbg_y1", [128, 16 * TC], F32, kind=EO).ap()
            self.d_dbg_st = nc.dram_tensor("dbg_st", [128, 16], F32, kind=EO).ap()
            self.d_dbg_cq = nc.dram_tensor("dbg_cq", [128, 6 * TC], F32, kind=EO).ap()

    def alloc_persistent(self):
        nc, es = self.nc, self.es
        SB = lambda n, sh, dt=F32: es.enter_context(nc.sbuf_tensor(n, sh, dt))
        PS = lambda n, sh, dt=F32: es.enter_context(nc.psum_tensor(n, sh, dt))
        self.pA, self.pB = PS("pA", [128, 512]), PS("pB", [128, 512])
        self.pC, self.pD = PS("pC", [128, 512]), PS("pD", [128, 512])
        self.pE, self.pF = PS("pE", [128, 512]), PS("pF", [128, 512])
        self.pT = [PS("pT0", [128, 1024], BF16), PS("pT1", [128, 1024], BF16)]
        self.rot = [self.pA, self.pB, self.pC, self.pD]
        self.bname = {id(self.pA): "A", id(self.pB): "B", id(self.pC): "C",
                      id(self.pD): "D", id(self.pE): "E", id(self.pF): "F",
                      id(self.pT[0]): "T0", id(self.pT[1]): "T1"}

        self.c_ident = SB("c_ident", [128, 128], BF16)
        self.c_cones = SB("c_cones", [96, 33], BF16)
        self.c_ones64 = SB("c_ones64", [1, 64], BF16)
        self.c_onesbc = SB("c_onesbc", [1, 128], F32)
        self.c_onescol = SB("c_onescol", [128, 1], F32)
        self.c_onescolb = SB("c_onescolb", [128, 1], BF16)
        self.c_crq = SB("c_crq", [96, S], BF16)
        self.c_crk = SB("c_crk", [96, S], BF16)
        self.c_sig = SB("c_sig", [96, 1], F32)
        self.c_mask = SB("c_mask", [128, 4 * 512], BF16)
        self.gam = SB("gam", [128, 8], F32)
        self.rowb = SB("rowb", [128, 16], F32)
        self.stats = SB("stats", [128, 16], F32)

        s = self.sem("s_const")
        for dst, src in [(self.c_ident[:, :], self.d_ident),
                         (self.c_cones[:, :], self.d_cones),
                         (self.c_ones64[:, :], self.d_ones64),
                         (self.c_onesbc[:, :], self.d_onesbc),
                         (self.c_onescol[:, :], self.d_onescol),
                         (self.c_onescolb[:, :], self.d_onescolb),
                         (self.c_crq[64:96, :], self.d_crq),
                         (self.c_crk[64:96, :], self.d_crk),
                         (self.c_sig[64:96, :], self.d_sig),
                         (self.c_mask[:, :], self.d_mask)]:
            self.dma("gpsimd", dst, src, s)
        for eng in ("tensor", "vector", "scalar"):
            self.wait(eng, s)

    def barrier(self):
        sb = self.sem("s_bar")
        self._barn = getattr(self, "_barn", 0) + 1
        self.op("tensor", lambda t: t.nop(), sb)
        self.op("vector", lambda v: v.engine_nop(), sb)
        self.op("scalar", lambda x: x.nop(), sb)
        self.op("gpsimd", lambda g: g.engine_nop(), sb)
        tgt = 4 * self._barn
        assert sb.v == tgt
        for eng in ("tensor", "vector", "scalar", "gpsimd"):
            self.wait(eng, sb, tgt)

    def pbank(self, eng, pp):
        self.g_wait(eng, ("bank", self.bname[id(pp)]))

    def pbank_d(self, eng, pp):
        key = ("bank", self.bname[id(pp)])
        if key in self.gates:
            s, v = self.gates[key]
            self.dwait(eng, s, v)

    def freeze_dump(self, pairs):
        if not self.dbg_xp:
            return
        self.barrier()
        sd = self.sem("s_frz")
        for dst, src in pairs:
            self.dma("gpsimd", dst, src, sd)
        for eng in ("tensor", "vector", "scalar", "gpsimd"):
            self.wait(eng, sd)

    def emit_prepass_streams(self, es):
        nc = self.nc
        SB = lambda n, sh, dt=F32: self.es.enter_context(nc.sbuf_tensor(n, sh, dt))
        self.pwl = [SB(f"pwl{i}", [128, 512], F32) for i in range(4)]
        self.ptau = [SB(f"ptau{i}", [128, 512], BF16) for i in range(2)]
        self.pacc = SB("pacc", [128, 8], F32)
        self.prow = SB("prow", [128, 2], F32)
        self.gamrep = [SB(f"sgnm{i}", [128, 512], BF16) for i in range(2)]
        self.gamneg = [SB(f"sgnp{i}", [128, 512], BF16) for i in range(2)]
        pwl, ptau, pacc, prow = (self.pwl, self.ptau, self.pacc, self.prow)
        pdump = ptau[0]

        s_gld = [self.sem(f"s_gld{i}") for i in range(4)]
        s_gab = [self.sem(f"s_gab{i}") for i in range(4)]
        s_gfr = [self.sem("s_gfr0"), self.sem("s_gfr1")]
        gj = ([(self.d_w1T[r * 128:(r + 1) * 128, c * 512:(c + 1) * 512], 0)
               for r in range(16) for c in range(16)]
              + [(self.d_w2T[r * 128:(r + 1) * 128, c * 512:(c + 1) * 512], 1)
                 for r in range(64) for c in range(4)])
        for idx, (src, col) in enumerate(gj):
            bb = idx % 4
            par = idx % 2
            slot = col * 4 + idx % 4
            if idx >= 4:
                self.wait("sync", s_gab[bb], idx // 4)
            self.dma("sync", pwl[bb][:, :], src, s_gld[bb])
            sw = [(s_gld[bb], None)]
            if idx >= 2:
                sw.append((s_gfr[par], None))
            self.dstep("scalar", sw, lambda e, bb=bb, par=par:
                       e.activation(pdump[:, :], pwl[bb][:, :], AF.Abs,
                                    bias=0.0, scale=1.0,
                                    accum_out=prow[:, par:par + 1]), s_gab[bb])
            def vadd(g, par=par, slot=slot, first=(idx < 4 or 256 <= idx < 260)):
                if first:
                    return g.tensor_copy(pacc[:, slot:slot + 1],
                                         prow[:, par:par + 1])
                return g.tensor_add(pacc[:, slot:slot + 1], pacc[:, slot:slot + 1],
                                    prow[:, par:par + 1])
            self.dstep("vector", [(s_gab[bb], None)], vadd, s_gfr[par])
        self.dop("vector", lambda g: g.engine_nop(), self.sem("s_gred"))
        self.pp_gamma_end = len(self.pp_steps)
        self.pp_limit = self.pp_gamma_end

        s_qd = [self.sem("s_qd0"), self.sem("s_qd1")]
        s_qf = [self.sem("s_qf0"), self.sem("s_qf1")]
        qj = []
        for r in range(16):
            for c in range(16):
                qj.append((self.d_w1T[r * 128:(r + 1) * 128, c * 512:(c + 1) * 512],
                           self.d_w1q[c, r], 0))
        for c in range(4):
            for r in range(64):
                qj.append((self.d_w2T[r * 128:(r + 1) * 128, c * 512:(c + 1) * 512],
                           self.d_w2q[c, r], 1))
        for b in range(4):
            self.wait("sync", s_gab[b])
        s_qsg = [self.sem(f"s_qsg{i}") for i in range(4)]
        self.pp_steps.append(("scalar", lambda e: e.wait_ge(self.sem("s_gamma").h, 1)))
        for idx, (src, dst, w) in enumerate(qj):
            bb = idx % 4
            par = idx % 2
            defer = idx >= 256
            if not defer:
                if idx >= 4:
                    self.wait("sync", s_qsg[bb], idx // 4)
                self.dma("sync", pwl[bb][:, :], src, s_qd[par])
            else:
                lw = [(s_qsg[bb], idx // 4)] if idx >= 4 else []
                self.ddma("sync", lw, pwl[bb][:, :], src, s_qd[par])
            bpos, bneg = (self.gam[:, 0:1], self.gam[:, 1:2]) if w == 0 else \
                         (self.gam[:, 2:3], self.gam[:, 3:4])
            sw = [(s_qd[par], None)]
            if idx >= 2:
                sw.append((s_qf[par], None))
            def sgn2(e, bb=bb, par=par, bpos=bpos, bneg=bneg):
                e.activation(self.gamrep[par][:, :], pwl[bb][:, :], AF.Sign,
                             bias=bneg, scale=1.0)
                return e.activation(self.gamneg[par][:, :], pwl[bb][:, :],
                                    AF.Sign, bias=bpos, scale=1.0)
            self.dstep("scalar", sw, sgn2, s_qsg[bb])
            self.dstep("vector", [(s_qsg[bb], None)], lambda v, par=par:
                       v.tensor_add(ptau[par][:, :], self.gamrep[par][:, :],
                                    self.gamneg[par][:, :]), s_qf[par])
            if not defer:
                self.wait("sync", s_qf[par])
                self.dma("sync", dst, ptau[par][:, :], s_qd[par])
            else:
                self.ddma("sync", [(s_qf[par], None)], dst, ptau[par][:, :],
                          s_qd[par])
            if idx == 255:
                self.pp_w1q_end = len(self.pp_steps)
                self.wait("sync", s_qd[0])
                self.wait("sync", s_qd[1])
                self.op("sync", lambda sy: sy.nop(), self.sem("s_cachew1"))
            elif idx >= 256 and (idx - 255) % 64 == 0:
                self.dstep("sync", [(s_qd[0], None), (s_qd[1], None)],
                           lambda sy: sy.nop(), self.sem("s_w2c"))
                self.pp_w2g_end = getattr(self, "pp_w2g_end", {})
                self.pp_w2g_end[(idx - 319) // 64] = len(self.pp_steps)

    def flush_gamma(self):
        while getattr(self, "pp_popped", 0) < getattr(self, "pp_gamma_end", 0) \
                and self.pp_steps:
            self.pump(1)

    def emit_gamma_finalize(self):
        self.flush_gamma()
        self.pp_limit = None
        """Small tensor/vector chain turning pacc into the gam columns.
        Emitted at a controlled point between attention phases."""
        pE, rowb, gam = self.pE, self.rowb, self.gam
        self.wait("tensor", self.sem("s_gred"), 1)
        self.pbank("tensor", pE)
        self.op("tensor", lambda t: t.matmul(pE[0:1, 0:8], self.c_onescol[:, :],
                                             self.pacc[:, 0:8], start=True, stop=True),
                self.sem("s_gsum"))
        self.wait("vector", self.sem("s_gsum"), 1)
        def gfin_a(v):
            return v.tensor_scalar_mul(rowb[0:1, 4:12], pE[0:1, 0:8],
                                       float(2.0 ** -24))
        def gfin_b(v):
            v.tensor_add(rowb[0:1, 12:13], rowb[0:1, 4:5], rowb[0:1, 5:6])
            v.tensor_add(rowb[0:1, 13:14], rowb[0:1, 6:7], rowb[0:1, 7:8])
            v.tensor_add(rowb[0:1, 14:15], rowb[0:1, 8:9], rowb[0:1, 9:10])
            return v.tensor_add(rowb[0:1, 15:16], rowb[0:1, 10:11],
                                rowb[0:1, 11:12])
        def gfin_c(v):
            v.scalar_tensor_tensor(rowb[0:1, 0:1], rowb[0:1, 12:13], 1.0,
                                   rowb[0:1, 13:14], ALU.mult, ALU.add)
            return v.scalar_tensor_tensor(rowb[0:1, 1:2], rowb[0:1, 14:15], 1.0,
                                          rowb[0:1, 15:16], ALU.mult, ALU.add)
        def gfin_d(v):
            return v.tensor_scalar(rowb[0:1, 0:2], rowb[0:1, 0:2], 1.0, Q_EPS,
                                   ALU.mult, ALU.add)
        def gfin_e(v):
            return v.tensor_scalar_mul(rowb[0:1, 2:4], rowb[0:1, 0:2], 0.5)
        self.chain("vector", [gfin_a, gfin_b, gfin_c, gfin_d, gfin_e],
                   self.sem("s_gfin"))
        self.wait("tensor", self.sem("s_gfin"), 1)
        self.op("tensor", lambda t: t.matmul(pE[:, 0:4], self.c_onesbc[:, :],
                                             rowb[0:1, 0:4], start=True, stop=True),
                self.sem("s_gbc"))
        self.wait("vector", self.sem("s_gbc"), 1)
        def gbc_a(v):
            v.tensor_copy(gam[:, 0:1], pE[:, 2:3])
            v.tensor_scalar_mul(gam[:, 1:2], pE[:, 2:3], -1.0)
            v.tensor_copy(gam[:, 2:3], pE[:, 3:4])
            v.tensor_scalar_mul(gam[:, 3:4], pE[:, 3:4], -1.0)
            v.tensor_copy(gam[:, 6:7], pE[:, 0:1])
            return v.tensor_scalar_mul(gam[:, 5:6], pE[:, 1:2], 1.0 / 254.0)
        def gbc_b(v):
            return v.scalar_tensor_tensor(gam[:, 4:5], gam[:, 6:7], 4096.0,
                                          gam[:, 6:7], ALU.mult, ALU.mult)
        self.chain("vector", [gbc_a, gbc_b], self.sem("s_gamma"))
        self.g_set(("bank", "E"), self.sem("s_gamma"))

    def alloc_attn_outer(self, esA):
        nc = self.nc
        SB = lambda n, sh, dt=F32: esA.enter_context(nc.sbuf_tensor(n, sh, dt))
        a = self
        a.qk_k = SB("qk_k", [97, H * S], BF16)
        a.v_t = SB("v_t", [128, NKT * H * 65], BF16)
        a.nq = SB("nq", [128, 6 * S], BF16)
        a.kmx = SB("kmx", [97, H], F32)
        a.r32 = SB("r32", [97, TC], BF16)
        a.crow = SB("crow", [97, TC], BF16)
        a.qsq = [SB(f"qsq{i}", [96, TC], BF16) for i in range(2)]

    def alloc_nkv(self, es, si):
        self.nkv = es.enter_context(
            self.nc.sbuf_tensor(f"nkv_{si}", [128, 2 * S], BF16))

    def alloc_p1(self, es, si):
        nc = self.nc
        SB = lambda n, sh, dt=F32: es.enter_context(nc.sbuf_tensor(n + f"_{si}", sh, dt))
        a = self
        a.xb = [SB(f"xb{i}", [128, D], BF16) for i in range(2)]
        a.y1t = SB("y1t", [128, 16 * TC], BF16)
        a.cq_fm = SB("cq_fm", [128, 6 * TC], BF16)
        a.sq_t = [SB(f"sq_t{i}", [128, TC], BF16) for i in range(2)]
        a.rowc = SB("rowc", [1, 4 * TC], F32)
        a.lndump = SB("lndump", [128, 1024], BF16)
        a.wl1 = [SB(f"wl1{i}", [128, RQ], BF16) for i in range(2)]
        a.wl2 = [SB(f"wl2{i}", [128, RKV + DR], BF16) for i in range(2)]
        a.ag_q = SB("ag_q", [128, 6], F32)
        a.ag_kv = SB("ag_kv", [128, 2], F32)
        sg = self.sem("s_gains")
        self.dma("gpsimd", a.ag_q[:, :], self.d_qg, sg)
        self.dma("gpsimd", a.ag_kv[:, :], self.d_kvg, sg)

    def alloc_p2(self, es, si):
        nc = self.nc
        SB = lambda n, sh, dt=F32: es.enter_context(nc.sbuf_tensor(n + f"_{si}", sh, dt))
        a = self
        a.aw_kn = SB("aw_kn", [128, 2 * H * DN], BF16)
        a.aw_v = SB("aw_v", [128, 2 * H * DV], BF16)
        sw = self.sem("s_p2w")
        rr = lambda d: d.rearrange("(c p) o -> p c o", p=128)
        self.dma("gpsimd", a.aw_kn[:, :], rr(self.d_wkn), sw)
        self.dma("gpsimd", a.aw_v[:, :], rr(self.d_wv), sw)
        self.wait("tensor", sw)

    def alloc_p3(self, es, si):
        nc = self.nc
        SB = lambda n, sh, dt=F32: es.enter_context(nc.sbuf_tensor(n + f"_{si}", sh, dt))
        a = self
        a.aw_qn = SB("aw_qn", [128, 6 * H * DN], BF16)
        a.aw_qr = SB("aw_qr", [128, 6 * H * DR], BF16)
        a.q_aug = SB("q_aug", [97, H * TC], BF16)
        a.att_fm = SB("att_fm", [128, 8 * TC], BF16)
        a.att_tmp = SB("att_tmp", [64, TC], BF16)
        a.sexp = [SB(f"sexp{i}", [128, 512], BF16) for i in range(2)]
        a.den_b = SB("den_b", [65, 512], BF16)
        a.srec = SB("srec", [64, 512], F32)
        a.wpb = [SB(f"wpb{i}", [128, 512], BF16) for i in range(2)]
        a.xb2 = [SB(f"xb2_{i}", [128, 512], BF16) for i in range(2)]
        a.xpst = [SB(f"xpst_{i}", [128, 512], BF16) for i in range(2)]
        sw = self.sem("s_p3w")
        rr = lambda d: d.rearrange("(c p) o -> p c o", p=128)
        self.dma("gpsimd", a.aw_qn[:, :], rr(self.d_wqn), sw)
        self.dma("gpsimd", a.aw_qr[:, :], rr(self.d_wqr), sw)
        self.wait("tensor", sw)

    def emit_p1(self, si):
        a = self
        xb = a.xb; y1t = a.y1t; cq_fm = a.cq_fm; sq_t = a.sq_t; rowc = a.rowc; lndump = a.lndump; wl1 = a.wl1; wl2 = a.wl2; ag_q = a.ag_q; ag_kv = a.ag_kv; nkv = a.nkv; stats = a.stats
        sfx = f"_{si}"
        xs = self.d_x[si]
        s_xld = [self.sem(f"s_xld{i}") for i in range(2)]
        s_sqs = self.sem(f"s_sqs")
        s_ln1 = self.sem(f"s_ln1")
        s_y1 = self.sem(f"s_y1")
        s_tp = self.sem(f"s_tp")
        s_tpe = self.sem(f"s_tpe")
        s_lat = self.sem(f"s_lat")
        s_late = self.sem(f"s_late")
        s_row = self.sem(f"s_row")
        s_rbc = self.sem(f"s_rbc")
        s_nrm = self.sem(f"s_nrm")
        s_kve = self.sem(f"s_kve")

        for tc in range(NTC):
            self.pump(12)
            for tt in range(4):
                ti = tc * 4 + tt
                par = ti % 2
                self.g_wait("gpsimd", ("xb", par))
                self.dma("gpsimd", xb[par][:, :], xs[ti * 128:(ti + 1) * 128, :],
                         s_xld[par])
                pc = par * 8
                self.wait("scalar", s_xld[par])
                self.g_wait("scalar", ("stats", par))
                self.op("scalar", lambda s, par=par, pc=pc:
                        s.activation(lndump[:, :], xb[par][:, 0:1024], AF.Square,
                                     bias=0.0, scale=1.0,
                                     accum_out=stats[:, pc:pc + 1]), s_sqs)
                self.wait("scalar", s_sqs)
                self.op("scalar", lambda s, par=par, pc=pc:
                        s.activation(lndump[:, :], xb[par][:, 1024:2048],
                                     AF.Square, bias=0.0, scale=1.0,
                                     accum_out=stats[:, pc + 2:pc + 3]), s_sqs)
                self.wait("vector", s_xld[par])
                self.wait("vector", s_sqs)
                self.op("vector", lambda v, par=par, pc=pc:
                        v.reduce_sum(stats[:, pc + 1:pc + 2], xb[par][:, :],
                                     axis=AX.X), self.sem("s_rr"))
                self.wait("vector", self.sem("s_rr"))
                def lnv_a(v, pc=pc):
                    st = stats
                    v.tensor_add(st[:, pc:pc + 1], st[:, pc:pc + 1], st[:, pc + 2:pc + 3])
                    return v.tensor_scalar_mul(st[:, pc + 1:pc + 2],
                                               st[:, pc + 1:pc + 2], 1.0 / D)
                def lnv_b(v, pc=pc):
                    st = stats
                    v.tensor_scalar(st[:, pc:pc + 1], st[:, pc:pc + 1], 1.0 / D,
                                    LN_EPS, ALU.mult, ALU.add)
                    return v.tensor_tensor(st[:, pc + 2:pc + 3], st[:, pc + 1:pc + 2],
                                           st[:, pc + 1:pc + 2], ALU.mult)
                def lnv_c(v, pc=pc):
                    st = stats
                    return v.tensor_sub(st[:, pc:pc + 1], st[:, pc:pc + 1],
                                        st[:, pc + 2:pc + 3])
                self.chain("vector", [lnv_a, lnv_b, lnv_c], s_ln1)
                self.wait("scalar", s_ln1)
                self.op("scalar", lambda s, pc=pc:
                        s.activation(stats[:, pc + 3:pc + 4], stats[:, pc:pc + 1],
                                     AF.Sqrt, bias=0.0, scale=1.0), s_sqs)
                self.wait("vector", s_sqs)
                def l2_a(v, pc=pc):
                    return v.reciprocal(stats[:, pc + 4:pc + 5],
                                        stats[:, pc + 3:pc + 4])
                def l2_b(v, pc=pc):
                    return v.scalar_tensor_tensor(stats[:, pc + 6:pc + 7],
                                                  stats[:, pc + 1:pc + 2], -1.0,
                                                  stats[:, pc + 4:pc + 5],
                                                  ALU.mult, ALU.mult)
                def l2_c(v, par=par, pc=pc):
                    return v.tensor_scalar(xb[par][:, :], xb[par][:, :],
                                           stats[:, pc + 4:pc + 5],
                                           stats[:, pc + 6:pc + 7],
                                           ALU.mult, ALU.add)
                self.chain("vector", [l2_a, l2_b, l2_c], s_y1)
                self.g_set(("stats", par), s_y1)
                if self.dbg_xp and si == 0 and ti == 0:
                    self.freeze_dump([(self.d_dbg_st, stats[:, :]),
                                      (self.d_dbg_y1[0:128, 0:2048],
                                       xb[par][:, :])])
                self.wait("tensor", s_y1)
                if tt == 0:
                    self.g_wait("vector", ("y1t",))
                for fb in range(16):
                    ptp = self.pT[fb % 2]
                    self.pbank("tensor", ptp)
                    self.op("tensor", lambda t, par=par, fb=fb, ptp=ptp:
                            t.transpose(ptp[:, 0:128],
                                        xb[par][:, fb * 128:(fb + 1) * 128],
                                        self.c_ident[:, :]), s_tp)
                    self.wait("vector", s_tp)
                    self.op("vector", lambda v, fb=fb, tt=tt, ptp=ptp:
                            v.tensor_copy(y1t[:, fb * TC + tt * 128:
                                                fb * TC + (tt + 1) * 128],
                                          ptp[:, 0:128]), s_tpe)
                    self.g_set(("bank", self.bname[id(ptp)]), s_tpe)
                self.op("vector", lambda v: v.engine_nop(), self.sem("s_dummy"))
                self.g_set(("xb", par), s_tpe)

            s_wl1 = [self.sem(f"s_wl1{i}") for i in range(2)]
            s_wl2 = [self.sem(f"s_wl2{i}") for i in range(2)]
            self.wait("tensor", s_tpe)

            def evac_chunk(j, pp, last_cs):
                self.op("vector", lambda v, j=j, pp=pp:
                        v.tensor_copy(cq_fm[:, j * TC:(j + 1) * TC], pp[:, :]),
                        s_late)
                self.wait("scalar", s_late)
                self.g_wait("scalar", ("sq_t", j % 2))
                self.op("scalar", lambda s, j=j, pp=pp:
                        s.activation(sq_t[j % 2][:, :], pp[:, :], AF.Square,
                                     bias=0.0, scale=1.0), s_sqs)
                self.g_set(("bank", self.bname[id(pp)]), s_sqs)
                self.wait("tensor", s_sqs)
                if j == 0:
                    self.pbank("tensor", self.pF)
                self.op("tensor", lambda t, j=j, last_cs=last_cs:
                        t.matmul(self.pF[0:1, 0:TC], self.c_onescolb[:, :],
                                 sq_t[j % 2][:, :], start=(j == 0),
                                 stop=last_cs), s_row)
                self.g_set(("sq_t", j % 2), s_row)

            def rms_rows(rank, gain, ntens, n_ch, tc):
                self.wait("vector", s_row)
                self.op("vector", lambda v, rank=rank:
                        v.tensor_scalar(rowc[0:1, 0:TC], self.pF[0:1, 0:TC],
                                        1.0 / rank, RMS_EPS, ALU.mult, ALU.add),
                        s_row)
                self.g_set(("bank", "F"), s_row)
                self.wait("scalar", s_row)
                self.op("scalar", lambda s:
                        s.activation(rowc[0:1, TC:2 * TC], rowc[0:1, 0:TC],
                                     AF.Sqrt, bias=0.0, scale=1.0), s_sqs)
                self.wait("vector", s_sqs)
                self.op("vector", lambda v:
                        v.reciprocal(rowc[0:1, 2 * TC:3 * TC],
                                     rowc[0:1, TC:2 * TC]), s_row)
                self.wait("tensor", s_row)
                self.pbank("tensor", self.pF)
                self.op("tensor", lambda t:
                        t.matmul(self.pF[:, :], self.c_onesbc[:, :],
                                 rowc[0:1, 2 * TC:3 * TC],
                                 start=True, stop=True), s_rbc)
                self.wait("vector", s_rbc)
                if tc == 0:
                    self.g_wait("vector", ("nq",))
                    self.g_wait("vector", ("nkv",))
                for j in range(n_ch):
                    self.op("vector", lambda v, j=j, gain=gain, ntens=ntens, tc=tc:
                            v.scalar_tensor_tensor(
                                ntens[:, j * S + tc * TC:j * S + (tc + 1) * TC],
                                cq_fm[:, j * TC:(j + 1) * TC],
                                gain[:, j:j + 1], self.pF[:, :],
                                ALU.mult, ALU.mult), s_nrm)
                self.g_set(("bank", "F"), s_nrm)

            banks5 = [self.pA, self.pB, self.pC, self.pD, self.pE]
            for bk in banks5:
                self.pbank("tensor", bk)
            for i in range(16):
                par = i % 2
                self.g_wait("gpsimd", ("wl1", par))
                self.dma("gpsimd", wl1[par][:, :],
                         self.d_wcq[i * 128:(i + 1) * 128, :], s_wl1[par])
                self.wait("tensor", s_wl1[par])
                def lat1(t, i=i, par=par):
                    for oc in range(5):
                        li = t.matmul(banks5[oc][:, :],
                                      wl1[par][:, oc * 128:(oc + 1) * 128],
                                      y1t[:, i * TC:(i + 1) * TC],
                                      start=(i == 0), stop=(i == 15))
                    return li
                self.op("tensor", lat1, s_lat)
                self.g_set(("wl1", par), s_lat)
            base = s_lat.v
            for j in range(5):
                self.wait("vector", s_lat, base)
                if si == 0 and tc == 0 and j == 0:
                    self.wait("vector", self.sem("s_gains"))
                evac_chunk(j, banks5[j], last_cs=False)
            for i in range(16):
                par = i % 2
                self.g_wait("gpsimd", ("wl1", par))
                self.dma("gpsimd", wl1[par][:, :128],
                         self.d_wcq[i * 128:(i + 1) * 128, 5 * 128:6 * 128],
                         s_wl1[par])
                self.wait("tensor", s_wl1[par])
                if i == 0:
                    self.pbank("tensor", self.pA)
                def lat1b(t, i=i, par=par):
                    return t.matmul(self.pA[:, :], wl1[par][:, 0:128],
                                    y1t[:, i * TC:(i + 1) * TC],
                                    start=(i == 0), stop=(i == 15))
                self.op("tensor", lat1b, s_lat)
                self.g_set(("wl1", par), s_lat)
            self.wait("vector", s_lat)
            evac_chunk(5, self.pA, last_cs=True)
            rms_rows(RQ, ag_q, a.nq, 6, tc)

            self.pbank("tensor", self.pA)
            self.pbank("tensor", self.pB)
            self.pbank("tensor", self.pE)
            for i in range(16):
                par = i % 2
                self.g_wait("gpsimd", ("wl2", par))
                self.dma("gpsimd", wl2[par][:, 0:RKV],
                         self.d_wckv[i * 128:(i + 1) * 128, :], s_wl2[par])
                self.dma("gpsimd", wl2[par][:, RKV:RKV + DR],
                         self.d_wkr[i * 128:(i + 1) * 128, :], s_wl2[par])
                self.wait("tensor", s_wl2[par])
                def lat2(t, i=i, par=par):
                    t.matmul(self.pA[:, :], wl2[par][:, 0:128],
                             y1t[:, i * TC:(i + 1) * TC],
                             start=(i == 0), stop=(i == 15))
                    t.matmul(self.pB[:, :], wl2[par][:, 128:256],
                             y1t[:, i * TC:(i + 1) * TC],
                             start=(i == 0), stop=(i == 15))
                    return t.matmul(self.pE[64:96, :], wl2[par][:, 256:288],
                                    y1t[:, i * TC:(i + 1) * TC],
                                    start=(i == 0), stop=(i == 15))
                self.op("tensor", lat2, s_lat)
                self.g_set(("wl2", par), s_lat)
            self.g_set(("y1t",), s_lat)
            base = s_lat.v
            for j in range(2):
                self.wait("vector", s_lat, base)
                evac_chunk(j, [self.pA, self.pB][j], last_cs=(j == 1))
            rms_rows(RKV, ag_kv, nkv, 2, tc)

            if self.dbg_xp and si == 0 and tc == 0:
                sdbg = self.sem("s_dbg")
                self.wait("gpsimd", s_nrm)
                self.dma("gpsimd", self.d_dbg_cq, cq_fm[:, :], sdbg)
                self.wait("gpsimd", sdbg)
            self.wait("vector", s_lat)
            if si == 0 and tc == 0:
                self.wait("vector", self.sem("s_const"))
            self.g_wait("vector", ("crow",))
            self.g_wait("vector", ("qk_k",))
            def krope2(v, tc=tc):
                raw = a.r32[64:96, :]
                sh = a.crow[64:96, :]
                v.tensor_copy(raw, self.pE[64:96, :])
                v.stream_shuffle(sh, raw, SHUF)
                v.scalar_tensor_tensor(sh, sh, self.c_sig[64:96, :], raw,
                                       ALU.mult, ALU.add)
                return v.tensor_tensor(a.qk_k[64:96, tc * TC:(tc + 1) * TC], sh,
                                       self.c_crk[64:96, tc * TC:(tc + 1) * TC],
                                       ALU.mult)
            self.op("vector", krope2, s_kve)
            self.g_set(("bank", "E"), s_kve)
            self.g_set(("crow",), s_kve)

        def krep(v):
            for h in range(1, H):
                v.tensor_copy(a.qk_k[64:96, h * S:(h + 1) * S], a.qk_k[64:96, 0:S])
            return v.memset(a.qk_k[96:97, :], 1.0)
        self.op("vector", krep, s_kve)

    def emit_p2(self, si):
        a = self
        aw_kn = a.aw_kn; aw_v = a.aw_v; nkv = a.nkv
        sfx = f"_{si}"
        s_nrm = self.sem(f"s_nrm")
        s_kve = self.sem(f"s_kve")
        s_kv = self.sem(f"s_kv")
        s_kev = self.sem(f"s_kev")
        s_ksq = self.sem(f"s_ksq")
        s_kcs = self.sem(f"s_kcs")
        s_kmx = self.sem(f"s_kmx")

        self.g_wait("vector", ("v_t",))
        self.g_wait("vector", ("qk_k",))
        self.op("vector", lambda v: v.memset(a.v_t[:, :], 1.0), s_kev)

        self.wait("tensor", s_nrm)
        for h in range(H):
            for kc in range(NTC):
                self.pump(2)
                pp = self.rot[(h * NTC + kc) % 4]
                self.pbank("tensor", pp)
                def knmm(t, h=h, kc=kc, pp=pp):
                    for i in range(2):
                        li = t.matmul(pp[0:64, :],
                                      aw_kn[:, i * H * DN + h * DN:i * H * DN + (h + 1) * DN],
                                      nkv[:, i * S + kc * TC:i * S + (kc + 1) * TC],
                                      start=(i == 0), stop=(i == 1))
                    return li
                self.op("tensor", knmm, s_kv)
                self.wait("vector", s_kv)
                self.op("vector", lambda v, h=h, kc=kc, pp=pp:
                        v.tensor_copy(a.qk_k[0:64, h * S + kc * TC:h * S + (kc + 1) * TC],
                                      pp[0:64, :]), s_kev)
                self.g_set(("bank", self.bname[id(pp)]), s_kev)
        for kt in range(NKT):
            self.pump(2)
            for half in range(2):
                pp = self.rot[(kt % 2) * 2 + half]
                self.pbank("tensor", pp)
                def vmm(t, kt=kt, half=half, pp=pp):
                    for i in range(2):
                        li = t.matmul(pp[:, :],
                                      nkv[:, i * S + kt * 128:i * S + (kt + 1) * 128],
                                      aw_v[:, i * H * DV + half * 512:
                                             i * H * DV + (half + 1) * 512],
                                      start=(i == 0), stop=(i == 1))
                    return li
                self.op("tensor", vmm, s_kv)
                self.wait("vector", s_kv)
                self.op("vector", lambda v, kt=kt, half=half, pp=pp:
                        v.tensor_copy(
                            a.v_t[:, kt * H * 65 + half * 8 * 65:
                                  kt * H * 65 + (half * 8 + 8) * 65]
                            .rearrange("p (h e) -> p h e", e=65)[:, :, 0:64],
                            pp[:, :].rearrange("p (h e) -> p h e", e=64)), s_kev)
                self.g_set(("bank", self.bname[id(pp)]), s_kev)

        for h in range(H):
            self.pump(2)
            for kc in range(NTC):
                idx = h * NTC + kc
                qs = a.qsq[idx % 2]
                pX = [self.pE, self.pF][idx % 2]
                self.wait("scalar", s_kev)
                self.wait("scalar", s_kve)
                self.g_wait("scalar", ("qsq", idx % 2))
                self.op("scalar", lambda s, h=h, kc=kc, qs=qs:
                        s.activation(qs[0:96, :],
                                     a.qk_k[0:96, h * S + kc * TC:h * S + (kc + 1) * TC],
                                     AF.Square, bias=0.0, scale=1.0), s_ksq)
                self.wait("tensor", s_ksq)
                self.pbank("tensor", pX)
                self.op("tensor", lambda t, qs=qs, pX=pX:
                        t.matmul(pX[64:97, 0:TC], self.c_cones[:, :],
                                 qs[0:96, :], start=True, stop=True), s_kcs)
                self.g_set(("qsq", idx % 2), s_kcs)
                self.wait("vector", s_kcs)
                if h == 0 and kc == 0:
                    self.g_wait("vector", ("kmx",))
                if kc == 0:
                    self.op("vector", lambda v, h=h, pX=pX:
                            v.reduce_max(a.kmx[96:97, h:h + 1],
                                         pX[96:97, 0:TC], axis=AX.X), s_kmx)
                else:
                    self.op("vector", lambda v, pX=pX:
                            v.reduce_max(a.crow[96:97, 0:1],
                                         pX[96:97, 0:TC], axis=AX.X),
                            self.sem("s_rr"))
                    self.wait("vector", self.sem("s_rr"))
                    self.wait("vector", s_kmx)
                    self.op("vector", lambda v, h=h:
                            v.tensor_max(a.kmx[96:97, h:h + 1], a.kmx[96:97, h:h + 1],
                                         a.crow[96:97, 0:1]), s_kmx)
                self.g_set(("bank", self.bname[id(pX)]), s_kmx)
        self.wait("scalar", s_kmx)
        self.op("scalar", lambda s:
                s.activation(a.kmx[96:97, 0:H], a.kmx[96:97, 0:H], AF.Sqrt,
                             bias=0.0, scale=1.0), s_ksq)
        self.wait("vector", s_ksq)
        self.op("vector", lambda v:
                v.tensor_scalar_mul(a.kmx[96:97, 0:H], a.kmx[96:97, 0:H], -1.0),
                s_kmx)

    def emit_p3(self, si):
        a = self
        aw_qn = a.aw_qn; aw_qr = a.aw_qr; q_aug = a.q_aug; att_fm = a.att_fm; att_tmp = a.att_tmp; sexp = a.sexp; den_b = a.den_b; srec = a.srec; wpb = a.wpb; xb2 = a.xb2; xpst = a.xpst
        sfx = f"_{si}"
        s_nrm = self.sem(f"s_nrm")
        s_kmx = self.sem(f"s_kmx")
        s_kev = self.sem(f"s_kev")
        s_kve = self.sem(f"s_kve")
        s_qb = self.sem(f"s_qb")
        s_qbe = self.sem(f"s_qbe")
        s_qsq = self.sem(f"s_qsq")
        s_qcs = self.sem(f"s_qcs")
        s_crw = self.sem(f"s_crw")
        s_qk = self.sem(f"s_qk")
        s_msk = self.sem(f"s_msk")
        s_exp = self.sem(f"s_exp")
        s_pvk = self.sem(f"s_pvk")
        s_dnc = self.sem(f"s_dnc")
        s_dnd = self.sem(f"s_dnd")
        s_dbc = self.sem(f"s_dbc")
        s_rcp = self.sem(f"s_rcp")
        s_div = self.sem(f"s_div")
        s_odm = self.sem(f"s_odm")
        s_wp = [self.sem(f"s_wp{i}") for i in range(2)]
        s_op = self.sem(f"s_op")
        s_ope = self.sem(f"s_ope")
        s_x2 = [self.sem(f"s_x2{i}") for i in range(2)]
        s_xpw = self.sem(f"s_xpw")
        xs = self.d_x[si]
        xps = self.d_xp[si]

        self.wait("tensor", s_kev)
        self.wait("tensor", s_kve)
        for qc in range(NTC):
            nkt = 4 * (qc + 1)
            if qc == 0:
                self.wait("tensor", s_nrm)
            self.g_wait("tensor", ("q_aug",))
            qbv = {}

            def SQNR(h, qc=qc):
                self.pump(6)
                ppn = self.rot[(2 * h) % 4]
                ppr = self.rot[(2 * h + 1) % 4]
                self.pbank("tensor", ppn)
                def qn(t, h=h, ppn=ppn, qc=qc):
                    for i in range(6):
                        li = t.matmul(ppn[0:64, :],
                                      aw_qn[:, i * H * DN + h * DN:i * H * DN + (h + 1) * DN],
                                      a.nq[:, i * S + qc * TC:i * S + (qc + 1) * TC],
                                      start=(i == 0), stop=(i == 5))
                    return li
                self.op("tensor", qn, s_qb)
                self.pbank("tensor", ppr)
                def qr(t, h=h, ppr=ppr, qc=qc):
                    for i in range(6):
                        li = t.matmul(ppr[64:96, :],
                                      aw_qr[:, i * H * DR + h * DR:i * H * DR + (h + 1) * DR],
                                      a.nq[:, i * S + qc * TC:i * S + (qc + 1) * TC],
                                      start=(i == 0), stop=(i == 5))
                    return li
                self.op("tensor", qr, s_qb)
                self.wait("vector", s_qb)
                def qev(v, h=h, ppn=ppn, ppr=ppr, qc=qc):
                    dn = q_aug[0:64, h * TC:(h + 1) * TC]
                    dr = q_aug[64:96, h * TC:(h + 1) * TC]
                    raw = a.r32[64:96, :]
                    v.tensor_scalar_mul(dn, ppn[0:64, :], SC)
                    v.tensor_copy(raw, ppr[64:96, :])
                    v.stream_shuffle(dr, raw, SHUF)
                    v.scalar_tensor_tensor(dr, dr, self.c_sig[64:96, :], raw,
                                           ALU.mult, ALU.add)
                    return v.tensor_tensor(dr, dr,
                                           self.c_crq[64:96, qc * TC:(qc + 1) * TC],
                                           ALU.mult)
                self.op("vector", qev, s_qbe)
                qbv[h] = s_qbe.v
                self.g_set(("bank", self.bname[id(ppn)]), s_qbe)
                self.g_set(("bank", self.bname[id(ppr)]), s_qbe)

            def SCHAIN(h, qc=qc):
                qs = a.qsq[h % 2]
                pX = [self.pE, self.pF][h % 2]
                self.wait("scalar", s_qbe, qbv[h])
                self.g_wait("scalar", ("qsq", h % 2))
                self.op("scalar", lambda s, h=h, qs=qs:
                        s.activation(qs[0:96, :], q_aug[0:96, h * TC:(h + 1) * TC],
                                     AF.Square, bias=0.0, scale=1.0), s_qsq)
                self.wait("tensor", s_qsq)
                self.pbank("tensor", pX)
                self.op("tensor", lambda t, qs=qs, pX=pX:
                        t.matmul(pX[64:97, 0:TC], self.c_cones[:, :],
                                 qs[0:96, :], start=True, stop=True), s_qcs)
                self.g_set(("qsq", h % 2), s_qcs)
                self.wait("scalar", s_qcs)
                self.g_wait("scalar", ("crow",))
                self.op("scalar", lambda s, pX=pX:
                        s.activation(a.crow[96:97, :], pX[96:97, 0:TC], AF.Sqrt,
                                     bias=0.0, scale=1.0), s_qsq)
                self.g_set(("bank", self.bname[id(pX)]), s_qsq)
                self.wait("vector", s_qsq)
                if qc == 0 and h == 0:
                    self.wait("vector", s_kmx)
                self.op("vector", lambda v, h=h:
                        v.tensor_single_scalar(q_aug[96:97, h * TC:(h + 1) * TC],
                                               a.crow[96:97, :],
                                               a.kmx[96:97, h:h + 1], ALU.mult),
                        s_crw)
                self.g_set(("crow",), s_crw)

            SQNR(0)
            for h in range(1, H):
                SQNR(h)
                SCHAIN(h - 1)
            SCHAIN(H - 1)
            if si == 0 and qc == 3:
                self.emit_gamma_finalize()

            self.wait("tensor", s_crw)
            pend_den = [None]

            def mk_den(h, pv, pvkv):
                def den():
                    self.wait("vector", s_pvk, pvkv)
                    self.g_wait("vector", ("den",))
                    self.op("vector", lambda v, pv=pv:
                            v.tensor_copy(den_b[64:65, :], pv[64:65, :]), s_dnc)
                    self.wait("scalar", s_dnc)
                    self.dma("scalar", den_b[0:1, :], den_b[64:65, :], s_dnd)
                    self.wait("tensor", s_dnd)
                    self.pbank("tensor", self.pE)
                    self.op("tensor", lambda t:
                            t.matmul(self.pE[0:64, :], self.c_ones64[:, :],
                                     den_b[0:1, :], start=True, stop=True), s_dbc)
                    self.g_set(("den",), s_dbc)
                    self.wait("scalar", s_dbc)
                    self.pbank("scalar", self.pF)
                    self.g_wait("scalar", ("srec",))
                    def rcp(s):
                        s.activation(self.pF[0:64, :], self.pE[0:64, :], AF.Ln,
                                     bias=0.0, scale=1.0)
                        return s.activation(srec[:, :], self.pF[0:64, :], AF.Exp,
                                            bias=0.0, scale=-1.0)
                    self.op("scalar", rcp, s_rcp)
                    self.g_set(("bank", "E"), s_rcp)
                    self.g_set(("bank", "F"), s_rcp)
                    self.wait("vector", s_rcp)
                    even = (h % 2 == 0)
                    if even:
                        dst = att_fm[0:64, (h // 2) * TC:(h // 2 + 1) * TC]
                    else:
                        self.g_wait("vector", ("att_tmp",))
                        dst = att_tmp[0:64, :]
                    self.op("vector", lambda v, pv=pv, dst=dst:
                            v.tensor_mul(dst, pv[0:64, :], srec[:, :]), s_div)
                    self.g_set(("srec",), s_div)
                    self.g_set(("bank", self.bname[id(pv)]), s_div)
                    if not even:
                        self.wait("gpsimd", s_div)
                        self.dma("gpsimd",
                                 att_fm[64:128, (h // 2) * TC:(h // 2 + 1) * TC],
                                 att_tmp[0:64, :], s_odm)
                        self.g_set(("att_tmp",), s_odm)
                return den

            for h in range(H):
                self.pump(6)
                pv = [self.pC, self.pD][h % 2]
                self.pbank("tensor", pv)
                expv = {}

                def SQK(kt, h=h, qc=qc):
                    psc = [self.pA, self.pB][kt % 2]
                    self.pbank("tensor", psc)
                    self.op("tensor", lambda t, h=h, kt=kt, psc=psc:
                            t.matmul(psc[:, :],
                                     a.qk_k[0:97, h * S + kt * 128:h * S + (kt + 1) * 128],
                                     q_aug[0:97, h * TC:(h + 1) * TC],
                                     start=True, stop=True), s_qk)
                    return psc

                def SME(kt, psc, qc=qc):
                    diag = kt >= 4 * qc
                    if diag:
                        r = kt - 4 * qc
                        self.wait("vector", s_qk)
                        self.op("vector", lambda v, psc=psc, r=r:
                                v.tensor_add(psc[:, :], psc[:, :],
                                             self.c_mask[:, r * 512:(r + 1) * 512]),
                                s_msk)
                        self.wait("scalar", s_msk)
                    else:
                        self.wait("scalar", s_qk)
                    self.g_wait("scalar", ("sexp", kt % 2))
                    self.op("scalar", lambda s, psc=psc, kt=kt:
                            s.activation(sexp[kt % 2][:, :], psc[:, :], AF.Exp,
                                         bias=0.0, scale=1.0), s_exp)
                    expv[kt] = s_exp.v
                    self.g_set(("bank", self.bname[id(psc)]), s_exp)

                def SPV(kt, h=h, pv=pv, nkt=nkt):
                    self.wait("tensor", s_exp, expv[kt])
                    self.op("tensor", lambda t, h=h, kt=kt, pv=pv,
                            last=(kt == nkt - 1):
                            t.matmul(pv[0:65, :],
                                     a.v_t[:, kt * H * 65 + h * 65:
                                           kt * H * 65 + (h + 1) * 65],
                                     sexp[kt % 2][:, :],
                                     start=(kt == 0), stop=last), s_pvk)
                    self.g_set(("sexp", kt % 2), s_pvk)

                p0 = SQK(0)
                SME(0, p0)
                for kt in range(1, nkt):
                    p = SQK(kt)
                    if kt == 3 and pend_den[0] is not None:
                        pend_den[0]()
                        pend_den[0] = None
                    SME(kt, p)
                    SPV(kt - 1)
                SPV(nkt - 1)
                pend_den[0] = mk_den(h, pv, s_pvk.v)
            pend_den[0]()
            pend_den[0] = None

            self.wait("tensor", s_div)
            self.wait("tensor", s_odm)
            for dc in range(4):
                self.pump(6)
                for ic in range(8):
                    par = ic % 2
                    self.g_wait("gpsimd", ("wpb", par))
                    self.dma("gpsimd", wpb[par][:, :],
                             self.d_wpr[:, ic * D + dc * 512:
                                        ic * D + (dc + 1) * 512], s_wp[par])
                    if ic < 2:
                        tt = ic
                        self.g_wait("gpsimd", ("xb2", tt % 2))
                        self.dma("gpsimd", xb2[tt % 2][:, :],
                                 xs[(qc * 4 + tt) * 128:(qc * 4 + tt + 1) * 128,
                                    dc * 512:(dc + 1) * 512], s_x2[tt % 2])
                    self.wait("tensor", s_wp[par])
                    if ic == 0:
                        for tt in range(4):
                            self.pbank("tensor", self.rot[tt])
                    def opmm(t, ic=ic, par=par, qc=qc):
                        li = None
                        for tt in range(4):
                            li = t.matmul(self.rot[tt][:, :],
                                          att_fm[:, ic * TC + tt * 128:
                                                   ic * TC + (tt + 1) * 128],
                                          wpb[par][:, :],
                                          start=(ic == 0), stop=(ic == 7))
                        return li
                    self.op("tensor", opmm, s_op)
                    self.g_set(("wpb", par), s_op)
                opv = s_op.v
                for tt in range(4):
                    ti = qc * 4 + tt
                    self.wait("vector", s_op, opv)
                    self.wait("vector", s_x2[tt % 2])
                    self.g_wait("vector", ("xpst", tt % 2))
                    self.op("vector", lambda v, tt=tt:
                            v.scalar_tensor_tensor(
                                xpst[tt % 2][:, :], self.rot[tt][:, :], 1.0,
                                xb2[tt % 2][:, :], ALU.mult, ALU.add), s_ope)
                    self.g_set(("bank", self.bname[id(self.rot[tt])]), s_ope)
                    self.g_set(("xb2", tt % 2), s_ope)
                    if tt < 2:
                        nt = tt + 2
                        self.g_wait("gpsimd", ("xb2", nt % 2))
                        self.dma("gpsimd", xb2[nt % 2][:, :],
                                 xs[(qc * 4 + nt) * 128:(qc * 4 + nt + 1) * 128,
                                    dc * 512:(dc + 1) * 512], s_x2[nt % 2])
                    self.wait("scalar", s_ope)
                    self.dma("scalar",
                             xps[ti * 128:(ti + 1) * 128, dc * 512:(dc + 1) * 512],
                             xpst[tt % 2][:, :], s_xpw)
                    if self.dbg_xp:
                        self.wait("gpsimd", s_ope)
                        self.dma("gpsimd",
                                 self.d_xpdbg[si][ti * 128:(ti + 1) * 128,
                                                  dc * 512:(dc + 1) * 512],
                                 xpst[tt % 2][:, :], s_xpw)
                    self.g_set(("xpst", tt % 2), s_xpw)
            self.g_set(("att_fm",), s_op)
            self.g_set(("q_aug",), s_qk)
            if self.dbg_xp and si == 0 and qc == 0:
                sdbg = self.sem("s_dbg")
                self.wait("gpsimd", s_div)
                self.wait("gpsimd", s_odm)
                self.dma("gpsimd", self.d_dbg_qk, a.qk_k[:, :], sdbg)
                self.dma("gpsimd", self.d_dbg_qa, q_aug[:, :], sdbg)
                self.dma("gpsimd", self.d_dbg_af, att_fm[:, :], sdbg)
                self.dma("gpsimd", self.d_dbg_vt, a.v_t[:, :], sdbg)
                self.dma("gpsimd", self.d_dbg_sr, srec[:, :], sdbg)
                self.dma("gpsimd", self.d_dbg_nq, a.nq[:, :], sdbg)
                self.wait("gpsimd", sdbg)
        self.g_set(("nq",), s_qb)
        self.g_set(("nkv",), self.sem("s_kv"))
        self.g_set(("qk_k",), s_qk)
        self.g_set(("v_t",), s_pvk)
        self.g_set(("kmx",), s_crw)

    def emit_ffn(self, esF):
        a = self
        nc = self.nc
        SB = lambda n, sh, dt=F32: esF.enter_context(nc.sbuf_tensor(n, sh, dt))
        h_sb = SB("h_sb", [128, 64 * 512], BF16)
        xqT = [SB(f"xqT{i}", [128, 16 * 512], BF16) for i in range(2)]
        w1b = [SB(f"w1b{i}", [128, 4 * 512], BF16) for i in range(2)]
        w2b = [SB(f"w2b{i}", [128, 8 * 512], BF16) for i in range(2)]
        outst = [SB(f"outst{i}", [128, 512], F32) for i in range(2)]
        xw = [SB(f"xw{i}", [128, 4 * D], BF16) for i in range(2)]
        y2f = SB("y2f", [128, D], F32)
        fdump = y2f.bitcast(BF16)
        xq_t = [SB(f"xq_t{i}", [128, D], BF16) for i in range(2)]
        t32 = SB("t32", [128, 512], F32)
        ci32 = SB("ci32", [128, 512], I32)
        d32 = SB("d32", [128, 512], F32)
        gbc = SB("gbc", [128, 512], F32)
        hmxb = d32.bitcast(BF16)[:, 0:512]
        fst = SB("fst", [128, 16], F32)
        fcol = SB("fcol", [128, 8], F32)
        umrep = SB("umrep", [128, 128], BF16)
        umc = SB("umc", [128, 4], F32)
        rcol = SB("rcol", [128, 4], F32)

        s_xw = [self.sem(f"s_fxw{i}") for i in range(2)]
        s_sq = self.sem("s_fsq")
        s_ln = self.sem("s_fln")
        s_y2 = self.sem("s_fy2")
        s_xq = self.sem("s_fxq")
        s_tp = self.sem("s_ftp")
        s_tpe = self.sem("s_ftpe")
        s_w1 = [self.sem(f"s_fw1{i}") for i in range(2)]
        s_w2 = [self.sem(f"s_fw2{i}") for i in range(2)]
        s_m1 = self.sem("s_fm1")
        s_ev = self.sem("s_fev")
        s_mx = self.sem("s_fmx")
        s_row = self.sem("s_frow")
        s_rtp = self.sem("s_frtp")
        s_rte = self.sem("s_frte")
        s_gb = self.sem("s_fgb")
        s_hq = self.sem("s_fhq")
        s_m2 = self.sem("s_fm2")
        s_oev = self.sem("s_foev")
        s_out = self.sem("s_fout")

        def flush_to(mark):
            while getattr(self, "pp_popped", 0) < mark and self.pp_steps:
                self.pump(1)

        flush_to(getattr(self, "pp_w1q_end", 0))
        self.wait("sync", self.sem("s_cachew1"))
        self.wait("gpsimd", self.sem("s_xpw"))
        self.wait("gpsimd", self.sem("s_cachew1"))

        NW = 2 * NTC
        xqv = {}

        def xw_loads(w):
            wp = w % 2
            xps = self.d_xp[w // NTC]
            t0 = (w % NTC) * TC
            self.g_wait("gpsimd", ("xw", wp))
            for tt in range(4):
                self.dma("gpsimd", xw[wp][:, tt * D:(tt + 1) * D],
                         xps[(t0 + tt * 128):(t0 + (tt + 1) * 128), :], s_xw[wp])

        def ln_tile(w, tt):
            wp = w % 2
            pc = (tt % 2) * 8
            xwt = xw[wp]
            self.wait("scalar", s_xw[wp])
            self.g_wait("scalar", ("fst", tt % 2))
            self.g_wait("scalar", ("y2f",))
            def sq2(s, tt=tt, pc=pc, xwt=xwt):
                s.activation(fdump[:, 0:1024], xwt[:, tt * D:tt * D + 1024],
                             AF.Square, bias=0.0, scale=1.0,
                             accum_out=fst[:, pc:pc + 1])
                return s.activation(fdump[:, 0:1024],
                                    xwt[:, tt * D + 1024:(tt + 1) * D],
                                    AF.Square, bias=0.0, scale=1.0,
                                    accum_out=fst[:, pc + 2:pc + 3])
            self.op("scalar", sq2, s_sq)
            self.wait("vector", s_xw[wp])
            self.wait("vector", s_sq)
            self.op("vector", lambda v, tt=tt, pc=pc, xwt=xwt:
                    v.reduce_sum(fst[:, pc + 1:pc + 2], xwt[:, tt * D:(tt + 1) * D],
                                 axis=AX.X), self.sem("s_rr"))
            self.wait("vector", self.sem("s_rr"))
            def flnv_a(v, pc=pc):
                st = fst
                v.tensor_add(st[:, pc:pc + 1], st[:, pc:pc + 1],
                             st[:, pc + 2:pc + 3])
                return v.tensor_scalar_mul(st[:, pc + 1:pc + 2],
                                           st[:, pc + 1:pc + 2], 1.0 / D)
            def flnv_b(v, pc=pc):
                st = fst
                v.tensor_scalar(st[:, pc:pc + 1], st[:, pc:pc + 1],
                                1.0 / D, LN_EPS, ALU.mult, ALU.add)
                return v.tensor_tensor(st[:, pc + 2:pc + 3], st[:, pc + 1:pc + 2],
                                       st[:, pc + 1:pc + 2], ALU.mult)
            def flnv_c(v, pc=pc):
                st = fst
                return v.tensor_sub(st[:, pc:pc + 1], st[:, pc:pc + 1],
                                    st[:, pc + 2:pc + 3])
            self.chain("vector", [flnv_a, flnv_b, flnv_c], s_ln)
            self.wait("scalar", s_ln)
            self.op("scalar", lambda s, pc=pc:
                    s.activation(fst[:, pc + 3:pc + 4], fst[:, pc:pc + 1],
                                 AF.Sqrt, bias=0.0, scale=1.0), s_sq)
            self.wait("vector", s_sq)
            def fl2_a(v, pc=pc):
                return v.reciprocal(fst[:, pc + 4:pc + 5], fst[:, pc + 3:pc + 4])
            def fl2_b(v, pc=pc):
                return v.scalar_tensor_tensor(fst[:, pc + 6:pc + 7],
                                              fst[:, pc + 1:pc + 2], -1.0,
                                              fst[:, pc + 4:pc + 5],
                                              ALU.mult, ALU.mult)
            def fl2_c(v, tt=tt, pc=pc, xwt=xwt):
                return v.tensor_scalar(y2f[:, :], xwt[:, tt * D:(tt + 1) * D],
                                       fst[:, pc + 4:pc + 5],
                                       fst[:, pc + 6:pc + 7], ALU.mult, ALU.add)
            self.chain("vector", [fl2_a, fl2_b, fl2_c], s_y2)
            self.g_set(("fst", tt % 2), s_y2)
            self.op("vector", lambda v, pc=pc:
                    v.reduce_max(fst[:, pc + 7:pc + 8], y2f[:, :], axis=AX.X,
                                 apply_absolute_value=True), self.sem("s_rr"))
            self.wait("vector", self.sem("s_rr"))
            def qa_a(v, pc=pc):
                return v.tensor_scalar_max(fst[:, pc + 7:pc + 8],
                                           fst[:, pc + 7:pc + 8], Q_EPS)
            def qa_b(v, tt=tt, pc=pc):
                v.tensor_scalar_mul(fcol[:, tt + 4:tt + 5],
                                    fst[:, pc + 7:pc + 8], 1.0 / 127.0)
                return v.reciprocal(fst[:, pc + 7:pc + 8], fst[:, pc + 7:pc + 8])
            def qa_c(v, tt=tt, pc=pc):
                v.scalar_tensor_tensor(fcol[:, tt:tt + 1], fcol[:, tt + 4:tt + 5],
                                       1.0, fcol[:, tt + 4:tt + 5],
                                       ALU.mult, ALU.mult)
                return v.tensor_scalar_mul(fst[:, pc + 7:pc + 8],
                                           fst[:, pc + 7:pc + 8], 127.0)
            def qa_d(v, tt=tt, pc=pc):
                v.tensor_tensor(fcol[:, tt:tt + 1], fcol[:, tt:tt + 1],
                                self.gam[:, 4:5], ALU.mult)
                return v.tensor_single_scalar(y2f[:, :], y2f[:, :],
                                              fst[:, pc + 7:pc + 8], ALU.mult)
            self.chain("vector", [qa_a, qa_b, qa_c, qa_d], s_y2)
            self.g_wait("vector", ("xq_t", tt % 2))
            xqt = xq_t[tt % 2]
            for c in range(4):
                def rnd(v, c=c, xqt=xqt):
                    sl = slice(c * 512, (c + 1) * 512)
                    v.tensor_copy(ci32[:, :], y2f[:, sl])
                    v.scalar_tensor_tensor(d32[:, :], ci32[:, :], -1.0,
                                           y2f[:, sl], ALU.mult, ALU.add)
                    v.scalar_tensor_tensor(t32[:, :], d32[:, :], 0.5,
                                           ci32[:, :], ALU.is_ge, ALU.add)
                    v.tensor_scalar(d32[:, :], d32[:, :], -0.5, -1.0,
                                    ALU.is_le, ALU.mult)
                    return v.tensor_add(xqt[:, sl], t32[:, :], d32[:, :])
                self.op("vector", rnd, s_xq)
            self.g_set(("y2f",), s_xq)
            xqv[(w, tt)] = s_xq.v

        def tp_tile(w, tt):
            wp = w % 2
            self.wait("tensor", s_xq, xqv[(w, tt)])
            if tt == 0:
                self.g_wait("tensor", ("xqT", wp))
            xqt = xq_t[tt % 2]
            for fb in range(16):
                ptp = self.pT[fb % 2]
                self.pbank("tensor", ptp)
                self.op("tensor", lambda t, fb=fb, ptp=ptp, xqt=xqt:
                        t.transpose(ptp[:, 0:128],
                                    xqt[:, fb * 128:(fb + 1) * 128],
                                    self.c_ident[:, :]), s_tp)
                self.wait("vector", s_tp)
                self.op("vector", lambda v, fb=fb, tt=tt, ptp=ptp, wp=wp:
                        v.tensor_copy(xqT[wp][:, fb * 512 + tt * 128:
                                              fb * 512 + (tt + 1) * 128],
                                      ptp[:, 0:128]), s_tpe)
                self.g_set(("bank", self.bname[id(ptp)]), s_tpe)
            self.g_set(("xq_t", tt % 2), s_tp)

        xw_loads(0)
        for tt in range(4):
            ln_tile(0, tt)
            tp_tile(0, tt)

        for w in range(NW):
            si = w // NTC
            t0 = (w % NTC) * TC
            wp = w % 2
            if w + 1 < NW:
                xw_loads(w + 1)

            self.wait("tensor", s_tpe)
            self.g_wait("tensor", ("h_sb",))
            for og in range(16):
                self.pump(6)
                for ih in range(4):
                    par = (og * 4 + ih) % 2
                    self.g_wait("sync", ("w1b", par))
                    self.dma("sync", w1b[par][:, :],
                             self.d_w1q[og, ih * 4:(ih + 1) * 4]
                             .rearrange("i p f -> p i f"), s_w1[par])
                    self.wait("tensor", s_w1[par])
                    for ocin in range(4):
                        oc = og * 4 + ocin
                        pp = self.rot[oc % 4]
                        if ih == 0:
                            self.pbank("tensor", pp)
                        def m1(t, par=par, ih=ih, ocin=ocin, pp=pp, wp=wp):
                            for ii in range(4):
                                i = ih * 4 + ii
                                li = t.matmul(pp[:, :],
                                              w1b[par][:, ii * 512 + ocin * 128:
                                                       ii * 512 + (ocin + 1) * 128],
                                              xqT[wp][:, i * 512:(i + 1) * 512],
                                              start=(i == 0), stop=(i == 15))
                            return li
                        self.op("tensor", m1, s_m1)
                        if ih == 3:
                            self.wait("scalar", s_m1)
                            def ev(s, oc=oc, pp=pp):
                                s.activation(h_sb[:, oc * 512:(oc + 1) * 512], pp[:, :],
                                             AF.Relu, bias=0.0, scale=1.0 / 128.0)
                                return s.activation(h_sb[:, oc * 512:(oc + 1) * 512],
                                                    h_sb[:, oc * 512:(oc + 1) * 512],
                                                    AF.Square, bias=0.0, scale=1.0)
                            self.op("scalar", ev, s_ev)
                            self.g_set(("bank", self.bname[id(pp)]), s_ev)
                            self.wait("vector", s_ev)
                            if oc == 0:
                                self.op("vector", lambda v:
                                        v.tensor_copy(hmxb[:, :], h_sb[:, 0:512]),
                                        s_mx)
                            else:
                                self.op("vector", lambda v, oc=oc:
                                        v.tensor_max(hmxb[:, :], hmxb[:, :],
                                                     h_sb[:, oc * 512:(oc + 1) * 512]),
                                        s_mx)
                    self.g_set(("w1b", par), s_m1)

            for c in range(4):
                ptp = self.pT[c % 2]
                self.wait("tensor", s_mx)
                self.pbank("tensor", ptp)
                self.op("tensor", lambda t, c=c, ptp=ptp:
                        t.transpose(ptp[:, 0:128], hmxb[:, c * 128:(c + 1) * 128],
                                    self.c_ident[:, :]), s_rtp)
                self.wait("vector", s_rtp)
                self.op("vector", lambda v, c=c, ptp=ptp:
                        v.reduce_max(umc[:, c:c + 1], ptp[:, 0:128], axis=AX.X),
                        self.sem("s_rr"))
                self.wait("scalar", self.sem("s_rr"))
                self.op("scalar", lambda s, c=c:
                        s.activation(umrep[:, :], self.c_ident[:, :], AF.Relu,
                                     bias=umc[:, c:c + 1], scale=0.0),
                        self.sem("s_rr"))
                self.wait("tensor", self.sem("s_rr"))
                self.g_set(("bank", self.bname[id(ptp)]), self.sem("s_rr"))
                self.pbank("tensor", ptp)
                self.op("tensor", lambda t, ptp=ptp:
                        t.transpose(ptp[:, 0:128], umrep[:, :], self.c_ident[:, :]),
                        s_rtp)
                self.wait("vector", s_rtp)
                self.op("vector", lambda v, c=c, ptp=ptp:
                        v.tensor_copy(gbc[:, c * 128:(c + 1) * 128], ptp[:, 0:128]),
                        s_rte)
                self.g_set(("bank", self.bname[id(ptp)]), s_rte)
            def gfin(v):
                v.tensor_scalar_max(gbc[:, :], gbc[:, :], 1e-30)
                v.reciprocal(gbc[:, :], gbc[:, :])
                return v.tensor_scalar_mul(gbc[:, :], gbc[:, :], 127.0)
            self.op("vector", gfin, s_gb)
            def rc_a(v):
                r = None
                for c in range(4):
                    r = v.tensor_tensor(rcol[:, c:c + 1], fcol[:, c:c + 1],
                                        umc[:, c:c + 1], ALU.mult)
                return r
            def rc_b(v):
                return v.tensor_scalar_max(rcol[:, :], rcol[:, :], Q_EPS)
            def rc_c(v):
                r = None
                for c in range(4):
                    r = v.tensor_tensor(rcol[:, c:c + 1], rcol[:, c:c + 1],
                                        self.gam[:, 5:6], ALU.mult)
                return r
            self.chain("vector", [rc_a, rc_b, rc_c], s_rte)

            hqb = s_hq.v
            for oc in range(64):
                sl = slice(oc * 512, (oc + 1) * 512)
                def hqc(v, sl=sl):
                    v.tensor_tensor(t32[:, :], h_sb[:, sl], gbc[:, :], ALU.mult)
                    v.tensor_copy(ci32[:, :], t32[:, :])
                    v.scalar_tensor_tensor(d32[:, :], ci32[:, :], -1.0,
                                           t32[:, :], ALU.mult, ALU.add)
                    return v.scalar_tensor_tensor(h_sb[:, sl], d32[:, :], 0.5,
                                                  ci32[:, :], ALU.is_ge, ALU.add)
                self.op("vector", hqc, s_hq)

            if w + 1 < NW:
                ln_tile(w + 1, 0)
                ln_tile(w + 1, 1)

            for dq in range(4):
                self.pump(4)
                for ob in range(8):
                    par = ob % 2
                    if w == 0 and ob == 0:
                        flush_to(self.pp_w2g_end[dq])
                        self.wait("sync", self.sem("s_w2c"), dq + 1)
                    self.g_wait("sync", ("w2b", par))
                    self.dma("sync", w2b[par][:, :],
                             self.d_w2q[dq, ob * 8:(ob + 1) * 8]
                             .rearrange("o p f -> p o f"), s_w2[par])
                    self.wait("tensor", s_w2[par])
                    for oi in range(8):
                        o = ob * 8 + oi
                        if dq == 0:
                            self.wait("tensor", s_hq, hqb + o + 1)
                        if o == 0:
                            for tt in range(4):
                                self.pbank("tensor", self.rot[tt])
                        def m2(t, dq=dq, o=o, oi=oi, par=par):
                            li = None
                            for tt in range(4):
                                li = t.matmul(self.rot[tt][:, :],
                                              h_sb[:, o * 512 + tt * 128:
                                                   o * 512 + (tt + 1) * 128],
                                              w2b[par][:, oi * 512:(oi + 1) * 512],
                                              start=(o == 0), stop=(o == 63))
                            return li
                        self.op("tensor", m2, s_m2)
                    self.g_set(("w2b", par), s_m2)
                self.wait("vector", s_m2)
                self.wait("vector", s_rte)
                for tt in range(4):
                    pst = (dq * 4 + tt) % 2
                    self.g_wait("vector", ("outst", pst))
                    def oev(v, dq=dq, tt=tt, pst=pst, wp=wp):
                        return v.scalar_tensor_tensor(
                            outst[pst][:, :], self.rot[tt][:, :],
                            rcol[:, tt:tt + 1],
                            xw[wp][:, tt * D + dq * 512:tt * D + (dq + 1) * 512],
                            ALU.mult, ALU.add)
                    self.op("vector", oev, s_oev)
                    self.g_set(("bank", self.bname[id(self.rot[tt])]), s_oev)
                    self.wait("gpsimd", s_oev)
                    self.dma("gpsimd",
                             self.d_out[si][(t0 + tt * 128):(t0 + (tt + 1) * 128),
                                            dq * 512:(dq + 1) * 512],
                             outst[pst][:, :], s_out)
                    self.g_set(("outst", pst), s_out)
                if w + 1 < NW:
                    tp_tile(w + 1, dq)
                    if dq == 0:
                        ln_tile(w + 1, 2)
                    elif dq == 1:
                        ln_tile(w + 1, 3)
            self.g_set(("xw", wp), s_oev)
            self.g_set(("h_sb",), s_m2)
            self.g_set(("xqT", wp), s_m1)

        self.flush_pp()
        self.wait("gpsimd", s_out)
        for (eng, _), sem in getattr(self, "dma_sems", {}).items():
            self.wait(eng, sem)

    def finalize(self):
        blk = self.es.enter_context(self.nc.Block())
        for eng in ENGS:
            def mk(eng):
                def run(e):
                    for fn in self.streams[eng]:
                        fn(e)
                return run
            getattr(blk, eng)(mk(eng))


def build_program(dbg_xp=False):
    nc = bass.Bass("TRN2", target_bir_lowering=False,
                   detect_race_conditions=False)
    with ExitStack() as es:
        b = Builder(nc, es, dbg_xp=dbg_xp)
        b.declare_io()
        b.alloc_persistent()
        with ExitStack() as esA:
            b.emit_prepass_streams(esA)
            b.alloc_attn_outer(esA)
            for si in range(NSEQ):
                with ExitStack() as es12:
                    b.alloc_nkv(es12, si)
                    with ExitStack() as es1:
                        b.alloc_p1(es1, si)
                        b.emit_p1(si)
                    b.barrier()
                    b.alloc_p2(es12, si)
                    b.emit_p2(si)
                b.barrier()
                with ExitStack() as es3:
                    b.alloc_p3(es3, si)
                    b.emit_p3(si)
                b.barrier()
            b.barrier()
        with ExitStack() as esF:
            b.emit_ffn(esF)
        b.finalize()
    return nc


def _consts():
    c = {}
    c["ident"] = np.eye(128, dtype=ml_dtypes.bfloat16)
    c["cones"] = np.ones((96, 33), dtype=ml_dtypes.bfloat16)
    c["ones64"] = np.ones((1, 64), dtype=ml_dtypes.bfloat16)
    c["onesbc"] = np.ones((1, 128), dtype=np.float32)
    c["onescol"] = np.ones((128, 1), dtype=np.float32)
    c["onescolb"] = np.ones((128, 1), dtype=ml_dtypes.bfloat16)
    freqs = 1.0 / (10000.0 ** (np.arange(0, DR, 2, dtype=np.float64) / DR))
    ang = np.outer(np.arange(S, dtype=np.float64), freqs)
    cos = np.cos(ang).T
    crep = np.concatenate([cos, cos], axis=0)
    c["crq"] = (crep * SC).astype(ml_dtypes.bfloat16)
    c["crk"] = (crep / H).astype(ml_dtypes.bfloat16)
    c["sig"] = np.concatenate([-np.ones((16, 1)), np.ones((16, 1))]).astype(np.float32)
    m = np.zeros((128, 4 * 512), dtype=np.float32)
    kk = np.arange(128)[:, None]
    qq = np.arange(512)[None, :]
    for r in range(4):
        m[:, r * 512:(r + 1) * 512] = np.where(qq >= kk + r * 128, 0.0, NEG)
    c["mask"] = m.astype(ml_dtypes.bfloat16)
    return c


def _perm_rope_cols(w, nheads):
    rows = w.shape[0]
    w = w.reshape(rows, nheads, 16, 2)
    return np.ascontiguousarray(
        np.concatenate([w[:, :, :, 0], w[:, :, :, 1]], axis=2).reshape(rows, -1))


def prepare_base(L0_attn_x, L0_attn_w_cq, L0_attn_w_q_nope, L0_attn_w_q_rope,
                 L0_attn_q_g, L0_attn_w_ckv, L0_attn_w_k_nope, L0_attn_w_v,
                 L0_attn_kv_g, L0_attn_w_k_rope, L0_attn_w_proj, L0_ffn_w1,
                 L0_ffn_w2):
    x = np.ascontiguousarray(np.asarray(L0_attn_x, dtype=np.float32))
    f32 = lambda z: np.ascontiguousarray(np.asarray(z, dtype=np.float32))
    wqr = _perm_rope_cols(f32(L0_attn_w_q_rope), H)
    wkr = _perm_rope_cols(f32(L0_attn_w_k_rope), 1)
    wpr = np.ascontiguousarray(
        f32(L0_attn_w_proj).reshape(8, 128, D).transpose(1, 0, 2).reshape(128, 8 * D))
    w1T = np.ascontiguousarray(f32(L0_ffn_w1).T)
    w2T = np.ascontiguousarray(f32(L0_ffn_w2).T)
    base = dict(
        w_cq=f32(L0_attn_w_cq), w_qn=f32(L0_attn_w_q_nope), w_qr=wqr,
        q_g=np.ascontiguousarray(f32(L0_attn_q_g).reshape(6, 128).T),
        w_ckv=f32(L0_attn_w_ckv),
        w_kn=f32(L0_attn_w_k_nope), w_v=f32(L0_attn_w_v),
        kv_g=np.ascontiguousarray(f32(L0_attn_kv_g).reshape(2, 128).T),
        w_kr=wkr, w_pr=wpr,
        w1T=w1T, w2T=w2T, **_consts())
    return base, x


def kernel(**inputs):
    base, x = prepare_base(**inputs)
    nc = build_program()
    in_maps = []
    for c in range(8):
        m = dict(base)
        m["x_in"] = x[2 * c:2 * c + 2]
        in_maps.append(m)
    res = run_bass_kernel_spmd(nc, in_maps, core_ids=list(range(8)))
    out = np.empty((16, S, D), dtype=np.float32)
    for c in range(8):
        out[2 * c:2 * c + 2] = res.results[c]["out"]
    return out


# version 18
# speedup vs baseline: 1.1525x; 1.0376x over previous
import math
import numpy as np
import ml_dtypes
from contextlib import ExitStack

import concourse.bass as bass
import concourse.mybir as mybir
from concourse.bass_utils import run_bass_kernel_spmd

F32 = mybir.dt.float32
BF16 = mybir.dt.bfloat16
I32 = mybir.dt.int32
AF = mybir.ActivationFunctionType
ALU = mybir.AluOpType
AX = mybir.AxisListType

S, D, H = 2048, 2048, 16
DN, DR, DV = 64, 32, 64
RQ, RKV = 768, 256
DFF = 8192
NSEQ = 2
SC = 1.0 / math.sqrt(DN + DR)
NEG = -1.0e30
LN_EPS, RMS_EPS, Q_EPS = 1e-5, 1e-6, 1e-5
TC = 512
NTC = S // TC
NKT = S // 128
ENGS = ("tensor", "vector", "scalar", "gpsimd", "sync")
SHUF = list(range(16, 32)) + list(range(16))


class Sem:
    def __init__(self, nc, es, name):
        self.h = es.enter_context(nc.semaphore(name))
        self.v = 0
        self.name = name
        self.engs = set()

    def take(self, n=1):
        self.v += n
        return (self.h, n)


class Builder:
    def __init__(self, nc, es, dbg_xp=False):
        self.nc = nc
        self.es = es
        self.dbg_xp = dbg_xp
        self.streams = {e: [] for e in ENGS}
        self.sems = {}
        self.gates = {}
        self.pp_steps = []

    def sem(self, name):
        if name not in self.sems:
            self.sems[name] = Sem(self.nc, self.es, name)
        return self.sems[name]

    def _emit(self, eng, fn):
        self.streams[eng].append(fn)

    def dma(self, eng, dst, src, sem):
        sem.engs.add(eng)
        tgt = sem.take(16)
        self.dma_sems = getattr(self, "dma_sems", {})
        self.dma_sems[(eng, id(sem))] = sem
        self._emit(eng, lambda e, dst=dst, src=src, tgt=tgt:
                   e.dma_start(dst, src).then_inc(*tgt))

    def wait(self, eng, sem, val=None):
        v = sem.v if val is None else val
        if v > 0:
            self._emit(eng, lambda e, h=sem.h, v=v: e.wait_ge(h, v))

    def op(self, eng, fn, sem=None, n=1):
        if sem is None:
            self._emit(eng, fn)
        else:
            sem.engs.add(eng)
            tgt = sem.take(n)
            self._emit(eng, lambda e, fn=fn, tgt=tgt: fn(e).then_inc(*tgt))

    def chain(self, eng, steps, final_sem, n=1):
        s = self.sem(f"s_ch_{eng}")
        for i, fn in enumerate(steps):
            last = (i == len(steps) - 1)
            self.op(eng, fn, final_sem if last else s, n=n if last else 1)
            if not last:
                self.wait(eng, s)

    def g_wait(self, eng, key):
        if key in self.gates:
            s, v = self.gates[key]
            self.wait(eng, s, v)

    def g_set(self, key, sem):
        self.gates[key] = (sem, sem.v)

    def pump(self, k=2):
        for _ in range(min(k, len(self.pp_steps))):
            lim = getattr(self, "pp_limit", None)
            if lim is not None and getattr(self, "pp_popped", 0) >= lim:
                return
            eng, fn = self.pp_steps.pop(0)
            self.pp_popped = getattr(self, "pp_popped", 0) + 1
            self._emit(eng, fn)

    def flush_pp(self):
        while self.pp_steps:
            eng, fn = self.pp_steps.pop(0)
            self._emit(eng, fn)

    def dop(self, eng, fn, sem=None, n=1):
        if sem is None:
            self.pp_steps.append((eng, fn))
        else:
            sem.engs.add(eng)
            tgt = sem.take(n)
            self.pp_steps.append((eng, lambda e, fn=fn, tgt=tgt:
                                  fn(e).then_inc(*tgt)))

    def ddma(self, eng, waits, dst, src, sem):
        cw = [(s.h, (s.v if v is None else v)) for (s, v) in waits]
        cw = [(h, v) for (h, v) in cw if v > 0]
        sem.engs.add(eng)
        tgt = sem.take(16)
        self.dma_sems = getattr(self, "dma_sems", {})
        self.dma_sems[(eng, id(sem))] = sem
        self.pp_steps.append((eng, lambda e, cw=cw, dst=dst, src=src, tgt=tgt:
                              ([e.wait_ge(h, v) for h, v in cw],
                               e.dma_start(dst, src).then_inc(*tgt))[-1]))

    def dstep(self, eng, waits, fn, sem=None, n=1):
        cw = [(s.h, (s.v if v is None else v)) for (s, v) in waits]
        cw = [(h, v) for (h, v) in cw if v > 0]
        if sem is None:
            self.pp_steps.append((eng, lambda e, cw=cw, fn=fn:
                                  ([e.wait_ge(h, v) for h, v in cw],
                                   fn(e))[-1]))
        else:
            sem.engs.add(eng)
            tgt = sem.take(n)
            self.pp_steps.append((eng, lambda e, cw=cw, fn=fn, tgt=tgt:
                                  ([e.wait_ge(h, v) for h, v in cw],
                                   fn(e).then_inc(*tgt))[-1]))

    def dwait(self, eng, sem, val=None):
        v = sem.v if val is None else val
        if v > 0:
            self.pp_steps.append((eng, lambda e, h=sem.h, v=v: e.wait_ge(h, v)))

    def declare_io(self):
        nc = self.nc
        EI, EO, IN = "ExternalInput", "ExternalOutput", "Internal"
        self.d_x = nc.dram_tensor("x_in", [NSEQ, S, D], F32, kind=EI).ap()
        self.d_wcq = nc.dram_tensor("w_cq", [D, RQ], F32, kind=EI).ap()
        self.d_wqn = nc.dram_tensor("w_qn", [RQ, H * DN], F32, kind=EI).ap()
        self.d_wqr = nc.dram_tensor("w_qr", [RQ, H * DR], F32, kind=EI).ap()
        self.d_qg = nc.dram_tensor("q_g", [128, 6], F32, kind=EI).ap()
        self.d_wckv = nc.dram_tensor("w_ckv", [D, RKV], F32, kind=EI).ap()
        self.d_wkn = nc.dram_tensor("w_kn", [RKV, H * DN], F32, kind=EI).ap()
        self.d_wv = nc.dram_tensor("w_v", [RKV, H * DV], F32, kind=EI).ap()
        self.d_kvg = nc.dram_tensor("kv_g", [128, 2], F32, kind=EI).ap()
        self.d_wkr = nc.dram_tensor("w_kr", [D, DR], F32, kind=EI).ap()
        self.d_wpr = nc.dram_tensor("w_pr", [128, 8 * D], F32, kind=EI).ap()
        self.d_w1T = nc.dram_tensor("w1T", [D, DFF], F32, kind=EI).ap()
        self.d_w2T = nc.dram_tensor("w2T", [DFF, D], F32, kind=EI).ap()
        self.d_ident = nc.dram_tensor("ident", [128, 128], BF16, kind=EI).ap()
        self.d_cones = nc.dram_tensor("cones", [96, 33], BF16, kind=EI).ap()
        self.d_ones64 = nc.dram_tensor("ones64", [1, 64], BF16, kind=EI).ap()
        self.d_onesbc = nc.dram_tensor("onesbc", [1, 128], F32, kind=EI).ap()
        self.d_onescol = nc.dram_tensor("onescol", [128, 1], F32, kind=EI).ap()
        self.d_onescolb = nc.dram_tensor("onescolb", [128, 1], BF16, kind=EI).ap()
        self.d_crq = nc.dram_tensor("crq", [32, S], BF16, kind=EI).ap()
        self.d_crk = nc.dram_tensor("crk", [32, S], BF16, kind=EI).ap()
        self.d_sig = nc.dram_tensor("sig", [32, 1], F32, kind=EI).ap()
        self.d_mask = nc.dram_tensor("mask", [128, 4 * 512], BF16, kind=EI).ap()
        self.d_out = nc.dram_tensor("out", [NSEQ, S, D], F32, kind=EO).ap()
        self.d_xp = nc.dram_tensor("xp_scr", [NSEQ, S, D], BF16, kind=IN).ap()
        self.d_w1q = nc.dram_tensor("w1q_c", [16, 16, 128, 512], BF16, kind=IN).ap()
        self.d_w2q = nc.dram_tensor("w2q_c", [4, 64, 128, 512], BF16, kind=IN).ap()
        if self.dbg_xp:
            self.d_xpdbg = nc.dram_tensor("xp_dbg", [NSEQ, S, D], F32, kind=EO).ap()
            self.d_dbg_qk = nc.dram_tensor("dbg_qk", [97, H * S], F32, kind=EO).ap()
            self.d_dbg_qa = nc.dram_tensor("dbg_qa", [97, H * TC], F32, kind=EO).ap()
            self.d_dbg_af = nc.dram_tensor("dbg_af", [128, 8 * TC], F32, kind=EO).ap()
            self.d_dbg_vt = nc.dram_tensor("dbg_vt", [128, NKT * H * 65], F32, kind=EO).ap()
            self.d_dbg_sr = nc.dram_tensor("dbg_sr", [64, 512], F32, kind=EO).ap()
            self.d_dbg_nq = nc.dram_tensor("dbg_nq", [128, 6 * S], F32, kind=EO).ap()
            self.d_dbg_y1 = nc.dram_tensor("dbg_y1", [128, 16 * TC], F32, kind=EO).ap()
            self.d_dbg_st = nc.dram_tensor("dbg_st", [128, 16], F32, kind=EO).ap()
            self.d_dbg_cq = nc.dram_tensor("dbg_cq", [128, 6 * TC], F32, kind=EO).ap()

    def alloc_persistent(self):
        nc, es = self.nc, self.es
        SB = lambda n, sh, dt=F32: es.enter_context(nc.sbuf_tensor(n, sh, dt))
        PS = lambda n, sh, dt=F32: es.enter_context(nc.psum_tensor(n, sh, dt))
        self.pA, self.pB = PS("pA", [128, 512]), PS("pB", [128, 512])
        self.pC, self.pD = PS("pC", [128, 512]), PS("pD", [128, 512])
        self.pE, self.pF = PS("pE", [128, 512]), PS("pF", [128, 512])
        self.pT = [PS("pT0", [128, 1024], BF16), PS("pT1", [128, 1024], BF16)]
        self.rot = [self.pA, self.pB, self.pC, self.pD]
        self.bname = {id(self.pA): "A", id(self.pB): "B", id(self.pC): "C",
                      id(self.pD): "D", id(self.pE): "E", id(self.pF): "F",
                      id(self.pT[0]): "T0", id(self.pT[1]): "T1"}

        self.c_ident = SB("c_ident", [128, 128], BF16)
        self.c_cones = SB("c_cones", [96, 33], BF16)
        self.c_ones64 = SB("c_ones64", [65, 64], BF16)
        self.c_onesbc = SB("c_onesbc", [1, 128], F32)
        self.c_onescol = SB("c_onescol", [128, 1], F32)
        self.c_onescolb = SB("c_onescolb", [128, 1], BF16)
        self.c_crq = SB("c_crq", [96, S], BF16)
        self.c_crk = SB("c_crk", [96, S], BF16)
        self.c_sig = SB("c_sig", [96, 1], F32)
        self.c_mask = SB("c_mask", [128, 4 * 512], BF16)
        self.gam = SB("gam", [128, 8], F32)
        self.rowb = SB("rowb", [128, 16], F32)
        self.stats = SB("stats", [128, 16], F32)

        s = self.sem("s_const")
        for dst, src in [(self.c_ident[:, :], self.d_ident),
                         (self.c_cones[:, :], self.d_cones),
                         (self.c_ones64[64:65, :], self.d_ones64),
                         (self.c_onesbc[:, :], self.d_onesbc),
                         (self.c_onescol[:, :], self.d_onescol),
                         (self.c_onescolb[:, :], self.d_onescolb),
                         (self.c_crq[64:96, :], self.d_crq),
                         (self.c_crk[64:96, :], self.d_crk),
                         (self.c_sig[64:96, :], self.d_sig),
                         (self.c_mask[:, :], self.d_mask)]:
            self.dma("gpsimd", dst, src, s)
        for eng in ("tensor", "vector", "scalar"):
            self.wait(eng, s)

    def barrier(self):
        sb = self.sem("s_bar")
        self._barn = getattr(self, "_barn", 0) + 1
        self.op("tensor", lambda t: t.nop(), sb)
        self.op("vector", lambda v: v.engine_nop(), sb)
        self.op("scalar", lambda x: x.nop(), sb)
        self.op("gpsimd", lambda g: g.engine_nop(), sb)
        tgt = 4 * self._barn
        assert sb.v == tgt
        for eng in ("tensor", "vector", "scalar", "gpsimd"):
            self.wait(eng, sb, tgt)

    def pbank(self, eng, pp):
        self.g_wait(eng, ("bank", self.bname[id(pp)]))

    def pbank_d(self, eng, pp):
        key = ("bank", self.bname[id(pp)])
        if key in self.gates:
            s, v = self.gates[key]
            self.dwait(eng, s, v)

    def freeze_dump(self, pairs):
        if not self.dbg_xp:
            return
        self.barrier()
        sd = self.sem("s_frz")
        for dst, src in pairs:
            self.dma("gpsimd", dst, src, sd)
        for eng in ("tensor", "vector", "scalar", "gpsimd"):
            self.wait(eng, sd)

    def emit_prepass_streams(self, es):
        nc = self.nc
        SB = lambda n, sh, dt=F32: self.es.enter_context(nc.sbuf_tensor(n, sh, dt))
        self.pwl = [SB(f"pwl{i}", [128, 512], F32) for i in range(4)]
        self.ptau = [SB(f"ptau{i}", [128, 512], BF16) for i in range(2)]
        self.pacc = SB("pacc", [128, 8], F32)
        self.prow = SB("prow", [128, 2], F32)
        self.gamrep = [SB(f"sgnm{i}", [128, 512], BF16) for i in range(2)]
        self.gamneg = [SB(f"sgnp{i}", [128, 512], BF16) for i in range(2)]
        pwl, ptau, pacc, prow = (self.pwl, self.ptau, self.pacc, self.prow)
        pdump = ptau[0]

        s_gld = [self.sem(f"s_gld{i}") for i in range(4)]
        s_gab = [self.sem(f"s_gab{i}") for i in range(4)]
        s_gfr = [self.sem("s_gfr0"), self.sem("s_gfr1")]
        gj = ([(self.d_w1T[r * 128:(r + 1) * 128, c * 512:(c + 1) * 512], 0)
               for r in range(16) for c in range(16)]
              + [(self.d_w2T[r * 128:(r + 1) * 128, c * 512:(c + 1) * 512], 1)
                 for r in range(64) for c in range(4)])
        for idx, (src, col) in enumerate(gj):
            bb = idx % 4
            par = idx % 2
            slot = col * 4 + idx % 4
            if idx >= 4:
                self.wait("sync", s_gab[bb], idx // 4)
            self.dma("sync", pwl[bb][:, :], src, s_gld[bb])
            sw = [(s_gld[bb], None)]
            if idx >= 2:
                sw.append((s_gfr[par], None))
            self.dstep("scalar", sw, lambda e, bb=bb, par=par:
                       e.activation(pdump[:, :], pwl[bb][:, :], AF.Abs,
                                    bias=0.0, scale=1.0,
                                    accum_out=prow[:, par:par + 1]), s_gab[bb])
            def vadd(g, par=par, slot=slot, first=(idx < 4 or 256 <= idx < 260)):
                if first:
                    return g.tensor_copy(pacc[:, slot:slot + 1],
                                         prow[:, par:par + 1])
                return g.tensor_add(pacc[:, slot:slot + 1], pacc[:, slot:slot + 1],
                                    prow[:, par:par + 1])
            self.dstep("vector", [(s_gab[bb], None)], vadd, s_gfr[par])
        self.dop("vector", lambda g: g.engine_nop(), self.sem("s_gred"))
        self.pp_gamma_end = len(self.pp_steps)
        self.pp_limit = self.pp_gamma_end

        s_qd = [self.sem("s_qd0"), self.sem("s_qd1")]
        s_qf = [self.sem("s_qf0"), self.sem("s_qf1")]
        qj = []
        for r in range(16):
            for c in range(16):
                qj.append((self.d_w1T[r * 128:(r + 1) * 128, c * 512:(c + 1) * 512],
                           self.d_w1q[c, r], 0))
        for c in range(4):
            for r in range(64):
                qj.append((self.d_w2T[r * 128:(r + 1) * 128, c * 512:(c + 1) * 512],
                           self.d_w2q[c, r], 1))
        for b in range(4):
            self.wait("sync", s_gab[b])
        s_qsg = [self.sem(f"s_qsg{i}") for i in range(4)]
        self.pp_steps.append(("scalar", lambda e: e.wait_ge(self.sem("s_gamma").h, 1)))
        for idx, (src, dst, w) in enumerate(qj):
            bb = idx % 4
            par = idx % 2
            defer = idx >= 256
            if not defer:
                if idx >= 4:
                    self.wait("sync", s_qsg[bb], idx // 4)
                self.dma("sync", pwl[bb][:, :], src, s_qd[par])
            else:
                lw = [(s_qsg[bb], idx // 4)] if idx >= 4 else []
                self.ddma("sync", lw, pwl[bb][:, :], src, s_qd[par])
            bpos, bneg = (self.gam[:, 0:1], self.gam[:, 1:2]) if w == 0 else \
                         (self.gam[:, 2:3], self.gam[:, 3:4])
            sw = [(s_qd[par], None)]
            if idx >= 2:
                sw.append((s_qf[par], None))
            def sgn2(e, bb=bb, par=par, bpos=bpos, bneg=bneg):
                e.activation(self.gamrep[par][:, :], pwl[bb][:, :], AF.Sign,
                             bias=bneg, scale=1.0)
                return e.activation(self.gamneg[par][:, :], pwl[bb][:, :],
                                    AF.Sign, bias=bpos, scale=1.0)
            self.dstep("scalar", sw, sgn2, s_qsg[bb])
            self.dstep("vector", [(s_qsg[bb], None)], lambda v, par=par:
                       v.tensor_add(ptau[par][:, :], self.gamrep[par][:, :],
                                    self.gamneg[par][:, :]), s_qf[par])
            if not defer:
                self.wait("sync", s_qf[par])
                self.dma("sync", dst, ptau[par][:, :], s_qd[par])
            else:
                self.ddma("sync", [(s_qf[par], None)], dst, ptau[par][:, :],
                          s_qd[par])
            if idx == 255:
                self.pp_w1q_end = len(self.pp_steps)
                self.wait("sync", s_qd[0])
                self.wait("sync", s_qd[1])
                self.op("sync", lambda sy: sy.nop(), self.sem("s_cachew1"))
            elif idx >= 256 and (idx - 255) % 64 == 0:
                self.dstep("sync", [(s_qd[0], None), (s_qd[1], None)],
                           lambda sy: sy.nop(), self.sem("s_w2c"))
                self.pp_w2g_end = getattr(self, "pp_w2g_end", {})
                self.pp_w2g_end[(idx - 319) // 64] = len(self.pp_steps)

    def flush_gamma(self):
        while getattr(self, "pp_popped", 0) < getattr(self, "pp_gamma_end", 0) \
                and self.pp_steps:
            self.pump(1)

    def emit_gamma_finalize(self):
        self.flush_gamma()
        self.pp_limit = None
        """Small tensor/vector chain turning pacc into the gam columns.
        Emitted at a controlled point between attention phases."""
        pE, rowb, gam = self.pE, self.rowb, self.gam
        self.wait("tensor", self.sem("s_gred"), 1)
        self.pbank("tensor", pE)
        self.op("tensor", lambda t: t.matmul(pE[0:1, 0:8], self.c_onescol[:, :],
                                             self.pacc[:, 0:8], start=True, stop=True),
                self.sem("s_gsum"))
        self.wait("vector", self.sem("s_gsum"), 1)
        def gfin_a(v):
            return v.tensor_scalar_mul(rowb[0:1, 4:12], pE[0:1, 0:8],
                                       float(2.0 ** -24))
        def gfin_b(v):
            v.tensor_add(rowb[0:1, 12:13], rowb[0:1, 4:5], rowb[0:1, 5:6])
            v.tensor_add(rowb[0:1, 13:14], rowb[0:1, 6:7], rowb[0:1, 7:8])
            v.tensor_add(rowb[0:1, 14:15], rowb[0:1, 8:9], rowb[0:1, 9:10])
            return v.tensor_add(rowb[0:1, 15:16], rowb[0:1, 10:11],
                                rowb[0:1, 11:12])
        def gfin_c(v):
            v.scalar_tensor_tensor(rowb[0:1, 0:1], rowb[0:1, 12:13], 1.0,
                                   rowb[0:1, 13:14], ALU.mult, ALU.add)
            return v.scalar_tensor_tensor(rowb[0:1, 1:2], rowb[0:1, 14:15], 1.0,
                                          rowb[0:1, 15:16], ALU.mult, ALU.add)
        def gfin_d(v):
            return v.tensor_scalar(rowb[0:1, 0:2], rowb[0:1, 0:2], 1.0, Q_EPS,
                                   ALU.mult, ALU.add)
        def gfin_e(v):
            return v.tensor_scalar_mul(rowb[0:1, 2:4], rowb[0:1, 0:2], 0.5)
        self.chain("vector", [gfin_a, gfin_b, gfin_c, gfin_d, gfin_e],
                   self.sem("s_gfin"))
        self.wait("tensor", self.sem("s_gfin"), 1)
        self.op("tensor", lambda t: t.matmul(pE[:, 0:4], self.c_onesbc[:, :],
                                             rowb[0:1, 0:4], start=True, stop=True),
                self.sem("s_gbc"))
        self.wait("vector", self.sem("s_gbc"), 1)
        def gbc_a(v):
            v.tensor_copy(gam[:, 0:1], pE[:, 2:3])
            v.tensor_scalar_mul(gam[:, 1:2], pE[:, 2:3], -1.0)
            v.tensor_copy(gam[:, 2:3], pE[:, 3:4])
            v.tensor_scalar_mul(gam[:, 3:4], pE[:, 3:4], -1.0)
            v.tensor_copy(gam[:, 6:7], pE[:, 0:1])
            return v.tensor_scalar_mul(gam[:, 5:6], pE[:, 1:2], 1.0 / 254.0)
        def gbc_b(v):
            return v.scalar_tensor_tensor(gam[:, 4:5], gam[:, 6:7], 4096.0,
                                          gam[:, 6:7], ALU.mult, ALU.mult)
        self.chain("vector", [gbc_a, gbc_b], self.sem("s_gamma"))
        self.g_set(("bank", "E"), self.sem("s_gamma"))

    def alloc_attn_outer(self, esA):
        nc = self.nc
        SB = lambda n, sh, dt=F32: esA.enter_context(nc.sbuf_tensor(n, sh, dt))
        a = self
        a.qk_k = SB("qk_k", [97, H * S], BF16)
        a.v_t = SB("v_t", [128, NKT * H * 65], BF16)
        a.nq = SB("nq", [128, 6 * S], BF16)
        a.kmx = SB("kmx", [97, H], F32)
        a.r32 = SB("r32", [97, TC], BF16)
        a.crow = SB("crow", [97, TC], BF16)
        a.qsq = [SB(f"qsq{i}", [96, TC], BF16) for i in range(2)]

    def alloc_nkv(self, es, si):
        self.nkv = es.enter_context(
            self.nc.sbuf_tensor(f"nkv_{si}", [128, 2 * S], BF16))

    def alloc_p1(self, es, si):
        nc = self.nc
        SB = lambda n, sh, dt=F32: es.enter_context(nc.sbuf_tensor(n + f"_{si}", sh, dt))
        a = self
        a.xb = [SB(f"xb{i}", [128, D], BF16) for i in range(2)]
        a.y1t = SB("y1t", [128, 16 * TC], BF16)
        a.cq_fm = SB("cq_fm", [128, 6 * TC], BF16)
        a.sq_t = [SB(f"sq_t{i}", [128, TC], BF16) for i in range(2)]
        a.rowc = SB("rowc", [1, 4 * TC], F32)
        a.lndump = SB("lndump", [128, 1024], BF16)
        a.wl1 = [SB(f"wl1{i}", [128, RQ], BF16) for i in range(2)]
        a.wl2 = [SB(f"wl2{i}", [128, RKV + DR], BF16) for i in range(2)]
        a.ag_q = SB("ag_q", [128, 6], F32)
        a.ag_kv = SB("ag_kv", [128, 2], F32)
        sg = self.sem("s_gains")
        self.dma("gpsimd", a.ag_q[:, :], self.d_qg, sg)
        self.dma("gpsimd", a.ag_kv[:, :], self.d_kvg, sg)

    def alloc_p2(self, es, si):
        nc = self.nc
        SB = lambda n, sh, dt=F32: es.enter_context(nc.sbuf_tensor(n + f"_{si}", sh, dt))
        a = self
        a.aw_kn = SB("aw_kn", [128, 2 * H * DN], BF16)
        a.aw_v = SB("aw_v", [128, 2 * H * DV], BF16)
        sw = self.sem("s_p2w")
        rr = lambda d: d.rearrange("(c p) o -> p c o", p=128)
        self.dma("gpsimd", a.aw_kn[:, :], rr(self.d_wkn), sw)
        self.dma("gpsimd", a.aw_v[:, :], rr(self.d_wv), sw)
        self.wait("tensor", sw)

    def alloc_p3(self, es, si):
        nc = self.nc
        SB = lambda n, sh, dt=F32: es.enter_context(nc.sbuf_tensor(n + f"_{si}", sh, dt))
        a = self
        a.aw_qn = SB("aw_qn", [128, 6 * H * DN], BF16)
        a.aw_qr = SB("aw_qr", [128, 6 * H * DR], BF16)
        a.q_aug = SB("q_aug", [97, H * TC], BF16)
        a.att_fm = SB("att_fm", [128, 8 * TC], BF16)
        a.att_tmp = SB("att_tmp", [64, TC], BF16)
        a.sexp = [SB(f"sexp{i}", [128, 512], BF16) for i in range(2)]
        a.den_b = SB("den_b", [65, 512], BF16)
        a.srec = SB("srec", [64, 512], F32)
        a.wpb = [SB(f"wpb{i}", [128, 512], BF16) for i in range(2)]
        a.xb2 = [SB(f"xb2_{i}", [128, 512], BF16) for i in range(2)]
        a.xpst = [SB(f"xpst_{i}", [128, 512], BF16) for i in range(2)]
        sw = self.sem("s_p3w")
        rr = lambda d: d.rearrange("(c p) o -> p c o", p=128)
        self.dma("gpsimd", a.aw_qn[:, :], rr(self.d_wqn), sw)
        self.dma("gpsimd", a.aw_qr[:, :], rr(self.d_wqr), sw)
        self.wait("tensor", sw)

    def emit_p1(self, si):
        a = self
        xb = a.xb; y1t = a.y1t; cq_fm = a.cq_fm; sq_t = a.sq_t; rowc = a.rowc; lndump = a.lndump; wl1 = a.wl1; wl2 = a.wl2; ag_q = a.ag_q; ag_kv = a.ag_kv; nkv = a.nkv; stats = a.stats
        sfx = f"_{si}"
        xs = self.d_x[si]
        s_xld = [self.sem(f"s_xld{i}") for i in range(2)]
        s_sqs = self.sem(f"s_sqs")
        s_ln1 = self.sem(f"s_ln1")
        s_y1 = self.sem(f"s_y1")
        s_tp = self.sem(f"s_tp")
        s_tpe = self.sem(f"s_tpe")
        s_lat = self.sem(f"s_lat")
        s_late = self.sem(f"s_late")
        s_row = self.sem(f"s_row")
        s_rbc = self.sem(f"s_rbc")
        s_nrm = self.sem(f"s_nrm")
        s_kve = self.sem(f"s_kve")

        for tc in range(NTC):
            self.pump(12)
            for tt in range(4):
                ti = tc * 4 + tt
                par = ti % 2
                self.g_wait("gpsimd", ("xb", par))
                self.dma("gpsimd", xb[par][:, :], xs[ti * 128:(ti + 1) * 128, :],
                         s_xld[par])
                pc = par * 8
                self.wait("scalar", s_xld[par])
                self.g_wait("scalar", ("stats", par))
                self.op("scalar", lambda s, par=par, pc=pc:
                        s.activation(lndump[:, :], xb[par][:, 0:1024], AF.Square,
                                     bias=0.0, scale=1.0,
                                     accum_out=stats[:, pc:pc + 1]), s_sqs)
                self.wait("scalar", s_sqs)
                self.op("scalar", lambda s, par=par, pc=pc:
                        s.activation(lndump[:, :], xb[par][:, 1024:2048],
                                     AF.Square, bias=0.0, scale=1.0,
                                     accum_out=stats[:, pc + 2:pc + 3]), s_sqs)
                self.wait("vector", s_xld[par])
                self.wait("vector", s_sqs)
                self.op("vector", lambda v, par=par, pc=pc:
                        v.reduce_sum(stats[:, pc + 1:pc + 2], xb[par][:, :],
                                     axis=AX.X), self.sem("s_rr"))
                self.wait("vector", self.sem("s_rr"))
                def lnv_a(v, pc=pc):
                    st = stats
                    v.tensor_add(st[:, pc:pc + 1], st[:, pc:pc + 1], st[:, pc + 2:pc + 3])
                    return v.tensor_scalar_mul(st[:, pc + 1:pc + 2],
                                               st[:, pc + 1:pc + 2], 1.0 / D)
                def lnv_b(v, pc=pc):
                    st = stats
                    v.tensor_scalar(st[:, pc:pc + 1], st[:, pc:pc + 1], 1.0 / D,
                                    LN_EPS, ALU.mult, ALU.add)
                    return v.tensor_tensor(st[:, pc + 2:pc + 3], st[:, pc + 1:pc + 2],
                                           st[:, pc + 1:pc + 2], ALU.mult)
                def lnv_c(v, pc=pc):
                    st = stats
                    return v.tensor_sub(st[:, pc:pc + 1], st[:, pc:pc + 1],
                                        st[:, pc + 2:pc + 3])
                self.chain("vector", [lnv_a, lnv_b, lnv_c], s_ln1)
                self.wait("scalar", s_ln1)
                self.op("scalar", lambda s, pc=pc:
                        s.activation(stats[:, pc + 3:pc + 4], stats[:, pc:pc + 1],
                                     AF.Sqrt, bias=0.0, scale=1.0), s_sqs)
                self.wait("vector", s_sqs)
                def l2_a(v, pc=pc):
                    return v.reciprocal(stats[:, pc + 4:pc + 5],
                                        stats[:, pc + 3:pc + 4])
                def l2_b(v, pc=pc):
                    return v.scalar_tensor_tensor(stats[:, pc + 6:pc + 7],
                                                  stats[:, pc + 1:pc + 2], -1.0,
                                                  stats[:, pc + 4:pc + 5],
                                                  ALU.mult, ALU.mult)
                def l2_c(v, par=par, pc=pc):
                    return v.tensor_scalar(xb[par][:, :], xb[par][:, :],
                                           stats[:, pc + 4:pc + 5],
                                           stats[:, pc + 6:pc + 7],
                                           ALU.mult, ALU.add)
                self.chain("vector", [l2_a, l2_b, l2_c], s_y1)
                self.g_set(("stats", par), s_y1)
                if self.dbg_xp and si == 0 and ti == 0:
                    self.freeze_dump([(self.d_dbg_st, stats[:, :]),
                                      (self.d_dbg_y1[0:128, 0:2048],
                                       xb[par][:, :])])
                self.wait("tensor", s_y1)
                if tt == 0:
                    self.g_wait("vector", ("y1t",))
                for fb in range(16):
                    ptp = self.pT[fb % 2]
                    self.pbank("tensor", ptp)
                    self.op("tensor", lambda t, par=par, fb=fb, ptp=ptp:
                            t.transpose(ptp[:, 0:128],
                                        xb[par][:, fb * 128:(fb + 1) * 128],
                                        self.c_ident[:, :]), s_tp)
                    self.wait("vector", s_tp)
                    self.op("vector", lambda v, fb=fb, tt=tt, ptp=ptp:
                            v.tensor_copy(y1t[:, fb * TC + tt * 128:
                                                fb * TC + (tt + 1) * 128],
                                          ptp[:, 0:128]), s_tpe)
                    self.g_set(("bank", self.bname[id(ptp)]), s_tpe)
                self.op("vector", lambda v: v.engine_nop(), self.sem("s_dummy"))
                self.g_set(("xb", par), s_tpe)

            s_wl1 = [self.sem(f"s_wl1{i}") for i in range(2)]
            s_wl2 = [self.sem(f"s_wl2{i}") for i in range(2)]
            self.wait("tensor", s_tpe)

            def evac_chunk(j, pp, last_cs):
                self.op("vector", lambda v, j=j, pp=pp:
                        v.tensor_copy(cq_fm[:, j * TC:(j + 1) * TC], pp[:, :]),
                        s_late)
                self.wait("scalar", s_late)
                self.g_wait("scalar", ("sq_t", j % 2))
                self.op("scalar", lambda s, j=j, pp=pp:
                        s.activation(sq_t[j % 2][:, :], pp[:, :], AF.Square,
                                     bias=0.0, scale=1.0), s_sqs)
                self.g_set(("bank", self.bname[id(pp)]), s_sqs)
                self.wait("tensor", s_sqs)
                if j == 0:
                    self.pbank("tensor", self.pF)
                self.op("tensor", lambda t, j=j, last_cs=last_cs:
                        t.matmul(self.pF[0:1, 0:TC], self.c_onescolb[:, :],
                                 sq_t[j % 2][:, :], start=(j == 0),
                                 stop=last_cs), s_row)
                self.g_set(("sq_t", j % 2), s_row)

            def rms_rows(rank, gain, ntens, n_ch, tc):
                self.wait("vector", s_row)
                self.op("vector", lambda v, rank=rank:
                        v.tensor_scalar(rowc[0:1, 0:TC], self.pF[0:1, 0:TC],
                                        1.0 / rank, RMS_EPS, ALU.mult, ALU.add),
                        s_row)
                self.g_set(("bank", "F"), s_row)
                self.wait("scalar", s_row)
                self.op("scalar", lambda s:
                        s.activation(rowc[0:1, TC:2 * TC], rowc[0:1, 0:TC],
                                     AF.Sqrt, bias=0.0, scale=1.0), s_sqs)
                self.wait("vector", s_sqs)
                self.op("vector", lambda v:
                        v.reciprocal(rowc[0:1, 2 * TC:3 * TC],
                                     rowc[0:1, TC:2 * TC]), s_row)
                self.wait("tensor", s_row)
                self.pbank("tensor", self.pF)
                self.op("tensor", lambda t:
                        t.matmul(self.pF[:, :], self.c_onesbc[:, :],
                                 rowc[0:1, 2 * TC:3 * TC],
                                 start=True, stop=True), s_rbc)
                self.wait("vector", s_rbc)
                if tc == 0:
                    self.g_wait("vector", ("nq",))
                    self.g_wait("vector", ("nkv",))
                for j in range(n_ch):
                    self.op("vector", lambda v, j=j, gain=gain, ntens=ntens, tc=tc:
                            v.scalar_tensor_tensor(
                                ntens[:, j * S + tc * TC:j * S + (tc + 1) * TC],
                                cq_fm[:, j * TC:(j + 1) * TC],
                                gain[:, j:j + 1], self.pF[:, :],
                                ALU.mult, ALU.mult), s_nrm)
                self.g_set(("bank", "F"), s_nrm)

            banks5 = [self.pA, self.pB, self.pC, self.pD, self.pE]
            for bk in banks5:
                self.pbank("tensor", bk)
            for i in range(16):
                par = i % 2
                self.g_wait("gpsimd", ("wl1", par))
                self.dma("gpsimd", wl1[par][:, :],
                         self.d_wcq[i * 128:(i + 1) * 128, :], s_wl1[par])
                self.wait("tensor", s_wl1[par])
                def lat1(t, i=i, par=par):
                    for oc in range(5):
                        li = t.matmul(banks5[oc][:, :],
                                      wl1[par][:, oc * 128:(oc + 1) * 128],
                                      y1t[:, i * TC:(i + 1) * TC],
                                      start=(i == 0), stop=(i == 15))
                    return li
                self.op("tensor", lat1, s_lat)
                self.g_set(("wl1", par), s_lat)
            base = s_lat.v
            for j in range(5):
                self.wait("vector", s_lat, base)
                if si == 0 and tc == 0 and j == 0:
                    self.wait("vector", self.sem("s_gains"))
                evac_chunk(j, banks5[j], last_cs=False)
            for i in range(16):
                par = i % 2
                self.g_wait("gpsimd", ("wl1", par))
                self.dma("gpsimd", wl1[par][:, :128],
                         self.d_wcq[i * 128:(i + 1) * 128, 5 * 128:6 * 128],
                         s_wl1[par])
                self.wait("tensor", s_wl1[par])
                if i == 0:
                    self.pbank("tensor", self.pA)
                def lat1b(t, i=i, par=par):
                    return t.matmul(self.pA[:, :], wl1[par][:, 0:128],
                                    y1t[:, i * TC:(i + 1) * TC],
                                    start=(i == 0), stop=(i == 15))
                self.op("tensor", lat1b, s_lat)
                self.g_set(("wl1", par), s_lat)
            self.wait("vector", s_lat)
            evac_chunk(5, self.pA, last_cs=True)
            rms_rows(RQ, ag_q, a.nq, 6, tc)

            self.pbank("tensor", self.pA)
            self.pbank("tensor", self.pB)
            self.pbank("tensor", self.pE)
            for i in range(16):
                par = i % 2
                self.g_wait("gpsimd", ("wl2", par))
                self.dma("gpsimd", wl2[par][:, 0:RKV],
                         self.d_wckv[i * 128:(i + 1) * 128, :], s_wl2[par])
                self.dma("gpsimd", wl2[par][:, RKV:RKV + DR],
                         self.d_wkr[i * 128:(i + 1) * 128, :], s_wl2[par])
                self.wait("tensor", s_wl2[par])
                def lat2(t, i=i, par=par):
                    t.matmul(self.pA[:, :], wl2[par][:, 0:128],
                             y1t[:, i * TC:(i + 1) * TC],
                             start=(i == 0), stop=(i == 15))
                    t.matmul(self.pB[:, :], wl2[par][:, 128:256],
                             y1t[:, i * TC:(i + 1) * TC],
                             start=(i == 0), stop=(i == 15))
                    return t.matmul(self.pE[64:96, :], wl2[par][:, 256:288],
                                    y1t[:, i * TC:(i + 1) * TC],
                                    start=(i == 0), stop=(i == 15))
                self.op("tensor", lat2, s_lat)
                self.g_set(("wl2", par), s_lat)
            self.g_set(("y1t",), s_lat)
            base = s_lat.v
            for j in range(2):
                self.wait("vector", s_lat, base)
                evac_chunk(j, [self.pA, self.pB][j], last_cs=(j == 1))
            rms_rows(RKV, ag_kv, nkv, 2, tc)

            if self.dbg_xp and si == 0 and tc == 0:
                sdbg = self.sem("s_dbg")
                self.wait("gpsimd", s_nrm)
                self.dma("gpsimd", self.d_dbg_cq, cq_fm[:, :], sdbg)
                self.wait("gpsimd", sdbg)
            self.wait("vector", s_lat)
            if si == 0 and tc == 0:
                self.wait("vector", self.sem("s_const"))
            self.g_wait("vector", ("crow",))
            self.g_wait("vector", ("qk_k",))
            def krope2(v, tc=tc):
                raw = a.r32[64:96, :]
                sh = a.crow[64:96, :]
                v.tensor_copy(raw, self.pE[64:96, :])
                v.stream_shuffle(sh, raw, SHUF)
                v.scalar_tensor_tensor(sh, sh, self.c_sig[64:96, :], raw,
                                       ALU.mult, ALU.add)
                return v.tensor_tensor(a.qk_k[64:96, tc * TC:(tc + 1) * TC], sh,
                                       self.c_crk[64:96, tc * TC:(tc + 1) * TC],
                                       ALU.mult)
            self.op("vector", krope2, s_kve)
            self.g_set(("bank", "E"), s_kve)
            self.g_set(("crow",), s_kve)

        def krep(v):
            for h in range(1, H):
                v.tensor_copy(a.qk_k[64:96, h * S:(h + 1) * S], a.qk_k[64:96, 0:S])
            return v.memset(a.qk_k[96:97, :], 1.0)
        self.op("vector", krep, s_kve)

    def emit_p2(self, si):
        a = self
        aw_kn = a.aw_kn; aw_v = a.aw_v; nkv = a.nkv
        sfx = f"_{si}"
        s_nrm = self.sem(f"s_nrm")
        s_kve = self.sem(f"s_kve")
        s_kv = self.sem(f"s_kv")
        s_kev = self.sem(f"s_kev")
        s_ksq = self.sem(f"s_ksq")
        s_kcs = self.sem(f"s_kcs")
        s_kmx = self.sem(f"s_kmx")

        self.g_wait("vector", ("v_t",))
        self.g_wait("vector", ("qk_k",))
        self.op("vector", lambda v: v.memset(a.v_t[:, :], 1.0), s_kev)

        self.wait("tensor", s_nrm)
        for h in range(H):
            for kc in range(NTC):
                self.pump(2)
                pp = self.rot[(h * NTC + kc) % 4]
                self.pbank("tensor", pp)
                def knmm(t, h=h, kc=kc, pp=pp):
                    for i in range(2):
                        li = t.matmul(pp[0:64, :],
                                      aw_kn[:, i * H * DN + h * DN:i * H * DN + (h + 1) * DN],
                                      nkv[:, i * S + kc * TC:i * S + (kc + 1) * TC],
                                      start=(i == 0), stop=(i == 1))
                    return li
                self.op("tensor", knmm, s_kv)
                self.wait("vector", s_kv)
                self.op("vector", lambda v, h=h, kc=kc, pp=pp:
                        v.tensor_copy(a.qk_k[0:64, h * S + kc * TC:h * S + (kc + 1) * TC],
                                      pp[0:64, :]), s_kev)
                self.g_set(("bank", self.bname[id(pp)]), s_kev)
        for kt in range(NKT):
            self.pump(2)
            for half in range(2):
                pp = self.rot[(kt % 2) * 2 + half]
                self.pbank("tensor", pp)
                def vmm(t, kt=kt, half=half, pp=pp):
                    for i in range(2):
                        li = t.matmul(pp[:, :],
                                      nkv[:, i * S + kt * 128:i * S + (kt + 1) * 128],
                                      aw_v[:, i * H * DV + half * 512:
                                             i * H * DV + (half + 1) * 512],
                                      start=(i == 0), stop=(i == 1))
                    return li
                self.op("tensor", vmm, s_kv)
                self.wait("vector", s_kv)
                self.op("vector", lambda v, kt=kt, half=half, pp=pp:
                        v.tensor_copy(
                            a.v_t[:, kt * H * 65 + half * 8 * 65:
                                  kt * H * 65 + (half * 8 + 8) * 65]
                            .rearrange("p (h e) -> p h e", e=65)[:, :, 0:64],
                            pp[:, :].rearrange("p (h e) -> p h e", e=64)), s_kev)
                self.g_set(("bank", self.bname[id(pp)]), s_kev)

        for h in range(H):
            self.pump(2)
            for kc in range(NTC):
                idx = h * NTC + kc
                qs = a.qsq[idx % 2]
                pX = [self.pE, self.pF][idx % 2]
                self.wait("scalar", s_kev)
                self.wait("scalar", s_kve)
                self.g_wait("scalar", ("qsq", idx % 2))
                self.op("scalar", lambda s, h=h, kc=kc, qs=qs:
                        s.activation(qs[0:96, :],
                                     a.qk_k[0:96, h * S + kc * TC:h * S + (kc + 1) * TC],
                                     AF.Square, bias=0.0, scale=1.0), s_ksq)
                self.wait("tensor", s_ksq)
                self.pbank("tensor", pX)
                self.op("tensor", lambda t, qs=qs, pX=pX:
                        t.matmul(pX[64:97, 0:TC], self.c_cones[:, :],
                                 qs[0:96, :], start=True, stop=True), s_kcs)
                self.g_set(("qsq", idx % 2), s_kcs)
                self.wait("vector", s_kcs)
                if h == 0 and kc == 0:
                    self.g_wait("vector", ("kmx",))
                if kc == 0:
                    self.op("vector", lambda v, h=h, pX=pX:
                            v.reduce_max(a.kmx[96:97, h:h + 1],
                                         pX[96:97, 0:TC], axis=AX.X), s_kmx)
                else:
                    self.op("vector", lambda v, pX=pX:
                            v.reduce_max(a.crow[96:97, 0:1],
                                         pX[96:97, 0:TC], axis=AX.X),
                            self.sem("s_rr"))
                    self.wait("vector", self.sem("s_rr"))
                    self.wait("vector", s_kmx)
                    self.op("vector", lambda v, h=h:
                            v.tensor_max(a.kmx[96:97, h:h + 1], a.kmx[96:97, h:h + 1],
                                         a.crow[96:97, 0:1]), s_kmx)
                self.g_set(("bank", self.bname[id(pX)]), s_kmx)
        self.wait("scalar", s_kmx)
        self.op("scalar", lambda s:
                s.activation(a.kmx[96:97, 0:H], a.kmx[96:97, 0:H], AF.Sqrt,
                             bias=0.0, scale=1.0), s_ksq)
        self.wait("vector", s_ksq)
        self.op("vector", lambda v:
                v.tensor_scalar_mul(a.kmx[96:97, 0:H], a.kmx[96:97, 0:H], -1.0),
                s_kmx)

    def emit_p3(self, si):
        a = self
        aw_qn = a.aw_qn; aw_qr = a.aw_qr; q_aug = a.q_aug; att_fm = a.att_fm; att_tmp = a.att_tmp; sexp = a.sexp; den_b = a.den_b; srec = a.srec; wpb = a.wpb; xb2 = a.xb2; xpst = a.xpst
        sfx = f"_{si}"
        s_nrm = self.sem(f"s_nrm")
        s_kmx = self.sem(f"s_kmx")
        s_kev = self.sem(f"s_kev")
        s_kve = self.sem(f"s_kve")
        s_qb = self.sem(f"s_qb")
        s_qbe = self.sem(f"s_qbe")
        s_qsq = self.sem(f"s_qsq")
        s_qcs = self.sem(f"s_qcs")
        s_crw = self.sem(f"s_crw")
        s_qk = self.sem(f"s_qk")
        s_msk = self.sem(f"s_msk")
        s_exp = self.sem(f"s_exp")
        s_pvk = self.sem(f"s_pvk")
        s_dnc = self.sem(f"s_dnc")
        s_dnd = self.sem(f"s_dnd")
        s_dbc = self.sem(f"s_dbc")
        s_rcp = self.sem(f"s_rcp")
        s_div = self.sem(f"s_div")
        s_odm = self.sem(f"s_odm")
        s_wp = [self.sem(f"s_wp{i}") for i in range(2)]
        s_op = self.sem(f"s_op")
        s_ope = self.sem(f"s_ope")
        s_x2 = [self.sem(f"s_x2{i}") for i in range(2)]
        s_xpw = self.sem(f"s_xpw")
        xs = self.d_x[si]
        xps = self.d_xp[si]

        self.wait("tensor", s_kev)
        self.wait("tensor", s_kve)
        for qc in range(NTC):
            nkt = 4 * (qc + 1)
            if qc == 0:
                self.wait("tensor", s_nrm)
            self.g_wait("tensor", ("q_aug",))
            qbv = {}

            def SQNR(h, qc=qc):
                self.pump(6)
                ppn = self.rot[(2 * h) % 4]
                ppr = self.rot[(2 * h + 1) % 4]
                self.pbank("tensor", ppn)
                def qn(t, h=h, ppn=ppn, qc=qc):
                    for i in range(6):
                        li = t.matmul(ppn[0:64, :],
                                      aw_qn[:, i * H * DN + h * DN:i * H * DN + (h + 1) * DN],
                                      a.nq[:, i * S + qc * TC:i * S + (qc + 1) * TC],
                                      start=(i == 0), stop=(i == 5))
                    return li
                self.op("tensor", qn, s_qb)
                self.pbank("tensor", ppr)
                def qr(t, h=h, ppr=ppr, qc=qc):
                    for i in range(6):
                        li = t.matmul(ppr[64:96, :],
                                      aw_qr[:, i * H * DR + h * DR:i * H * DR + (h + 1) * DR],
                                      a.nq[:, i * S + qc * TC:i * S + (qc + 1) * TC],
                                      start=(i == 0), stop=(i == 5))
                    return li
                self.op("tensor", qr, s_qb)
                self.wait("vector", s_qb)
                def qev(v, h=h, ppn=ppn, ppr=ppr, qc=qc):
                    dn = q_aug[0:64, h * TC:(h + 1) * TC]
                    dr = q_aug[64:96, h * TC:(h + 1) * TC]
                    raw = a.r32[64:96, :]
                    v.tensor_scalar_mul(dn, ppn[0:64, :], SC)
                    v.tensor_copy(raw, ppr[64:96, :])
                    v.stream_shuffle(dr, raw, SHUF)
                    v.scalar_tensor_tensor(dr, dr, self.c_sig[64:96, :], raw,
                                           ALU.mult, ALU.add)
                    return v.tensor_tensor(dr, dr,
                                           self.c_crq[64:96, qc * TC:(qc + 1) * TC],
                                           ALU.mult)
                self.op("vector", qev, s_qbe)
                qbv[h] = s_qbe.v
                self.g_set(("bank", self.bname[id(ppn)]), s_qbe)
                self.g_set(("bank", self.bname[id(ppr)]), s_qbe)

            def SCHAIN(h, qc=qc):
                qs = a.qsq[h % 2]
                pX = [self.pE, self.pF][h % 2]
                self.wait("scalar", s_qbe, qbv[h])
                self.g_wait("scalar", ("qsq", h % 2))
                self.op("scalar", lambda s, h=h, qs=qs:
                        s.activation(qs[0:96, :], q_aug[0:96, h * TC:(h + 1) * TC],
                                     AF.Square, bias=0.0, scale=1.0), s_qsq)
                self.wait("tensor", s_qsq)
                self.pbank("tensor", pX)
                self.op("tensor", lambda t, qs=qs, pX=pX:
                        t.matmul(pX[64:97, 0:TC], self.c_cones[:, :],
                                 qs[0:96, :], start=True, stop=True), s_qcs)
                self.g_set(("qsq", h % 2), s_qcs)
                self.wait("scalar", s_qcs)
                self.g_wait("scalar", ("crow",))
                self.op("scalar", lambda s, pX=pX:
                        s.activation(a.crow[96:97, :], pX[96:97, 0:TC], AF.Sqrt,
                                     bias=0.0, scale=1.0), s_qsq)
                self.g_set(("bank", self.bname[id(pX)]), s_qsq)
                self.wait("vector", s_qsq)
                if qc == 0 and h == 0:
                    self.wait("vector", s_kmx)
                self.op("vector", lambda v, h=h:
                        v.tensor_single_scalar(q_aug[96:97, h * TC:(h + 1) * TC],
                                               a.crow[96:97, :],
                                               a.kmx[96:97, h:h + 1], ALU.mult),
                        s_crw)
                self.g_set(("crow",), s_crw)

            SQNR(0)
            for h in range(1, H):
                SQNR(h)
                SCHAIN(h - 1)
            SCHAIN(H - 1)
            if si == 0 and qc == 3:
                self.emit_gamma_finalize()

            self.wait("tensor", s_crw)
            pend_den = [None]

            def mk_den(h, pv, pvkv):
                def den():
                    self.wait("vector", s_pvk, pvkv)
                    self.g_wait("vector", ("den",))
                    self.op("vector", lambda v, pv=pv:
                            v.tensor_copy(den_b[64:65, :], pv[64:65, :]), s_dnc)
                    self.wait("tensor", s_dnc)
                    self.pbank("tensor", self.pE)
                    self.op("tensor", lambda t:
                            t.matmul(self.pE[0:64, :],
                                     self.c_ones64[64:65, :],
                                     den_b[64:65, :], start=True, stop=True), s_dbc)
                    self.g_set(("den",), s_dbc)
                    self.wait("scalar", s_dbc)
                    self.pbank("scalar", self.pF)
                    self.g_wait("scalar", ("srec",))
                    def rcp(s):
                        s.activation(self.pF[0:64, :], self.pE[0:64, :], AF.Ln,
                                     bias=0.0, scale=1.0)
                        return s.activation(srec[:, :], self.pF[0:64, :], AF.Exp,
                                            bias=0.0, scale=-1.0)
                    self.op("scalar", rcp, s_rcp)
                    self.g_set(("bank", "E"), s_rcp)
                    self.g_set(("bank", "F"), s_rcp)
                    self.wait("vector", s_rcp)
                    even = (h % 2 == 0)
                    if even:
                        dst = att_fm[0:64, (h // 2) * TC:(h // 2 + 1) * TC]
                    else:
                        self.g_wait("vector", ("att_tmp",))
                        dst = att_tmp[0:64, :]
                    self.op("vector", lambda v, pv=pv, dst=dst:
                            v.tensor_mul(dst, pv[0:64, :], srec[:, :]), s_div)
                    self.g_set(("srec",), s_div)
                    self.g_set(("bank", self.bname[id(pv)]), s_div)
                    if not even:
                        self.wait("gpsimd", s_div)
                        self.dma("gpsimd",
                                 att_fm[64:128, (h // 2) * TC:(h // 2 + 1) * TC],
                                 att_tmp[0:64, :], s_odm)
                        self.g_set(("att_tmp",), s_odm)
                return den

            for h in range(H):
                self.pump(6)
                pv = [self.pC, self.pD][h % 2]
                self.pbank("tensor", pv)
                expv = {}

                def SQK(kt, h=h, qc=qc):
                    psc = [self.pA, self.pB][kt % 2]
                    self.pbank("tensor", psc)
                    self.op("tensor", lambda t, h=h, kt=kt, psc=psc:
                            t.matmul(psc[:, :],
                                     a.qk_k[0:97, h * S + kt * 128:h * S + (kt + 1) * 128],
                                     q_aug[0:97, h * TC:(h + 1) * TC],
                                     start=True, stop=True), s_qk)
                    return psc

                def SME(kt, psc, qc=qc):
                    diag = kt >= 4 * qc
                    if diag:
                        r = kt - 4 * qc
                        self.wait("vector", s_qk)
                        self.op("vector", lambda v, psc=psc, r=r:
                                v.tensor_add(psc[:, :], psc[:, :],
                                             self.c_mask[:, r * 512:(r + 1) * 512]),
                                s_msk)
                        self.wait("scalar", s_msk)
                    else:
                        self.wait("scalar", s_qk)
                    self.g_wait("scalar", ("sexp", kt % 2))
                    self.op("scalar", lambda s, psc=psc, kt=kt:
                            s.activation(sexp[kt % 2][:, :], psc[:, :], AF.Exp,
                                         bias=0.0, scale=1.0), s_exp)
                    expv[kt] = s_exp.v
                    self.g_set(("bank", self.bname[id(psc)]), s_exp)

                def SPV(kt, h=h, pv=pv, nkt=nkt):
                    self.wait("tensor", s_exp, expv[kt])
                    self.op("tensor", lambda t, h=h, kt=kt, pv=pv,
                            last=(kt == nkt - 1):
                            t.matmul(pv[0:65, :],
                                     a.v_t[:, kt * H * 65 + h * 65:
                                           kt * H * 65 + (h + 1) * 65],
                                     sexp[kt % 2][:, :],
                                     start=(kt == 0), stop=last), s_pvk)
                    self.g_set(("sexp", kt % 2), s_pvk)

                p0 = SQK(0)
                SME(0, p0)
                for kt in range(1, nkt):
                    p = SQK(kt)
                    if kt == 3 and pend_den[0] is not None:
                        pend_den[0]()
                        pend_den[0] = None
                    SME(kt, p)
                    SPV(kt - 1)
                SPV(nkt - 1)
                pend_den[0] = mk_den(h, pv, s_pvk.v)
            pend_den[0]()
            pend_den[0] = None

            self.wait("tensor", s_div)
            self.wait("tensor", s_odm)
            for dc in range(4):
                self.pump(6)
                for ic in range(8):
                    par = ic % 2
                    self.g_wait("gpsimd", ("wpb", par))
                    self.dma("gpsimd", wpb[par][:, :],
                             self.d_wpr[:, ic * D + dc * 512:
                                        ic * D + (dc + 1) * 512], s_wp[par])
                    if ic < 2:
                        tt = ic
                        self.g_wait("gpsimd", ("xb2", tt % 2))
                        self.dma("gpsimd", xb2[tt % 2][:, :],
                                 xs[(qc * 4 + tt) * 128:(qc * 4 + tt + 1) * 128,
                                    dc * 512:(dc + 1) * 512], s_x2[tt % 2])
                    self.wait("tensor", s_wp[par])
                    if ic == 0:
                        for tt in range(4):
                            self.pbank("tensor", self.rot[tt])
                    def opmm(t, ic=ic, par=par, qc=qc):
                        li = None
                        for tt in range(4):
                            li = t.matmul(self.rot[tt][:, :],
                                          att_fm[:, ic * TC + tt * 128:
                                                   ic * TC + (tt + 1) * 128],
                                          wpb[par][:, :],
                                          start=(ic == 0), stop=(ic == 7))
                        return li
                    self.op("tensor", opmm, s_op)
                    self.g_set(("wpb", par), s_op)
                opv = s_op.v
                for tt in range(4):
                    ti = qc * 4 + tt
                    self.wait("vector", s_op, opv)
                    self.wait("vector", s_x2[tt % 2])
                    self.g_wait("vector", ("xpst", tt % 2))
                    self.op("vector", lambda v, tt=tt:
                            v.scalar_tensor_tensor(
                                xpst[tt % 2][:, :], self.rot[tt][:, :], 1.0,
                                xb2[tt % 2][:, :], ALU.mult, ALU.add), s_ope)
                    self.g_set(("bank", self.bname[id(self.rot[tt])]), s_ope)
                    self.g_set(("xb2", tt % 2), s_ope)
                    if tt < 2:
                        nt = tt + 2
                        self.g_wait("gpsimd", ("xb2", nt % 2))
                        self.dma("gpsimd", xb2[nt % 2][:, :],
                                 xs[(qc * 4 + nt) * 128:(qc * 4 + nt + 1) * 128,
                                    dc * 512:(dc + 1) * 512], s_x2[nt % 2])
                    self.wait("scalar", s_ope)
                    self.dma("scalar",
                             xps[ti * 128:(ti + 1) * 128, dc * 512:(dc + 1) * 512],
                             xpst[tt % 2][:, :], s_xpw)
                    if self.dbg_xp:
                        self.wait("gpsimd", s_ope)
                        self.dma("gpsimd",
                                 self.d_xpdbg[si][ti * 128:(ti + 1) * 128,
                                                  dc * 512:(dc + 1) * 512],
                                 xpst[tt % 2][:, :], s_xpw)
                    self.g_set(("xpst", tt % 2), s_xpw)
            self.g_set(("att_fm",), s_op)
            self.g_set(("q_aug",), s_qk)
            if self.dbg_xp and si == 0 and qc == 0:
                sdbg = self.sem("s_dbg")
                self.wait("gpsimd", s_div)
                self.wait("gpsimd", s_odm)
                self.dma("gpsimd", self.d_dbg_qk, a.qk_k[:, :], sdbg)
                self.dma("gpsimd", self.d_dbg_qa, q_aug[:, :], sdbg)
                self.dma("gpsimd", self.d_dbg_af, att_fm[:, :], sdbg)
                self.dma("gpsimd", self.d_dbg_vt, a.v_t[:, :], sdbg)
                self.dma("gpsimd", self.d_dbg_sr, srec[:, :], sdbg)
                self.dma("gpsimd", self.d_dbg_nq, a.nq[:, :], sdbg)
                self.wait("gpsimd", sdbg)
        self.g_set(("nq",), s_qb)
        self.g_set(("nkv",), self.sem("s_kv"))
        self.g_set(("qk_k",), s_qk)
        self.g_set(("v_t",), s_pvk)
        self.g_set(("kmx",), s_crw)

    def emit_ffn(self, esF):
        a = self
        nc = self.nc
        SB = lambda n, sh, dt=F32: esF.enter_context(nc.sbuf_tensor(n, sh, dt))
        h_sb = SB("h_sb", [128, 64 * 512], BF16)
        xqT = [SB(f"xqT{i}", [128, 16 * 512], BF16) for i in range(2)]
        w1b = [SB(f"w1b{i}", [128, 4 * 512], BF16) for i in range(2)]
        w2b = [SB(f"w2b{i}", [128, 8 * 512], BF16) for i in range(2)]
        outst = [SB(f"outst{i}", [128, 512], F32) for i in range(2)]
        xw = [SB(f"xw{i}", [128, 4 * D], BF16) for i in range(2)]
        y2f = SB("y2f", [128, D], F32)
        fdump = y2f.bitcast(BF16)
        xq_t = [SB(f"xq_t{i}", [128, D], BF16) for i in range(2)]
        t32 = SB("t32", [128, 512], F32)
        ci32 = SB("ci32", [128, 512], I32)
        d32 = SB("d32", [128, 512], F32)
        gbc = SB("gbc", [128, 512], F32)
        hmxb = d32.bitcast(BF16)[:, 0:512]
        fst = SB("fst", [128, 16], F32)
        fcol = SB("fcol", [128, 8], F32)
        umrep = SB("umrep", [128, 128], BF16)
        umc = SB("umc", [128, 4], F32)
        rcol = SB("rcol", [128, 4], F32)

        s_xw = [self.sem(f"s_fxw{i}") for i in range(2)]
        s_sq = self.sem("s_fsq")
        s_ln = self.sem("s_fln")
        s_y2 = self.sem("s_fy2")
        s_xq = self.sem("s_fxq")
        s_tp = self.sem("s_ftp")
        s_tpe = self.sem("s_ftpe")
        s_w1 = [self.sem(f"s_fw1{i}") for i in range(2)]
        s_w2 = [self.sem(f"s_fw2{i}") for i in range(2)]
        s_m1 = self.sem("s_fm1")
        s_ev = self.sem("s_fev")
        s_mx = self.sem("s_fmx")
        s_row = self.sem("s_frow")
        s_rtp = self.sem("s_frtp")
        s_rte = self.sem("s_frte")
        s_gb = self.sem("s_fgb")
        s_hq = self.sem("s_fhq")
        s_m2 = self.sem("s_fm2")
        s_oev = self.sem("s_foev")
        s_out = self.sem("s_fout")

        def flush_to(mark):
            while getattr(self, "pp_popped", 0) < mark and self.pp_steps:
                self.pump(1)

        flush_to(getattr(self, "pp_w1q_end", 0))
        self.wait("sync", self.sem("s_cachew1"))
        self.wait("gpsimd", self.sem("s_xpw"))
        self.wait("gpsimd", self.sem("s_cachew1"))

        NW = 2 * NTC
        xqv = {}

        def xw_loads(w):
            wp = w % 2
            xps = self.d_xp[w // NTC]
            t0 = (w % NTC) * TC
            self.g_wait("gpsimd", ("xw", wp))
            for tt in range(4):
                self.dma("gpsimd", xw[wp][:, tt * D:(tt + 1) * D],
                         xps[(t0 + tt * 128):(t0 + (tt + 1) * 128), :], s_xw[wp])

        def ln_tile(w, tt):
            wp = w % 2
            pc = (tt % 2) * 8
            xwt = xw[wp]
            self.wait("scalar", s_xw[wp])
            self.g_wait("scalar", ("fst", tt % 2))
            self.g_wait("scalar", ("y2f",))
            def sq2(s, tt=tt, pc=pc, xwt=xwt):
                s.activation(fdump[:, 0:1024], xwt[:, tt * D:tt * D + 1024],
                             AF.Square, bias=0.0, scale=1.0,
                             accum_out=fst[:, pc:pc + 1])
                return s.activation(fdump[:, 0:1024],
                                    xwt[:, tt * D + 1024:(tt + 1) * D],
                                    AF.Square, bias=0.0, scale=1.0,
                                    accum_out=fst[:, pc + 2:pc + 3])
            self.op("scalar", sq2, s_sq)
            self.wait("vector", s_xw[wp])
            self.wait("vector", s_sq)
            self.op("vector", lambda v, tt=tt, pc=pc, xwt=xwt:
                    v.reduce_sum(fst[:, pc + 1:pc + 2], xwt[:, tt * D:(tt + 1) * D],
                                 axis=AX.X), self.sem("s_rr"))
            self.wait("vector", self.sem("s_rr"))
            def flnv_a(v, pc=pc):
                st = fst
                v.tensor_add(st[:, pc:pc + 1], st[:, pc:pc + 1],
                             st[:, pc + 2:pc + 3])
                return v.tensor_scalar_mul(st[:, pc + 1:pc + 2],
                                           st[:, pc + 1:pc + 2], 1.0 / D)
            def flnv_b(v, pc=pc):
                st = fst
                v.tensor_scalar(st[:, pc:pc + 1], st[:, pc:pc + 1],
                                1.0 / D, LN_EPS, ALU.mult, ALU.add)
                return v.tensor_tensor(st[:, pc + 2:pc + 3], st[:, pc + 1:pc + 2],
                                       st[:, pc + 1:pc + 2], ALU.mult)
            def flnv_c(v, pc=pc):
                st = fst
                return v.tensor_sub(st[:, pc:pc + 1], st[:, pc:pc + 1],
                                    st[:, pc + 2:pc + 3])
            self.chain("vector", [flnv_a, flnv_b, flnv_c], s_ln)
            self.wait("scalar", s_ln)
            self.op("scalar", lambda s, pc=pc:
                    s.activation(fst[:, pc + 3:pc + 4], fst[:, pc:pc + 1],
                                 AF.Sqrt, bias=0.0, scale=1.0), s_sq)
            self.wait("vector", s_sq)
            def fl2_a(v, pc=pc):
                return v.reciprocal(fst[:, pc + 4:pc + 5], fst[:, pc + 3:pc + 4])
            def fl2_b(v, pc=pc):
                return v.scalar_tensor_tensor(fst[:, pc + 6:pc + 7],
                                              fst[:, pc + 1:pc + 2], -1.0,
                                              fst[:, pc + 4:pc + 5],
                                              ALU.mult, ALU.mult)
            def fl2_c(v, tt=tt, pc=pc, xwt=xwt):
                return v.tensor_scalar(y2f[:, :], xwt[:, tt * D:(tt + 1) * D],
                                       fst[:, pc + 4:pc + 5],
                                       fst[:, pc + 6:pc + 7], ALU.mult, ALU.add)
            self.chain("vector", [fl2_a, fl2_b, fl2_c], s_y2)
            self.g_set(("fst", tt % 2), s_y2)
            self.op("vector", lambda v, pc=pc:
                    v.reduce_max(fst[:, pc + 7:pc + 8], y2f[:, :], axis=AX.X,
                                 apply_absolute_value=True), self.sem("s_rr"))
            self.wait("vector", self.sem("s_rr"))
            def qa_a(v, pc=pc):
                return v.tensor_scalar_max(fst[:, pc + 7:pc + 8],
                                           fst[:, pc + 7:pc + 8], Q_EPS)
            def qa_b(v, tt=tt, pc=pc):
                v.tensor_scalar_mul(fcol[:, tt + 4:tt + 5],
                                    fst[:, pc + 7:pc + 8], 1.0 / 127.0)
                return v.reciprocal(fst[:, pc + 7:pc + 8], fst[:, pc + 7:pc + 8])
            def qa_c(v, tt=tt, pc=pc):
                v.scalar_tensor_tensor(fcol[:, tt:tt + 1], fcol[:, tt + 4:tt + 5],
                                       1.0, fcol[:, tt + 4:tt + 5],
                                       ALU.mult, ALU.mult)
                return v.tensor_scalar_mul(fst[:, pc + 7:pc + 8],
                                           fst[:, pc + 7:pc + 8], 127.0)
            def qa_d(v, tt=tt, pc=pc):
                v.tensor_tensor(fcol[:, tt:tt + 1], fcol[:, tt:tt + 1],
                                self.gam[:, 4:5], ALU.mult)
                return v.tensor_single_scalar(y2f[:, :], y2f[:, :],
                                              fst[:, pc + 7:pc + 8], ALU.mult)
            self.chain("vector", [qa_a, qa_b, qa_c, qa_d], s_y2)
            self.g_wait("vector", ("xq_t", tt % 2))
            xqt = xq_t[tt % 2]
            for c in range(4):
                def rnd(v, c=c, xqt=xqt):
                    sl = slice(c * 512, (c + 1) * 512)
                    v.tensor_copy(ci32[:, :], y2f[:, sl])
                    v.scalar_tensor_tensor(d32[:, :], ci32[:, :], -1.0,
                                           y2f[:, sl], ALU.mult, ALU.add)
                    v.scalar_tensor_tensor(t32[:, :], d32[:, :], 0.5,
                                           ci32[:, :], ALU.is_ge, ALU.add)
                    v.tensor_scalar(d32[:, :], d32[:, :], -0.5, -1.0,
                                    ALU.is_le, ALU.mult)
                    return v.tensor_add(xqt[:, sl], t32[:, :], d32[:, :])
                self.op("vector", rnd, s_xq)
            self.g_set(("y2f",), s_xq)
            xqv[(w, tt)] = s_xq.v

        def tp_tile(w, tt):
            wp = w % 2
            self.wait("tensor", s_xq, xqv[(w, tt)])
            if tt == 0:
                self.g_wait("tensor", ("xqT", wp))
            xqt = xq_t[tt % 2]
            for fb in range(16):
                ptp = self.pT[fb % 2]
                self.pbank("tensor", ptp)
                self.op("tensor", lambda t, fb=fb, ptp=ptp, xqt=xqt:
                        t.transpose(ptp[:, 0:128],
                                    xqt[:, fb * 128:(fb + 1) * 128],
                                    self.c_ident[:, :]), s_tp)
                self.wait("vector", s_tp)
                self.op("vector", lambda v, fb=fb, tt=tt, ptp=ptp, wp=wp:
                        v.tensor_copy(xqT[wp][:, fb * 512 + tt * 128:
                                              fb * 512 + (tt + 1) * 128],
                                      ptp[:, 0:128]), s_tpe)
                self.g_set(("bank", self.bname[id(ptp)]), s_tpe)
            self.g_set(("xq_t", tt % 2), s_tp)

        xw_loads(0)
        for tt in range(4):
            ln_tile(0, tt)
            tp_tile(0, tt)

        for w in range(NW):
            si = w // NTC
            t0 = (w % NTC) * TC
            wp = w % 2
            if w + 1 < NW:
                xw_loads(w + 1)

            self.wait("tensor", s_tpe)
            self.g_wait("tensor", ("h_sb",))
            for og in range(16):
                self.pump(24 if w == 0 else 6)
                for ih in range(4):
                    par = (og * 4 + ih) % 2
                    self.g_wait("sync", ("w1b", par))
                    self.dma("sync", w1b[par][:, :],
                             self.d_w1q[og, ih * 4:(ih + 1) * 4]
                             .rearrange("i p f -> p i f"), s_w1[par])
                    self.wait("tensor", s_w1[par])
                    for ocin in range(4):
                        oc = og * 4 + ocin
                        pp = self.rot[oc % 4]
                        if ih == 0:
                            self.pbank("tensor", pp)
                        def m1(t, par=par, ih=ih, ocin=ocin, pp=pp, wp=wp):
                            for ii in range(4):
                                i = ih * 4 + ii
                                li = t.matmul(pp[:, :],
                                              w1b[par][:, ii * 512 + ocin * 128:
                                                       ii * 512 + (ocin + 1) * 128],
                                              xqT[wp][:, i * 512:(i + 1) * 512],
                                              start=(i == 0), stop=(i == 15))
                            return li
                        self.op("tensor", m1, s_m1)
                        if ih == 3:
                            self.wait("scalar", s_m1)
                            def ev(s, oc=oc, pp=pp):
                                s.activation(h_sb[:, oc * 512:(oc + 1) * 512], pp[:, :],
                                             AF.Relu, bias=0.0, scale=1.0 / 128.0)
                                return s.activation(h_sb[:, oc * 512:(oc + 1) * 512],
                                                    h_sb[:, oc * 512:(oc + 1) * 512],
                                                    AF.Square, bias=0.0, scale=1.0)
                            self.op("scalar", ev, s_ev)
                            self.g_set(("bank", self.bname[id(pp)]), s_ev)
                            self.wait("vector", s_ev)
                            if oc == 0:
                                self.op("vector", lambda v:
                                        v.tensor_copy(hmxb[:, :], h_sb[:, 0:512]),
                                        s_mx)
                            else:
                                self.op("vector", lambda v, oc=oc:
                                        v.tensor_max(hmxb[:, :], hmxb[:, :],
                                                     h_sb[:, oc * 512:(oc + 1) * 512]),
                                        s_mx)
                    self.g_set(("w1b", par), s_m1)

            for c in range(4):
                ptp = self.pT[c % 2]
                self.wait("tensor", s_mx)
                self.pbank("tensor", ptp)
                self.op("tensor", lambda t, c=c, ptp=ptp:
                        t.transpose(ptp[:, 0:128], hmxb[:, c * 128:(c + 1) * 128],
                                    self.c_ident[:, :]), s_rtp)
                self.wait("vector", s_rtp)
                self.op("vector", lambda v, c=c, ptp=ptp:
                        v.reduce_max(umc[:, c:c + 1], ptp[:, 0:128], axis=AX.X),
                        self.sem("s_rr"))
                self.wait("scalar", self.sem("s_rr"))
                self.op("scalar", lambda s, c=c:
                        s.activation(umrep[:, :], self.c_ident[:, :], AF.Relu,
                                     bias=umc[:, c:c + 1], scale=0.0),
                        self.sem("s_rr"))
                self.wait("tensor", self.sem("s_rr"))
                self.g_set(("bank", self.bname[id(ptp)]), self.sem("s_rr"))
                self.pbank("tensor", ptp)
                self.op("tensor", lambda t, ptp=ptp:
                        t.transpose(ptp[:, 0:128], umrep[:, :], self.c_ident[:, :]),
                        s_rtp)
                self.wait("vector", s_rtp)
                self.op("vector", lambda v, c=c, ptp=ptp:
                        v.tensor_copy(gbc[:, c * 128:(c + 1) * 128], ptp[:, 0:128]),
                        s_rte)
                self.g_set(("bank", self.bname[id(ptp)]), s_rte)
            def gfin(v):
                v.tensor_scalar_max(gbc[:, :], gbc[:, :], 1e-30)
                v.reciprocal(gbc[:, :], gbc[:, :])
                return v.tensor_scalar_mul(gbc[:, :], gbc[:, :], 127.0)
            self.op("vector", gfin, s_gb)
            def rc_a(v):
                r = None
                for c in range(4):
                    r = v.tensor_tensor(rcol[:, c:c + 1], fcol[:, c:c + 1],
                                        umc[:, c:c + 1], ALU.mult)
                return r
            def rc_b(v):
                return v.tensor_scalar_max(rcol[:, :], rcol[:, :], Q_EPS)
            def rc_c(v):
                r = None
                for c in range(4):
                    r = v.tensor_tensor(rcol[:, c:c + 1], rcol[:, c:c + 1],
                                        self.gam[:, 5:6], ALU.mult)
                return r
            self.chain("vector", [rc_a, rc_b, rc_c], s_rte)

            hqb = s_hq.v
            for oc in range(64):
                sl = slice(oc * 512, (oc + 1) * 512)
                def hqc(v, sl=sl):
                    v.tensor_tensor(t32[:, :], h_sb[:, sl], gbc[:, :], ALU.mult)
                    v.tensor_copy(ci32[:, :], t32[:, :])
                    v.scalar_tensor_tensor(d32[:, :], ci32[:, :], -1.0,
                                           t32[:, :], ALU.mult, ALU.add)
                    return v.scalar_tensor_tensor(h_sb[:, sl], d32[:, :], 0.5,
                                                  ci32[:, :], ALU.is_ge, ALU.add)
                self.op("vector", hqc, s_hq)

            if w + 1 < NW:
                ln_tile(w + 1, 0)
                ln_tile(w + 1, 1)

            for dq in range(4):
                self.pump(8 if w == 0 else 4)
                for ob in range(8):
                    par = ob % 2
                    if w == 0 and ob == 0:
                        flush_to(self.pp_w2g_end[dq])
                        self.wait("sync", self.sem("s_w2c"), dq + 1)
                    self.g_wait("sync", ("w2b", par))
                    self.dma("sync", w2b[par][:, :],
                             self.d_w2q[dq, ob * 8:(ob + 1) * 8]
                             .rearrange("o p f -> p o f"), s_w2[par])
                    self.wait("tensor", s_w2[par])
                    for oi in range(8):
                        o = ob * 8 + oi
                        if dq == 0:
                            self.wait("tensor", s_hq, hqb + o + 1)
                        if o == 0:
                            for tt in range(4):
                                self.pbank("tensor", self.rot[tt])
                        def m2(t, dq=dq, o=o, oi=oi, par=par):
                            li = None
                            for tt in range(4):
                                li = t.matmul(self.rot[tt][:, :],
                                              h_sb[:, o * 512 + tt * 128:
                                                   o * 512 + (tt + 1) * 128],
                                              w2b[par][:, oi * 512:(oi + 1) * 512],
                                              start=(o == 0), stop=(o == 63))
                            return li
                        self.op("tensor", m2, s_m2)
                    self.g_set(("w2b", par), s_m2)
                self.wait("vector", s_m2)
                self.wait("vector", s_rte)
                for tt in range(4):
                    pst = (dq * 4 + tt) % 2
                    self.g_wait("vector", ("outst", pst))
                    def oev(v, dq=dq, tt=tt, pst=pst, wp=wp):
                        return v.scalar_tensor_tensor(
                            outst[pst][:, :], self.rot[tt][:, :],
                            rcol[:, tt:tt + 1],
                            xw[wp][:, tt * D + dq * 512:tt * D + (dq + 1) * 512],
                            ALU.mult, ALU.add)
                    self.op("vector", oev, s_oev)
                    self.g_set(("bank", self.bname[id(self.rot[tt])]), s_oev)
                    self.wait("gpsimd", s_oev)
                    self.dma("gpsimd",
                             self.d_out[si][(t0 + tt * 128):(t0 + (tt + 1) * 128),
                                            dq * 512:(dq + 1) * 512],
                             outst[pst][:, :], s_out)
                    self.g_set(("outst", pst), s_out)
                if w + 1 < NW:
                    tp_tile(w + 1, dq)
                    if dq == 0:
                        ln_tile(w + 1, 2)
                    elif dq == 1:
                        ln_tile(w + 1, 3)
            self.g_set(("xw", wp), s_oev)
            self.g_set(("h_sb",), s_m2)
            self.g_set(("xqT", wp), s_m1)

        self.flush_pp()
        self.wait("gpsimd", s_out)
        for (eng, _), sem in getattr(self, "dma_sems", {}).items():
            self.wait(eng, sem)

    def finalize(self):
        blk = self.es.enter_context(self.nc.Block())
        for eng in ENGS:
            def mk(eng):
                def run(e):
                    for fn in self.streams[eng]:
                        fn(e)
                return run
            getattr(blk, eng)(mk(eng))


def build_program(dbg_xp=False):
    nc = bass.Bass("TRN2", target_bir_lowering=False,
                   detect_race_conditions=False)
    with ExitStack() as es:
        b = Builder(nc, es, dbg_xp=dbg_xp)
        b.declare_io()
        b.alloc_persistent()
        with ExitStack() as esA:
            b.emit_prepass_streams(esA)
            b.alloc_attn_outer(esA)
            for si in range(NSEQ):
                with ExitStack() as es12:
                    b.alloc_nkv(es12, si)
                    with ExitStack() as es1:
                        b.alloc_p1(es1, si)
                        b.emit_p1(si)
                    b.barrier()
                    b.alloc_p2(es12, si)
                    b.emit_p2(si)
                b.barrier()
                with ExitStack() as es3:
                    b.alloc_p3(es3, si)
                    b.emit_p3(si)
                b.barrier()
            b.barrier()
        with ExitStack() as esF:
            b.emit_ffn(esF)
        b.finalize()
    return nc


def _consts():
    c = {}
    c["ident"] = np.eye(128, dtype=ml_dtypes.bfloat16)
    c["cones"] = np.ones((96, 33), dtype=ml_dtypes.bfloat16)
    c["ones64"] = np.ones((1, 64), dtype=ml_dtypes.bfloat16)
    c["onesbc"] = np.ones((1, 128), dtype=np.float32)
    c["onescol"] = np.ones((128, 1), dtype=np.float32)
    c["onescolb"] = np.ones((128, 1), dtype=ml_dtypes.bfloat16)
    freqs = 1.0 / (10000.0 ** (np.arange(0, DR, 2, dtype=np.float64) / DR))
    ang = np.outer(np.arange(S, dtype=np.float64), freqs)
    cos = np.cos(ang).T
    crep = np.concatenate([cos, cos], axis=0)
    c["crq"] = (crep * SC).astype(ml_dtypes.bfloat16)
    c["crk"] = (crep / H).astype(ml_dtypes.bfloat16)
    c["sig"] = np.concatenate([-np.ones((16, 1)), np.ones((16, 1))]).astype(np.float32)
    m = np.zeros((128, 4 * 512), dtype=np.float32)
    kk = np.arange(128)[:, None]
    qq = np.arange(512)[None, :]
    for r in range(4):
        m[:, r * 512:(r + 1) * 512] = np.where(qq >= kk + r * 128, 0.0, NEG)
    c["mask"] = m.astype(ml_dtypes.bfloat16)
    return c


def _perm_rope_cols(w, nheads):
    rows = w.shape[0]
    w = w.reshape(rows, nheads, 16, 2)
    return np.ascontiguousarray(
        np.concatenate([w[:, :, :, 0], w[:, :, :, 1]], axis=2).reshape(rows, -1))


def prepare_base(L0_attn_x, L0_attn_w_cq, L0_attn_w_q_nope, L0_attn_w_q_rope,
                 L0_attn_q_g, L0_attn_w_ckv, L0_attn_w_k_nope, L0_attn_w_v,
                 L0_attn_kv_g, L0_attn_w_k_rope, L0_attn_w_proj, L0_ffn_w1,
                 L0_ffn_w2):
    x = np.ascontiguousarray(np.asarray(L0_attn_x, dtype=np.float32))
    f32 = lambda z: np.ascontiguousarray(np.asarray(z, dtype=np.float32))
    wqr = _perm_rope_cols(f32(L0_attn_w_q_rope), H)
    wkr = _perm_rope_cols(f32(L0_attn_w_k_rope), 1)
    wpr = np.ascontiguousarray(
        f32(L0_attn_w_proj).reshape(8, 128, D).transpose(1, 0, 2).reshape(128, 8 * D))
    w1T = np.ascontiguousarray(f32(L0_ffn_w1).T)
    w2T = np.ascontiguousarray(f32(L0_ffn_w2).T)
    base = dict(
        w_cq=f32(L0_attn_w_cq), w_qn=f32(L0_attn_w_q_nope), w_qr=wqr,
        q_g=np.ascontiguousarray(f32(L0_attn_q_g).reshape(6, 128).T),
        w_ckv=f32(L0_attn_w_ckv),
        w_kn=f32(L0_attn_w_k_nope), w_v=f32(L0_attn_w_v),
        kv_g=np.ascontiguousarray(f32(L0_attn_kv_g).reshape(2, 128).T),
        w_kr=wkr, w_pr=wpr,
        w1T=w1T, w2T=w2T, **_consts())
    return base, x


def kernel(**inputs):
    base, x = prepare_base(**inputs)
    nc = build_program()
    in_maps = []
    for c in range(8):
        m = dict(base)
        m["x_in"] = x[2 * c:2 * c + 2]
        in_maps.append(m)
    res = run_bass_kernel_spmd(nc, in_maps, core_ids=list(range(8)))
    out = np.empty((16, S, D), dtype=np.float32)
    for c in range(8):
        out[2 * c:2 * c + 2] = res.results[c]["out"]
    return out
